# version 09
# baseline (speedup 1.0000x reference)
import math
from contextlib import ExitStack

import numpy as np

import concourse.bass as bass
import concourse.mybir as mybir
from concourse.bass_utils import run_bass_kernel_spmd

F32 = mybir.dt.float32
BF16 = mybir.dt.bfloat16
I32 = mybir.dt.int32
AX = mybir.AxisListType
OP = mybir.AluOpType
AF = mybir.ActivationFunctionType

N_CORES = 8
T = 4096
D = 1024
NT = T // 128



class Sem:
    __slots__ = ("h", "cnt", "name", "dma_sem_q")

    def __init__(self, h, name):
        self.h = h
        self.cnt = 0
        self.name = name
        self.dma_sem_q = None


class EngineQ:

    def __init__(self, k, eng, name, dma_queue=False):
        self.k = k
        self.e = eng
        self.name = name
        self.sem = k.new_sem("sq_" + name)
        self.waited = {}
        self.dma_queue = dma_queue
        if dma_queue:
            self.pool = [k.new_sem(f"dq_{name}{i}") for i in range(8)]
            self.rr = 0

    def _wait(self, deps):
        for t in deps:
            if t is None:
                continue
            sem, val = t
            if sem is self.sem:
                val = sem.cnt
            if self.waited.get(sem, 0) >= val:
                continue
            self.e.wait_ge(sem.h, val)
            self.waited[sem] = val

    def op(self, name, *args, deps=(), inc=True, **kw):
        self._wait(deps)
        ins = getattr(self.e, name)(*args, **kw)
        if inc:
            self.sem.cnt += 1
            ins.then_inc(self.sem.h, 1)
            return (self.sem, self.sem.cnt)
        return None

    def dma(self, out, in_, deps=(), **kw):
        s = self.pool[self.rr % len(self.pool)]
        self.rr += 1
        self._wait(list(deps) + [(s, s.cnt)])
        ins = self.e.dma_start(out=out, in_=in_, **kw)
        s.cnt += 16
        ins.then_inc(s.h, 16)
        return (s, s.cnt)

    def wait(self, deps):
        self._wait(deps)


class Buf:

    __slots__ = ("ap", "wtoks", "rtoks")

    def __init__(self, ap):
        self.ap = ap
        self.wtoks = []
        self.rtoks = []

    @property
    def wtok(self):
        return self.wtoks[-1] if self.wtoks else None

    def rdeps(self):
        return list(self.wtoks)

    def wdeps(self):
        return list(self.rtoks) + list(self.wtoks)

    def did_read(self, tok):
        if tok is not None:
            self.rtoks.append(tok)

    def did_write(self, tok):
        self.wtoks = [tok] if tok is not None else []
        self.rtoks = []

    def did_write_acc(self, tok):
        if tok is not None:
            self.wtoks.append(tok)

    def __getitem__(self, key):
        return self.ap[key]


def rd(*bufs):
    deps = []
    for b in bufs:
        deps += b.rdeps()
    return deps


class K:

    def __init__(self, nc, ctx):
        self.nc = nc
        self.ctx = ctx
        self.sems = []
        self.tensor = EngineQ(self, nc.tensor, "pe")
        self.vector = EngineQ(self, nc.vector, "dve")
        self.scalar = EngineQ(self, nc.scalar, "act")
        self.gpsimd = EngineQ(self, nc.gpsimd, "pool", dma_queue=True)
        self.sync = EngineQ(self, nc.sync, "sp", dma_queue=True)
        self._uid = 0

    def new_sem(self, name):
        h = self.ctx.enter_context(self.nc.semaphore(name))
        s = Sem(h, name)
        self.sems.append(s)
        return s

    def uid(self):
        self._uid += 1
        return self._uid

    def sbuf(self, name, shape, dt=F32):
        t = self.ctx.enter_context(self.nc.sbuf_tensor(name, shape, dt))
        return Buf(t.ap() if hasattr(t, "ap") else t)

    def psum(self, name, shape, dt=F32):
        t = self.ctx.enter_context(self.nc.psum_tensor(name, shape, dt))
        return Buf(t.ap() if hasattr(t, "ap") else t)

    def copy(self, eng, dst_buf, dst_ap, src_buf, src_ap, **kw):
        tok = eng.op("tensor_copy", dst_ap, src_ap,
                     deps=rd(src_buf) + dst_buf.wdeps(), **kw)
        dst_buf.did_write(tok)
        src_buf.did_read(tok)
        return tok



def mm_group(k, out_buf, out_ap, pairs, deps_extra=()):
    deps = list(deps_extra) + out_buf.wdeps()
    seen = set()
    for lb, _, rbuf, _ in pairs:
        for b in (lb, rbuf):
            if id(b) not in seen:
                seen.add(id(b))
                deps += b.rdeps()
    n = len(pairs)
    tok = None
    for i, (lb, la, rbuf, ra) in enumerate(pairs):
        tok = k.tensor.op(
            "matmul", out_ap, la, ra,
            deps=deps if i == 0 else (),
            inc=(i == n - 1),
            start=(i == 0), stop=(i == n - 1),
        )
    out_buf.did_write(tok)
    for lb, _, rbuf, _ in pairs:
        lb.did_read(tok)
        rbuf.did_read(tok)
    return tok



C1 = np.float32(1.5 * 2.0**23)
C2 = np.float32(1.5 * 2.0**52)
EPS5 = np.float32(1e-5)


class Prelude:

    def __init__(self, k):
        nc = k.nc
        ident_np = np.eye(128, dtype=np.float32)
        self.ident_dram = nc.inline_tensor(ident_np, name="c_ident").ap()
        self.ident = k.sbuf("sb_ident", [128, 128], F32)
        import ml_dtypes
        self.identbf_dram = nc.inline_tensor(
            ident_np.astype(ml_dtypes.bfloat16), name="c_identbf").ap()
        self.ident_bf = k.sbuf("sb_identbf", [128, 128], BF16)
        triu_np = np.triu(np.ones((128, 128), dtype=np.float32))
        self.triu_dram = nc.inline_tensor(triu_np, name="c_triu").ap()
        self.triu = k.sbuf("sb_triu", [128, 128], F32)
        self.j = k.sbuf("sb_j", [128, 1], F32)
        self.ones_col = k.sbuf("sb_ones", [128, 8], F32)
        self.consts = k.sbuf("sb_consts", [128, 8], F32)
        self.eps5 = self.consts.ap[:, 0:1]
        self.half = self.consts.ap[:, 1:2]
        self.zero = self.consts.ap[:, 2:3]
        self.one = self.consts.ap[:, 3:4]
        self.c_eighth = self.consts.ap[:, 4:5]
        self.c_nshift = self.consts.ap[:, 5:6]

    def emit(self, k):
        v = k.vector
        t1 = k.sync.dma(self.ident.ap[:], self.ident_dram[:])
        self.ident.did_write(t1)
        t2 = k.sync.dma(self.triu.ap[:], self.triu_dram[:])
        self.triu.did_write(t2)
        t3 = k.sync.dma(self.ident_bf.ap[:], self.identbf_dram[:])
        self.ident_bf.did_write(t3)
        tok = v.op("memset", self.ones_col.ap[:], 1.0, deps=self.ones_col.wdeps())
        self.ones_col.did_write(tok)
        for col, val in [(0, 1e-5), (1, 0.5), (2, 0.0), (3, 1.0),
                         (4, 0.125), (5, -8.0)]:
            tok = v.op("memset", self.consts.ap[:, col:col + 1], float(val),
                       deps=self.consts.wdeps())
            self.consts.did_write(tok)
        tok = v.op("tensor_scalar", self.j.ap[:], self.ones_col.ap[:, 0:1],
                   float(2.0**-30), 1.0, OP.add, OP.subtract,
                   deps=rd(self.ones_col) + self.j.wdeps())
        self.j.did_write(tok)
        self.ones_col.did_read(tok)
        tok = v.op("tensor_scalar", self.j.ap[:], self.j.ap[:],
                   float(-(2.0**30)), 1.0, OP.mult, OP.add,
                   deps=self.j.wdeps())
        self.j.did_write(tok)


def rne_round(k, out_buf, out_ap, x_buf, x_ap, scale_ap, pre, scratch_a, scratch_b,
              scale_buf=None):
    v = k.vector
    sa, sb = scratch_a, scratch_b
    a_ap = sa.ap[:, : x_ap.shape[-1]] if sa.ap.shape[-1] != x_ap.shape[-1] else sa.ap[:]
    b_ap = sb.ap[:, : x_ap.shape[-1]] if sb.ap.shape[-1] != x_ap.shape[-1] else sb.ap[:]
    sdeps = rd(scale_buf) if scale_buf is not None else []
    t = v.op("tensor_scalar", a_ap, x_ap, scale_ap, float(C1), OP.mult, OP.add,
             deps=rd(x_buf, pre.j) + sa.wdeps() + sdeps)
    sa.did_write(t); x_buf.did_read(t)
    if scale_buf is not None:
        scale_buf.did_read(t)
    t = v.op("tensor_scalar", a_ap, a_ap, float(C1), pre.j.ap[:], OP.subtract, OP.mult,
             deps=sa.wdeps())
    sa.did_write(t); pre.j.did_read(t)
    t = v.op("tensor_scalar", b_ap, x_ap, scale_ap, float(C2), OP.mult, OP.add,
             deps=rd(x_buf) + sb.wdeps() + sdeps)
    sb.did_write(t); x_buf.did_read(t)
    if scale_buf is not None:
        scale_buf.did_read(t)
    t = v.op("scalar_tensor_tensor", out_ap, b_ap, float(C2), a_ap,
             OP.subtract, OP.add,
             deps=rd(sa, sb) + out_buf.wdeps())
    out_buf.did_write(t); sa.did_read(t); sb.did_read(t)
    return t


def emit_ln(k, x_src_ap, x_src_buf, xn_tm, stats, scratch, t0, pre=None):
    v, s, sy = k.vector, k.scalar, k.sync
    tok = sy.dma(xn_tm.ap[:].rearrange("p (a d) -> p a d", a=4),
                 x_src_ap[t0:t0 + 512, :].rearrange("(a p) d -> p a d", p=128),
                 deps=xn_tm.wdeps() + rd(x_src_buf))
    xn_tm.did_write(tok)
    x_src_buf.did_read(tok)
    for j in range(4):
        xs = xn_tm.ap[:, j * 1024:(j + 1) * 1024]
        t = v.op("reduce_sum", stats.ap[:, j:j + 1], xs, AX.X,
                 deps=rd(xn_tm) + stats.wdeps())
        stats.did_write(t); xn_tm.did_read(t)
    t = v.op("tensor_scalar_mul", stats.ap[:, 0:4], stats.ap[:, 0:4], 1.0 / 1024.0,
             deps=stats.wdeps())
    stats.did_write(t)
    for j in range(4):
        xs = xn_tm.ap[:, j * 1024:(j + 1) * 1024]
        t = v.op("tensor_scalar", xs, xs, stats.ap[:, j:j + 1], None,
                 OP.subtract, deps=rd(stats) + xn_tm.wdeps())
        xn_tm.did_write(t); stats.did_read(t)
        t = s.op("activation", scratch.ap[:, j * 1024:(j + 1) * 1024], xs,
                 AF.Square, accum_out=stats.ap[:, 8 + j:9 + j],
                 deps=rd(xn_tm) + scratch.wdeps() + stats.wdeps())
        stats.did_write(t); scratch.did_write(t); xn_tm.did_read(t)
    t = v.op("tensor_scalar_mul", stats.ap[:, 8:12], stats.ap[:, 8:12],
             1.0 / 1024.0, deps=stats.wdeps())
    stats.did_write(t)
    t = v.op("tensor_scalar_add", stats.ap[:, 8:12], stats.ap[:, 8:12],
             float(EPS5), deps=stats.wdeps())
    stats.did_write(t)
    t = s.op("activation", stats.ap[:, 12:16], stats.ap[:, 8:12], AF.Sqrt,
             deps=stats.wdeps())
    stats.did_write(t)
    t = v.op("reciprocal", stats.ap[:, 12:16], stats.ap[:, 12:16], deps=stats.wdeps())
    stats.did_write(t)
    t = v.op("tensor_tensor", stats.ap[:, 4:8], stats.ap[:, 12:16],
             stats.ap[:, 12:16], OP.mult, deps=stats.wdeps())
    stats.did_write(t)
    t = v.op("tensor_tensor", stats.ap[:, 4:8], stats.ap[:, 4:8],
             stats.ap[:, 8:12], OP.mult, deps=stats.wdeps())
    stats.did_write(t)
    t = v.op("tensor_scalar", stats.ap[:, 4:8], stats.ap[:, 4:8], -0.5, 1.5,
             OP.mult, OP.add, deps=stats.wdeps())
    stats.did_write(t)
    t = v.op("tensor_tensor", stats.ap[:, 12:16], stats.ap[:, 12:16],
             stats.ap[:, 4:8], OP.mult, deps=stats.wdeps())
    stats.did_write(t)
    for j in range(4):
        xs = xn_tm.ap[:, j * 1024:(j + 1) * 1024]
        t = v.op("tensor_scalar", xs, xs, stats.ap[:, 12 + j:13 + j], None,
                 OP.mult, deps=rd(stats) + xn_tm.wdeps())
        xn_tm.did_write(t); stats.did_read(t)
    return xn_tm


def emit_tm_to_fm(k, src_tm, dst_fm, pre, psA, psB, n_sub=4, width=1024,
                  src_bf16=False):
    te, s = k.tensor, k.scalar
    nchunk = width // 128
    ntok = n_sub * 128
    ps = [psA, psB]
    grp = 0
    pairs = [(c, j) for c in range(nchunk) for j in range(n_sub)]
    for g0 in range(0, len(pairs), 4):
        bank = ps[grp % 2]
        grp += 1
        gtoks = []
        wd = bank.wdeps() + rd(src_tm, pre.ident)
        ident_ap = pre.ident_bf.ap[:] if src_bf16 else pre.ident.ap[:]
        bank_view = bank.ap.bitcast(BF16) if src_bf16 else bank.ap
        for i, (c, j) in enumerate(pairs[g0:g0 + 4]):
            src = src_tm.ap[:, j * width + c * 128: j * width + (c + 1) * 128]
            t = k.tensor.op("transpose", bank_view[:, i * 128:(i + 1) * 128], src,
                            ident_ap,
                            deps=wd if i == 0 else (), inc=(i == len(pairs[g0:g0 + 4]) - 1))
        bank.did_write(t)
        src_tm.did_read(t)
        (pre.ident_bf if src_bf16 else pre.ident).did_read(t)
        for i, (c, j) in enumerate(pairs[g0:g0 + 4]):
            dst = dst_fm.ap[:, c * ntok + j * 128: c * ntok + (j + 1) * 128]
            tt = s.op("activation", dst, bank_view[:, i * 128:(i + 1) * 128], AF.Copy,
                      deps=rd(bank) + dst_fm.wdeps())
            dst_fm.did_write(tt)
            bank.did_read(tt)
    return dst_fm



def emit_actquant(k, xn_tm, n_sub, width, small, small_col, xq_f32, pre,
                  scr_a, scr_b, out_scale_col, sw_b, clip1=False):
    v = k.vector
    sc0 = small_col
    dq0 = small_col + n_sub
    src3 = xn_tm.ap[:, :n_sub * width].rearrange("p (a w) -> p a w", a=n_sub)
    t = v.op("reduce_max", small.ap[:, sc0:sc0 + n_sub], src3, AX.X,
             apply_absolute_value=True, deps=rd(xn_tm) + small.wdeps())
    small.did_write(t); xn_tm.did_read(t)
    t = v.op("tensor_scalar_max", small.ap[:, sc0:sc0 + n_sub],
             small.ap[:, sc0:sc0 + n_sub], float(EPS5), deps=small.wdeps())
    small.did_write(t)
    if sw_b is not None:
        t = v.op("scalar_tensor_tensor", small.ap[:, dq0:dq0 + n_sub],
                 small.ap[:, sc0:sc0 + n_sub], float(1.0 / 127.0),
                 sw_b.ap[:, 0:1].broadcast_to([128, n_sub]),
                 OP.mult, OP.mult, deps=small.wdeps() + rd(sw_b))
        small.did_write(t); sw_b.did_read(t)
    else:
        t = v.op("tensor_scalar_mul", small.ap[:, dq0:dq0 + n_sub],
                 small.ap[:, sc0:sc0 + n_sub], float(1.0 / 127.0),
                 deps=small.wdeps())
        small.did_write(t)
    t = v.op("reciprocal", small.ap[:, sc0:sc0 + n_sub],
             small.ap[:, sc0:sc0 + n_sub], deps=small.wdeps())
    small.did_write(t)
    t = v.op("tensor_scalar_mul", small.ap[:, sc0:sc0 + n_sub],
             small.ap[:, sc0:sc0 + n_sub], 127.0, deps=small.wdeps())
    small.did_write(t)
    for j in range(n_sub):
        for q0 in range(0, width, 1024):
            qw = min(1024, width - q0)
            off = j * width + q0
            rne_round(k, xq_f32, xq_f32.ap[:, off:off + qw],
                      xn_tm, xn_tm.ap[:, off:off + qw],
                      small.ap[:, sc0 + j:sc0 + j + 1], pre, scr_a, scr_b,
                      scale_buf=small)
    return dq0


def emit_wquant_bitlin(k, w_dram, n_in, n_out, wq_dst, wq_off, pre,
                       scr_w, scr_a, scr_b, small, sw_b, ones_row, ps):
    v, s, sy = k.vector, k.scalar, k.sync
    nch = n_in // 128
    acc = small.ap[:, 14:15]
    tmp = small.ap[:, 13:14]
    t = v.op("memset", acc, 0.0, deps=small.wdeps())
    small.did_write(t)
    for c in range(nch):
        t = sy.dma(scr_w.ap[:, :n_out], w_dram[c * 128:(c + 1) * 128, :],
                   deps=scr_w.wdeps())
        scr_w.did_write(t)
        t = v.op("reduce_sum", tmp, scr_w.ap[:, :n_out], AX.X,
                 apply_absolute_value=True, deps=rd(scr_w) + small.wdeps())
        small.did_write(t); scr_w.did_read(t)
        t = v.op("tensor_tensor", acc, acc, tmp, OP.add, deps=small.wdeps())
        small.did_write(t)
    t = k.gpsimd.op("tensor_reduce", small.ap[0:1, 12:13], acc, AX.C, OP.add,
                    deps=small.wdeps())
    small.did_write(t)
    t = v.op("tensor_scalar_mul", small.ap[0:1, 12:13], small.ap[0:1, 12:13],
             float(1.0 / (n_in * n_out)), deps=small.wdeps())
    small.did_write(t)
    t = v.op("tensor_scalar_max", small.ap[0:1, 12:13], small.ap[0:1, 12:13],
             float(EPS5), deps=small.wdeps())
    small.did_write(t)
    tk = k.tensor.op("matmul", ps.ap[:, 0:1], ones_row.ap[0:1, :],
                     small.ap[0:1, 12:13], start=True, stop=True,
                     deps=rd(ones_row, small) + ps.wdeps())
    ps.did_write(tk); ones_row.did_read(tk); small.did_read(tk)
    t = v.op("tensor_copy", sw_b.ap[:, 0:1], ps.ap[:, 0:1],
             deps=rd(ps) + sw_b.wdeps())
    sw_b.did_write(t); ps.did_read(t)
    t = v.op("reciprocal", sw_b.ap[:, 1:2], sw_b.ap[:, 0:1], deps=sw_b.wdeps())
    sw_b.did_write(t)
    for c in range(nch):
        t = sy.dma(scr_w.ap[:, :n_out], w_dram[c * 128:(c + 1) * 128, :],
                   deps=scr_w.wdeps())
        scr_w.did_write(t)
        for j in range(0, n_out, 1024):
            w = min(1024, n_out - j)
            rne_round(k, scr_w, scr_w.ap[:, j:j + w], scr_w,
                      scr_w.ap[:, j:j + w], sw_b.ap[:, 1:2], pre, scr_a, scr_b,
                      scale_buf=sw_b)
        dst = wq_dst.ap.bitcast(BF16)[:, wq_off + c * n_out: wq_off + (c + 1) * n_out]
        t = v.op("tensor_scalar", dst, scr_w.ap[:, :n_out], -1.0, 1.0,
                 OP.max, OP.min, deps=rd(scr_w) + wq_dst.wdeps())
        wq_dst.did_write(t); scr_w.did_read(t)
    return



def emit_bitlin_mm(k, xq_fm_ap, xq_fm_buf, wq_ap_fn, wq_buf, n_k, psum, j,
                   o, owidth):
    pairs = []
    for c in range(n_k):
        pairs.append((xq_fm_buf, xq_fm_ap(c, j), wq_buf, wq_ap_fn(c, o)))
    return mm_group(k, psum, psum.ap[:, :owidth], pairs)


def emit_ffn_bitlin(k, B, pre, x_src_ap, x_src_buf, g_dram, g_buf,
                    w_drams, variant, hq_dram, hq_buf, dqh_dram, dqh_buf):
    v, s, sy = k.vector, k.scalar, k.sync
    WRES, G0, G1 = B["WRES"], B["G0"], B["G1"]
    B["SCRW"].wtoks = list(G1.wtoks)
    B["SCRW"].rtoks = list(G1.rtoks)
    H0, SMALL, SM2 = B["H0"], B["SMALL"], B["SMALL2"]
    SWA, SWB, SWC = B["SWA"], B["SWB"], B["SWC"]

    if variant == "gelu":
        wlist = [("w1", SWA, 0)]
    else:
        wlist = [("wg", SWA, 0), ("wu", SWB, 32768)]
    for nm, swb, off in wlist:
        emit_wquant_bitlin(k, w_drams[nm], 1024, 4096, WRES, off, pre,
                           B["SCRW"], B["SA"], B["SB"], SMALL, swb,
                           B["ONESROW"], B["PS4"])
    alias_sync(G1, B["SCRW"])

    def xq_ap(c, j):
        return H0.ap[:, c * 512 + j * 128: c * 512 + (j + 1) * 128]

    def wq_ap(off):
        def f(c, o):
            return WRES.ap[:, off + c * 4096 + o * 512: off + c * 4096 + (o + 1) * 512]
        return f

    for t0 in range(0, T, 512):
        emit_ln(k, x_src_ap, x_src_buf, G0, B["STATS"], G1, t0, pre)
        emit_actquant(k, G0, 4, 1024, SMALL, 0, G1, pre, B["SA"], B["SB"],
                      4, None)
        tok = v.op("scalar_tensor_tensor", SM2.ap[:, 0:4], SMALL.ap[:, 4:8],
                   1.0, SWA.ap[:, 0:1].broadcast_to([128, 4]), OP.mult,
                   OP.mult, deps=rd(SMALL, SWA) + SM2.wdeps())
        SM2.did_write(tok); SMALL.did_read(tok); SWA.did_read(tok)
        if variant == "swiglu":
            tok = v.op("scalar_tensor_tensor", SM2.ap[:, 4:8],
                       SMALL.ap[:, 4:8], 1.0,
                       SWB.ap[:, 0:1].broadcast_to([128, 4]), OP.mult,
                       OP.mult, deps=rd(SMALL, SWB) + SM2.wdeps())
            SM2.did_write(tok); SMALL.did_read(tok); SWB.did_read(tok)
        xq_fm = sub_buf(H0, 0, 4096)
        emit_tm_to_fm(k, G1, xq_fm, pre, B["PS2"], B["PS3"])
        merge_sub(H0, xq_fm)
        for j in range(4):
            for o in range(8):
                ps = B["PS0"] if o % 2 == 0 else B["PS1"]
                emit_bitlin_mm(k, xq_ap, H0, wq_ap(0), WRES, 8, ps, j, o, 512)
                fn = AF.Gelu if variant == "gelu" else AF.Silu
                tok = s.op("activation", G1.ap[:, o * 512:(o + 1) * 512],
                           ps.ap[:, 0:512], fn, scale=SM2.ap[:, j:j + 1],
                           deps=rd(ps, SM2) + G1.wdeps())
                G1.did_write(tok); ps.did_read(tok); SM2.did_read(tok)
            if variant == "swiglu":
                for o in range(8):
                    ps = B["PS0"] if o % 2 == 0 else B["PS1"]
                    emit_bitlin_mm(k, xq_ap, H0, wq_ap(32768), WRES, 8, ps,
                                   j, o, 512)
                    tok = s.op("activation", G0.ap[:, o * 512:(o + 1) * 512],
                               ps.ap[:, 0:512], AF.Copy,
                               scale=SM2.ap[:, 4 + j:5 + j],
                               deps=rd(ps, SM2) + G0.wdeps())
                    G0.did_write(tok); ps.did_read(tok); SM2.did_read(tok)
                tok = v.op("tensor_tensor", G1.ap[:], G1.ap[:], G0.ap[:],
                           OP.mult, deps=rd(G0) + G1.wdeps())
                G1.did_write(tok); G0.did_read(tok)
            emit_actquant(k, G1, 1, 4096, SMALL, 8, G0, pre, B["SA"],
                          B["SB"], 1, None)
            tok = v.op("tensor_copy", B["HQ16"].ap[:, 0:4096], G0.ap[:],
                       deps=rd(G0) + B["HQ16"].wdeps())
            B["HQ16"].did_write(tok); G0.did_read(tok)
            r0 = t0 + j * 128
            tok = sy.dma(hq_dram[r0:r0 + 128, :], B["HQ16"].ap[:, 0:4096],
                         deps=rd(B["HQ16"]) + hq_buf.wdeps())
            hq_buf.did_write(tok); B["HQ16"].did_read(tok)
            col = t0 // 128 + j
            tok = v.op("tensor_copy", B["DQH"].ap[:, col:col + 1],
                       SMALL.ap[:, 9:10], deps=rd(SMALL) + B["DQH"].wdeps())
            B["DQH"].did_write(tok); SMALL.did_read(tok)

    w2 = "w2" if variant == "gelu" else "wd"
    B["SCRW"].wtoks = list(G1.wtoks)
    B["SCRW"].rtoks = list(G1.rtoks)
    emit_wquant_bitlin(k, w_drams[w2], 4096, 1024, WRES, 0, pre,
                       B["SCRW"], B["SA"], B["SB"], SMALL, SWC,
                       B["ONESROW"], B["PS4"])
    alias_sync(G1, B["SCRW"])
    for t0 in range(0, T, 512):
        c0 = t0 // 128
        tok = v.op("scalar_tensor_tensor", SM2.ap[:, 8:12],
                   B["DQH"].ap[:, c0:c0 + 4], 1.0,
                   SWC.ap[:, 0:1].broadcast_to([128, 4]), OP.mult,
                   OP.mult, deps=rd(SWC, B["DQH"]) + SM2.wdeps())
        SM2.did_write(tok); SWC.did_read(tok); B["DQH"].did_read(tok)
        for j in range(4):
            r0 = t0 + j * 128
            tok = sy.dma(B["HQ16"].ap[:, 0:4096], hq_dram[r0:r0 + 128, :],
                         deps=B["HQ16"].wdeps() + rd(hq_buf))
            B["HQ16"].did_write(tok); hq_buf.did_read(tok)
            hq_fm = sub_buf(H0, 4096, 4096)
            emit_tm_to_fm(k, B["HQ16"], hq_fm, pre, B["PS2"], B["PS3"],
                          n_sub=1, width=4096, src_bf16=True)
            merge_sub(H0, hq_fm)
            for o2 in range(2):
                ps = B["PS0"] if o2 == 0 else B["PS1"]
                pairs = []
                for c in range(32):
                    pairs.append((H0,
                                  H0.ap[:, 4096 + c * 128: 4096 + (c + 1) * 128],
                                  WRES,
                                  WRES.ap[:, c * 1024 + o2 * 512:
                                          c * 1024 + (o2 + 1) * 512]))
                mm_group(k, ps, ps.ap[:, 0:512], pairs)
                tok = s.op("activation", G1.ap[:, o2 * 512:(o2 + 1) * 512],
                           ps.ap[:, 0:512], AF.Silu,
                           scale=SM2.ap[:, 8 + j:9 + j],
                           deps=rd(ps, SM2) + G1.wdeps())
                G1.did_write(tok); ps.did_read(tok); SM2.did_read(tok)
            tok = sy.dma(g_dram[r0:r0 + 128, :], G1.ap[:, 0:1024],
                         deps=rd(G1) + g_buf.wdeps())
            g_buf.did_write(tok); G1.did_read(tok)


def alias_sync(dst, src_alias):
    dst.wtoks = list(src_alias.wtoks)
    dst.rtoks = list(src_alias.rtoks)


def sub_buf(parent, off, n):
    b = Buf(parent.ap[:, off:off + n])
    b.wtoks = list(parent.wtoks)
    b.rtoks = list(parent.rtoks)
    return b


def merge_sub(parent, child):
    parent.wtoks = list(set(parent.wtoks + child.wtoks))
    parent.rtoks = list(set(parent.rtoks + child.rtoks))


def make_buffers(k):
    B = {}
    B["WRES"] = k.sbuf("WRES", [128, 65536], BF16)
    B["G0"] = k.sbuf("G0", [128, 4096], F32)
    B["G1"] = k.sbuf("G1", [128, 4096], F32)
    B["H0"] = k.sbuf("H0", [128, 8192], BF16)
    B["HQ16"] = k.sbuf("HQ16", [128, 4096], BF16)
    B["SCRW"] = Buf(B["G1"].ap[:])
    B["SA"] = k.sbuf("SA", [128, 1024], F32)
    B["SB"] = k.sbuf("SB", [128, 1024], F32)
    B["SC"] = k.sbuf("SC", [128, 1024], F32)
    B["BLK"] = k.sbuf("BLK", [128, 1024], F32)
    B["SMALL"] = k.sbuf("SMALL", [128, 64], F32)
    B["SMALL2"] = k.sbuf("SMALL2", [128, 64], F32)
    B["STATS"] = k.sbuf("STATS", [128, 16], F32)
    B["SWA"] = k.sbuf("SWA", [128, 2], F32)
    B["SWB"] = k.sbuf("SWB", [128, 2], F32)
    B["SWC"] = k.sbuf("SWC", [128, 2], F32)
    B["ONESROW"] = k.sbuf("ONESROW", [1, 128], F32)
    B["DQH"] = k.sbuf("DQH", [128, 32], F32)
    B["XT"] = Buf(B["H0"].ap.bitcast(F32))
    B["BIAS1"] = Buf(B["HQ16"].ap.bitcast(F32)[:, 0:1024])
    B["BIAS2"] = Buf(B["HQ16"].ap.bitcast(F32)[:, 1024:2048])
    for i in range(8):
        B[f"PS{i}"] = k.psum(f"PS{i}", [128, 512], F32)
    return B


def emit_buffer_init(k, B):
    tok = k.vector.op("memset", B["ONESROW"].ap[:], 1.0,
                      deps=B["ONESROW"].wdeps())
    B["ONESROW"].did_write(tok)



def emit_moe(k, B, pre, x_src_ap, x_src_buf, g_dram, g_buf, w):
    v, s, sy = k.vector, k.scalar, k.sync
    W = Buf(B["WRES"].ap.bitcast(F32))
    G0, G1 = B["G0"], B["G1"]
    SMALL, SM2 = B["SMALL"], B["SMALL2"]
    HFM = Buf(B["H0"].ap.bitcast(F32))

    def wload(dst_off, src_ap, nch, ncol):
        wdeps = W.wdeps()
        first = True
        for c in range(nch):
            t = sy.dma(W.ap[:, dst_off + c * ncol: dst_off + (c + 1) * ncol],
                       src_ap[c * 128:(c + 1) * 128, :], deps=wdeps)
            if first:
                W.did_write(t)
                first = False
            else:
                W.did_write_acc(t)
        return

    wload(0, w["Ws1"], 8, 512)
    wload(4096, w["Ws2"], 4, 1024)
    wload(8192, w["Wr"], 8, 8)
    EX0 = 8448

    def exp_off(e):
        return EX0 + (e % 2) * 8192

    wload(exp_off(0), w["W1"][0], 8, 512)
    wload(exp_off(0) + 4096, w["W2"][0], 4, 1024)

    for t0 in range(0, T, 512):
        emit_ln(k, x_src_ap, x_src_buf, G0, B["STATS"], G1, t0, pre)
        xn_fm = sub_buf(G1, 0, 4096)
        emit_tm_to_fm(k, G0, xn_fm, pre, B["PS2"], B["PS3"])
        alias_sync(G1, xn_fm)

        def xn_ap(c, j):
            return G1.ap[:, c * 512 + j * 128: c * 512 + (j + 1) * 128]

        def xn_mv(c):
            return G1.ap[:, c * 512:(c + 1) * 512]

        for j in range(4):
            ps = B["PS4"]
            pairs = [(G1, xn_ap(c, j), W, W.ap[:, 8192 + c * 8: 8192 + (c + 1) * 8])
                     for c in range(8)]
            mm_group(k, ps, ps.ap[:, 0:8], pairs)
            t = v.op("max", SMALL.ap[:, 0:8], ps.ap[:, 0:8],
                     deps=rd(ps) + SMALL.wdeps())
            SMALL.did_write(t); ps.did_read(t)
            t = v.op("tensor_tensor", SMALL.ap[:, 8:9], SMALL.ap[:, 1:2],
                     SMALL.ap[:, 0:1], OP.subtract, deps=SMALL.wdeps())
            SMALL.did_write(t)
            t = s.op("activation", SMALL.ap[:, 9:10], SMALL.ap[:, 8:9], AF.Exp,
                     deps=SMALL.wdeps())
            SMALL.did_write(t)
            t = v.op("tensor_scalar_add", SMALL.ap[:, 9:10], SMALL.ap[:, 9:10],
                     1.0, deps=SMALL.wdeps())
            SMALL.did_write(t)
            t = v.op("reciprocal", SMALL.ap[:, 9:10], SMALL.ap[:, 9:10],
                     deps=SMALL.wdeps())
            SMALL.did_write(t)
            t = v.op("tensor_single_scalar", SMALL.ap[:, 9:10],
                     SMALL.ap[:, 9:10], 1.0, OP.is_equal, deps=SMALL.wdeps())
            SMALL.did_write(t)
            t = v.op("tensor_scalar", SM2.ap[:, 8 * j:8 * j + 8], ps.ap[:, 0:8],
                     SMALL.ap[:, 0:1], SMALL.ap[:, 9:10], OP.is_equal, OP.mult,
                     deps=rd(ps, SMALL) + SM2.wdeps())
            SM2.did_write(t); ps.did_read(t); SMALL.did_read(t)

        for e in range(-1, 8):
            if 0 <= e < 7:
                wload(exp_off(e + 1), w["W1"][e + 1], 8, 512)
                wload(exp_off(e + 1) + 4096, w["W2"][e + 1], 4, 1024)
            w1off = 0 if e < 0 else exp_off(e)
            w2off = 4096 if e < 0 else exp_off(e) + 4096
            for f in range(4):
                ps = B["PS0"] if f % 2 == 0 else B["PS1"]
                pairs = [(W, W.ap[:, w1off + c * 512 + f * 128:
                               w1off + c * 512 + (f + 1) * 128],
                          G1, xn_mv(c)) for c in range(8)]
                mm_group(k, ps, ps.ap[:, 0:512], pairs)
                hslice = HFM.ap[:, f * 512:(f + 1) * 512]
                t = s.op("activation", hslice, ps.ap[:, 0:512], AF.Relu,
                         deps=rd(ps) + HFM.wdeps())
                HFM.did_write(t); ps.did_read(t)
                t = v.op("tensor_tensor", hslice, hslice, ps.ap[:, 0:512],
                         OP.mult, deps=rd(ps) + HFM.wdeps())
                HFM.did_write(t); ps.did_read(t)
            for j in range(4):
                for o2 in range(2):
                    ps = B["PS0"] if o2 == 0 else B["PS1"]
                    pairs = [(HFM, HFM.ap[:, f * 512 + j * 128:
                                  f * 512 + (j + 1) * 128],
                              W, W.ap[:, w2off + f * 1024 + o2 * 512:
                                      w2off + f * 1024 + (o2 + 1) * 512])
                             for f in range(4)]
                    mm_group(k, ps, ps.ap[:, 0:512], pairs)
                    dst = G0.ap[:, j * 1024 + o2 * 512: j * 1024 + (o2 + 1) * 512]
                    if e < 0:
                        t = s.op("activation", dst, ps.ap[:, 0:512], AF.Copy,
                                 deps=rd(ps) + G0.wdeps())
                        G0.did_write(t); ps.did_read(t)
                    else:
                        t = v.op("scalar_tensor_tensor", dst, ps.ap[:, 0:512],
                                 SM2.ap[:, 8 * j + e:8 * j + e + 1], dst,
                                 OP.mult, OP.add, deps=rd(ps, SM2) + G0.wdeps())
                        G0.did_write(t); ps.did_read(t); SM2.did_read(t)
        for j in range(4):
            t = s.op("activation", G0.ap[:, j * 1024:(j + 1) * 1024],
                     G0.ap[:, j * 1024:(j + 1) * 1024], AF.Silu,
                     deps=G0.wdeps())
            G0.did_write(t)
            r0 = t0 + j * 128
            t = sy.dma(g_dram[r0:r0 + 128, :],
                       G0.ap[:, j * 1024:(j + 1) * 1024],
                       deps=rd(G0) + g_buf.wdeps())
            g_buf.did_write(t); G0.did_read(t)



def emit_l1_attn(k, B, pre, x_in_ap, x_in_buf, x_out_ap, x_out_buf,
                 g_dram, g_buf, w):
    v, s, sy = k.vector, k.scalar, k.sync
    W = Buf(B["WRES"].ap.bitcast(F32))
    G0, G1 = B["G0"], B["G1"]
    QO, KO, VO, PO, OO = 0, 8192, 16384, 24704, 26752

    BQ = B["SMALL2"]
    tok = sy.dma(BQ.ap[:, 0:24], w["b_qkv"].rearrange("(c p) -> p c", p=128),
                 deps=BQ.wdeps(), allow_slow_non_contiguous=True)
    BQ.did_write(tok)
    tok = sy.dma(B["BIAS1"].ap[:, 0:1024],
                 w["b_out"][None, :].partition_broadcast(128),
                 deps=B["BIAS1"].wdeps())
    B["BIAS1"].did_write(tok)
    tok = sy.dma(B["BIAS2"].ap[:, 0:1024],
                 w["b_qkv"][None, 2048:3072].partition_broadcast(128),
                 deps=B["BIAS2"].wdeps())
    B["BIAS2"].did_write(tok)
    qb = [sub_buf(W, QO, 4096), sub_buf(W, QO + 4096, 4096)]
    kb = [sub_buf(W, KO, 4096), sub_buf(W, KO + 4096, 4096)]
    vb = [sub_buf(W, VO, 4160), sub_buf(W, VO + 4160, 4160)]
    pb = [sub_buf(W, PO + tj * 512, 512) for tj in range(4)]
    ob = sub_buf(W, OO, 4096)
    for sl in range(2):
        dst = vb[sl].ap[:].rearrange(
            "p (a h o) -> p a h o", a=4, h=16)[:, :, :, 64:65]
        tok = v.op("memset", dst, 1.0, deps=vb[sl].wdeps())
        vb[sl].did_write(tok)

    def stage_w(dram_ap, col0, ncol, dstb):
        wdeps = dstb.wdeps()
        first = True
        for c in range(8):
            t = sy.dma(dstb.ap[:, c * ncol:(c + 1) * ncol],
                       dram_ap[c * 128:(c + 1) * 128, col0:col0 + ncol],
                       deps=wdeps)
            if first:
                dstb.did_write(t)
                first = False
            else:
                dstb.did_write_acc(t)

    def compute_kv(i_blk, slot, with_q):
        t0 = i_blk * 512
        emit_ln(k, x_in_ap, x_in_buf, G0, B["STATS"], G1, t0, pre)
        xn_fm = sub_buf(G1, 0, 4096)
        emit_tm_to_fm(k, G0, xn_fm, pre, B["PS2"], B["PS3"])
        alias_sync(G1, xn_fm)
        proj_list = [("k", 1024, kb[slot])] + \
            ([("q", 0, qb[slot])] if with_q else [])
        sgi = [0]
        for nm, wcol0, dstb in proj_list:
            for half in range(2):
                sgb = G0 if sgi[0] % 2 == 0 else B["XT"]
                sgi[0] += 1
                stage_w(w["W_qkv"], wcol0 + half * 512, 512, sgb)
                for hp in range(4):
                    ps = B["PS0"] if hp % 2 == 0 else B["PS1"]
                    pairs = [(sgb, sgb.ap[:, c * 512 + hp * 128:
                                  c * 512 + hp * 128 + 128],
                              G1, G1.ap[:, c * 512:(c + 1) * 512])
                             for c in range(8)]
                    mm_group(k, ps, ps.ap[:, 0:512], pairs)
                    bias_col = (wcol0 + half * 512 + hp * 128) // 128
                    dst = dstb.ap[:, (half * 4 + hp) * 512:
                                  (half * 4 + hp + 1) * 512]
                    t = s.op("activation", dst, ps.ap[:, 0:512], AF.Identity,
                             bias=BQ.ap[:, bias_col:bias_col + 1],
                             deps=rd(ps, BQ) + dstb.wdeps())
                    if half == 0 and hp == 0:
                        dstb.did_write(t)
                    else:
                        dstb.did_write_acc(t)
                    ps.did_read(t); BQ.did_read(t)
        for half in range(2):
            sgb = G0 if sgi[0] % 2 == 0 else B["XT"]
            sgi[0] += 1
            stage_w(w["W_qkv"], 2048 + half * 512, 512, sgb)
            for j in range(4):
                ps = B["PS0"] if j % 2 == 0 else B["PS1"]
                pairs = [(G1, G1.ap[:, c * 512 + j * 128:
                              c * 512 + j * 128 + 128],
                          sgb, sgb.ap[:, c * 512:(c + 1) * 512])
                         for c in range(8)]
                mm_group(k, ps, ps.ap[:, 0:512], pairs)
                base = j * 1040 + half * 520
                dst = vb[slot].ap[:, base:base + 520].rearrange(
                    "p (h o) -> p h o", h=8)[:, :, 0:64]
                t = v.op("scalar_tensor_tensor", dst, ps.ap[:, 0:512], 1.0,
                         B["BIAS2"].ap[:, half * 512:(half + 1) * 512]
                         .rearrange("p (h o) -> p h o", h=8),
                         OP.mult, OP.add, deps=rd(ps, B["BIAS2"]) + vb[slot].wdeps())
                vb[slot].did_write_acc(t)
                ps.did_read(t); B["BIAS2"].did_read(t)

    def attend(j):
        qslot = j % 2
        for h in range(16):
            hp, hh = h // 2, h % 2
            q_ap = qb[qslot].ap[64 * hh:64 * hh + 64,
                                hp * 512:(hp + 1) * 512]
            for tgt in range(2):
                kslot = (j + tgt) % 2
                psO = B["PS2"] if (2 * h + tgt) % 2 == 0 else B["PS3"]
                for tj in range(4):
                    ps_s = B["PS0"] if tj % 2 == 0 else B["PS1"]
                    lhs = kb[kslot].ap[64 * hh:64 * hh + 64,
                                       hp * 512 + tj * 128:
                                       hp * 512 + tj * 128 + 128]
                    tok = k.tensor.op("matmul", ps_s.ap[:, 0:512], lhs, q_ap,
                                      start=True, stop=True,
                                      deps=rd(kb[kslot], qb[qslot]) + ps_s.wdeps())
                    ps_s.did_write(tok)
                    kb[kslot].did_read(tok); qb[qslot].did_read(tok)
                    t = s.op("activation", pb[tj].ap[:], ps_s.ap[:, 0:512],
                             AF.Exp, scale=pre.c_eighth, bias=pre.c_nshift,
                             deps=rd(ps_s, pre.consts) + pb[tj].wdeps())
                    pb[tj].did_write(t); ps_s.did_read(t)
                    pre.consts.did_read(t)
                    vap = vb[kslot].ap[:, tj * 1040 + h * 65:
                                       tj * 1040 + (h + 1) * 65]
                    tok = k.tensor.op("matmul", psO.ap[0:65, 0:512], vap,
                                      pb[tj].ap[:],
                                      start=(tj == 0), stop=(tj == 3),
                                      deps=rd(vb[kslot], pb[tj]) +
                                      (psO.wdeps() if tj == 0 else []),
                                      inc=True)
                    vb[kslot].did_read(tok); pb[tj].did_read(tok)
                    if tj == 3:
                        psO.did_write(tok)
                t = v.op("reciprocal", B["SA"].ap[0:1, 0:512],
                         psO.ap[64:65, 0:512], deps=rd(psO) + B["SA"].wdeps())
                B["SA"].did_write(t); psO.did_read(t)
                tok = k.tensor.op("matmul", B["PS4"].ap[0:64, 0:512],
                                  B["ONESROW"].ap[0:1, 0:64],
                                  B["SA"].ap[0:1, 0:512], start=True, stop=True,
                                  deps=rd(B["ONESROW"], B["SA"]) + B["PS4"].wdeps())
                B["PS4"].did_write(tok)
                B["ONESROW"].did_read(tok); B["SA"].did_read(tok)
                t = s.op("activation", B["SB"].ap[0:64, 0:512],
                         B["PS4"].ap[0:64, 0:512], AF.Copy,
                         deps=rd(B["PS4"]) + B["SB"].wdeps())
                B["SB"].did_write(t); B["PS4"].did_read(t)
                if tgt == 0:
                    t = v.op("tensor_tensor", B["SB"].ap[0:64, 512:1024],
                             psO.ap[0:64, 0:512], B["SB"].ap[0:64, 0:512],
                             OP.mult, deps=rd(psO) + B["SB"].wdeps())
                    B["SB"].did_write(t); psO.did_read(t)
                else:
                    t = v.op("tensor_tensor", B["SA"].ap[0:64, 0:512],
                             psO.ap[0:64, 0:512], B["SB"].ap[0:64, 0:512],
                             OP.mult, deps=rd(psO) + B["SA"].wdeps() + rd(B["SB"]))
                    B["SA"].did_write(t); psO.did_read(t); B["SB"].did_read(t)
                    t = v.op("tensor_tensor", B["SB"].ap[0:64, 512:1024],
                             B["SB"].ap[0:64, 512:1024],
                             B["SA"].ap[0:64, 0:512], OP.add,
                             deps=rd(B["SA"]) + B["SB"].wdeps())
                    B["SB"].did_write(t); B["SA"].did_read(t)
                    odst = ob.ap[64 * hh:64 * hh + 64,
                                 hp * 512:(hp + 1) * 512]
                    t = s.op("activation", odst, B["SB"].ap[0:64, 512:1024],
                             AF.Copy, deps=rd(B["SB"]) + ob.wdeps())
                    if h == 0:
                        ob.did_write(t)
                    else:
                        ob.did_write_acc(t)
                    B["SB"].did_read(t)
        for half in range(2):
            sgb = G0 if half == 0 else B["XT"]
            stage_w(w["W_out"], half * 512, 512, sgb)
            for sub in range(4):
                ps = B["PS5"] if sub % 2 == 0 else B["PS6"]
                pairs = [(ob, ob.ap[:, c * 512 + sub * 128:
                              c * 512 + sub * 128 + 128],
                          sgb, sgb.ap[:, c * 512:(c + 1) * 512])
                         for c in range(8)]
                mm_group(k, ps, ps.ap[:, 0:512], pairs)
                dst = G1.ap[:, sub * 1024 + half * 512:
                            sub * 1024 + (half + 1) * 512]
                t = v.op("scalar_tensor_tensor", dst, ps.ap[:, 0:512], 1.0,
                         B["BIAS1"].ap[:, half * 512:(half + 1) * 512],
                         OP.mult, OP.add, deps=rd(ps, B["BIAS1"]) + G1.wdeps())
                G1.did_write(t); ps.did_read(t); B["BIAS1"].did_read(t)
        t0 = j * 512
        tok = sy.dma(B["SC"].ap[:, 0:512].rearrange("p (a d) -> p a d", a=1),
                     x_in_ap[t0:t0 + 128, 0:512].rearrange("(a p) d -> p a d", p=128),
                     deps=B["SC"].wdeps() + rd(x_in_buf)) if False else None
        tok = sy.dma(B["XT"].ap[:].rearrange("p (a d) -> p a d", a=4),
                     x_in_ap[t0:t0 + 512, :].rearrange("(a p) d -> p a d", p=128),
                     deps=B["XT"].wdeps() + rd(x_in_buf))
        B["XT"].did_write(tok); x_in_buf.did_read(tok)
        tok = sy.dma(G0.ap[:].rearrange("p (a d) -> p a d", a=4),
                     g_dram[t0:t0 + 512, :].rearrange("(a p) d -> p a d", p=128),
                     deps=G0.wdeps() + rd(g_buf))
        G0.did_write(tok); g_buf.did_read(tok)
        t = v.op("tensor_tensor", G1.ap[:], G1.ap[:], G0.ap[:], OP.mult,
                 deps=rd(G0) + G1.wdeps())
        G1.did_write(t); G0.did_read(t)
        t = v.op("tensor_tensor", G1.ap[:], G1.ap[:], B["XT"].ap[:], OP.add,
                 deps=rd(B["XT"]) + G1.wdeps())
        G1.did_write(t); B["XT"].did_read(t)
        tok = sy.dma(x_out_ap[t0:t0 + 512, :].rearrange("(a p) d -> p a d", p=128),
                     G1.ap[:].rearrange("p (a d) -> p a d", a=4),
                     deps=rd(G1) + x_out_buf.wdeps())
        x_out_buf.did_write(tok); G1.did_read(tok)

    for i in range(9):
        blk = i if i < 8 else 0
        compute_kv(blk, i % 2, with_q=(i < 8))
        if i >= 1:
            attend(i - 1)


def barrier(k):
    engs = [k.tensor, k.vector, k.scalar, k.gpsimd, k.sync]
    toks = [(e.sem, e.sem.cnt) for e in engs]
    for e in engs:
        if getattr(e, "dma_queue", False):
            toks += [(s, s.cnt) for s in e.pool]
    for e in engs:
        e.wait(toks)



PI = float(np.pi)
TWO_PI_HI = float(np.float32(6.28125))
TWO_PI_LO = float(np.float32(2 * np.pi) - np.float32(6.28125))
INV_2PI = float(np.float32(1.0 / (2 * np.pi)))


def emit_rope_tables(k, B, pre, rope_buf):
    v, s, g = k.vector, k.scalar, k.gpsimd
    SA, SB, SM = B["SA"], B["SB"], B["SMALL"]

    t = g.op("iota", SA.ap[0:1, 0:32].bitcast(I32), pattern=[[1, 32]],
             channel_multiplier=0, deps=SA.wdeps())
    SA.did_write(t)
    t = v.op("tensor_copy", SA.ap[0:1, 32:64], SA.ap[0:1, 0:32].bitcast(I32),
             deps=SA.wdeps())
    SA.did_write(t)
    t = s.op("activation", SA.ap[0:1, 0:32], SA.ap[0:1, 32:64], AF.Exp,
             scale=float(-np.log(10000.0) / 32.0), deps=SA.wdeps())
    SA.did_write(t)
    ps = B["PS4"]
    tok = k.tensor.op("matmul", ps.ap[:, 0:32], B["ONESROW"].ap[0:1, :],
                      SA.ap[0:1, 0:32], start=True, stop=True,
                      deps=rd(B["ONESROW"], SA) + ps.wdeps())
    ps.did_write(tok); B["ONESROW"].did_read(tok); SA.did_read(tok)
    th = Buf(SM.ap[:, 32:64])
    t = v.op("tensor_copy", th.ap, ps.ap[:, 0:32], deps=rd(ps) + SM.wdeps())
    th.did_write(t); SM.did_write(t); ps.did_read(t)
    t = g.op("iota", SM.ap[:, 0:1].bitcast(I32), pattern=[[0, 1]],
             channel_multiplier=1, deps=SM.wdeps())
    SM.did_write(t)
    t = v.op("tensor_copy", SM.ap[:, 1:2], SM.ap[:, 0:1].bitcast(I32),
             deps=SM.wdeps())
    SM.did_write(t)

    def sincos(dst_ap, base_is_cos, sub):
        t = v.op("tensor_scalar_add", SM.ap[:, 2:3], SM.ap[:, 1:2],
                 float(128 * sub), deps=SM.wdeps())
        SM.did_write(t)
        t = v.op("tensor_scalar", SA.ap[:, 0:32], th.ap, SM.ap[:, 2:3],
                 (PI / 2.0) if base_is_cos else 0.0, OP.mult, OP.add,
                 deps=rd(th) + SA.wdeps() + SM.rdeps())
        SA.did_write(t); th.did_read(t)
        t = v.op("tensor_scalar", SB.ap[:, 32:64], SA.ap[:, 0:32],
                 float(INV_2PI), -0.5, OP.mult, OP.add, deps=rd(SA) + SB.wdeps())
        SB.did_write(t)
        t = v.op("tensor_scalar", SA.ap[:, 32:64], SB.ap[:, 32:64], float(C1),
                 float(C1), OP.add, OP.subtract, deps=rd(SB) + SA.wdeps())
        SA.did_write(t)
        t = v.op("tensor_scalar", SA.ap[:, 32:64], SA.ap[:, 32:64],
                 pre.j.ap[:], None, OP.mult, deps=SA.wdeps() + rd(pre.j))
        SA.did_write(t); pre.j.did_read(t)
        t = v.op("tensor_scalar", SB.ap[:, 0:32], SB.ap[:, 32:64], float(C2),
                 float(C2), OP.add, OP.subtract, deps=SB.wdeps())
        SB.did_write(t)
        t = v.op("tensor_tensor", SB.ap[:, 0:32], SB.ap[:, 0:32],
                 SA.ap[:, 32:64], OP.add, deps=SB.wdeps() + rd(SA))
        SB.did_write(t)
        t = v.op("scalar_tensor_tensor", SA.ap[:, 0:32], SB.ap[:, 0:32],
                 float(-TWO_PI_HI), SA.ap[:, 0:32], OP.mult, OP.add,
                 deps=rd(SB) + SA.wdeps())
        SA.did_write(t); SB.did_read(t)
        t = v.op("scalar_tensor_tensor", SA.ap[:, 0:32], SB.ap[:, 0:32],
                 float(-TWO_PI_LO), SA.ap[:, 0:32], OP.mult, OP.add,
                 deps=rd(SB) + SA.wdeps())
        SA.did_write(t); SB.did_read(t)
        t = v.op("tensor_scalar", SA.ap[:, 0:32], SA.ap[:, 0:32], float(-PI),
                 None, OP.add, deps=SA.wdeps())
        SA.did_write(t)
        t = v.op("tensor_scalar", SA.ap[:, 0:32], SA.ap[:, 0:32], float(-PI),
                 float(PI), OP.max, OP.min, deps=SA.wdeps())
        SA.did_write(t)
        t = s.op("activation", SA.ap[:, 32:64], SA.ap[:, 0:32], AF.Sin,
                 deps=SA.wdeps())
        SA.did_write(t)
        t = v.op("tensor_scalar_mul", dst_ap, SA.ap[:, 32:64], -1.0,
                 deps=rd(SA) + rope_buf.wdeps())
        rope_buf.did_write(t); SA.did_read(t)

    for sub in range(32):
        sincos(rope_buf.ap[:, sub * 64:sub * 64 + 32], True, sub)
        sincos(rope_buf.ap[:, sub * 64 + 32:sub * 64 + 64], False, sub)


def emit_rope_apply(k, B, pre, q_tm, rope_buf, sub, nheads=16):
    v = k.vector
    SA, SB = B["SA"], B["SB"]
    n = nheads * 32
    q3 = q_tm.ap[:, 0:nheads * 64].rearrange("p (h i two) -> p h i two", h=nheads, two=2)
    qe, qo = q3[:, :, :, 0], q3[:, :, :, 1]
    cosb = rope_buf.ap[:, sub * 64:sub * 64 + 32].unsqueeze(1).broadcast_to([128, nheads, 32])
    sinb = rope_buf.ap[:, sub * 64 + 32:sub * 64 + 64].unsqueeze(1).broadcast_to([128, nheads, 32])
    sa3 = SA.ap[:, 0:n].rearrange("p (h i) -> p h i", h=nheads)
    sa3b = SA.ap[:, n:2 * n].rearrange("p (h i) -> p h i", h=nheads)
    sb3 = SB.ap[:, 0:n].rearrange("p (h i) -> p h i", h=nheads)
    t = v.op("tensor_copy", sa3, qe, deps=rd(q_tm) + SA.wdeps())
    SA.did_write(t); q_tm.did_read(t)
    t = v.op("tensor_copy", sb3, qo, deps=rd(q_tm) + SB.wdeps())
    SB.did_write(t); q_tm.did_read(t)
    t = v.op("tensor_tensor", qe, sa3, cosb, OP.mult,
             deps=rd(SA, rope_buf) + q_tm.wdeps())
    q_tm.did_write(t); SA.did_read(t); rope_buf.did_read(t)
    t = v.op("tensor_tensor", sa3b, sb3, sinb, OP.mult,
             deps=rd(SB, rope_buf) + SA.wdeps())
    SA.did_write(t); SB.did_read(t); rope_buf.did_read(t)
    t = v.op("tensor_tensor", qe, qe, sa3b, OP.subtract,
             deps=rd(SA) + q_tm.wdeps())
    q_tm.did_write(t); SA.did_read(t)
    t = v.op("tensor_tensor", qo, sb3, cosb, OP.mult,
             deps=rd(SB, rope_buf) + q_tm.wdeps())
    q_tm.did_write(t); SB.did_read(t); rope_buf.did_read(t)
    t = v.op("tensor_tensor", sb3, sa3, sinb, OP.mult,
             deps=rd(SA, rope_buf) + SB.wdeps())
    SB.did_write(t); SA.did_read(t); rope_buf.did_read(t)
    t = v.op("tensor_tensor", qo, qo, sb3, OP.add,
             deps=rd(SB) + q_tm.wdeps())
    q_tm.did_write(t); SB.did_read(t)


def emit_rne_plain(k, out_buf, out_ap, x_buf, x_ap, pre, bias=0.0):
    v = k.vector
    sa, sb = k._rne_sa, k._rne_sb
    w = x_ap.shape[-1]
    t = v.op("tensor_scalar", sa.ap[:, :w], x_ap, float(bias) + float(C1), None,
             OP.add, deps=rd(x_buf) + sa.wdeps())
    sa.did_write(t); x_buf.did_read(t)
    t = v.op("tensor_scalar", sa.ap[:, :w], sa.ap[:, :w], float(C1),
             pre.j.ap[:], OP.subtract, OP.mult, deps=sa.wdeps() + rd(pre.j))
    sa.did_write(t); pre.j.did_read(t)
    t = v.op("tensor_scalar", sb.ap[:, :w], x_ap, float(bias) + float(C2), None,
             OP.add, deps=rd(x_buf) + sb.wdeps())
    sb.did_write(t); x_buf.did_read(t)
    t = v.op("scalar_tensor_tensor", out_ap, sb.ap[:, :w], float(C2),
             sa.ap[:, :w], OP.subtract, OP.add,
             deps=rd(sa, sb) + out_buf.wdeps())
    out_buf.did_write(t); sa.did_read(t); sb.did_read(t)
    return t


def emit_l3_attn(k, B, pre, x_in_ap, x_in_buf, x_out_ap, x_out_buf,
                 g_dram, g_buf, w, scr):
    v, s, sy = k.vector, k.scalar, k.sync
    W = Buf(B["WRES"].ap.bitcast(F32))
    G0, G1 = B["G0"], B["G1"]

    tok = sy.dma(B["BIAS1"].ap[:, 0:1024],
                 w["b_attn"][None, 0:1024].partition_broadcast(128),
                 deps=B["BIAS1"].wdeps())
    B["BIAS1"].did_write(tok)
    tok = sy.dma(B["BIAS2"].ap[:, 0:1024],
                 w["b_attn"][None, 1024:2048].partition_broadcast(128),
                 deps=B["BIAS2"].wdeps())
    B["BIAS2"].did_write(tok)
    VB_OFF = 19520
    tok = sy.dma(W.ap[:, VB_OFF:VB_OFF + 1024],
                 w["b_attn"][None, 2048:3072].partition_broadcast(128),
                 deps=W.wdeps())
    W.did_write(tok)

    def stage_w(dram_ap, col0, ncol):
        wdeps = W.wdeps()
        first = True
        for c in range(8):
            t = sy.dma(W.ap[:, c * ncol:(c + 1) * ncol],
                       dram_ap[c * 128:(c + 1) * 128, col0:col0 + ncol],
                       deps=wdeps)
            if first:
                W.did_write(t)
                first = False
            else:
                W.did_write_acc(t)

    QTM, KTM, VTM = 6144, 10240, 14336
    VB = 19520
    STG8 = 20544
    for i in range(8):
        t0 = i * 512
        emit_ln(k, x_in_ap, x_in_buf, G0, B["STATS"], G1, t0, pre)
        xn_fm = sub_buf(G1, 0, 4096)
        emit_tm_to_fm(k, G0, xn_fm, pre, B["PS2"], B["PS3"])
        alias_sync(G1, xn_fm)
        for proj, dsto in (("q", QTM), ("k", KTM), ("v", VTM)):
            wcol0 = {"q": 0, "k": 1024, "v": 2048}[proj]
            for half in range(2):
                stage_w(w["W_attn"], wcol0 + half * 512, 512)
                for j in range(4):
                    ps = B["PS0"] if j % 2 == 0 else B["PS1"]
                    pairs = [(G1, G1.ap[:, c * 512 + j * 128:
                                  c * 512 + j * 128 + 128],
                              W, W.ap[:, c * 512:(c + 1) * 512])
                             for c in range(8)]
                    mm_group(k, ps, ps.ap[:, 0:512], pairs)
                    if proj != "v":
                        bt = B["BIAS1"] if proj == "q" else B["BIAS2"]
                        dst = W.ap[:, dsto + j * 1024 + half * 512:
                                   dsto + j * 1024 + (half + 1) * 512]
                        t = v.op("scalar_tensor_tensor", dst, ps.ap[:, 0:512],
                                 1.0, bt.ap[:, half * 512:(half + 1) * 512],
                                 OP.mult, OP.add, deps=rd(ps, bt) + W.wdeps())
                        W.did_write(t); ps.did_read(t); bt.did_read(t)
                    else:
                        base = VTM + j * 1040 + half * 520
                        dst = W.ap[:, base:base + 520].rearrange(
                            "p (h o) -> p h o", h=8)[:, :, 0:64]
                        t = v.op("scalar_tensor_tensor", dst, ps.ap[:, 0:512],
                                 1.0,
                                 W.ap[:, VB + half * 512:VB + (half + 1) * 512]
                                 .rearrange("p (h o) -> p h o", h=8),
                                 OP.mult, OP.add, deps=rd(ps) + W.wdeps())
                        W.did_write(t); ps.did_read(t)
            for j in range(4):
                sub = 4 * i + j
                if proj == "v":
                    ones_dst = W.ap[:, VTM + j * 1040:VTM + (j + 1) * 1040].rearrange(
                        "p (h o) -> p h o", h=16)[:, :, 64:65]
                    t = v.op("memset", ones_dst, 1.0, deps=W.wdeps())
                    W.did_write(t)
                    tok = sy.dma(scr["v"].ap[sub],
                                 W.ap[:, VTM + j * 1040:VTM + (j + 1) * 1040],
                                 deps=rd(W) + scr["v"].wdeps())
                    scr["v"].did_write(tok); W.did_read(tok)
                else:
                    qk_tm = sub_buf(W, dsto + j * 1024, 1024)
                    stg = sub_buf(W, STG8, 1024)
                    emit_tm_to_fm(k, qk_tm, stg, pre, B["PS2"],
                                  B["PS3"], n_sub=1, width=1024)
                    merge_sub(W, stg)
                    WB = B["WRES"].ap
                    SPH, SPL = 43136, 44160
                    t = v.op("tensor_copy", WB[:, SPH:SPH + 1024],
                             W.ap[:, STG8:STG8 + 1024],
                             deps=rd(W) + W.wdeps())
                    W.did_write(t)
                    t = v.op("tensor_tensor", WB[:, SPL:SPL + 1024],
                             W.ap[:, STG8:STG8 + 1024], WB[:, SPH:SPH + 1024],
                             OP.subtract, deps=W.wdeps())
                    W.did_write(t)
                    dname = "q" if proj == "q" else "kk"
                    for comp, off in ((0, SPH), (1, SPL)):
                        tok = sy.dma(
                            scr[dname].ap[comp][:, :, sub * 128:(sub + 1) * 128]
                            .rearrange("c p t -> p c t"),
                            WB[:, off:off + 1024].rearrange(
                                "p (c t) -> p c t", c=8),
                            deps=rd(W) + scr[dname].wdeps())
                        scr[dname].did_write_acc(tok); W.did_read(tok)

    QF, KF, VT2, OACC = 4096, 8192, 9216, 10768
    hgroups = [list(range(0, 7)), list(range(7, 14)), [14, 15]]
    KFB, VTB, P2 = 14864, 15888, 16928
    qfb = sub_buf(W, QF, 4096)
    kfb = [sub_buf(W, KF, 1024), sub_buf(W, KFB, 1024)]
    vtb = [sub_buf(W, VT2, 1040), sub_buf(W, VTB, 1040)]
    pb = [sub_buf(W, 10256, 512), sub_buf(W, P2, 512)]

    WB = B["WRES"].ap
    KB_BF = [16384, 29728]
    QF_BF = 8192

    def k_dma(tj):
        b = kfb[tj % 2]
        base = KB_BF[tj % 2]
        wdeps = b.wdeps()
        for comp in range(2):
            tok = sy.dma(WB[:, base + comp * 1024:base + (comp + 1) * 1024]
                         .rearrange("p (c t) -> p c t", c=8),
                         scr["kk"].ap[comp][:, :, tj * 128:(tj + 1) * 128]
                         .rearrange("c p t -> p c t"),
                         deps=wdeps + rd(scr["kk"]))
            if comp == 0:
                b.did_write(tok)
            else:
                b.did_write_acc(tok)
            scr["kk"].did_read(tok)
        b2 = vtb[tj % 2]
        tok = sy.dma(b2.ap[:], scr["v"].ap[tj], deps=b2.wdeps() + rd(scr["v"]))
        b2.did_write(tok); scr["v"].did_read(tok)

    for qt in range(8):
        t0 = qt * 512
        qdeps = qfb.wdeps()
        for comp in range(2):
            tok = sy.dma(WB[:, QF_BF + comp * 4096:QF_BF + (comp + 1) * 4096]
                         .rearrange("p (c t) -> p c t", c=8),
                         scr["q"].ap[comp][:, :, t0:t0 + 512]
                         .rearrange("c p t -> p c t"),
                         deps=qdeps + rd(scr["q"]))
            if comp == 0:
                qfb.did_write(tok)
            else:
                qfb.did_write_acc(tok)
            scr["q"].did_read(tok)
        ntj = 4 * qt + 4
        hgroups = [[0, 1, 2, 3, 4, 5], [6, 7, 8, 9, 10, 11], [12, 13, 14, 15]]
        for hg in hgroups:
            obanks = {h: B[f"PS{2 + idx}"] for idx, h in enumerate(hg)}
            k_dma(0)
            for tj in range(ntj):
                if tj + 1 < ntj:
                    k_dma(tj + 1)
                kb, vb = kfb[tj % 2], vtb[tj % 2]
                r = tj - 4 * qt
                for hi, h in enumerate(hg):
                    hp, hh = h // 2, h % 2
                    psS = B["PS0"] if hi % 2 == 0 else B["PS1"]
                    pp = pb[hi % 2]
                    kbase = KB_BF[tj % 2]
                    khi = WB[64 * hh:64 * hh + 64,
                             kbase + hp * 128:kbase + hp * 128 + 128]
                    klo = WB[64 * hh:64 * hh + 64,
                             kbase + 1024 + hp * 128:kbase + 1024 + hp * 128 + 128]
                    qhi = WB[64 * hh:64 * hh + 64,
                             QF_BF + hp * 512:QF_BF + (hp + 1) * 512]
                    qlo = WB[64 * hh:64 * hh + 64,
                             QF_BF + 4096 + hp * 512:QF_BF + 4096 + (hp + 1) * 512]
                    tok = k.tensor.op("matmul", psS.ap[:, 0:512], khi, qhi,
                                      start=True, stop=False,
                                      deps=rd(kb, qfb) + psS.wdeps())
                    tok = k.tensor.op("matmul", psS.ap[:, 0:512], khi, qlo,
                                      start=False, stop=False)
                    tok = k.tensor.op("matmul", psS.ap[:, 0:512], klo, qhi,
                                      start=False, stop=True)
                    psS.did_write(tok); kb.did_read(tok); qfb.did_read(tok)
                    if r < 0:
                        t = s.op("activation", pp.ap[:], psS.ap[:, 0:512],
                                 AF.Exp, scale=pre.c_eighth, bias=pre.c_nshift,
                                 deps=rd(psS, pre.consts) + pp.wdeps())
                        pp.did_write(t); psS.did_read(t)
                        pre.consts.did_read(t)
                    else:
                        for sb_i in range(4):
                            pdst = pp.ap[:, sb_i * 128:(sb_i + 1) * 128]
                            pssl = psS.ap[:, sb_i * 128:(sb_i + 1) * 128]
                            if sb_i > r:
                                t = s.op("activation", pdst, pssl, AF.Exp,
                                         scale=pre.c_eighth, bias=pre.c_nshift,
                                         deps=rd(psS, pre.consts) + pp.wdeps())
                                pp.did_write_acc(t); psS.did_read(t)
                                pre.consts.did_read(t)
                            elif sb_i == r:
                                t = s.op("activation", pdst, pssl, AF.Exp,
                                         scale=pre.c_eighth, bias=pre.c_nshift,
                                         deps=rd(psS, pre.consts) + pp.wdeps())
                                pp.did_write_acc(t); psS.did_read(t)
                                pre.consts.did_read(t)
                                t = v.op("tensor_tensor", pdst, pdst,
                                         pre.triu.ap[:], OP.mult,
                                         deps=rd(pre.triu) + pp.wdeps())
                                pp.did_write_acc(t); pre.triu.did_read(t)
                            else:
                                t = v.op("memset", pdst, 0.0, deps=pp.wdeps())
                                pp.did_write_acc(t)
                    vap = vb.ap[:, h * 65:(h + 1) * 65]
                    tok = k.tensor.op("matmul", obanks[h].ap[0:65, 0:512], vap,
                                      pp.ap[:],
                                      start=(tj == 0), stop=(tj == ntj - 1),
                                      deps=rd(vb, pp) + (obanks[h].wdeps() if tj == 0 else []),
                                      inc=True)
                    vb.did_read(tok); pp.did_read(tok)
                    if tj == ntj - 1:
                        obanks[h].did_write(tok)
            for h in hg:
                hp, hh = h // 2, h % 2
                psO = obanks[h]
                t = v.op("reciprocal", B["SA"].ap[0:1, 0:512],
                         psO.ap[64:65, 0:512], deps=rd(psO) + B["SA"].wdeps())
                B["SA"].did_write(t); psO.did_read(t)
                tok = k.tensor.op("matmul", B["PS0"].ap[0:64, 0:512],
                                  B["ONESROW"].ap[0:1, 0:64],
                                  B["SA"].ap[0:1, 0:512], start=True, stop=True,
                                  deps=rd(B["ONESROW"], B["SA"]) + B["PS0"].wdeps())
                B["PS0"].did_write(tok)
                B["ONESROW"].did_read(tok); B["SA"].did_read(tok)
                t = s.op("activation", B["SB"].ap[0:64, 0:512],
                         B["PS0"].ap[0:64, 0:512], AF.Copy,
                         deps=rd(B["PS0"]) + B["SB"].wdeps())
                B["SB"].did_write(t); B["PS0"].did_read(t)
                t = v.op("tensor_tensor", B["SB"].ap[0:64, 512:1024],
                         psO.ap[0:64, 0:512], B["SB"].ap[0:64, 0:512],
                         OP.mult, deps=rd(psO) + B["SB"].wdeps())
                B["SB"].did_write(t); psO.did_read(t)
                odst = W.ap[64 * hh:64 * hh + 64,
                            OACC + hp * 512:OACC + (hp + 1) * 512]
                t = s.op("activation", odst, B["SB"].ap[0:64, 512:1024],
                         AF.Copy, deps=rd(B["SB"]) + W.wdeps())
                W.did_write(t); B["SB"].did_read(t)
        tok = sy.dma(B["BIAS1"].ap[:, 0:1024],
                     w["b_proj"][None, :].partition_broadcast(128),
                     deps=B["BIAS1"].wdeps())
        B["BIAS1"].did_write(tok)
        for half in range(2):
            stage_w(w["W_proj"], half * 512, 512)
            for sub in range(4):
                ps = B["PS0"]
                pairs = [(W, W.ap[:, OACC + c * 512 + sub * 128:
                              OACC + c * 512 + sub * 128 + 128],
                          W, W.ap[:, c * 512:(c + 1) * 512])
                         for c in range(8)]
                mm_group(k, ps, ps.ap[:, 0:512], pairs)
                dst = G1.ap[:, sub * 1024 + half * 512:
                            sub * 1024 + (half + 1) * 512]
                t = v.op("scalar_tensor_tensor", dst, ps.ap[:, 0:512], 1.0,
                         B["BIAS1"].ap[:, half * 512:(half + 1) * 512],
                         OP.mult, OP.add, deps=rd(ps, B["BIAS1"]) + G1.wdeps())
                G1.did_write(t); ps.did_read(t); B["BIAS1"].did_read(t)
        tok = sy.dma(G0.ap[:].rearrange("p (a d) -> p a d", a=4),
                     x_in_ap[t0:t0 + 512, :].rearrange("(a p) d -> p a d", p=128),
                     deps=G0.wdeps() + rd(x_in_buf))
        G0.did_write(tok); x_in_buf.did_read(tok)
        tok = sy.dma(B["XT"].ap[:].rearrange("p (a d) -> p a d", a=4),
                     g_dram[t0:t0 + 512, :].rearrange("(a p) d -> p a d", p=128),
                     deps=B["XT"].wdeps() + rd(g_buf))
        B["XT"].did_write(tok); g_buf.did_read(tok)
        t = v.op("tensor_tensor", G1.ap[:], G1.ap[:], B["XT"].ap[:], OP.mult,
                 deps=rd(B["XT"]) + G1.wdeps())
        G1.did_write(t); B["XT"].did_read(t)
        t = v.op("tensor_tensor", G1.ap[:], G1.ap[:], G0.ap[:], OP.add,
                 deps=rd(G0) + G1.wdeps())
        G1.did_write(t); G0.did_read(t)
        tok = sy.dma(x_out_ap[t0:t0 + 512, :].rearrange("(a p) d -> p a d", p=128),
                     G1.ap[:].rearrange("p (a d) -> p a d", a=4),
                     deps=rd(G1) + x_out_buf.wdeps())
        x_out_buf.did_write(tok); G1.did_read(tok)



def emit_nvfp4_quant(k, B, pre, x_buf, x_ap, out_buf, out_ap, width,
                     gs6_pp, rgs_pp, gs_buf, blkbuf, blk_off):
    v, s, g = k.vector, k.scalar, k.gpsimd
    nb = width // 16
    am = blkbuf.ap[:, blk_off:blk_off + nb]
    sc8 = blkbuf.ap[:, blk_off + nb:blk_off + 2 * nb]
    rdq = blkbuf.ap[:, blk_off + 2 * nb:blk_off + 3 * nb]
    tmp = blkbuf.ap[:, blk_off + 3 * nb:blk_off + 4 * nb]
    t = v.op("reduce_max", am, x_ap.rearrange("p (b s) -> p b s", s=16), AX.X,
             apply_absolute_value=True, deps=rd(x_buf) + blkbuf.wdeps())
    blkbuf.did_write(t); x_buf.did_read(t)
    t = v.op("tensor_scalar", sc8, am, gs6_pp, None, OP.mult,
             deps=blkbuf.wdeps() + rd(gs_buf))
    blkbuf.did_write(t); gs_buf.did_read(t)
    t = v.op("tensor_scalar", sc8, sc8, 0.0, 448.0, OP.max, OP.min,
             deps=blkbuf.wdeps())
    blkbuf.did_write(t)
    t = v.op("memset", tmp, 0.015625, deps=blkbuf.wdeps())
    blkbuf.did_write(t)
    for thr in [0.03125, 0.0625, 0.125, 0.25, 0.5, 1.0, 2.0, 4.0, 8.0, 16.0,
                32.0, 64.0, 128.0, 256.0]:
        t = v.op("tensor_scalar", am, sc8, float(thr), 1.0, OP.is_ge, OP.add,
                 deps=blkbuf.wdeps())
        blkbuf.did_write(t)
        t = v.op("tensor_tensor", tmp, tmp, am, OP.mult, deps=blkbuf.wdeps())
        blkbuf.did_write(t)
    t = v.op("tensor_scalar_mul", am, tmp, float(1.5 * 2.0**20),
             deps=blkbuf.wdeps())
    blkbuf.did_write(t)
    t = v.op("tensor_tensor", rdq, sc8, am, OP.add, deps=blkbuf.wdeps())
    blkbuf.did_write(t)
    t = v.op("tensor_tensor", rdq, rdq, am, OP.subtract, deps=blkbuf.wdeps())
    blkbuf.did_write(t)
    t = v.op("tensor_scalar", rdq, rdq, pre.j.ap[:], None, OP.mult,
             deps=blkbuf.wdeps() + rd(pre.j))
    blkbuf.did_write(t); pre.j.did_read(t)
    t = v.op("tensor_scalar_mul", am, tmp, float(1.5 * 2.0**49),
             deps=blkbuf.wdeps())
    blkbuf.did_write(t)
    t = v.op("tensor_tensor", tmp, sc8, am, OP.add, deps=blkbuf.wdeps())
    blkbuf.did_write(t)
    t = v.op("tensor_tensor", tmp, tmp, am, OP.subtract, deps=blkbuf.wdeps())
    blkbuf.did_write(t)
    t = v.op("tensor_tensor", sc8, tmp, rdq, OP.add, deps=blkbuf.wdeps())
    blkbuf.did_write(t)
    t = v.op("tensor_scalar", tmp, sc8, rgs_pp, None, OP.mult,
             deps=blkbuf.wdeps() + rd(gs_buf))
    blkbuf.did_write(t); gs_buf.did_read(t)
    t = v.op("tensor_single_scalar", rdq, tmp, 0.0, OP.is_equal,
             deps=blkbuf.wdeps())
    blkbuf.did_write(t)
    t = v.op("tensor_tensor", tmp, tmp, rdq, OP.add, deps=blkbuf.wdeps())
    blkbuf.did_write(t)
    t = v.op("reciprocal", rdq, tmp, deps=blkbuf.wdeps())
    blkbuf.did_write(t)
    SA, SB, SC = B["SA"], B["SB"], B["SC"]
    for q0 in range(0, width, 1024):
        qn = min(1024, width - q0)
        xs = x_ap[:, q0:q0 + qn]
        sa, sb, sc = SA.ap[:, 0:qn], SB.ap[:, 0:qn], SC.ap[:, 0:qn]
        rdq_b = rdq[:, q0 // 16:(q0 + qn) // 16].unsqueeze(2).broadcast_to(
            [128, qn // 16, 16])
        sc8_b = sc8[:, q0 // 16:(q0 + qn) // 16].unsqueeze(2).broadcast_to(
            [128, qn // 16, 16])
        sa3 = sa.rearrange("p (b s) -> p b s", s=16)
        ot = out_ap[:, q0:q0 + qn]
        ot3 = ot.rearrange("p (b s) -> p b s", s=16)
        t = s.op("activation", sa, xs, AF.Abs, deps=rd(x_buf) + SA.wdeps())
        SA.did_write(t); x_buf.did_read(t)
        t = g.op("tensor_tensor", sa3, sa3, rdq_b, OP.mult,
                 deps=SA.wdeps() + rd(blkbuf))
        SA.did_write(t); blkbuf.did_read(t)
        t = v.op("tensor_scalar_min", sa, sa, 6.0, deps=SA.wdeps())
        SA.did_write(t)
        t = v.op("tensor_scalar", sb, sa, 2.0, 1.0, OP.is_ge, OP.add,
                 deps=rd(SA) + SB.wdeps())
        SB.did_write(t)
        t = v.op("tensor_scalar", ot, sa, 4.0, 1.0, OP.is_ge, OP.add,
                 deps=rd(SA) + out_buf.wdeps())
        out_buf.did_write(t)
        t = g.op("tensor_tensor", sb, sb, ot, OP.mult,
                 deps=SB.wdeps() + rd(out_buf))
        SB.did_write(t); out_buf.did_read(t)
        M32c, M64c = float(1.5 * 2.0**22), float(1.5 * 2.0**51)
        t = v.op("scalar_tensor_tensor", sc, sb, M32c, sa, OP.mult, OP.add,
                 deps=rd(SA, SB) + SC.wdeps())
        SC.did_write(t); SA.did_read(t); SB.did_read(t)
        t = v.op("scalar_tensor_tensor", sc, sb, -M32c, sc, OP.mult, OP.add,
                 deps=rd(SB) + SC.wdeps())
        SC.did_write(t); SB.did_read(t)
        t = v.op("tensor_scalar", sc, sc, pre.j.ap[:], None, OP.mult,
                 deps=SC.wdeps() + rd(pre.j))
        SC.did_write(t); pre.j.did_read(t)
        t = v.op("scalar_tensor_tensor", sa, sb, M64c, sa, OP.mult, OP.add,
                 deps=SA.wdeps() + rd(SB))
        SA.did_write(t); SB.did_read(t)
        t = v.op("scalar_tensor_tensor", sa, sb, -M64c, sa, OP.mult, OP.add,
                 deps=SA.wdeps() + rd(SB))
        SA.did_write(t); SB.did_read(t)
        t = g.op("tensor_tensor", sa, sa, sc, OP.add,
                 deps=SA.wdeps() + rd(SC))
        SA.did_write(t); SC.did_read(t)
        t = s.op("activation", sb, xs, AF.Sign, deps=rd(x_buf) + SB.wdeps())
        SB.did_write(t); x_buf.did_read(t)
        t = g.op("tensor_tensor", sa, sa, sb, OP.mult,
                 deps=SA.wdeps() + rd(SB))
        SA.did_write(t); SB.did_read(t)
        t = v.op("tensor_tensor", ot3, sa3, sc8_b, OP.mult,
                 deps=rd(SA, blkbuf) + out_buf.wdeps())
        out_buf.did_write(t); SA.did_read(t); blkbuf.did_read(t)
    return


def emit_global_absmax(k, B, w_dram, nch, ncol, dst_col):
    v, sy = k.vector, k.sync
    SMALL = B["SMALL"]
    SCR = B["SCRW"]
    SCR.wtoks = list(set(SCR.wtoks + B["G1"].wtoks))
    SCR.rtoks = list(set(SCR.rtoks + B["G1"].rtoks))
    t = v.op("memset", SMALL.ap[:, 30:31], 0.0, deps=SMALL.wdeps())
    SMALL.did_write(t)
    for c in range(nch):
        t = sy.dma(SCR.ap[:, :ncol], w_dram[c * 128:(c + 1) * 128, :],
                   deps=SCR.wdeps())
        SCR.did_write(t)
        t = v.op("reduce_max", SMALL.ap[:, 31:32], SCR.ap[:, :ncol], AX.X,
                 apply_absolute_value=True, deps=rd(SCR) + SMALL.wdeps())
        SMALL.did_write(t); SCR.did_read(t)
        t = v.op("tensor_tensor", SMALL.ap[:, 30:31], SMALL.ap[:, 30:31],
                 SMALL.ap[:, 31:32], OP.max, deps=SMALL.wdeps())
        SMALL.did_write(t)
    t = k.gpsimd.op("tensor_reduce", SMALL.ap[0:1, dst_col:dst_col + 1],
                    SMALL.ap[:, 30:31], AX.C, OP.max, deps=SMALL.wdeps())
    SMALL.did_write(t)
    B["G1"].wtoks = list(set(B["G1"].wtoks + SCR.wtoks))
    B["G1"].rtoks = list(set(B["G1"].rtoks + SCR.rtoks))
    return


def emit_bcast_scalar(k, B, src_ap, dst_ap, dst_buf, src_buf):
    v = k.vector
    ps = B["PS4"]
    tok = k.tensor.op("matmul", ps.ap[:, 0:1], B["ONESROW"].ap[0:1, :],
                      src_ap, start=True, stop=True,
                      deps=rd(B["ONESROW"], src_buf) + ps.wdeps())
    ps.did_write(tok); B["ONESROW"].did_read(tok); src_buf.did_read(tok)
    t = v.op("tensor_copy", dst_ap, ps.ap[:, 0:1], deps=rd(ps) + dst_buf.wdeps())
    dst_buf.did_write(t); ps.did_read(t)
    return t


def emit_ffn_nvfp4(k, B, pre, x_src_ap, x_src_buf, g_dram, g_buf, w,
                   hq_dram, hq_buf):
    v, s, sy, g = k.vector, k.scalar, k.sync, k.gpsimd
    W, G0, G1 = B["WRES"], B["G0"], B["G1"]
    SMALL, SM2, BLK, HQ = B["SMALL"], B["SMALL2"], B["BLK"], B["HQ16"]
    SCRW = B["SCRW"]
    B["SCRW"].wtoks = list(G1.wtoks)
    B["SCRW"].rtoks = list(G1.rtoks)

    tok = sy.dma(SMALL.ap[0:1, 20:21], w["s_in"][None, :], deps=SMALL.wdeps())
    SMALL.did_write(tok)
    tok = sy.dma(SMALL.ap[0:1, 21:22], w["s_in_down"][None, :],
                 deps=SMALL.wdeps())
    SMALL.did_write(tok)

    def scalar_gs_tiles(src_col, c6, cr):
        t = v.op("tensor_scalar_mul", SMALL.ap[0:1, 24:25],
                 SMALL.ap[0:1, src_col:src_col + 1], float(1.0 / 6.0),
                 deps=SMALL.wdeps())
        SMALL.did_write(t)
        emit_bcast_scalar(k, B, SMALL.ap[0:1, 24:25], SM2.ap[:, c6:c6 + 1],
                          SM2, SMALL)
        t = v.op("reciprocal", SMALL.ap[0:1, 24:25],
                 SMALL.ap[0:1, src_col:src_col + 1], deps=SMALL.wdeps())
        SMALL.did_write(t)
        emit_bcast_scalar(k, B, SMALL.ap[0:1, 24:25], SM2.ap[:, cr:cr + 1],
                          SM2, SMALL)

    scalar_gs_tiles(20, 40, 41)
    scalar_gs_tiles(21, 42, 43)

    def weight_gs(dram, nch, ncol, gs_col):
        emit_global_absmax(k, B, dram, nch, ncol, 25)
        t = v.op("reciprocal", SMALL.ap[0:1, gs_col:gs_col + 1],
                 SMALL.ap[0:1, 25:26], deps=SMALL.wdeps())
        SMALL.did_write(t)
        t = v.op("tensor_scalar_mul", SMALL.ap[0:1, gs_col:gs_col + 1],
                 SMALL.ap[0:1, gs_col:gs_col + 1], 2688.0,
                 deps=SMALL.wdeps())
        SMALL.did_write(t)

    def out_scale_tile(gsx_col, gsw_col, dst_col):
        t = v.op("tensor_tensor", SMALL.ap[0:1, 24:25],
                 SMALL.ap[0:1, gsx_col:gsx_col + 1],
                 SMALL.ap[0:1, gsw_col:gsw_col + 1], OP.mult,
                 deps=SMALL.wdeps())
        SMALL.did_write(t)
        t = v.op("reciprocal", SMALL.ap[0:1, 24:25], SMALL.ap[0:1, 24:25],
                 deps=SMALL.wdeps())
        SMALL.did_write(t)
        emit_bcast_scalar(k, B, SMALL.ap[0:1, 24:25], SM2.ap[:, dst_col:dst_col + 1],
                          SM2, SMALL)

    def quant_weight(dram, nch, ncol, gs_col, wq_off_bf):
        B["SCRW"].wtoks = list(G1.wtoks)
        B["SCRW"].rtoks = list(G1.rtoks)
        t = v.op("tensor_scalar_mul", SMALL.ap[0:1, 24:25],
                 SMALL.ap[0:1, gs_col:gs_col + 1], float(1.0 / 6.0),
                 deps=SMALL.wdeps())
        SMALL.did_write(t)
        emit_bcast_scalar(k, B, SMALL.ap[0:1, 24:25], SM2.ap[:, 44:45], SM2, SMALL)
        t = v.op("reciprocal", SMALL.ap[0:1, 24:25],
                 SMALL.ap[0:1, gs_col:gs_col + 1], deps=SMALL.wdeps())
        SMALL.did_write(t)
        emit_bcast_scalar(k, B, SMALL.ap[0:1, 24:25], SM2.ap[:, 45:46], SM2, SMALL)
        n_out = nch * 128
        for c in range(nch):
            tok = sy.dma(SCRW.ap[:, :ncol], dram[c * 128:(c + 1) * 128, :],
                         deps=SCRW.wdeps())
            SCRW.did_write(tok)
            emit_nvfp4_quant(k, B, pre, SCRW, SCRW.ap[:, :ncol], HQ,
                             HQ.ap[:, :ncol], ncol,
                             SM2.ap[:, 44:45], SM2.ap[:, 45:46], SM2, BLK, 0)
            for h0 in range(0, ncol // 128, 4):
                nh = min(4, ncol // 128 - h0)
                bank = B["PS2"] if (h0 // 4) % 2 == 0 else B["PS3"]
                bv = bank.ap.bitcast(BF16)
                wdeps2 = bank.wdeps() + rd(HQ, pre.ident_bf)
                tk = None
                for ii in range(nh):
                    hch = h0 + ii
                    tk = k.tensor.op(
                        "transpose", bv[:, ii * 128:(ii + 1) * 128],
                        HQ.ap[:, hch * 128:(hch + 1) * 128],
                        pre.ident_bf.ap[:],
                        deps=wdeps2 if ii == 0 else (), inc=(ii == nh - 1))
                bank.did_write(tk); HQ.did_read(tk); pre.ident_bf.did_read(tk)
                for ii in range(nh):
                    hch = h0 + ii
                    dst = W.ap[:, wq_off_bf + hch * n_out + c * 128:
                               wq_off_bf + hch * n_out + (c + 1) * 128]
                    t2 = s.op("activation", dst, bv[:, ii * 128:(ii + 1) * 128],
                              AF.Copy, deps=rd(bank) + W.wdeps())
                    W.did_write(t2); bank.did_read(t2)
        alias_sync(G1, SCRW)

    weight_gs(w["gate_w"], 32, 1024, 26)
    weight_gs(w["up_w"], 32, 1024, 27)
    out_scale_tile(20, 26, 46)
    out_scale_tile(20, 27, 47)
    quant_weight(w["gate_w"], 32, 1024, 26, 0)
    quant_weight(w["up_w"], 32, 1024, 27, 32768)
    barrier(k)

    def xq_ap(c, j):
        return B["H0"].ap[:, c * 512 + j * 128: c * 512 + (j + 1) * 128]

    def wq_ap(off):
        def f(c, o):
            return W.ap[:, off + c * 4096 + o * 512: off + c * 4096 + (o + 1) * 512]
        return f

    for t0 in range(0, T, 512):
        emit_ln(k, x_src_ap, x_src_buf, G0, B["STATS"], G1, t0, pre)
        for j in range(4):
            emit_nvfp4_quant(k, B, pre, G0, G0.ap[:, j * 1024:(j + 1) * 1024],
                             HQ, HQ.ap[:, j * 1024:(j + 1) * 1024], 1024,
                             SM2.ap[:, 40:41], SM2.ap[:, 41:42], SM2, BLK, 0)
        xq_fm = sub_buf(B["H0"], 0, 4096)
        emit_tm_to_fm(k, HQ, xq_fm, pre, B["PS2"], B["PS3"], src_bf16=True)
        merge_sub(B["H0"], xq_fm)
        for j in range(4):
            for o in range(8):
                ps = B["PS0"] if o % 2 == 0 else B["PS1"]
                emit_bitlin_mm(k, xq_ap, B["H0"], wq_ap(0), W, 8, ps, j, o, 512)
                t = s.op("activation", G1.ap[:, o * 512:(o + 1) * 512],
                         ps.ap[:, 0:512], AF.Silu, scale=SM2.ap[:, 46:47],
                         deps=rd(ps, SM2) + G1.wdeps())
                G1.did_write(t); ps.did_read(t); SM2.did_read(t)
            for o in range(8):
                ps = B["PS0"] if o % 2 == 0 else B["PS1"]
                emit_bitlin_mm(k, xq_ap, B["H0"], wq_ap(32768), W, 8, ps, j, o, 512)
                t = s.op("activation", G0.ap[:, o * 512:(o + 1) * 512],
                         ps.ap[:, 0:512], AF.Copy, scale=SM2.ap[:, 47:48],
                         deps=rd(ps, SM2) + G0.wdeps())
                G0.did_write(t); ps.did_read(t); SM2.did_read(t)
            if t0 == 512 and j == 1 and "dbg_gate" in B:
                tok = sy.dma(B["dbg_gate"].ap[0:128, :], G1.ap[:],
                             deps=rd(G1) + B["dbg_gate"].wdeps())
                B["dbg_gate"].did_write(tok); G1.did_read(tok)
                tok = sy.dma(B["dbg_up"].ap[0:128, :], G0.ap[:],
                             deps=rd(G0) + B["dbg_up"].wdeps())
                B["dbg_up"].did_write(tok); G0.did_read(tok)
            t = v.op("tensor_tensor", G1.ap[:], G1.ap[:], G0.ap[:], OP.mult,
                     deps=rd(G0) + G1.wdeps())
            G1.did_write(t); G0.did_read(t)
            emit_nvfp4_quant(k, B, pre, G1, G1.ap[:], HQ, HQ.ap[:, 0:4096],
                             4096, SM2.ap[:, 42:43], SM2.ap[:, 43:44], SM2,
                             BLK, 0)
            r0 = t0 + j * 128
            tok = sy.dma(hq_dram[r0:r0 + 128, :], HQ.ap[:, 0:4096],
                         deps=rd(HQ) + hq_buf.wdeps())
            hq_buf.did_write(tok); HQ.did_read(tok)

    barrier(k)
    weight_gs(w["down_w"], 8, 4096, 28)
    out_scale_tile(21, 28, 48)
    quant_weight(w["down_w"], 8, 4096, 28, 0)
    barrier(k)
    for t0 in range(0, T, 512):
        for j in range(4):
            r0 = t0 + j * 128
            tok = sy.dma(HQ.ap[:, 0:4096], hq_dram[r0:r0 + 128, :],
                         deps=HQ.wdeps() + rd(hq_buf))
            HQ.did_write(tok); hq_buf.did_read(tok)
            hq_fm = sub_buf(B["H0"], 4096, 4096)
            emit_tm_to_fm(k, HQ, hq_fm, pre, B["PS2"], B["PS3"], n_sub=1,
                          width=4096, src_bf16=True)
            merge_sub(B["H0"], hq_fm)
            if t0 == 0 and j == 0 and "dbg_hqfm" in B:
                t = v.op("tensor_copy", G0.ap[:], B["H0"].ap[:, 4096:8192],
                         deps=rd(B["H0"]) + G0.wdeps())
                G0.did_write(t); B["H0"].did_read(t)
                tok = sy.dma(B["dbg_hqfm"].ap[0:128, :], G0.ap[:],
                             deps=rd(G0) + B["dbg_hqfm"].wdeps())
                B["dbg_hqfm"].did_write(tok); G0.did_read(tok)
            for o2 in range(2):
                ps = B["PS0"] if o2 == 0 else B["PS1"]
                pairs = []
                for c in range(32):
                    pairs.append((B["H0"],
                                  B["H0"].ap[:, 4096 + c * 128: 4096 + (c + 1) * 128],
                                  W,
                                  W.ap[:, c * 1024 + o2 * 512:
                                       c * 1024 + (o2 + 1) * 512]))
                mm_group(k, ps, ps.ap[:, 0:512], pairs)
                t = s.op("activation", G1.ap[:, o2 * 512:(o2 + 1) * 512],
                         ps.ap[:, 0:512], AF.Silu, scale=SM2.ap[:, 48:49],
                         deps=rd(ps, SM2) + G1.wdeps())
                G1.did_write(t); ps.did_read(t); SM2.did_read(t)
            tok = sy.dma(g_dram[r0:r0 + 128, :], G1.ap[:, 0:1024],
                         deps=rd(G1) + g_buf.wdeps())
            g_buf.did_write(tok); G1.did_read(tok)



def emit_l0_attn(k, B, pre, x_in_ap, x_in_buf, x_out_ap, x_out_buf,
                 g_dram, g_buf, w):
    v, s, sy, gp = k.vector, k.scalar, k.sync, k.gpsimd
    W = Buf(B["WRES"].ap.bitcast(F32))
    G0, G1 = B["G0"], B["G1"]
    SMALL, SM2, DQH = B["SMALL"], B["SMALL2"], B["DQH"]
    QTM, KTM, VTM, RF = 0, 4096, 8192, 12288
    SOFF, ZER, TOP, MSK = 16384, 17408, 17536, 17664
    FF, LL, DD, ELN, E2 = 17792, 17920, 18048, 18176, 18304
    SFM, KS, QF, KF2, KHF, KHT, ASB = 18432, 18560, 18688, 18816, 18944, 19072, 19200
    OO = 19328
    QNW, KNW, GNW = 23424, 23552, 23680

    tok = sy.dma(W.ap[:, QNW:QNW + 128],
                 w["q_norm_w"][None, :].partition_broadcast(128), deps=W.wdeps())
    W.did_write(tok)
    tok = sy.dma(W.ap[:, KNW:KNW + 128],
                 w["k_norm_w"][None, :].partition_broadcast(128), deps=W.wdeps())
    W.did_write_acc(tok)
    tok = sy.dma(W.ap[:, GNW:GNW + 1024],
                 w["g_norm_w"][None, :].partition_broadcast(128), deps=W.wdeps())
    W.did_write_acc(tok)
    tok = sy.dma(SM2.ap[:, 48:56], w["A_log"][None, :].partition_broadcast(128),
                 deps=SM2.wdeps())
    SM2.did_write(tok)
    tok = sy.dma(SM2.ap[:, 56:64], w["dt_bias"][None, :].partition_broadcast(128),
                 deps=SM2.wdeps())
    SM2.did_write_acc(tok)
    t = s.op("activation", SM2.ap[:, 48:56], SM2.ap[:, 48:56], AF.Exp,
             deps=SM2.wdeps())
    SM2.did_write(t)
    t = v.op("memset", W.ap[:, SOFF:SOFF + 1024], 0.0, deps=W.wdeps())
    W.did_write_acc(t)
    t = v.op("memset", W.ap[:, ZER:ZER + 128], 0.0, deps=W.wdeps())
    W.did_write_acc(t)

    def stage_w(dram_ap, col0, ncol, dst=None):
        dstb = dst if dst is not None else G0
        wdeps = dstb.wdeps()
        first = True
        for c in range(8):
            t = sy.dma(dstb.ap[:, c * ncol:(c + 1) * ncol],
                       dram_ap[c * 128:(c + 1) * 128, col0:col0 + ncol],
                       deps=wdeps)
            if first:
                dstb.did_write(t); first = False
            else:
                dstb.did_write_acc(t)

    for ti in range(8):
        t0 = ti * 512
        emit_ln(k, x_in_ap, x_in_buf, G0, B["STATS"], G1, t0, pre)
        xn_fm = sub_buf(G1, 0, 4096)
        emit_tm_to_fm(k, G0, xn_fm, pre, B["PS2"], B["PS3"])
        alias_sync(G1, xn_fm)
        stg_i = 0
        for proj, dsto in (("Wq", QTM), ("Wk", KTM), ("Wv", VTM), ("Wr", RF)):
            fn = AF.Sigmoid if proj == "Wr" else AF.Silu
            for half in range(2):
                sgb = G0 if stg_i % 2 == 0 else B["XT"]
                stg_i += 1
                stage_w(w[proj], half * 512, 512, dst=sgb)
                for j in range(4):
                    ps = B["PS0"] if j % 2 == 0 else B["PS1"]
                    pairs = [(G1, G1.ap[:, c * 512 + j * 128:
                                  c * 512 + j * 128 + 128],
                              sgb, sgb.ap[:, c * 512:(c + 1) * 512])
                             for c in range(8)]
                    mm_group(k, ps, ps.ap[:, 0:512], pairs)
                    dst = W.ap[:, dsto + j * 1024 + half * 512:
                               dsto + j * 1024 + (half + 1) * 512]
                    t = s.op("activation", dst, ps.ap[:, 0:512], fn,
                             deps=rd(ps) + W.wdeps())
                    W.did_write(t); ps.did_read(t)
        stage_w(w["Wa"], 0, 8)
        for j in range(4):
            ps = B["PS4"]
            pairs = [(G1, G1.ap[:, c * 512 + j * 128: c * 512 + j * 128 + 128],
                      G0, G0.ap[:, c * 8:(c + 1) * 8]) for c in range(8)]
            mm_group(k, ps, ps.ap[:, 0:8], pairs)
            t = v.op("scalar_tensor_tensor", SMALL.ap[:, 40:48], ps.ap[:, 0:8],
                     1.0, SM2.ap[:, 56:64], OP.mult, OP.add,
                     deps=rd(ps, SM2) + SMALL.wdeps())
            SMALL.did_write(t); ps.did_read(t); SM2.did_read(t)
            t = s.op("activation", SMALL.ap[:, 40:48], SMALL.ap[:, 40:48],
                     AF.Exp, deps=SMALL.wdeps())
            SMALL.did_write(t)
            t = v.op("tensor_scalar_add", SMALL.ap[:, 40:48],
                     SMALL.ap[:, 40:48], 1.0, deps=SMALL.wdeps())
            SMALL.did_write(t)
            t = s.op("activation", SMALL.ap[:, 40:48], SMALL.ap[:, 40:48],
                     AF.Ln, deps=SMALL.wdeps())
            SMALL.did_write(t)
            t = v.op("scalar_tensor_tensor", DQH.ap[:, 8 * j:8 * j + 8],
                     SMALL.ap[:, 40:48], -1.0, SM2.ap[:, 48:56], OP.mult,
                     OP.mult, deps=rd(SMALL, SM2) + DQH.wdeps())
            DQH.did_write(t); SMALL.did_read(t); SM2.did_read(t)
        for dsto, nw in ((QTM, QNW), (KTM, KNW)):
            for j in range(4):
                sl = W.ap[:, dsto + j * 1024: dsto + (j + 1) * 1024]
                sl3 = sl.rearrange("p (h d) -> p h d", h=8)
                t = s.op("activation", G0.ap[:, 0:1024], sl, AF.Square,
                         deps=rd(W) + G0.wdeps())
                G0.did_write(t); W.did_read(t)
                t = v.op("reduce_sum", SMALL.ap[:, 0:8],
                         G0.ap[:, 0:1024].rearrange("p (h d) -> p h d", h=8),
                         AX.X, deps=rd(G0) + SMALL.wdeps())
                SMALL.did_write(t); G0.did_read(t)
                t = v.op("tensor_scalar", SMALL.ap[:, 0:8], SMALL.ap[:, 0:8],
                         float(1.0 / 128.0), float(EPS5), OP.mult, OP.add,
                         deps=SMALL.wdeps())
                SMALL.did_write(t)
                t = s.op("activation", SMALL.ap[:, 8:16], SMALL.ap[:, 0:8],
                         AF.Sqrt, deps=SMALL.wdeps())
                SMALL.did_write(t)
                t = v.op("reciprocal", SMALL.ap[:, 8:16], SMALL.ap[:, 8:16],
                         deps=SMALL.wdeps())
                SMALL.did_write(t)
                t = v.op("tensor_tensor", SMALL.ap[:, 16:24],
                         SMALL.ap[:, 8:16], SMALL.ap[:, 8:16], OP.mult,
                         deps=SMALL.wdeps())
                SMALL.did_write(t)
                t = v.op("tensor_tensor", SMALL.ap[:, 16:24],
                         SMALL.ap[:, 16:24], SMALL.ap[:, 0:8], OP.mult,
                         deps=SMALL.wdeps())
                SMALL.did_write(t)
                t = v.op("tensor_scalar", SMALL.ap[:, 16:24],
                         SMALL.ap[:, 16:24], -0.5, 1.5, OP.mult, OP.add,
                         deps=SMALL.wdeps())
                SMALL.did_write(t)
                t = v.op("tensor_tensor", SMALL.ap[:, 8:16],
                         SMALL.ap[:, 8:16], SMALL.ap[:, 16:24], OP.mult,
                         deps=SMALL.wdeps())
                SMALL.did_write(t)
                t = v.op("tensor_tensor", sl3, sl3,
                         SMALL.ap[:, 8:16].unsqueeze(2).broadcast_to([128, 8, 128]),
                         OP.mult, deps=rd(SMALL) + W.wdeps())
                W.did_write(t); SMALL.did_read(t)
                t = v.op("tensor_tensor", sl3, sl3,
                         W.ap[:, nw:nw + 128].unsqueeze(1).broadcast_to([128, 8, 128]),
                         OP.mult, deps=W.wdeps())
                W.did_write(t)
        for j in range(4):
            for h in range(8):
                sl = W.ap[:, RF + j * 1024 + h * 128: RF + j * 1024 + (h + 1) * 128]
                src = sl
                for r in range(4):
                    t = v.op("max", SMALL.ap[:, 8 * r:8 * r + 8], src,
                             deps=rd(W) + SMALL.wdeps())
                    SMALL.did_write(t); W.did_read(t)
                    t = v.op("match_replace", W.ap[:, TOP:TOP + 128],
                             SMALL.ap[:, 8 * r:8 * r + 8], src, -1e30,
                             deps=rd(W, SMALL) + W.wdeps())
                    W.did_write(t); SMALL.did_read(t)
                    src = W.ap[:, TOP:TOP + 128]
                t = v.op("reduce_sum", SMALL.ap[:, 32:33], SMALL.ap[:, 0:32],
                         AX.X, deps=SMALL.wdeps())
                SMALL.did_write(t)
                t = v.op("tensor_scalar_add", SMALL.ap[:, 32:33],
                         SMALL.ap[:, 32:33], 1e-9, deps=SMALL.wdeps())
                SMALL.did_write(t)
                t = v.op("reciprocal", SMALL.ap[:, 32:33], SMALL.ap[:, 32:33],
                         deps=SMALL.wdeps())
                SMALL.did_write(t)
                t = v.op("tensor_scalar", W.ap[:, MSK:MSK + 128], sl,
                         SMALL.ap[:, 31:32], None, OP.is_ge,
                         deps=rd(SMALL) + W.wdeps())
                W.did_write(t); SMALL.did_read(t)
                t = v.op("scalar_tensor_tensor", sl, sl, SMALL.ap[:, 32:33],
                         W.ap[:, MSK:MSK + 128], OP.mult, OP.mult,
                         deps=W.wdeps() + rd(SMALL))
                W.did_write(t); SMALL.did_read(t)
                t = v.op("tensor_scalar", sl, sl,
                         DQH.ap[:, 8 * j + h:8 * j + h + 1], None, OP.mult,
                         deps=W.wdeps() + rd(DQH))
                W.did_write(t); DQH.did_read(t)
        if ti == 0:
            slots = ["FF", "LL", "DD", "ELN", "E2", "SFM", "KS", "QF",
                     "KF2", "KHF", "KHT", "ASB"]
            sc2 = []
            for p in range(2):
                base = 17792 + p * 6912
                sc2.append({n: sub_buf(W, base + i * 128, 128)
                            for i, n in enumerate(slots)})
            sbh = [sub_buf(W, SOFF + h * 128, 128) for h in range(8)]
        oo = sub_buf(W, OO, 4096)
        for j in range(4):
            for h in range(8):
                p = (j * 8 + h) % 2
                sl = sc2[p]
                trbank = B["PS2"] if p == 0 else B["PS3"]
                psA = B["PS0"] if p == 0 else B["PS5"]
                psO = B["PS1"] if p == 0 else B["PS6"]
                psU = B["PS4"] if p == 0 else B["PS7"]

                def tr128(src_ap, src_b, dst_b):
                    tk = k.tensor.op("transpose", trbank.ap[:, 0:128],
                                     src_ap, pre.ident.ap[:],
                                     deps=trbank.wdeps() + rd(src_b, pre.ident))
                    trbank.did_write(tk); src_b.did_read(tk)
                    pre.ident.did_read(tk)
                    t2 = s.op("activation", dst_b.ap[:], trbank.ap[:, 0:128],
                              AF.Copy, deps=rd(trbank) + dst_b.wdeps())
                    dst_b.did_write(t2); trbank.did_read(t2)
                fsl = W.ap[:, RF + j * 1024 + h * 128:
                           RF + j * 1024 + (h + 1) * 128]
                tr128(fsl, W, sl["FF"])
                t = v.op("tensor_tensor_scan", sl["LL"].ap[:], sl["FF"].ap[:],
                         W.ap[:, ZER:ZER + 128], 0.0, OP.add, OP.add,
                         deps=rd(sl["FF"], W) + sl["LL"].wdeps())
                sl["LL"].did_write(t); sl["FF"].did_read(t); W.did_read(t)
                t = s.op("activation", sl["DD"].ap[:], sl["LL"].ap[:], AF.Exp,
                         deps=rd(sl["LL"]) + sl["DD"].wdeps())
                sl["DD"].did_write(t); sl["LL"].did_read(t)
                t = v.op("tensor_scalar_mul", sl["ELN"].ap[:], sl["LL"].ap[:],
                         -1.0, deps=rd(sl["LL"]) + sl["ELN"].wdeps())
                sl["ELN"].did_write(t); sl["LL"].did_read(t)
                t = s.op("activation", sl["ELN"].ap[:], sl["ELN"].ap[:],
                         AF.Exp, deps=sl["ELN"].wdeps())
                sl["ELN"].did_write(t)
                t = v.op("tensor_scalar", sl["E2"].ap[:], sl["LL"].ap[:],
                         -1.0, sl["LL"].ap[:, 127:128], OP.mult, OP.add,
                         deps=rd(sl["LL"]) + sl["E2"].wdeps())
                sl["E2"].did_write(t); sl["LL"].did_read(t)
                t = s.op("activation", sl["E2"].ap[:], sl["E2"].ap[:], AF.Exp,
                         deps=sl["E2"].wdeps())
                sl["E2"].did_write(t)
                t = s.op("activation", sl["SFM"].ap[:], sl["FF"].ap[:],
                         AF.Exp, deps=rd(sl["FF"]) + sl["SFM"].wdeps())
                sl["SFM"].did_write(t); sl["FF"].did_read(t)
                t = v.op("tensor_scalar", sl["SFM"].ap[:], sl["SFM"].ap[:],
                         -1.0, 1.0, OP.mult, OP.add, deps=sl["SFM"].wdeps())
                sl["SFM"].did_write(t)
                qsl = W.ap[:, QTM + j * 1024 + h * 128:
                           QTM + j * 1024 + (h + 1) * 128]
                tr128(qsl, W, sl["QF"])
                t = v.op("tensor_tensor", sl["QF"].ap[:], sl["QF"].ap[:],
                         sl["DD"].ap[:], OP.mult,
                         deps=rd(sl["DD"]) + sl["QF"].wdeps())
                sl["QF"].did_write(t); sl["DD"].did_read(t)
                ksl = W.ap[:, KTM + j * 1024 + h * 128:
                           KTM + j * 1024 + (h + 1) * 128]
                tr128(ksl, W, sl["KS"])
                t = v.op("tensor_tensor", sl["KS"].ap[:], sl["KS"].ap[:],
                         sl["SFM"].ap[:], OP.mult,
                         deps=rd(sl["SFM"]) + sl["KS"].wdeps())
                sl["KS"].did_write(t); sl["SFM"].did_read(t)
                t = v.op("tensor_tensor", sl["KF2"].ap[:], sl["KS"].ap[:],
                         sl["ELN"].ap[:], OP.mult,
                         deps=rd(sl["KS"], sl["ELN"]) + sl["KF2"].wdeps())
                sl["KF2"].did_write(t); sl["KS"].did_read(t)
                sl["ELN"].did_read(t)
                t = v.op("tensor_tensor", sl["KHF"].ap[:], sl["KS"].ap[:],
                         sl["E2"].ap[:], OP.mult,
                         deps=rd(sl["KS"], sl["E2"]) + sl["KHF"].wdeps())
                sl["KHF"].did_write(t); sl["KS"].did_read(t)
                sl["E2"].did_read(t)
                tr128(sl["KHF"].ap[:], sl["KHF"], sl["KHT"])
                tk = k.tensor.op("matmul", psA.ap[:, 0:128], sl["KF2"].ap[:],
                                 sl["QF"].ap[:], start=True, stop=True,
                                 deps=rd(sl["KF2"], sl["QF"]) + psA.wdeps())
                psA.did_write(tk); sl["KF2"].did_read(tk)
                sl["QF"].did_read(tk)
                t = v.op("tensor_tensor", sl["ASB"].ap[:], psA.ap[:, 0:128],
                         pre.triu.ap[:], OP.mult,
                         deps=rd(psA, pre.triu) + sl["ASB"].wdeps())
                sl["ASB"].did_write(t); psA.did_read(t); pre.triu.did_read(t)
                vsl = W.ap[:, VTM + j * 1024 + h * 128:
                           VTM + j * 1024 + (h + 1) * 128]
                tk = k.tensor.op("matmul", psO.ap[:, 0:128], sl["ASB"].ap[:],
                                 vsl, start=True, stop=False,
                                 deps=rd(sl["ASB"], W) + psO.wdeps(), inc=True)
                psO.did_write(tk); sl["ASB"].did_read(tk); W.did_read(tk)
                tk = k.tensor.op("matmul", psO.ap[:, 0:128], sl["QF"].ap[:],
                                 sbh[h].ap[:], start=False, stop=True,
                                 deps=rd(sl["QF"], sbh[h]), inc=True)
                psO.did_write(tk); sl["QF"].did_read(tk); sbh[h].did_read(tk)
                tk = k.tensor.op("matmul", psU.ap[:, 0:128], sl["KHT"].ap[:],
                                 vsl, start=True, stop=True,
                                 deps=rd(sl["KHT"], W) + psU.wdeps())
                psU.did_write(tk); sl["KHT"].did_read(tk); W.did_read(tk)
                t = v.op("scalar_tensor_tensor", sbh[h].ap[:], sbh[h].ap[:],
                         sl["DD"].ap[:, 127:128], psU.ap[:, 0:128],
                         OP.mult, OP.add,
                         deps=sbh[h].wdeps() + rd(sl["DD"], psU))
                sbh[h].did_write(t); sl["DD"].did_read(t); psU.did_read(t)
                t = s.op("activation",
                         oo.ap[:, j * 1024 + h * 128:j * 1024 + (h + 1) * 128],
                         psO.ap[:, 0:128], AF.Copy,
                         deps=rd(psO) + oo.wdeps())
                oo.did_write_acc(t); psO.did_read(t)
            osl = oo.ap[:, j * 1024:(j + 1) * 1024]
            t = s.op("activation", G0.ap[:, 0:1024], osl, AF.Square,
                     accum_out=SMALL.ap[:, 0:1],
                     deps=rd(oo) + G0.wdeps() + SMALL.wdeps())
            G0.did_write(t); oo.did_read(t); SMALL.did_write(t)
            t = v.op("tensor_scalar", SMALL.ap[:, 0:1], SMALL.ap[:, 0:1],
                     float(1.0 / 1024.0), float(EPS5), OP.mult, OP.add,
                     deps=SMALL.wdeps())
            SMALL.did_write(t)
            t = s.op("activation", SMALL.ap[:, 1:2], SMALL.ap[:, 0:1], AF.Sqrt,
                     deps=SMALL.wdeps())
            SMALL.did_write(t)
            t = v.op("reciprocal", SMALL.ap[:, 1:2], SMALL.ap[:, 1:2],
                     deps=SMALL.wdeps())
            SMALL.did_write(t)
            t = v.op("tensor_tensor", SMALL.ap[:, 2:3], SMALL.ap[:, 1:2],
                     SMALL.ap[:, 1:2], OP.mult, deps=SMALL.wdeps())
            SMALL.did_write(t)
            t = v.op("tensor_tensor", SMALL.ap[:, 2:3], SMALL.ap[:, 2:3],
                     SMALL.ap[:, 0:1], OP.mult, deps=SMALL.wdeps())
            SMALL.did_write(t)
            t = v.op("tensor_scalar", SMALL.ap[:, 2:3], SMALL.ap[:, 2:3],
                     -0.5, 1.5, OP.mult, OP.add, deps=SMALL.wdeps())
            SMALL.did_write(t)
            t = v.op("tensor_tensor", SMALL.ap[:, 1:2], SMALL.ap[:, 1:2],
                     SMALL.ap[:, 2:3], OP.mult, deps=SMALL.wdeps())
            SMALL.did_write(t)
            t = v.op("tensor_scalar", osl, osl, SMALL.ap[:, 1:2], None,
                     OP.mult, deps=oo.wdeps() + rd(SMALL))
            oo.did_write(t); SMALL.did_read(t)
            t = v.op("tensor_tensor", osl, osl, W.ap[:, GNW:GNW + 1024],
                     OP.mult, deps=oo.wdeps() + rd(W))
            oo.did_write(t); W.did_read(t)
        o_fm = sub_buf(G0, 0, 4096)
        emit_tm_to_fm(k, oo, o_fm, pre, B["PS2"], B["PS3"])
        merge_sub(W, oo)
        merge_sub(G0, o_fm)
        for half in range(2):
            stage_w(w["Wo"], half * 512, 512, dst=G1)
            for j in range(4):
                ps = B["PS5"] if j % 2 == 0 else B["PS6"]
                pairs = [(G0, G0.ap[:, c * 512 + j * 128:
                              c * 512 + j * 128 + 128],
                          G1, G1.ap[:, c * 512:(c + 1) * 512])
                         for c in range(8)]
                mm_group(k, ps, ps.ap[:, 0:512], pairs)
                db = B["SA"] if j % 2 == 0 else B["SB"]
                t = s.op("activation", db.ap[:, 0:512], ps.ap[:, 0:512],
                         AF.Copy, deps=rd(ps) + db.wdeps())
                db.did_write(t); ps.did_read(t)
                r0 = t0 + j * 128
                csl = slice(half * 512, (half + 1) * 512)
                tok = sy.dma(B["SC"].ap[:, 0:512], g_dram[r0:r0 + 128, csl],
                             deps=B["SC"].wdeps() + rd(g_buf))
                B["SC"].did_write(tok); g_buf.did_read(tok)
                tok = sy.dma(B["SC"].ap[:, 512:1024], x_in_ap[r0:r0 + 128, csl],
                             deps=B["SC"].wdeps() + rd(x_in_buf))
                B["SC"].did_write_acc(tok); x_in_buf.did_read(tok)
                t = v.op("tensor_tensor", db.ap[:, 0:512], db.ap[:, 0:512],
                         B["SC"].ap[:, 0:512], OP.mult,
                         deps=db.wdeps() + rd(B["SC"]))
                db.did_write(t); B["SC"].did_read(t)
                t = v.op("tensor_tensor", db.ap[:, 0:512], db.ap[:, 0:512],
                         B["SC"].ap[:, 512:1024], OP.add,
                         deps=db.wdeps() + rd(B["SC"]))
                db.did_write(t); B["SC"].did_read(t)
                tok = sy.dma(x_out_ap[r0:r0 + 128, csl], db.ap[:, 0:512],
                             deps=rd(db) + x_out_buf.wdeps())
                x_out_buf.did_write_acc(tok); db.did_read(tok)


def emit_l2_attn(k, B, pre, x_in_ap, x_in_buf, x_out_ap, x_out_buf,
                 g_dram, g_buf, w):
    v, s, sy, gp = k.vector, k.scalar, k.sync, k.gpsimd
    W = Buf(B["WRES"].ap.bitcast(F32))
    G0, G1 = B["G0"], B["G1"]
    SMALL = B["SMALL"]
    QKV, CTX, CTF = 0, 12288, 16384
    SCR0, SCR1, SCO = 20480, 21504, 22528
    BIA, BOU = 23552, 26624
    ONES = B["ONESROW"]

    tok = sy.dma(W.ap[0:1, BIA:BIA + 3072], w["b_qkv"][None, :],
                 deps=W.wdeps())
    W.did_write(tok)
    tok = sy.dma(W.ap[0:1, BOU:BOU + 1024], w["b_out"][None, :],
                 deps=W.wdeps())
    W.did_write_acc(tok)

    for ti in range(8):
        t0 = ti * 512
        emit_ln(k, x_in_ap, x_in_buf, G0, B["STATS"], G1, t0, pre)
        xn_fm = sub_buf(G1, 0, 4096)
        emit_tm_to_fm(k, G0, xn_fm, pre, B["PS2"], B["PS3"])
        alias_sync(G1, xn_fm)
        stg = [sub_buf(W, CTX, 4096), sub_buf(W, CTF, 4096)]
        qkvb = sub_buf(W, QKV, 12288)
        for p in range(6):
            sb = stg[p % 2]
            wdeps = sb.wdeps()
            first = True
            for c in range(8):
                t = sy.dma(sb.ap[:, c * 512:(c + 1) * 512],
                           w["W_qkv"][c * 128:(c + 1) * 128,
                                      p * 512:(p + 1) * 512], deps=wdeps)
                if first:
                    sb.did_write(t); first = False
                else:
                    sb.did_write_acc(t)
            for j in range(4):
                ps = B["PS0"] if j % 2 == 0 else B["PS1"]
                pairs = [(G1, G1.ap[:, c * 512 + j * 128:
                              c * 512 + j * 128 + 128],
                          sb, sb.ap[:, c * 512:(c + 1) * 512])
                         for c in range(8)]
                pairs.append((ONES, ONES.ap[0:1, 0:128],
                              W, W.ap[0:1, BIA + p * 512:BIA + (p + 1) * 512]))
                mm_group(k, ps, ps.ap[:, 0:512], pairs)
                t = s.op("activation",
                         qkvb.ap[:, j * 3072 + p * 512:j * 3072 + (p + 1) * 512],
                         ps.ap[:, 0:512], AF.Copy, deps=rd(ps) + qkvb.wdeps())
                qkvb.did_write_acc(t); ps.did_read(t)
        for b_ in stg:
            merge_sub(W, b_)
        merge_sub(W, qkvb)
        scrb = [sub_buf(W, SCR0, 1024), sub_buf(W, SCR1, 1024)]
        scob = sub_buf(W, SCO, 1024)
        ctxb = sub_buf(W, CTX, 4096)
        for j in range(4):
            qo = QKV + j * 3072
            q3 = W.ap[:, qo:qo + 3072].rearrange("p (h e) -> p h e", h=16)
            sco3 = scob.ap[:, j * 256:(j + 1) * 256].rearrange(
                "p (h g) -> p h g", h=16)
            for g in range(16):
                sc = scrb[g % 2]
                kg = q3[:, g:g + 1, 64:128].broadcast_to([128, 16, 64])
                eng = v if g < 6 else gp
                t = eng.op("tensor_tensor", sc.ap[:, 0:1024].rearrange(
                               "p (h e) -> p h e", h=16),
                           q3[:, :, 0:64], kg, OP.mult,
                           deps=rd(W) + sc.wdeps())
                sc.did_write(t); W.did_read(t)
                t = v.op("reduce_sum", sco3[:, :, g:g + 1],
                         sc.ap[:, 0:1024].rearrange("p (h e) -> p h e", h=16),
                         AX.X, deps=rd(sc) + scob.wdeps())
                scob.did_write_acc(t); sc.did_read(t)
            t = v.op("reduce_max", SMALL.ap[:, 0:16], sco3, AX.X,
                     deps=rd(scob) + SMALL.wdeps())
            SMALL.did_write(t); scob.did_read(t)
            t = v.op("tensor_scalar_mul", SMALL.ap[:, 0:16],
                     SMALL.ap[:, 0:16], 0.125, deps=SMALL.wdeps())
            SMALL.did_write(t)
            t = v.op("scalar_tensor_tensor", sco3, sco3, 0.125,
                     SMALL.ap[:, 0:16].unsqueeze(2).broadcast_to([128, 16, 16]),
                     OP.mult, OP.subtract,
                     deps=scob.wdeps() + rd(SMALL))
            scob.did_write(t); SMALL.did_read(t)
            t = s.op("activation", scob.ap[:, j * 256:(j + 1) * 256],
                     scob.ap[:, j * 256:(j + 1) * 256], AF.Exp,
                     deps=scob.wdeps())
            scob.did_write(t)
            t = v.op("reduce_sum", SMALL.ap[:, 16:32], sco3, AX.X,
                     deps=rd(scob) + SMALL.wdeps())
            SMALL.did_write(t); scob.did_read(t)
            t = v.op("reciprocal", SMALL.ap[:, 16:32], SMALL.ap[:, 16:32],
                     deps=SMALL.wdeps())
            SMALL.did_write(t)
            t = v.op("tensor_tensor", sco3, sco3,
                     SMALL.ap[:, 16:32].unsqueeze(2).broadcast_to([128, 16, 16]),
                     OP.mult, deps=scob.wdeps() + rd(SMALL))
            scob.did_write(t); SMALL.did_read(t)
            ctx3 = ctxb.ap[:, j * 1024:(j + 1) * 1024].rearrange(
                "p (h e) -> p h e", h=16)
            for g in range(16):
                sc = scrb[g % 2]
                vg = q3[:, g:g + 1, 128:192].broadcast_to([128, 16, 64])
                ag = sco3[:, :, g:g + 1].broadcast_to([128, 16, 64])
                eng = v if g < 6 else gp
                t = eng.op("tensor_tensor", sc.ap[:, 0:1024].rearrange(
                               "p (h e) -> p h e", h=16),
                           vg, ag, OP.mult,
                           deps=rd(W, scob) + sc.wdeps())
                sc.did_write(t); W.did_read(t); scob.did_read(t)
                if g == 0:
                    t = v.op("tensor_copy",
                             ctx3, sc.ap[:, 0:1024].rearrange(
                                 "p (h e) -> p h e", h=16),
                             deps=rd(sc) + ctxb.wdeps())
                else:
                    t = v.op("tensor_tensor", ctx3, ctx3,
                             sc.ap[:, 0:1024].rearrange("p (h e) -> p h e",
                                                        h=16),
                             OP.add, deps=rd(sc) + ctxb.wdeps())
                ctxb.did_write(t); sc.did_read(t)
        for b_ in scrb:
            merge_sub(W, b_)
        merge_sub(W, scob)
        merge_sub(W, ctxb)
        c_tm = sub_buf(W, CTX, 4096)
        c_fm = sub_buf(W, CTF, 4096)
        emit_tm_to_fm(k, c_tm, c_fm, pre, B["PS2"], B["PS3"])
        merge_sub(W, c_tm)
        merge_sub(W, c_fm)
        for half in range(2):
            wdeps = G1.wdeps()
            first = True
            for c in range(8):
                tt = sy.dma(G1.ap[:, c * 512:(c + 1) * 512],
                            w["W_out"][c * 128:(c + 1) * 128,
                                       half * 512:(half + 1) * 512],
                            deps=wdeps)
                if first:
                    G1.did_write(tt); first = False
                else:
                    G1.did_write_acc(tt)
            for j in range(4):
                ps = B["PS5"] if j % 2 == 0 else B["PS6"]
                pairs = [(W, W.ap[:, CTF + c * 512 + j * 128:
                              CTF + c * 512 + j * 128 + 128],
                          G1, G1.ap[:, c * 512:(c + 1) * 512])
                         for c in range(8)]
                pairs.append((ONES, ONES.ap[0:1, 0:128],
                              W, W.ap[0:1, BOU + half * 512:
                                      BOU + (half + 1) * 512]))
                mm_group(k, ps, ps.ap[:, 0:512], pairs)
                db = B["SA"] if j % 2 == 0 else B["SB"]
                t = s.op("activation", db.ap[:, 0:512], ps.ap[:, 0:512],
                         AF.Copy, deps=rd(ps) + db.wdeps())
                db.did_write(t); ps.did_read(t)
                r0 = t0 + j * 128
                csl = slice(half * 512, (half + 1) * 512)
                tok = sy.dma(B["SC"].ap[:, 0:512], g_dram[r0:r0 + 128, csl],
                             deps=B["SC"].wdeps() + rd(g_buf))
                B["SC"].did_write(tok); g_buf.did_read(tok)
                tok = sy.dma(B["SC"].ap[:, 512:1024], x_in_ap[r0:r0 + 128, csl],
                             deps=B["SC"].wdeps() + rd(x_in_buf))
                B["SC"].did_write_acc(tok); x_in_buf.did_read(tok)
                t = v.op("tensor_tensor", db.ap[:, 0:512], db.ap[:, 0:512],
                         B["SC"].ap[:, 0:512], OP.mult,
                         deps=db.wdeps() + rd(B["SC"]))
                db.did_write(t); B["SC"].did_read(t)
                t = v.op("tensor_tensor", db.ap[:, 0:512], db.ap[:, 0:512],
                         B["SC"].ap[:, 512:1024], OP.add,
                         deps=db.wdeps() + rd(B["SC"]))
                db.did_write(t); B["SC"].did_read(t)
                tok = sy.dma(x_out_ap[r0:r0 + 128, csl], db.ap[:, 0:512],
                             deps=rd(db) + x_out_buf.wdeps())
                x_out_buf.did_write_acc(tok); db.did_read(tok)


DEBUG_DUMPS = False


def build_program():
    nc = bass.Bass("TRN2", target_bir_lowering=False)

    def inp(name, shape):
        return nc.dram_tensor(name, shape, F32, kind="ExternalInput").ap()

    x = inp("x", [4096, 1024])
    w = {}
    for n in ("L0_attn_Wq", "L0_attn_Wk", "L0_attn_Wv", "L0_attn_Wr",
              "L0_attn_Wo", "L1_attn_W_out", "L2_attn_W_out",
              "L3_attn_W_proj"):
        w[n] = inp(n, [1024, 1024])
    w["L0_attn_Wa"] = inp("L0_attn_Wa", [1024, 8])
    for n in ("L0_attn_A_log", "L0_attn_dt_bias"):
        w[n] = inp(n, [8])
    for n in ("L0_attn_q_norm_w", "L0_attn_k_norm_w"):
        w[n] = inp(n, [128])
    w["L0_attn_g_norm_w"] = inp("L0_attn_g_norm_w", [1024])
    w["L0_ffn_w1T"] = inp("L0_ffn_w1T", [1024, 4096])
    w["L0_ffn_w2T"] = inp("L0_ffn_w2T", [4096, 1024])
    for n in ("L1_attn_W_qkv", "L2_attn_W_qkv", "L3_attn_W_attn"):
        w[n] = inp(n, [1024, 3072])
    for n in ("L1_attn_b_qkv", "L2_attn_b_qkv", "L3_attn_b_attn"):
        w[n] = inp(n, [3072])
    for n in ("L1_attn_b_out", "L2_attn_b_out", "L3_attn_b_proj"):
        w[n] = inp(n, [1024])
    w["L1_ffn_Ws1"] = inp("L1_ffn_Ws1", [1024, 512])
    w["L1_ffn_Ws2"] = inp("L1_ffn_Ws2", [512, 1024])
    w["L1_ffn_W1"] = inp("L1_ffn_W1", [8, 1024, 512])
    w["L1_ffn_W2"] = inp("L1_ffn_W2", [8, 512, 1024])
    w["L1_ffn_Wr"] = inp("L1_ffn_Wr", [1024, 8])
    w["L2_ffn_w_gateT"] = inp("L2_ffn_w_gateT", [1024, 4096])
    w["L2_ffn_w_upT"] = inp("L2_ffn_w_upT", [1024, 4096])
    w["L2_ffn_w_downT"] = inp("L2_ffn_w_downT", [4096, 1024])
    w["L3_ffn_gate_w"] = inp("L3_ffn_gate_w", [4096, 1024])
    w["L3_ffn_up_w"] = inp("L3_ffn_up_w", [4096, 1024])
    w["L3_ffn_down_w"] = inp("L3_ffn_down_w", [1024, 4096])
    w["L3_ffn_s_in"] = inp("L3_ffn_s_in", [1])
    w["L3_ffn_s_in_down"] = inp("L3_ffn_s_in_down", [1])

    xo = nc.dram_tensor("xo", [4096, 1024], F32, kind="ExternalOutput").ap()
    xA = nc.dram_tensor("xA", [4096, 1024], F32, kind="Internal").ap()
    xB = nc.dram_tensor("xB", [4096, 1024], F32, kind="Internal").ap()
    g = nc.dram_tensor("g_scr", [4096, 1024], F32, kind="Internal").ap()
    hq = nc.dram_tensor("hq_scr", [4096, 4096], BF16, kind="Internal").ap()
    dqh = nc.dram_tensor("dqh_scr", [4096], F32, kind="Internal").ap()
    l3q = nc.dram_tensor("l3q", [2, 8, 128, 4096], BF16, kind="Internal").ap()
    l3k = nc.dram_tensor("l3k", [2, 8, 128, 4096], BF16, kind="Internal").ap()
    l3v = nc.dram_tensor("l3v", [32, 128, 1040], F32, kind="Internal").ap()

    with ExitStack() as ctx:
        k = K(nc, ctx)
        pre = Prelude(k)
        B = make_buffers(k)
        pre.emit(k)
        emit_buffer_init(k, B)
        xib, xab, xbb, xob = Buf(x), Buf(xA), Buf(xB), Buf(xo)
        gb, hqb, dqhb = Buf(g), Buf(hq), Buf(dqh)
        scr = {"q": Buf(l3q), "kk": Buf(l3k), "v": Buf(l3v)}
        phases = [
            ("L0_ffn", lambda: emit_ffn_bitlin(
                k, B, pre, x, xib, g, gb,
                {"w1": w["L0_ffn_w1T"], "w2": w["L0_ffn_w2T"]},
                "gelu", hq, hqb, dqh, dqhb)),
            ("L0_attn", lambda: emit_l0_attn(
                k, B, pre, x, xib, xA, xab, g, gb,
                {n: w["L0_attn_" + n] for n in
                 ("Wq", "Wk", "Wv", "Wr", "Wa", "A_log", "dt_bias",
                  "q_norm_w", "k_norm_w", "g_norm_w", "Wo")})),
            ("L1_ffn", lambda: emit_moe(
                k, B, pre, xA, xab, g, gb,
                {n: w["L1_ffn_" + n] for n in
                 ("Ws1", "Ws2", "W1", "W2", "Wr")})),
            ("L1_attn", lambda: emit_l1_attn(
                k, B, pre, xA, xab, xB, xbb, g, gb,
                {n: w["L1_attn_" + n] for n in
                 ("W_qkv", "b_qkv", "W_out", "b_out")})),
            ("L2_ffn", lambda: emit_ffn_bitlin(
                k, B, pre, xB, xbb, g, gb,
                {"wg": w["L2_ffn_w_gateT"], "wu": w["L2_ffn_w_upT"],
                 "wd": w["L2_ffn_w_downT"]},
                "swiglu", hq, hqb, dqh, dqhb)),
            ("L2_attn", lambda: emit_l2_attn(
                k, B, pre, xB, xbb, xA, xab, g, gb,
                {n: w["L2_attn_" + n] for n in
                 ("W_qkv", "b_qkv", "W_out", "b_out")})),
            ("L3_ffn", lambda: emit_ffn_nvfp4(
                k, B, pre, xA, xab, g, gb,
                {"gate_w": w["L3_ffn_gate_w"], "up_w": w["L3_ffn_up_w"],
                 "down_w": w["L3_ffn_down_w"], "s_in": w["L3_ffn_s_in"],
                 "s_in_down": w["L3_ffn_s_in_down"]},
                hq, hqb)),
            ("L3_attn", lambda: emit_l3_attn(
                k, B, pre, xA, xab, xo, xob, g, gb,
                {"W_attn": w["L3_attn_W_attn"],
                 "b_attn": w["L3_attn_b_attn"],
                 "W_proj": w["L3_attn_W_proj"],
                 "b_proj": w["L3_attn_b_proj"]}, scr)),
        ]
        dumps = {}
        if DEBUG_DUMPS:
            for i in range(4):
                dumps[i] = nc.dram_tensor(f"dump{i}", [4096, 1024], F32,
                                          kind="ExternalOutput").ap()
        dump_after = {"L0_attn": (0, xA), "L1_attn": (1, xB),
                      "L2_attn": (2, xA), "L3_attn": (3, xo)}
        for name, fn in phases:
            with nc.named_scope(name):
                fn()
                barrier(k)
                if DEBUG_DUMPS and name in dump_after:
                    i, src = dump_after[name]
                    k.sync.dma(dumps[i], src[:, :])
                    barrier(k)
    return nc


def kernel(**inputs):
    nc = build_program()
    x = inputs["L0_attn_hidden_states"]
    base = {}
    for n, a in inputs.items():
        if n == "L0_attn_hidden_states":
            continue
        if n in ("L0_ffn_w1", "L0_ffn_w2", "L2_ffn_w_gate", "L2_ffn_w_up",
                 "L2_ffn_w_down"):
            nn = {"L0_ffn_w1": "L0_ffn_w1T", "L0_ffn_w2": "L0_ffn_w2T",
                  "L2_ffn_w_gate": "L2_ffn_w_gateT",
                  "L2_ffn_w_up": "L2_ffn_w_upT",
                  "L2_ffn_w_down": "L2_ffn_w_downT"}[n]
            base[nn] = np.ascontiguousarray(np.asarray(a).T)
        else:
            base[n] = np.asarray(a)
    inmaps = [dict(base, x=np.ascontiguousarray(x[b])) for b in range(8)]
    res = run_bass_kernel_spmd(nc, inmaps, core_ids=list(range(8)))
    out = np.stack([np.asarray(res.results[b]["xo"]) for b in range(8)])
    return out.astype(np.float32)


# version 10 (final)
# speedup vs baseline: 1.0052x; 1.0052x over previous
import math
from contextlib import ExitStack

import numpy as np

import concourse.bass as bass
import concourse.mybir as mybir
from concourse.bass_utils import run_bass_kernel_spmd

F32 = mybir.dt.float32
BF16 = mybir.dt.bfloat16
I32 = mybir.dt.int32
AX = mybir.AxisListType
OP = mybir.AluOpType
AF = mybir.ActivationFunctionType

N_CORES = 8
T = 4096
D = 1024
NT = T // 128



class Sem:
    __slots__ = ("h", "cnt", "name", "dma_sem_q")

    def __init__(self, h, name):
        self.h = h
        self.cnt = 0
        self.name = name
        self.dma_sem_q = None


class EngineQ:

    def __init__(self, k, eng, name, dma_queue=False):
        self.k = k
        self.e = eng
        self.name = name
        self.sem = k.new_sem("sq_" + name)
        self.waited = {}
        self.dma_queue = dma_queue
        if dma_queue:
            self.pool = [k.new_sem(f"dq_{name}{i}") for i in range(8)]
            self.rr = 0

    def _wait(self, deps):
        for t in deps:
            if t is None:
                continue
            sem, val = t
            if sem is self.sem:
                val = sem.cnt
            if self.waited.get(sem, 0) >= val:
                continue
            self.e.wait_ge(sem.h, val)
            self.waited[sem] = val

    def op(self, name, *args, deps=(), inc=True, **kw):
        self._wait(deps)
        ins = getattr(self.e, name)(*args, **kw)
        if inc:
            self.sem.cnt += 1
            ins.then_inc(self.sem.h, 1)
            return (self.sem, self.sem.cnt)
        return None

    def dma(self, out, in_, deps=(), **kw):
        s = self.pool[self.rr % len(self.pool)]
        self.rr += 1
        self._wait(list(deps) + [(s, s.cnt)])
        ins = self.e.dma_start(out=out, in_=in_, **kw)
        s.cnt += 16
        ins.then_inc(s.h, 16)
        return (s, s.cnt)

    def wait(self, deps):
        self._wait(deps)


class Buf:

    __slots__ = ("ap", "wtoks", "rtoks")

    def __init__(self, ap):
        self.ap = ap
        self.wtoks = []
        self.rtoks = []

    @property
    def wtok(self):
        return self.wtoks[-1] if self.wtoks else None

    def rdeps(self):
        return list(self.wtoks)

    def wdeps(self):
        return list(self.rtoks) + list(self.wtoks)

    def did_read(self, tok):
        if tok is not None:
            self.rtoks.append(tok)

    def did_write(self, tok):
        self.wtoks = [tok] if tok is not None else []
        self.rtoks = []

    def did_write_acc(self, tok):
        if tok is not None:
            self.wtoks.append(tok)

    def __getitem__(self, key):
        return self.ap[key]


def rd(*bufs):
    deps = []
    for b in bufs:
        deps += b.rdeps()
    return deps


class K:

    def __init__(self, nc, ctx):
        self.nc = nc
        self.ctx = ctx
        self.sems = []
        self.tensor = EngineQ(self, nc.tensor, "pe")
        self.vector = EngineQ(self, nc.vector, "dve")
        self.scalar = EngineQ(self, nc.scalar, "act")
        self.gpsimd = EngineQ(self, nc.gpsimd, "pool", dma_queue=True)
        self.sync = EngineQ(self, nc.sync, "sp", dma_queue=True)
        self._uid = 0

    def new_sem(self, name):
        h = self.ctx.enter_context(self.nc.semaphore(name))
        s = Sem(h, name)
        self.sems.append(s)
        return s

    def uid(self):
        self._uid += 1
        return self._uid

    def sbuf(self, name, shape, dt=F32):
        t = self.ctx.enter_context(self.nc.sbuf_tensor(name, shape, dt))
        return Buf(t.ap() if hasattr(t, "ap") else t)

    def psum(self, name, shape, dt=F32):
        t = self.ctx.enter_context(self.nc.psum_tensor(name, shape, dt))
        return Buf(t.ap() if hasattr(t, "ap") else t)

    def copy(self, eng, dst_buf, dst_ap, src_buf, src_ap, **kw):
        tok = eng.op("tensor_copy", dst_ap, src_ap,
                     deps=rd(src_buf) + dst_buf.wdeps(), **kw)
        dst_buf.did_write(tok)
        src_buf.did_read(tok)
        return tok



def mm_group(k, out_buf, out_ap, pairs, deps_extra=()):
    deps = list(deps_extra) + out_buf.wdeps()
    seen = set()
    for lb, _, rbuf, _ in pairs:
        for b in (lb, rbuf):
            if id(b) not in seen:
                seen.add(id(b))
                deps += b.rdeps()
    n = len(pairs)
    tok = None
    for i, (lb, la, rbuf, ra) in enumerate(pairs):
        tok = k.tensor.op(
            "matmul", out_ap, la, ra,
            deps=deps if i == 0 else (),
            inc=(i == n - 1),
            start=(i == 0), stop=(i == n - 1),
        )
    out_buf.did_write(tok)
    for lb, _, rbuf, _ in pairs:
        lb.did_read(tok)
        rbuf.did_read(tok)
    return tok



C1 = np.float32(1.5 * 2.0**23)
C2 = np.float32(1.5 * 2.0**52)
EPS5 = np.float32(1e-5)


class Prelude:

    def __init__(self, k):
        nc = k.nc
        ident_np = np.eye(128, dtype=np.float32)
        self.ident_dram = nc.inline_tensor(ident_np, name="c_ident").ap()
        self.ident = k.sbuf("sb_ident", [128, 128], F32)
        import ml_dtypes
        self.identbf_dram = nc.inline_tensor(
            ident_np.astype(ml_dtypes.bfloat16), name="c_identbf").ap()
        self.ident_bf = k.sbuf("sb_identbf", [128, 128], BF16)
        triu_np = np.triu(np.ones((128, 128), dtype=np.float32))
        self.triu_dram = nc.inline_tensor(triu_np, name="c_triu").ap()
        self.triu = k.sbuf("sb_triu", [128, 128], F32)
        self.j = k.sbuf("sb_j", [128, 1], F32)
        self.ones_col = k.sbuf("sb_ones", [128, 8], F32)
        self.consts = k.sbuf("sb_consts", [128, 8], F32)
        self.eps5 = self.consts.ap[:, 0:1]
        self.half = self.consts.ap[:, 1:2]
        self.zero = self.consts.ap[:, 2:3]
        self.one = self.consts.ap[:, 3:4]
        self.c_eighth = self.consts.ap[:, 4:5]
        self.c_nshift = self.consts.ap[:, 5:6]

    def emit(self, k):
        v = k.vector
        t1 = k.sync.dma(self.ident.ap[:], self.ident_dram[:])
        self.ident.did_write(t1)
        t2 = k.sync.dma(self.triu.ap[:], self.triu_dram[:])
        self.triu.did_write(t2)
        t3 = k.sync.dma(self.ident_bf.ap[:], self.identbf_dram[:])
        self.ident_bf.did_write(t3)
        tok = v.op("memset", self.ones_col.ap[:], 1.0, deps=self.ones_col.wdeps())
        self.ones_col.did_write(tok)
        for col, val in [(0, 1e-5), (1, 0.5), (2, 0.0), (3, 1.0),
                         (4, 0.125), (5, -8.0)]:
            tok = v.op("memset", self.consts.ap[:, col:col + 1], float(val),
                       deps=self.consts.wdeps())
            self.consts.did_write(tok)
        tok = v.op("tensor_scalar", self.j.ap[:], self.ones_col.ap[:, 0:1],
                   float(2.0**-30), 1.0, OP.add, OP.subtract,
                   deps=rd(self.ones_col) + self.j.wdeps())
        self.j.did_write(tok)
        self.ones_col.did_read(tok)
        tok = v.op("tensor_scalar", self.j.ap[:], self.j.ap[:],
                   float(-(2.0**30)), 1.0, OP.mult, OP.add,
                   deps=self.j.wdeps())
        self.j.did_write(tok)


def rne_round(k, out_buf, out_ap, x_buf, x_ap, scale_ap, pre, scratch_a, scratch_b,
              scale_buf=None):
    v = k.vector
    sa, sb = scratch_a, scratch_b
    a_ap = sa.ap[:, : x_ap.shape[-1]] if sa.ap.shape[-1] != x_ap.shape[-1] else sa.ap[:]
    b_ap = sb.ap[:, : x_ap.shape[-1]] if sb.ap.shape[-1] != x_ap.shape[-1] else sb.ap[:]
    sdeps = rd(scale_buf) if scale_buf is not None else []
    t = v.op("tensor_scalar", a_ap, x_ap, scale_ap, float(C1), OP.mult, OP.add,
             deps=rd(x_buf, pre.j) + sa.wdeps() + sdeps)
    sa.did_write(t); x_buf.did_read(t)
    if scale_buf is not None:
        scale_buf.did_read(t)
    t = v.op("tensor_scalar", a_ap, a_ap, float(C1), pre.j.ap[:], OP.subtract, OP.mult,
             deps=sa.wdeps())
    sa.did_write(t); pre.j.did_read(t)
    t = v.op("tensor_scalar", b_ap, x_ap, scale_ap, float(C2), OP.mult, OP.add,
             deps=rd(x_buf) + sb.wdeps() + sdeps)
    sb.did_write(t); x_buf.did_read(t)
    if scale_buf is not None:
        scale_buf.did_read(t)
    t = v.op("scalar_tensor_tensor", out_ap, b_ap, float(C2), a_ap,
             OP.subtract, OP.add,
             deps=rd(sa, sb) + out_buf.wdeps())
    out_buf.did_write(t); sa.did_read(t); sb.did_read(t)
    return t


def emit_ln(k, x_src_ap, x_src_buf, xn_tm, stats, scratch, t0, pre=None):
    v, s, sy = k.vector, k.scalar, k.sync
    tok = sy.dma(xn_tm.ap[:].rearrange("p (a d) -> p a d", a=4),
                 x_src_ap[t0:t0 + 512, :].rearrange("(a p) d -> p a d", p=128),
                 deps=xn_tm.wdeps() + rd(x_src_buf))
    xn_tm.did_write(tok)
    x_src_buf.did_read(tok)
    for j in range(4):
        xs = xn_tm.ap[:, j * 1024:(j + 1) * 1024]
        t = v.op("reduce_sum", stats.ap[:, j:j + 1], xs, AX.X,
                 deps=rd(xn_tm) + stats.wdeps())
        stats.did_write(t); xn_tm.did_read(t)
    t = v.op("tensor_scalar_mul", stats.ap[:, 0:4], stats.ap[:, 0:4], 1.0 / 1024.0,
             deps=stats.wdeps())
    stats.did_write(t)
    for j in range(4):
        xs = xn_tm.ap[:, j * 1024:(j + 1) * 1024]
        t = v.op("tensor_scalar", xs, xs, stats.ap[:, j:j + 1], None,
                 OP.subtract, deps=rd(stats) + xn_tm.wdeps())
        xn_tm.did_write(t); stats.did_read(t)
        t = s.op("activation", scratch.ap[:, j * 1024:(j + 1) * 1024], xs,
                 AF.Square, accum_out=stats.ap[:, 8 + j:9 + j],
                 deps=rd(xn_tm) + scratch.wdeps() + stats.wdeps())
        stats.did_write(t); scratch.did_write(t); xn_tm.did_read(t)
    t = v.op("tensor_scalar_mul", stats.ap[:, 8:12], stats.ap[:, 8:12],
             1.0 / 1024.0, deps=stats.wdeps())
    stats.did_write(t)
    t = v.op("tensor_scalar_add", stats.ap[:, 8:12], stats.ap[:, 8:12],
             float(EPS5), deps=stats.wdeps())
    stats.did_write(t)
    t = s.op("activation", stats.ap[:, 12:16], stats.ap[:, 8:12], AF.Sqrt,
             deps=stats.wdeps())
    stats.did_write(t)
    t = v.op("reciprocal", stats.ap[:, 12:16], stats.ap[:, 12:16], deps=stats.wdeps())
    stats.did_write(t)
    t = v.op("tensor_tensor", stats.ap[:, 4:8], stats.ap[:, 12:16],
             stats.ap[:, 12:16], OP.mult, deps=stats.wdeps())
    stats.did_write(t)
    t = v.op("tensor_tensor", stats.ap[:, 4:8], stats.ap[:, 4:8],
             stats.ap[:, 8:12], OP.mult, deps=stats.wdeps())
    stats.did_write(t)
    t = v.op("tensor_scalar", stats.ap[:, 4:8], stats.ap[:, 4:8], -0.5, 1.5,
             OP.mult, OP.add, deps=stats.wdeps())
    stats.did_write(t)
    t = v.op("tensor_tensor", stats.ap[:, 12:16], stats.ap[:, 12:16],
             stats.ap[:, 4:8], OP.mult, deps=stats.wdeps())
    stats.did_write(t)
    for j in range(4):
        xs = xn_tm.ap[:, j * 1024:(j + 1) * 1024]
        t = v.op("tensor_scalar", xs, xs, stats.ap[:, 12 + j:13 + j], None,
                 OP.mult, deps=rd(stats) + xn_tm.wdeps())
        xn_tm.did_write(t); stats.did_read(t)
    return xn_tm


def emit_tm_to_fm(k, src_tm, dst_fm, pre, psA, psB, n_sub=4, width=1024,
                  src_bf16=False):
    te, s = k.tensor, k.scalar
    nchunk = width // 128
    ntok = n_sub * 128
    ps = [psA, psB]
    grp = 0
    pairs = [(c, j) for c in range(nchunk) for j in range(n_sub)]
    for g0 in range(0, len(pairs), 4):
        bank = ps[grp % 2]
        grp += 1
        gtoks = []
        wd = bank.wdeps() + rd(src_tm, pre.ident)
        ident_ap = pre.ident_bf.ap[:] if src_bf16 else pre.ident.ap[:]
        bank_view = bank.ap.bitcast(BF16) if src_bf16 else bank.ap
        for i, (c, j) in enumerate(pairs[g0:g0 + 4]):
            src = src_tm.ap[:, j * width + c * 128: j * width + (c + 1) * 128]
            t = k.tensor.op("transpose", bank_view[:, i * 128:(i + 1) * 128], src,
                            ident_ap,
                            deps=wd if i == 0 else (), inc=(i == len(pairs[g0:g0 + 4]) - 1))
        bank.did_write(t)
        src_tm.did_read(t)
        (pre.ident_bf if src_bf16 else pre.ident).did_read(t)
        for i, (c, j) in enumerate(pairs[g0:g0 + 4]):
            dst = dst_fm.ap[:, c * ntok + j * 128: c * ntok + (j + 1) * 128]
            tt = s.op("activation", dst, bank_view[:, i * 128:(i + 1) * 128], AF.Copy,
                      deps=rd(bank) + dst_fm.wdeps())
            dst_fm.did_write(tt)
            bank.did_read(tt)
    return dst_fm



def emit_actquant(k, xn_tm, n_sub, width, small, small_col, xq_f32, pre,
                  scr_a, scr_b, out_scale_col, sw_b, clip1=False):
    v = k.vector
    sc0 = small_col
    dq0 = small_col + n_sub
    src3 = xn_tm.ap[:, :n_sub * width].rearrange("p (a w) -> p a w", a=n_sub)
    t = v.op("reduce_max", small.ap[:, sc0:sc0 + n_sub], src3, AX.X,
             apply_absolute_value=True, deps=rd(xn_tm) + small.wdeps())
    small.did_write(t); xn_tm.did_read(t)
    t = v.op("tensor_scalar_max", small.ap[:, sc0:sc0 + n_sub],
             small.ap[:, sc0:sc0 + n_sub], float(EPS5), deps=small.wdeps())
    small.did_write(t)
    if sw_b is not None:
        t = v.op("scalar_tensor_tensor", small.ap[:, dq0:dq0 + n_sub],
                 small.ap[:, sc0:sc0 + n_sub], float(1.0 / 127.0),
                 sw_b.ap[:, 0:1].broadcast_to([128, n_sub]),
                 OP.mult, OP.mult, deps=small.wdeps() + rd(sw_b))
        small.did_write(t); sw_b.did_read(t)
    else:
        t = v.op("tensor_scalar_mul", small.ap[:, dq0:dq0 + n_sub],
                 small.ap[:, sc0:sc0 + n_sub], float(1.0 / 127.0),
                 deps=small.wdeps())
        small.did_write(t)
    t = v.op("reciprocal", small.ap[:, sc0:sc0 + n_sub],
             small.ap[:, sc0:sc0 + n_sub], deps=small.wdeps())
    small.did_write(t)
    t = v.op("tensor_scalar_mul", small.ap[:, sc0:sc0 + n_sub],
             small.ap[:, sc0:sc0 + n_sub], 127.0, deps=small.wdeps())
    small.did_write(t)
    for j in range(n_sub):
        for q0 in range(0, width, 1024):
            qw = min(1024, width - q0)
            off = j * width + q0
            rne_round(k, xq_f32, xq_f32.ap[:, off:off + qw],
                      xn_tm, xn_tm.ap[:, off:off + qw],
                      small.ap[:, sc0 + j:sc0 + j + 1], pre, scr_a, scr_b,
                      scale_buf=small)
    return dq0


def emit_wquant_bitlin(k, w_dram, n_in, n_out, wq_dst, wq_off, pre,
                       scr_w, scr_a, scr_b, small, sw_b, ones_row, ps):
    v, s, sy = k.vector, k.scalar, k.sync
    nch = n_in // 128
    acc = small.ap[:, 14:15]
    tmp = small.ap[:, 13:14]
    t = v.op("memset", acc, 0.0, deps=small.wdeps())
    small.did_write(t)
    for c in range(nch):
        t = sy.dma(scr_w.ap[:, :n_out], w_dram[c * 128:(c + 1) * 128, :],
                   deps=scr_w.wdeps())
        scr_w.did_write(t)
        t = v.op("reduce_sum", tmp, scr_w.ap[:, :n_out], AX.X,
                 apply_absolute_value=True, deps=rd(scr_w) + small.wdeps())
        small.did_write(t); scr_w.did_read(t)
        t = v.op("tensor_tensor", acc, acc, tmp, OP.add, deps=small.wdeps())
        small.did_write(t)
    t = k.gpsimd.op("tensor_reduce", small.ap[0:1, 12:13], acc, AX.C, OP.add,
                    deps=small.wdeps())
    small.did_write(t)
    t = v.op("tensor_scalar_mul", small.ap[0:1, 12:13], small.ap[0:1, 12:13],
             float(1.0 / (n_in * n_out)), deps=small.wdeps())
    small.did_write(t)
    t = v.op("tensor_scalar_max", small.ap[0:1, 12:13], small.ap[0:1, 12:13],
             float(EPS5), deps=small.wdeps())
    small.did_write(t)
    tk = k.tensor.op("matmul", ps.ap[:, 0:1], ones_row.ap[0:1, :],
                     small.ap[0:1, 12:13], start=True, stop=True,
                     deps=rd(ones_row, small) + ps.wdeps())
    ps.did_write(tk); ones_row.did_read(tk); small.did_read(tk)
    t = v.op("tensor_copy", sw_b.ap[:, 0:1], ps.ap[:, 0:1],
             deps=rd(ps) + sw_b.wdeps())
    sw_b.did_write(t); ps.did_read(t)
    t = v.op("reciprocal", sw_b.ap[:, 1:2], sw_b.ap[:, 0:1], deps=sw_b.wdeps())
    sw_b.did_write(t)
    for c in range(nch):
        t = sy.dma(scr_w.ap[:, :n_out], w_dram[c * 128:(c + 1) * 128, :],
                   deps=scr_w.wdeps())
        scr_w.did_write(t)
        for j in range(0, n_out, 1024):
            w = min(1024, n_out - j)
            rne_round(k, scr_w, scr_w.ap[:, j:j + w], scr_w,
                      scr_w.ap[:, j:j + w], sw_b.ap[:, 1:2], pre, scr_a, scr_b,
                      scale_buf=sw_b)
        dst = wq_dst.ap.bitcast(BF16)[:, wq_off + c * n_out: wq_off + (c + 1) * n_out]
        t = v.op("tensor_scalar", dst, scr_w.ap[:, :n_out], -1.0, 1.0,
                 OP.max, OP.min, deps=rd(scr_w) + wq_dst.wdeps())
        wq_dst.did_write(t); scr_w.did_read(t)
    return



def emit_bitlin_mm(k, xq_fm_ap, xq_fm_buf, wq_ap_fn, wq_buf, n_k, psum, j,
                   o, owidth):
    pairs = []
    for c in range(n_k):
        pairs.append((xq_fm_buf, xq_fm_ap(c, j), wq_buf, wq_ap_fn(c, o)))
    return mm_group(k, psum, psum.ap[:, :owidth], pairs)


def emit_ffn_bitlin(k, B, pre, x_src_ap, x_src_buf, g_dram, g_buf,
                    w_drams, variant, hq_dram, hq_buf, dqh_dram, dqh_buf):
    v, s, sy = k.vector, k.scalar, k.sync
    WRES, G0, G1 = B["WRES"], B["G0"], B["G1"]
    B["SCRW"].wtoks = list(G1.wtoks)
    B["SCRW"].rtoks = list(G1.rtoks)
    H0, SMALL, SM2 = B["H0"], B["SMALL"], B["SMALL2"]
    SWA, SWB, SWC = B["SWA"], B["SWB"], B["SWC"]

    if variant == "gelu":
        wlist = [("w1", SWA, 0)]
    else:
        wlist = [("wg", SWA, 0), ("wu", SWB, 32768)]
    for nm, swb, off in wlist:
        emit_wquant_bitlin(k, w_drams[nm], 1024, 4096, WRES, off, pre,
                           B["SCRW"], B["SA"], B["SB"], SMALL, swb,
                           B["ONESROW"], B["PS4"])
    alias_sync(G1, B["SCRW"])

    def xq_ap(c, j):
        return H0.ap[:, c * 512 + j * 128: c * 512 + (j + 1) * 128]

    def wq_ap(off):
        def f(c, o):
            return WRES.ap[:, off + c * 4096 + o * 512: off + c * 4096 + (o + 1) * 512]
        return f

    for t0 in range(0, T, 512):
        emit_ln(k, x_src_ap, x_src_buf, G0, B["STATS"], G1, t0, pre)
        emit_actquant(k, G0, 4, 1024, SMALL, 0, G1, pre, B["SA"], B["SB"],
                      4, None)
        tok = v.op("scalar_tensor_tensor", SM2.ap[:, 0:4], SMALL.ap[:, 4:8],
                   1.0, SWA.ap[:, 0:1].broadcast_to([128, 4]), OP.mult,
                   OP.mult, deps=rd(SMALL, SWA) + SM2.wdeps())
        SM2.did_write(tok); SMALL.did_read(tok); SWA.did_read(tok)
        if variant == "swiglu":
            tok = v.op("scalar_tensor_tensor", SM2.ap[:, 4:8],
                       SMALL.ap[:, 4:8], 1.0,
                       SWB.ap[:, 0:1].broadcast_to([128, 4]), OP.mult,
                       OP.mult, deps=rd(SMALL, SWB) + SM2.wdeps())
            SM2.did_write(tok); SMALL.did_read(tok); SWB.did_read(tok)
        xq_fm = sub_buf(H0, 0, 4096)
        emit_tm_to_fm(k, G1, xq_fm, pre, B["PS2"], B["PS3"])
        merge_sub(H0, xq_fm)
        for j in range(4):
            for o in range(8):
                ps = B["PS0"] if o % 2 == 0 else B["PS1"]
                emit_bitlin_mm(k, xq_ap, H0, wq_ap(0), WRES, 8, ps, j, o, 512)
                fn = AF.Gelu if variant == "gelu" else AF.Silu
                tok = s.op("activation", G1.ap[:, o * 512:(o + 1) * 512],
                           ps.ap[:, 0:512], fn, scale=SM2.ap[:, j:j + 1],
                           deps=rd(ps, SM2) + G1.wdeps())
                G1.did_write(tok); ps.did_read(tok); SM2.did_read(tok)
            if variant == "swiglu":
                for o in range(8):
                    ps = B["PS0"] if o % 2 == 0 else B["PS1"]
                    emit_bitlin_mm(k, xq_ap, H0, wq_ap(32768), WRES, 8, ps,
                                   j, o, 512)
                    tok = s.op("activation", G0.ap[:, o * 512:(o + 1) * 512],
                               ps.ap[:, 0:512], AF.Copy,
                               scale=SM2.ap[:, 4 + j:5 + j],
                               deps=rd(ps, SM2) + G0.wdeps())
                    G0.did_write(tok); ps.did_read(tok); SM2.did_read(tok)
                tok = v.op("tensor_tensor", G1.ap[:], G1.ap[:], G0.ap[:],
                           OP.mult, deps=rd(G0) + G1.wdeps())
                G1.did_write(tok); G0.did_read(tok)
            emit_actquant(k, G1, 1, 4096, SMALL, 8, G0, pre, B["SA"],
                          B["SB"], 1, None)
            tok = v.op("tensor_copy", B["HQ16"].ap[:, 0:4096], G0.ap[:],
                       deps=rd(G0) + B["HQ16"].wdeps())
            B["HQ16"].did_write(tok); G0.did_read(tok)
            r0 = t0 + j * 128
            tok = sy.dma(hq_dram[r0:r0 + 128, :], B["HQ16"].ap[:, 0:4096],
                         deps=rd(B["HQ16"]) + hq_buf.wdeps())
            hq_buf.did_write(tok); B["HQ16"].did_read(tok)
            col = t0 // 128 + j
            tok = v.op("tensor_copy", B["DQH"].ap[:, col:col + 1],
                       SMALL.ap[:, 9:10], deps=rd(SMALL) + B["DQH"].wdeps())
            B["DQH"].did_write(tok); SMALL.did_read(tok)

    w2 = "w2" if variant == "gelu" else "wd"
    B["SCRW"].wtoks = list(G1.wtoks)
    B["SCRW"].rtoks = list(G1.rtoks)
    emit_wquant_bitlin(k, w_drams[w2], 4096, 1024, WRES, 0, pre,
                       B["SCRW"], B["SA"], B["SB"], SMALL, SWC,
                       B["ONESROW"], B["PS4"])
    alias_sync(G1, B["SCRW"])
    for t0 in range(0, T, 512):
        c0 = t0 // 128
        tok = v.op("scalar_tensor_tensor", SM2.ap[:, 8:12],
                   B["DQH"].ap[:, c0:c0 + 4], 1.0,
                   SWC.ap[:, 0:1].broadcast_to([128, 4]), OP.mult,
                   OP.mult, deps=rd(SWC, B["DQH"]) + SM2.wdeps())
        SM2.did_write(tok); SWC.did_read(tok); B["DQH"].did_read(tok)
        for j in range(4):
            r0 = t0 + j * 128
            tok = sy.dma(B["HQ16"].ap[:, 0:4096], hq_dram[r0:r0 + 128, :],
                         deps=B["HQ16"].wdeps() + rd(hq_buf))
            B["HQ16"].did_write(tok); hq_buf.did_read(tok)
            hq_fm = sub_buf(H0, 4096, 4096)
            emit_tm_to_fm(k, B["HQ16"], hq_fm, pre, B["PS2"], B["PS3"],
                          n_sub=1, width=4096, src_bf16=True)
            merge_sub(H0, hq_fm)
            for o2 in range(2):
                ps = B["PS0"] if o2 == 0 else B["PS1"]
                pairs = []
                for c in range(32):
                    pairs.append((H0,
                                  H0.ap[:, 4096 + c * 128: 4096 + (c + 1) * 128],
                                  WRES,
                                  WRES.ap[:, c * 1024 + o2 * 512:
                                          c * 1024 + (o2 + 1) * 512]))
                mm_group(k, ps, ps.ap[:, 0:512], pairs)
                tok = s.op("activation", G1.ap[:, o2 * 512:(o2 + 1) * 512],
                           ps.ap[:, 0:512], AF.Silu,
                           scale=SM2.ap[:, 8 + j:9 + j],
                           deps=rd(ps, SM2) + G1.wdeps())
                G1.did_write(tok); ps.did_read(tok); SM2.did_read(tok)
            tok = sy.dma(g_dram[r0:r0 + 128, :], G1.ap[:, 0:1024],
                         deps=rd(G1) + g_buf.wdeps())
            g_buf.did_write(tok); G1.did_read(tok)


def alias_sync(dst, src_alias):
    dst.wtoks = list(src_alias.wtoks)
    dst.rtoks = list(src_alias.rtoks)


def sub_buf(parent, off, n):
    b = Buf(parent.ap[:, off:off + n])
    b.wtoks = list(parent.wtoks)
    b.rtoks = list(parent.rtoks)
    return b


def merge_sub(parent, child):
    parent.wtoks = list(set(parent.wtoks + child.wtoks))
    parent.rtoks = list(set(parent.rtoks + child.rtoks))


def make_buffers(k):
    B = {}
    B["WRES"] = k.sbuf("WRES", [128, 65536], BF16)
    B["G0"] = k.sbuf("G0", [128, 4096], F32)
    B["G1"] = k.sbuf("G1", [128, 4096], F32)
    B["H0"] = k.sbuf("H0", [128, 8192], BF16)
    B["HQ16"] = k.sbuf("HQ16", [128, 4096], BF16)
    B["SCRW"] = Buf(B["G1"].ap[:])
    B["SA"] = k.sbuf("SA", [128, 1024], F32)
    B["SB"] = k.sbuf("SB", [128, 1024], F32)
    B["SC"] = k.sbuf("SC", [128, 1024], F32)
    B["BLK"] = k.sbuf("BLK", [128, 1024], F32)
    B["SMALL"] = k.sbuf("SMALL", [128, 64], F32)
    B["SMALL2"] = k.sbuf("SMALL2", [128, 64], F32)
    B["STATS"] = k.sbuf("STATS", [128, 16], F32)
    B["SWA"] = k.sbuf("SWA", [128, 2], F32)
    B["SWB"] = k.sbuf("SWB", [128, 2], F32)
    B["SWC"] = k.sbuf("SWC", [128, 2], F32)
    B["ONESROW"] = k.sbuf("ONESROW", [1, 128], F32)
    B["DQH"] = k.sbuf("DQH", [128, 32], F32)
    B["XT"] = Buf(B["H0"].ap.bitcast(F32))
    B["BIAS1"] = Buf(B["HQ16"].ap.bitcast(F32)[:, 0:1024])
    B["BIAS2"] = Buf(B["HQ16"].ap.bitcast(F32)[:, 1024:2048])
    for i in range(8):
        B[f"PS{i}"] = k.psum(f"PS{i}", [128, 512], F32)
    return B


def emit_buffer_init(k, B):
    tok = k.vector.op("memset", B["ONESROW"].ap[:], 1.0,
                      deps=B["ONESROW"].wdeps())
    B["ONESROW"].did_write(tok)



def emit_moe(k, B, pre, x_src_ap, x_src_buf, g_dram, g_buf, w):
    v, s, sy = k.vector, k.scalar, k.sync
    W = Buf(B["WRES"].ap.bitcast(F32))
    G0, G1 = B["G0"], B["G1"]
    SMALL, SM2 = B["SMALL"], B["SMALL2"]
    HFM = Buf(B["H0"].ap.bitcast(F32))

    def wload(dst_off, src_ap, nch, ncol):
        wdeps = W.wdeps()
        first = True
        for c in range(nch):
            t = sy.dma(W.ap[:, dst_off + c * ncol: dst_off + (c + 1) * ncol],
                       src_ap[c * 128:(c + 1) * 128, :], deps=wdeps)
            if first:
                W.did_write(t)
                first = False
            else:
                W.did_write_acc(t)
        return

    wload(0, w["Ws1"], 8, 512)
    wload(4096, w["Ws2"], 4, 1024)
    wload(8192, w["Wr"], 8, 8)
    EX0 = 8448

    def exp_off(e):
        return EX0 + (e % 2) * 8192

    wload(exp_off(0), w["W1"][0], 8, 512)
    wload(exp_off(0) + 4096, w["W2"][0], 4, 1024)

    for t0 in range(0, T, 512):
        emit_ln(k, x_src_ap, x_src_buf, G0, B["STATS"], G1, t0, pre)
        xn_fm = sub_buf(G1, 0, 4096)
        emit_tm_to_fm(k, G0, xn_fm, pre, B["PS2"], B["PS3"])
        alias_sync(G1, xn_fm)

        def xn_ap(c, j):
            return G1.ap[:, c * 512 + j * 128: c * 512 + (j + 1) * 128]

        def xn_mv(c):
            return G1.ap[:, c * 512:(c + 1) * 512]

        for j in range(4):
            ps = B["PS4"]
            pairs = [(G1, xn_ap(c, j), W, W.ap[:, 8192 + c * 8: 8192 + (c + 1) * 8])
                     for c in range(8)]
            mm_group(k, ps, ps.ap[:, 0:8], pairs)
            t = v.op("max", SMALL.ap[:, 0:8], ps.ap[:, 0:8],
                     deps=rd(ps) + SMALL.wdeps())
            SMALL.did_write(t); ps.did_read(t)
            t = v.op("tensor_tensor", SMALL.ap[:, 8:9], SMALL.ap[:, 1:2],
                     SMALL.ap[:, 0:1], OP.subtract, deps=SMALL.wdeps())
            SMALL.did_write(t)
            t = s.op("activation", SMALL.ap[:, 9:10], SMALL.ap[:, 8:9], AF.Exp,
                     deps=SMALL.wdeps())
            SMALL.did_write(t)
            t = v.op("tensor_scalar_add", SMALL.ap[:, 9:10], SMALL.ap[:, 9:10],
                     1.0, deps=SMALL.wdeps())
            SMALL.did_write(t)
            t = v.op("reciprocal", SMALL.ap[:, 9:10], SMALL.ap[:, 9:10],
                     deps=SMALL.wdeps())
            SMALL.did_write(t)
            t = v.op("tensor_single_scalar", SMALL.ap[:, 9:10],
                     SMALL.ap[:, 9:10], 1.0, OP.is_equal, deps=SMALL.wdeps())
            SMALL.did_write(t)
            t = v.op("tensor_scalar", SM2.ap[:, 8 * j:8 * j + 8], ps.ap[:, 0:8],
                     SMALL.ap[:, 0:1], SMALL.ap[:, 9:10], OP.is_equal, OP.mult,
                     deps=rd(ps, SMALL) + SM2.wdeps())
            SM2.did_write(t); ps.did_read(t); SMALL.did_read(t)

        for e in range(-1, 8):
            if 0 <= e < 7:
                wload(exp_off(e + 1), w["W1"][e + 1], 8, 512)
                wload(exp_off(e + 1) + 4096, w["W2"][e + 1], 4, 1024)
            w1off = 0 if e < 0 else exp_off(e)
            w2off = 4096 if e < 0 else exp_off(e) + 4096
            for f in range(4):
                ps = B["PS0"] if f % 2 == 0 else B["PS1"]
                pairs = [(W, W.ap[:, w1off + c * 512 + f * 128:
                               w1off + c * 512 + (f + 1) * 128],
                          G1, xn_mv(c)) for c in range(8)]
                mm_group(k, ps, ps.ap[:, 0:512], pairs)
                hslice = HFM.ap[:, f * 512:(f + 1) * 512]
                t = s.op("activation", hslice, ps.ap[:, 0:512], AF.Relu,
                         deps=rd(ps) + HFM.wdeps())
                HFM.did_write(t); ps.did_read(t)
                t = v.op("tensor_tensor", hslice, hslice, ps.ap[:, 0:512],
                         OP.mult, deps=rd(ps) + HFM.wdeps())
                HFM.did_write(t); ps.did_read(t)
            for j in range(4):
                for o2 in range(2):
                    ps = B["PS0"] if o2 == 0 else B["PS1"]
                    pairs = [(HFM, HFM.ap[:, f * 512 + j * 128:
                                  f * 512 + (j + 1) * 128],
                              W, W.ap[:, w2off + f * 1024 + o2 * 512:
                                      w2off + f * 1024 + (o2 + 1) * 512])
                             for f in range(4)]
                    mm_group(k, ps, ps.ap[:, 0:512], pairs)
                    dst = G0.ap[:, j * 1024 + o2 * 512: j * 1024 + (o2 + 1) * 512]
                    if e < 0:
                        t = s.op("activation", dst, ps.ap[:, 0:512], AF.Copy,
                                 deps=rd(ps) + G0.wdeps())
                        G0.did_write(t); ps.did_read(t)
                    else:
                        t = v.op("scalar_tensor_tensor", dst, ps.ap[:, 0:512],
                                 SM2.ap[:, 8 * j + e:8 * j + e + 1], dst,
                                 OP.mult, OP.add, deps=rd(ps, SM2) + G0.wdeps())
                        G0.did_write(t); ps.did_read(t); SM2.did_read(t)
        for j in range(4):
            t = s.op("activation", G0.ap[:, j * 1024:(j + 1) * 1024],
                     G0.ap[:, j * 1024:(j + 1) * 1024], AF.Silu,
                     deps=G0.wdeps())
            G0.did_write(t)
            r0 = t0 + j * 128
            t = sy.dma(g_dram[r0:r0 + 128, :],
                       G0.ap[:, j * 1024:(j + 1) * 1024],
                       deps=rd(G0) + g_buf.wdeps())
            g_buf.did_write(t); G0.did_read(t)



def emit_l1_attn(k, B, pre, x_in_ap, x_in_buf, x_out_ap, x_out_buf,
                 g_dram, g_buf, w):
    v, s, sy = k.vector, k.scalar, k.sync
    W = Buf(B["WRES"].ap.bitcast(F32))
    G0, G1 = B["G0"], B["G1"]
    QO, KO, VO, PO, OO = 0, 8192, 16384, 24704, 26752

    BQ = B["SMALL2"]
    tok = sy.dma(BQ.ap[:, 0:24], w["b_qkv"].rearrange("(c p) -> p c", p=128),
                 deps=BQ.wdeps(), allow_slow_non_contiguous=True)
    BQ.did_write(tok)
    tok = sy.dma(B["BIAS1"].ap[:, 0:1024],
                 w["b_out"][None, :].partition_broadcast(128),
                 deps=B["BIAS1"].wdeps())
    B["BIAS1"].did_write(tok)
    tok = sy.dma(B["BIAS2"].ap[:, 0:1024],
                 w["b_qkv"][None, 2048:3072].partition_broadcast(128),
                 deps=B["BIAS2"].wdeps())
    B["BIAS2"].did_write(tok)
    qb = [sub_buf(W, QO, 4096), sub_buf(W, QO + 4096, 4096)]
    kb = [sub_buf(W, KO, 4096), sub_buf(W, KO + 4096, 4096)]
    vb = [sub_buf(W, VO, 4160), sub_buf(W, VO + 4160, 4160)]
    pb = [sub_buf(W, PO + tj * 512, 512) for tj in range(4)]
    ob = sub_buf(W, OO, 4096)
    for sl in range(2):
        dst = vb[sl].ap[:].rearrange(
            "p (a h o) -> p a h o", a=4, h=16)[:, :, :, 64:65]
        tok = v.op("memset", dst, 1.0, deps=vb[sl].wdeps())
        vb[sl].did_write(tok)

    def stage_w(dram_ap, col0, ncol, dstb):
        wdeps = dstb.wdeps()
        first = True
        for c in range(8):
            t = sy.dma(dstb.ap[:, c * ncol:(c + 1) * ncol],
                       dram_ap[c * 128:(c + 1) * 128, col0:col0 + ncol],
                       deps=wdeps)
            if first:
                dstb.did_write(t)
                first = False
            else:
                dstb.did_write_acc(t)

    def compute_kv(i_blk, slot, with_q):
        t0 = i_blk * 512
        emit_ln(k, x_in_ap, x_in_buf, G0, B["STATS"], G1, t0, pre)
        xn_fm = sub_buf(G1, 0, 4096)
        emit_tm_to_fm(k, G0, xn_fm, pre, B["PS2"], B["PS3"])
        alias_sync(G1, xn_fm)
        proj_list = [("k", 1024, kb[slot])] + \
            ([("q", 0, qb[slot])] if with_q else [])
        sgi = [0]
        for nm, wcol0, dstb in proj_list:
            for half in range(2):
                sgb = G0 if sgi[0] % 2 == 0 else B["XT"]
                sgi[0] += 1
                stage_w(w["W_qkv"], wcol0 + half * 512, 512, sgb)
                for hp in range(4):
                    ps = B["PS0"] if hp % 2 == 0 else B["PS1"]
                    pairs = [(sgb, sgb.ap[:, c * 512 + hp * 128:
                                  c * 512 + hp * 128 + 128],
                              G1, G1.ap[:, c * 512:(c + 1) * 512])
                             for c in range(8)]
                    mm_group(k, ps, ps.ap[:, 0:512], pairs)
                    bias_col = (wcol0 + half * 512 + hp * 128) // 128
                    bstart = dstb.ap.offset * 2
                    WB16 = B["WRES"].ap
                    hi = WB16[:, bstart + (half * 4 + hp) * 512:
                              bstart + (half * 4 + hp + 1) * 512]
                    lo = WB16[:, bstart + 4096 + (half * 4 + hp) * 512:
                              bstart + 4096 + (half * 4 + hp + 1) * 512]
                    t = s.op("activation", hi, ps.ap[:, 0:512], AF.Identity,
                             bias=BQ.ap[:, bias_col:bias_col + 1],
                             deps=rd(ps, BQ) + dstb.wdeps())
                    if half == 0 and hp == 0:
                        dstb.did_write(t)
                    else:
                        dstb.did_write_acc(t)
                    ps.did_read(t); BQ.did_read(t)
                    t = v.op("scalar_tensor_tensor", lo, ps.ap[:, 0:512],
                             BQ.ap[:, bias_col:bias_col + 1], hi,
                             OP.add, OP.subtract,
                             deps=rd(ps, BQ) + dstb.wdeps())
                    dstb.did_write_acc(t); ps.did_read(t); BQ.did_read(t)
        for half in range(2):
            sgb = G0 if sgi[0] % 2 == 0 else B["XT"]
            sgi[0] += 1
            stage_w(w["W_qkv"], 2048 + half * 512, 512, sgb)
            for j in range(4):
                ps = B["PS0"] if j % 2 == 0 else B["PS1"]
                pairs = [(G1, G1.ap[:, c * 512 + j * 128:
                              c * 512 + j * 128 + 128],
                          sgb, sgb.ap[:, c * 512:(c + 1) * 512])
                         for c in range(8)]
                mm_group(k, ps, ps.ap[:, 0:512], pairs)
                base = j * 1040 + half * 520
                dst = vb[slot].ap[:, base:base + 520].rearrange(
                    "p (h o) -> p h o", h=8)[:, :, 0:64]
                t = v.op("scalar_tensor_tensor", dst, ps.ap[:, 0:512], 1.0,
                         B["BIAS2"].ap[:, half * 512:(half + 1) * 512]
                         .rearrange("p (h o) -> p h o", h=8),
                         OP.mult, OP.add, deps=rd(ps, B["BIAS2"]) + vb[slot].wdeps())
                vb[slot].did_write_acc(t)
                ps.did_read(t); B["BIAS2"].did_read(t)

    def attend(j):
        qslot = j % 2
        for h in range(16):
            hp, hh = h // 2, h % 2
            q_ap = qb[qslot].ap[64 * hh:64 * hh + 64,
                                hp * 512:(hp + 1) * 512]
            for tgt in range(2):
                kslot = (j + tgt) % 2
                psO = B["PS2"] if (2 * h + tgt) % 2 == 0 else B["PS3"]
                for tj in range(4):
                    ps_s = B["PS0"] if tj % 2 == 0 else B["PS1"]
                    WB16 = B["WRES"].ap
                    kbst = kb[kslot].ap.offset * 2
                    qbst = qb[qslot].ap.offset * 2
                    ksl = slice(kbst + hp * 512 + tj * 128,
                                kbst + hp * 512 + tj * 128 + 128)
                    qsl = slice(qbst + hp * 512, qbst + (hp + 1) * 512)
                    khi = WB16[64 * hh:64 * hh + 64, ksl]
                    klo = WB16[64 * hh:64 * hh + 64,
                               ksl.start + 4096:ksl.stop + 4096]
                    qhi = WB16[64 * hh:64 * hh + 64, qsl]
                    qlo = WB16[64 * hh:64 * hh + 64,
                               qsl.start + 4096:qsl.stop + 4096]
                    tok = k.tensor.op("matmul", ps_s.ap[:, 0:512], khi, qhi,
                                      start=True, stop=False,
                                      deps=rd(kb[kslot], qb[qslot]) + ps_s.wdeps())
                    tok = k.tensor.op("matmul", ps_s.ap[:, 0:512], khi, qlo,
                                      start=False, stop=False)
                    tok = k.tensor.op("matmul", ps_s.ap[:, 0:512], klo, qhi,
                                      start=False, stop=True)
                    ps_s.did_write(tok)
                    kb[kslot].did_read(tok); qb[qslot].did_read(tok)
                    t = s.op("activation", pb[tj].ap[:], ps_s.ap[:, 0:512],
                             AF.Exp, scale=pre.c_eighth, bias=pre.c_nshift,
                             deps=rd(ps_s, pre.consts) + pb[tj].wdeps())
                    pb[tj].did_write(t); ps_s.did_read(t)
                    pre.consts.did_read(t)
                    vap = vb[kslot].ap[:, tj * 1040 + h * 65:
                                       tj * 1040 + (h + 1) * 65]
                    tok = k.tensor.op("matmul", psO.ap[0:65, 0:512], vap,
                                      pb[tj].ap[:],
                                      start=(tj == 0), stop=(tj == 3),
                                      deps=rd(vb[kslot], pb[tj]) +
                                      (psO.wdeps() if tj == 0 else []),
                                      inc=True)
                    vb[kslot].did_read(tok); pb[tj].did_read(tok)
                    if tj == 3:
                        psO.did_write(tok)
                t = v.op("reciprocal", B["SA"].ap[0:1, 0:512],
                         psO.ap[64:65, 0:512], deps=rd(psO) + B["SA"].wdeps())
                B["SA"].did_write(t); psO.did_read(t)
                tok = k.tensor.op("matmul", B["PS4"].ap[0:64, 0:512],
                                  B["ONESROW"].ap[0:1, 0:64],
                                  B["SA"].ap[0:1, 0:512], start=True, stop=True,
                                  deps=rd(B["ONESROW"], B["SA"]) + B["PS4"].wdeps())
                B["PS4"].did_write(tok)
                B["ONESROW"].did_read(tok); B["SA"].did_read(tok)
                t = s.op("activation", B["SB"].ap[0:64, 0:512],
                         B["PS4"].ap[0:64, 0:512], AF.Copy,
                         deps=rd(B["PS4"]) + B["SB"].wdeps())
                B["SB"].did_write(t); B["PS4"].did_read(t)
                if tgt == 0:
                    t = v.op("tensor_tensor", B["SB"].ap[0:64, 512:1024],
                             psO.ap[0:64, 0:512], B["SB"].ap[0:64, 0:512],
                             OP.mult, deps=rd(psO) + B["SB"].wdeps())
                    B["SB"].did_write(t); psO.did_read(t)
                else:
                    t = v.op("tensor_tensor", B["SA"].ap[0:64, 0:512],
                             psO.ap[0:64, 0:512], B["SB"].ap[0:64, 0:512],
                             OP.mult, deps=rd(psO) + B["SA"].wdeps() + rd(B["SB"]))
                    B["SA"].did_write(t); psO.did_read(t); B["SB"].did_read(t)
                    t = v.op("tensor_tensor", B["SB"].ap[0:64, 512:1024],
                             B["SB"].ap[0:64, 512:1024],
                             B["SA"].ap[0:64, 0:512], OP.add,
                             deps=rd(B["SA"]) + B["SB"].wdeps())
                    B["SB"].did_write(t); B["SA"].did_read(t)
                    odst = ob.ap[64 * hh:64 * hh + 64,
                                 hp * 512:(hp + 1) * 512]
                    t = s.op("activation", odst, B["SB"].ap[0:64, 512:1024],
                             AF.Copy, deps=rd(B["SB"]) + ob.wdeps())
                    if h == 0:
                        ob.did_write(t)
                    else:
                        ob.did_write_acc(t)
                    B["SB"].did_read(t)
        for half in range(2):
            sgb = G0 if half == 0 else B["XT"]
            stage_w(w["W_out"], half * 512, 512, sgb)
            for sub in range(4):
                ps = B["PS5"] if sub % 2 == 0 else B["PS6"]
                pairs = [(ob, ob.ap[:, c * 512 + sub * 128:
                              c * 512 + sub * 128 + 128],
                          sgb, sgb.ap[:, c * 512:(c + 1) * 512])
                         for c in range(8)]
                mm_group(k, ps, ps.ap[:, 0:512], pairs)
                dst = G1.ap[:, sub * 1024 + half * 512:
                            sub * 1024 + (half + 1) * 512]
                t = v.op("scalar_tensor_tensor", dst, ps.ap[:, 0:512], 1.0,
                         B["BIAS1"].ap[:, half * 512:(half + 1) * 512],
                         OP.mult, OP.add, deps=rd(ps, B["BIAS1"]) + G1.wdeps())
                G1.did_write(t); ps.did_read(t); B["BIAS1"].did_read(t)
        t0 = j * 512
        tok = sy.dma(B["SC"].ap[:, 0:512].rearrange("p (a d) -> p a d", a=1),
                     x_in_ap[t0:t0 + 128, 0:512].rearrange("(a p) d -> p a d", p=128),
                     deps=B["SC"].wdeps() + rd(x_in_buf)) if False else None
        tok = sy.dma(B["XT"].ap[:].rearrange("p (a d) -> p a d", a=4),
                     x_in_ap[t0:t0 + 512, :].rearrange("(a p) d -> p a d", p=128),
                     deps=B["XT"].wdeps() + rd(x_in_buf))
        B["XT"].did_write(tok); x_in_buf.did_read(tok)
        tok = sy.dma(G0.ap[:].rearrange("p (a d) -> p a d", a=4),
                     g_dram[t0:t0 + 512, :].rearrange("(a p) d -> p a d", p=128),
                     deps=G0.wdeps() + rd(g_buf))
        G0.did_write(tok); g_buf.did_read(tok)
        t = v.op("tensor_tensor", G1.ap[:], G1.ap[:], G0.ap[:], OP.mult,
                 deps=rd(G0) + G1.wdeps())
        G1.did_write(t); G0.did_read(t)
        t = v.op("tensor_tensor", G1.ap[:], G1.ap[:], B["XT"].ap[:], OP.add,
                 deps=rd(B["XT"]) + G1.wdeps())
        G1.did_write(t); B["XT"].did_read(t)
        tok = sy.dma(x_out_ap[t0:t0 + 512, :].rearrange("(a p) d -> p a d", p=128),
                     G1.ap[:].rearrange("p (a d) -> p a d", a=4),
                     deps=rd(G1) + x_out_buf.wdeps())
        x_out_buf.did_write(tok); G1.did_read(tok)

    for i in range(9):
        blk = i if i < 8 else 0
        compute_kv(blk, i % 2, with_q=(i < 8))
        if i >= 1:
            attend(i - 1)


def barrier(k):
    engs = [k.tensor, k.vector, k.scalar, k.gpsimd, k.sync]
    toks = [(e.sem, e.sem.cnt) for e in engs]
    for e in engs:
        if getattr(e, "dma_queue", False):
            toks += [(s, s.cnt) for s in e.pool]
    for e in engs:
        e.wait(toks)



PI = float(np.pi)
TWO_PI_HI = float(np.float32(6.28125))
TWO_PI_LO = float(np.float32(2 * np.pi) - np.float32(6.28125))
INV_2PI = float(np.float32(1.0 / (2 * np.pi)))


def emit_rope_tables(k, B, pre, rope_buf):
    v, s, g = k.vector, k.scalar, k.gpsimd
    SA, SB, SM = B["SA"], B["SB"], B["SMALL"]

    t = g.op("iota", SA.ap[0:1, 0:32].bitcast(I32), pattern=[[1, 32]],
             channel_multiplier=0, deps=SA.wdeps())
    SA.did_write(t)
    t = v.op("tensor_copy", SA.ap[0:1, 32:64], SA.ap[0:1, 0:32].bitcast(I32),
             deps=SA.wdeps())
    SA.did_write(t)
    t = s.op("activation", SA.ap[0:1, 0:32], SA.ap[0:1, 32:64], AF.Exp,
             scale=float(-np.log(10000.0) / 32.0), deps=SA.wdeps())
    SA.did_write(t)
    ps = B["PS4"]
    tok = k.tensor.op("matmul", ps.ap[:, 0:32], B["ONESROW"].ap[0:1, :],
                      SA.ap[0:1, 0:32], start=True, stop=True,
                      deps=rd(B["ONESROW"], SA) + ps.wdeps())
    ps.did_write(tok); B["ONESROW"].did_read(tok); SA.did_read(tok)
    th = Buf(SM.ap[:, 32:64])
    t = v.op("tensor_copy", th.ap, ps.ap[:, 0:32], deps=rd(ps) + SM.wdeps())
    th.did_write(t); SM.did_write(t); ps.did_read(t)
    t = g.op("iota", SM.ap[:, 0:1].bitcast(I32), pattern=[[0, 1]],
             channel_multiplier=1, deps=SM.wdeps())
    SM.did_write(t)
    t = v.op("tensor_copy", SM.ap[:, 1:2], SM.ap[:, 0:1].bitcast(I32),
             deps=SM.wdeps())
    SM.did_write(t)

    def sincos(dst_ap, base_is_cos, sub):
        t = v.op("tensor_scalar_add", SM.ap[:, 2:3], SM.ap[:, 1:2],
                 float(128 * sub), deps=SM.wdeps())
        SM.did_write(t)
        t = v.op("tensor_scalar", SA.ap[:, 0:32], th.ap, SM.ap[:, 2:3],
                 (PI / 2.0) if base_is_cos else 0.0, OP.mult, OP.add,
                 deps=rd(th) + SA.wdeps() + SM.rdeps())
        SA.did_write(t); th.did_read(t)
        t = v.op("tensor_scalar", SB.ap[:, 32:64], SA.ap[:, 0:32],
                 float(INV_2PI), -0.5, OP.mult, OP.add, deps=rd(SA) + SB.wdeps())
        SB.did_write(t)
        t = v.op("tensor_scalar", SA.ap[:, 32:64], SB.ap[:, 32:64], float(C1),
                 float(C1), OP.add, OP.subtract, deps=rd(SB) + SA.wdeps())
        SA.did_write(t)
        t = v.op("tensor_scalar", SA.ap[:, 32:64], SA.ap[:, 32:64],
                 pre.j.ap[:], None, OP.mult, deps=SA.wdeps() + rd(pre.j))
        SA.did_write(t); pre.j.did_read(t)
        t = v.op("tensor_scalar", SB.ap[:, 0:32], SB.ap[:, 32:64], float(C2),
                 float(C2), OP.add, OP.subtract, deps=SB.wdeps())
        SB.did_write(t)
        t = v.op("tensor_tensor", SB.ap[:, 0:32], SB.ap[:, 0:32],
                 SA.ap[:, 32:64], OP.add, deps=SB.wdeps() + rd(SA))
        SB.did_write(t)
        t = v.op("scalar_tensor_tensor", SA.ap[:, 0:32], SB.ap[:, 0:32],
                 float(-TWO_PI_HI), SA.ap[:, 0:32], OP.mult, OP.add,
                 deps=rd(SB) + SA.wdeps())
        SA.did_write(t); SB.did_read(t)
        t = v.op("scalar_tensor_tensor", SA.ap[:, 0:32], SB.ap[:, 0:32],
                 float(-TWO_PI_LO), SA.ap[:, 0:32], OP.mult, OP.add,
                 deps=rd(SB) + SA.wdeps())
        SA.did_write(t); SB.did_read(t)
        t = v.op("tensor_scalar", SA.ap[:, 0:32], SA.ap[:, 0:32], float(-PI),
                 None, OP.add, deps=SA.wdeps())
        SA.did_write(t)
        t = v.op("tensor_scalar", SA.ap[:, 0:32], SA.ap[:, 0:32], float(-PI),
                 float(PI), OP.max, OP.min, deps=SA.wdeps())
        SA.did_write(t)
        t = s.op("activation", SA.ap[:, 32:64], SA.ap[:, 0:32], AF.Sin,
                 deps=SA.wdeps())
        SA.did_write(t)
        t = v.op("tensor_scalar_mul", dst_ap, SA.ap[:, 32:64], -1.0,
                 deps=rd(SA) + rope_buf.wdeps())
        rope_buf.did_write(t); SA.did_read(t)

    for sub in range(32):
        sincos(rope_buf.ap[:, sub * 64:sub * 64 + 32], True, sub)
        sincos(rope_buf.ap[:, sub * 64 + 32:sub * 64 + 64], False, sub)


def emit_rope_apply(k, B, pre, q_tm, rope_buf, sub, nheads=16):
    v = k.vector
    SA, SB = B["SA"], B["SB"]
    n = nheads * 32
    q3 = q_tm.ap[:, 0:nheads * 64].rearrange("p (h i two) -> p h i two", h=nheads, two=2)
    qe, qo = q3[:, :, :, 0], q3[:, :, :, 1]
    cosb = rope_buf.ap[:, sub * 64:sub * 64 + 32].unsqueeze(1).broadcast_to([128, nheads, 32])
    sinb = rope_buf.ap[:, sub * 64 + 32:sub * 64 + 64].unsqueeze(1).broadcast_to([128, nheads, 32])
    sa3 = SA.ap[:, 0:n].rearrange("p (h i) -> p h i", h=nheads)
    sa3b = SA.ap[:, n:2 * n].rearrange("p (h i) -> p h i", h=nheads)
    sb3 = SB.ap[:, 0:n].rearrange("p (h i) -> p h i", h=nheads)
    t = v.op("tensor_copy", sa3, qe, deps=rd(q_tm) + SA.wdeps())
    SA.did_write(t); q_tm.did_read(t)
    t = v.op("tensor_copy", sb3, qo, deps=rd(q_tm) + SB.wdeps())
    SB.did_write(t); q_tm.did_read(t)
    t = v.op("tensor_tensor", qe, sa3, cosb, OP.mult,
             deps=rd(SA, rope_buf) + q_tm.wdeps())
    q_tm.did_write(t); SA.did_read(t); rope_buf.did_read(t)
    t = v.op("tensor_tensor", sa3b, sb3, sinb, OP.mult,
             deps=rd(SB, rope_buf) + SA.wdeps())
    SA.did_write(t); SB.did_read(t); rope_buf.did_read(t)
    t = v.op("tensor_tensor", qe, qe, sa3b, OP.subtract,
             deps=rd(SA) + q_tm.wdeps())
    q_tm.did_write(t); SA.did_read(t)
    t = v.op("tensor_tensor", qo, sb3, cosb, OP.mult,
             deps=rd(SB, rope_buf) + q_tm.wdeps())
    q_tm.did_write(t); SB.did_read(t); rope_buf.did_read(t)
    t = v.op("tensor_tensor", sb3, sa3, sinb, OP.mult,
             deps=rd(SA, rope_buf) + SB.wdeps())
    SB.did_write(t); SA.did_read(t); rope_buf.did_read(t)
    t = v.op("tensor_tensor", qo, qo, sb3, OP.add,
             deps=rd(SB) + q_tm.wdeps())
    q_tm.did_write(t); SB.did_read(t)


def emit_rne_plain(k, out_buf, out_ap, x_buf, x_ap, pre, bias=0.0):
    v = k.vector
    sa, sb = k._rne_sa, k._rne_sb
    w = x_ap.shape[-1]
    t = v.op("tensor_scalar", sa.ap[:, :w], x_ap, float(bias) + float(C1), None,
             OP.add, deps=rd(x_buf) + sa.wdeps())
    sa.did_write(t); x_buf.did_read(t)
    t = v.op("tensor_scalar", sa.ap[:, :w], sa.ap[:, :w], float(C1),
             pre.j.ap[:], OP.subtract, OP.mult, deps=sa.wdeps() + rd(pre.j))
    sa.did_write(t); pre.j.did_read(t)
    t = v.op("tensor_scalar", sb.ap[:, :w], x_ap, float(bias) + float(C2), None,
             OP.add, deps=rd(x_buf) + sb.wdeps())
    sb.did_write(t); x_buf.did_read(t)
    t = v.op("scalar_tensor_tensor", out_ap, sb.ap[:, :w], float(C2),
             sa.ap[:, :w], OP.subtract, OP.add,
             deps=rd(sa, sb) + out_buf.wdeps())
    out_buf.did_write(t); sa.did_read(t); sb.did_read(t)
    return t


def emit_l3_attn(k, B, pre, x_in_ap, x_in_buf, x_out_ap, x_out_buf,
                 g_dram, g_buf, w, scr):
    v, s, sy = k.vector, k.scalar, k.sync
    W = Buf(B["WRES"].ap.bitcast(F32))
    G0, G1 = B["G0"], B["G1"]

    tok = sy.dma(B["BIAS1"].ap[:, 0:1024],
                 w["b_attn"][None, 0:1024].partition_broadcast(128),
                 deps=B["BIAS1"].wdeps())
    B["BIAS1"].did_write(tok)
    tok = sy.dma(B["BIAS2"].ap[:, 0:1024],
                 w["b_attn"][None, 1024:2048].partition_broadcast(128),
                 deps=B["BIAS2"].wdeps())
    B["BIAS2"].did_write(tok)
    VB_OFF = 19520
    tok = sy.dma(W.ap[:, VB_OFF:VB_OFF + 1024],
                 w["b_attn"][None, 2048:3072].partition_broadcast(128),
                 deps=W.wdeps())
    W.did_write(tok)

    def stage_w(dram_ap, col0, ncol):
        wdeps = W.wdeps()
        first = True
        for c in range(8):
            t = sy.dma(W.ap[:, c * ncol:(c + 1) * ncol],
                       dram_ap[c * 128:(c + 1) * 128, col0:col0 + ncol],
                       deps=wdeps)
            if first:
                W.did_write(t)
                first = False
            else:
                W.did_write_acc(t)

    QTM, KTM, VTM = 6144, 10240, 14336
    VB = 19520
    STG8 = 20544
    for i in range(8):
        t0 = i * 512
        emit_ln(k, x_in_ap, x_in_buf, G0, B["STATS"], G1, t0, pre)
        xn_fm = sub_buf(G1, 0, 4096)
        emit_tm_to_fm(k, G0, xn_fm, pre, B["PS2"], B["PS3"])
        alias_sync(G1, xn_fm)
        for proj, dsto in (("q", QTM), ("k", KTM), ("v", VTM)):
            wcol0 = {"q": 0, "k": 1024, "v": 2048}[proj]
            for half in range(2):
                stage_w(w["W_attn"], wcol0 + half * 512, 512)
                for j in range(4):
                    ps = B["PS0"] if j % 2 == 0 else B["PS1"]
                    pairs = [(G1, G1.ap[:, c * 512 + j * 128:
                                  c * 512 + j * 128 + 128],
                              W, W.ap[:, c * 512:(c + 1) * 512])
                             for c in range(8)]
                    mm_group(k, ps, ps.ap[:, 0:512], pairs)
                    if proj != "v":
                        bt = B["BIAS1"] if proj == "q" else B["BIAS2"]
                        dst = W.ap[:, dsto + j * 1024 + half * 512:
                                   dsto + j * 1024 + (half + 1) * 512]
                        t = v.op("scalar_tensor_tensor", dst, ps.ap[:, 0:512],
                                 1.0, bt.ap[:, half * 512:(half + 1) * 512],
                                 OP.mult, OP.add, deps=rd(ps, bt) + W.wdeps())
                        W.did_write(t); ps.did_read(t); bt.did_read(t)
                    else:
                        base = VTM + j * 1040 + half * 520
                        dst = W.ap[:, base:base + 520].rearrange(
                            "p (h o) -> p h o", h=8)[:, :, 0:64]
                        t = v.op("scalar_tensor_tensor", dst, ps.ap[:, 0:512],
                                 1.0,
                                 W.ap[:, VB + half * 512:VB + (half + 1) * 512]
                                 .rearrange("p (h o) -> p h o", h=8),
                                 OP.mult, OP.add, deps=rd(ps) + W.wdeps())
                        W.did_write(t); ps.did_read(t)
            for j in range(4):
                sub = 4 * i + j
                if proj == "v":
                    ones_dst = W.ap[:, VTM + j * 1040:VTM + (j + 1) * 1040].rearrange(
                        "p (h o) -> p h o", h=16)[:, :, 64:65]
                    t = v.op("memset", ones_dst, 1.0, deps=W.wdeps())
                    W.did_write(t)
                    tok = sy.dma(scr["v"].ap[sub],
                                 W.ap[:, VTM + j * 1040:VTM + (j + 1) * 1040],
                                 deps=rd(W) + scr["v"].wdeps())
                    scr["v"].did_write(tok); W.did_read(tok)
                else:
                    qk_tm = sub_buf(W, dsto + j * 1024, 1024)
                    stg = sub_buf(W, STG8, 1024)
                    emit_tm_to_fm(k, qk_tm, stg, pre, B["PS2"],
                                  B["PS3"], n_sub=1, width=1024)
                    merge_sub(W, stg)
                    WB = B["WRES"].ap
                    SPH, SPL = 43136, 44160
                    t = v.op("tensor_copy", WB[:, SPH:SPH + 1024],
                             W.ap[:, STG8:STG8 + 1024],
                             deps=rd(W) + W.wdeps())
                    W.did_write(t)
                    t = v.op("tensor_tensor", WB[:, SPL:SPL + 1024],
                             W.ap[:, STG8:STG8 + 1024], WB[:, SPH:SPH + 1024],
                             OP.subtract, deps=W.wdeps())
                    W.did_write(t)
                    dname = "q" if proj == "q" else "kk"
                    for comp, off in ((0, SPH), (1, SPL)):
                        tok = sy.dma(
                            scr[dname].ap[comp][:, :, sub * 128:(sub + 1) * 128]
                            .rearrange("c p t -> p c t"),
                            WB[:, off:off + 1024].rearrange(
                                "p (c t) -> p c t", c=8),
                            deps=rd(W) + scr[dname].wdeps())
                        scr[dname].did_write_acc(tok); W.did_read(tok)

    QF, KF, VT2, OACC = 4096, 8192, 9216, 10768
    hgroups = [list(range(0, 7)), list(range(7, 14)), [14, 15]]
    KFB, VTB, P2 = 14864, 15888, 16928
    qfb = sub_buf(W, QF, 4096)
    kfb = [sub_buf(W, KF, 1024), sub_buf(W, KFB, 1024)]
    vtb = [sub_buf(W, VT2, 1040), sub_buf(W, VTB, 1040)]
    pb = [sub_buf(W, 10256, 512), sub_buf(W, P2, 512)]

    WB = B["WRES"].ap
    KB_BF = [16384, 29728]
    QF_BF = 8192

    def k_dma(tj):
        b = kfb[tj % 2]
        base = KB_BF[tj % 2]
        wdeps = b.wdeps()
        for comp in range(2):
            tok = sy.dma(WB[:, base + comp * 1024:base + (comp + 1) * 1024]
                         .rearrange("p (c t) -> p c t", c=8),
                         scr["kk"].ap[comp][:, :, tj * 128:(tj + 1) * 128]
                         .rearrange("c p t -> p c t"),
                         deps=wdeps + rd(scr["kk"]))
            if comp == 0:
                b.did_write(tok)
            else:
                b.did_write_acc(tok)
            scr["kk"].did_read(tok)
        b2 = vtb[tj % 2]
        tok = sy.dma(b2.ap[:], scr["v"].ap[tj], deps=b2.wdeps() + rd(scr["v"]))
        b2.did_write(tok); scr["v"].did_read(tok)

    for qt in range(8):
        t0 = qt * 512
        qdeps = qfb.wdeps()
        for comp in range(2):
            tok = sy.dma(WB[:, QF_BF + comp * 4096:QF_BF + (comp + 1) * 4096]
                         .rearrange("p (c t) -> p c t", c=8),
                         scr["q"].ap[comp][:, :, t0:t0 + 512]
                         .rearrange("c p t -> p c t"),
                         deps=qdeps + rd(scr["q"]))
            if comp == 0:
                qfb.did_write(tok)
            else:
                qfb.did_write_acc(tok)
            scr["q"].did_read(tok)
        ntj = 4 * qt + 4
        hgroups = [[0, 1, 2, 3, 4, 5], [6, 7, 8, 9, 10, 11], [12, 13, 14, 15]]
        for hg in hgroups:
            obanks = {h: B[f"PS{2 + idx}"] for idx, h in enumerate(hg)}
            k_dma(0)
            for tj in range(ntj):
                if tj + 1 < ntj:
                    k_dma(tj + 1)
                kb, vb = kfb[tj % 2], vtb[tj % 2]
                r = tj - 4 * qt
                for hi, h in enumerate(hg):
                    hp, hh = h // 2, h % 2
                    psS = B["PS0"] if hi % 2 == 0 else B["PS1"]
                    pp = pb[hi % 2]
                    kbase = KB_BF[tj % 2]
                    khi = WB[64 * hh:64 * hh + 64,
                             kbase + hp * 128:kbase + hp * 128 + 128]
                    klo = WB[64 * hh:64 * hh + 64,
                             kbase + 1024 + hp * 128:kbase + 1024 + hp * 128 + 128]
                    qhi = WB[64 * hh:64 * hh + 64,
                             QF_BF + hp * 512:QF_BF + (hp + 1) * 512]
                    qlo = WB[64 * hh:64 * hh + 64,
                             QF_BF + 4096 + hp * 512:QF_BF + 4096 + (hp + 1) * 512]
                    tok = k.tensor.op("matmul", psS.ap[:, 0:512], khi, qhi,
                                      start=True, stop=False,
                                      deps=rd(kb, qfb) + psS.wdeps())
                    tok = k.tensor.op("matmul", psS.ap[:, 0:512], khi, qlo,
                                      start=False, stop=False)
                    tok = k.tensor.op("matmul", psS.ap[:, 0:512], klo, qhi,
                                      start=False, stop=True)
                    psS.did_write(tok); kb.did_read(tok); qfb.did_read(tok)
                    if r < 0:
                        t = s.op("activation", pp.ap[:], psS.ap[:, 0:512],
                                 AF.Exp, scale=pre.c_eighth, bias=pre.c_nshift,
                                 deps=rd(psS, pre.consts) + pp.wdeps())
                        pp.did_write(t); psS.did_read(t)
                        pre.consts.did_read(t)
                    else:
                        for sb_i in range(4):
                            pdst = pp.ap[:, sb_i * 128:(sb_i + 1) * 128]
                            pssl = psS.ap[:, sb_i * 128:(sb_i + 1) * 128]
                            if sb_i > r:
                                t = s.op("activation", pdst, pssl, AF.Exp,
                                         scale=pre.c_eighth, bias=pre.c_nshift,
                                         deps=rd(psS, pre.consts) + pp.wdeps())
                                pp.did_write_acc(t); psS.did_read(t)
                                pre.consts.did_read(t)
                            elif sb_i == r:
                                t = s.op("activation", pdst, pssl, AF.Exp,
                                         scale=pre.c_eighth, bias=pre.c_nshift,
                                         deps=rd(psS, pre.consts) + pp.wdeps())
                                pp.did_write_acc(t); psS.did_read(t)
                                pre.consts.did_read(t)
                                t = v.op("tensor_tensor", pdst, pdst,
                                         pre.triu.ap[:], OP.mult,
                                         deps=rd(pre.triu) + pp.wdeps())
                                pp.did_write_acc(t); pre.triu.did_read(t)
                            else:
                                t = v.op("memset", pdst, 0.0, deps=pp.wdeps())
                                pp.did_write_acc(t)
                    vap = vb.ap[:, h * 65:(h + 1) * 65]
                    tok = k.tensor.op("matmul", obanks[h].ap[0:65, 0:512], vap,
                                      pp.ap[:],
                                      start=(tj == 0), stop=(tj == ntj - 1),
                                      deps=rd(vb, pp) + (obanks[h].wdeps() if tj == 0 else []),
                                      inc=True)
                    vb.did_read(tok); pp.did_read(tok)
                    if tj == ntj - 1:
                        obanks[h].did_write(tok)
            for h in hg:
                hp, hh = h // 2, h % 2
                psO = obanks[h]
                t = v.op("reciprocal", B["SA"].ap[0:1, 0:512],
                         psO.ap[64:65, 0:512], deps=rd(psO) + B["SA"].wdeps())
                B["SA"].did_write(t); psO.did_read(t)
                tok = k.tensor.op("matmul", B["PS0"].ap[0:64, 0:512],
                                  B["ONESROW"].ap[0:1, 0:64],
                                  B["SA"].ap[0:1, 0:512], start=True, stop=True,
                                  deps=rd(B["ONESROW"], B["SA"]) + B["PS0"].wdeps())
                B["PS0"].did_write(tok)
                B["ONESROW"].did_read(tok); B["SA"].did_read(tok)
                t = s.op("activation", B["SB"].ap[0:64, 0:512],
                         B["PS0"].ap[0:64, 0:512], AF.Copy,
                         deps=rd(B["PS0"]) + B["SB"].wdeps())
                B["SB"].did_write(t); B["PS0"].did_read(t)
                t = v.op("tensor_tensor", B["SB"].ap[0:64, 512:1024],
                         psO.ap[0:64, 0:512], B["SB"].ap[0:64, 0:512],
                         OP.mult, deps=rd(psO) + B["SB"].wdeps())
                B["SB"].did_write(t); psO.did_read(t)
                odst = W.ap[64 * hh:64 * hh + 64,
                            OACC + hp * 512:OACC + (hp + 1) * 512]
                t = s.op("activation", odst, B["SB"].ap[0:64, 512:1024],
                         AF.Copy, deps=rd(B["SB"]) + W.wdeps())
                W.did_write(t); B["SB"].did_read(t)
        tok = sy.dma(B["BIAS1"].ap[:, 0:1024],
                     w["b_proj"][None, :].partition_broadcast(128),
                     deps=B["BIAS1"].wdeps())
        B["BIAS1"].did_write(tok)
        for half in range(2):
            stage_w(w["W_proj"], half * 512, 512)
            for sub in range(4):
                ps = B["PS0"]
                pairs = [(W, W.ap[:, OACC + c * 512 + sub * 128:
                              OACC + c * 512 + sub * 128 + 128],
                          W, W.ap[:, c * 512:(c + 1) * 512])
                         for c in range(8)]
                mm_group(k, ps, ps.ap[:, 0:512], pairs)
                dst = G1.ap[:, sub * 1024 + half * 512:
                            sub * 1024 + (half + 1) * 512]
                t = v.op("scalar_tensor_tensor", dst, ps.ap[:, 0:512], 1.0,
                         B["BIAS1"].ap[:, half * 512:(half + 1) * 512],
                         OP.mult, OP.add, deps=rd(ps, B["BIAS1"]) + G1.wdeps())
                G1.did_write(t); ps.did_read(t); B["BIAS1"].did_read(t)
        tok = sy.dma(G0.ap[:].rearrange("p (a d) -> p a d", a=4),
                     x_in_ap[t0:t0 + 512, :].rearrange("(a p) d -> p a d", p=128),
                     deps=G0.wdeps() + rd(x_in_buf))
        G0.did_write(tok); x_in_buf.did_read(tok)
        tok = sy.dma(B["XT"].ap[:].rearrange("p (a d) -> p a d", a=4),
                     g_dram[t0:t0 + 512, :].rearrange("(a p) d -> p a d", p=128),
                     deps=B["XT"].wdeps() + rd(g_buf))
        B["XT"].did_write(tok); g_buf.did_read(tok)
        t = v.op("tensor_tensor", G1.ap[:], G1.ap[:], B["XT"].ap[:], OP.mult,
                 deps=rd(B["XT"]) + G1.wdeps())
        G1.did_write(t); B["XT"].did_read(t)
        t = v.op("tensor_tensor", G1.ap[:], G1.ap[:], G0.ap[:], OP.add,
                 deps=rd(G0) + G1.wdeps())
        G1.did_write(t); G0.did_read(t)
        tok = sy.dma(x_out_ap[t0:t0 + 512, :].rearrange("(a p) d -> p a d", p=128),
                     G1.ap[:].rearrange("p (a d) -> p a d", a=4),
                     deps=rd(G1) + x_out_buf.wdeps())
        x_out_buf.did_write(tok); G1.did_read(tok)



def emit_nvfp4_quant(k, B, pre, x_buf, x_ap, out_buf, out_ap, width,
                     gs6_pp, rgs_pp, gs_buf, blkbuf, blk_off):
    v, s, g = k.vector, k.scalar, k.gpsimd
    nb = width // 16
    am = blkbuf.ap[:, blk_off:blk_off + nb]
    sc8 = blkbuf.ap[:, blk_off + nb:blk_off + 2 * nb]
    rdq = blkbuf.ap[:, blk_off + 2 * nb:blk_off + 3 * nb]
    tmp = blkbuf.ap[:, blk_off + 3 * nb:blk_off + 4 * nb]
    t = v.op("reduce_max", am, x_ap.rearrange("p (b s) -> p b s", s=16), AX.X,
             apply_absolute_value=True, deps=rd(x_buf) + blkbuf.wdeps())
    blkbuf.did_write(t); x_buf.did_read(t)
    t = v.op("tensor_scalar", sc8, am, gs6_pp, None, OP.mult,
             deps=blkbuf.wdeps() + rd(gs_buf))
    blkbuf.did_write(t); gs_buf.did_read(t)
    t = v.op("tensor_scalar", sc8, sc8, 0.0, 448.0, OP.max, OP.min,
             deps=blkbuf.wdeps())
    blkbuf.did_write(t)
    t = v.op("memset", tmp, 0.015625, deps=blkbuf.wdeps())
    blkbuf.did_write(t)
    for thr in [0.03125, 0.0625, 0.125, 0.25, 0.5, 1.0, 2.0, 4.0, 8.0, 16.0,
                32.0, 64.0, 128.0, 256.0]:
        t = v.op("tensor_scalar", am, sc8, float(thr), 1.0, OP.is_ge, OP.add,
                 deps=blkbuf.wdeps())
        blkbuf.did_write(t)
        t = v.op("tensor_tensor", tmp, tmp, am, OP.mult, deps=blkbuf.wdeps())
        blkbuf.did_write(t)
    t = v.op("tensor_scalar_mul", am, tmp, float(1.5 * 2.0**20),
             deps=blkbuf.wdeps())
    blkbuf.did_write(t)
    t = v.op("tensor_tensor", rdq, sc8, am, OP.add, deps=blkbuf.wdeps())
    blkbuf.did_write(t)
    t = v.op("tensor_tensor", rdq, rdq, am, OP.subtract, deps=blkbuf.wdeps())
    blkbuf.did_write(t)
    t = v.op("tensor_scalar", rdq, rdq, pre.j.ap[:], None, OP.mult,
             deps=blkbuf.wdeps() + rd(pre.j))
    blkbuf.did_write(t); pre.j.did_read(t)
    t = v.op("tensor_scalar_mul", am, tmp, float(1.5 * 2.0**49),
             deps=blkbuf.wdeps())
    blkbuf.did_write(t)
    t = v.op("tensor_tensor", tmp, sc8, am, OP.add, deps=blkbuf.wdeps())
    blkbuf.did_write(t)
    t = v.op("tensor_tensor", tmp, tmp, am, OP.subtract, deps=blkbuf.wdeps())
    blkbuf.did_write(t)
    t = v.op("tensor_tensor", sc8, tmp, rdq, OP.add, deps=blkbuf.wdeps())
    blkbuf.did_write(t)
    t = v.op("tensor_scalar", tmp, sc8, rgs_pp, None, OP.mult,
             deps=blkbuf.wdeps() + rd(gs_buf))
    blkbuf.did_write(t); gs_buf.did_read(t)
    t = v.op("tensor_single_scalar", rdq, tmp, 0.0, OP.is_equal,
             deps=blkbuf.wdeps())
    blkbuf.did_write(t)
    t = v.op("tensor_tensor", tmp, tmp, rdq, OP.add, deps=blkbuf.wdeps())
    blkbuf.did_write(t)
    t = v.op("reciprocal", rdq, tmp, deps=blkbuf.wdeps())
    blkbuf.did_write(t)
    SA, SB, SC = B["SA"], B["SB"], B["SC"]
    for q0 in range(0, width, 1024):
        qn = min(1024, width - q0)
        xs = x_ap[:, q0:q0 + qn]
        sa, sb, sc = SA.ap[:, 0:qn], SB.ap[:, 0:qn], SC.ap[:, 0:qn]
        rdq_b = rdq[:, q0 // 16:(q0 + qn) // 16].unsqueeze(2).broadcast_to(
            [128, qn // 16, 16])
        sc8_b = sc8[:, q0 // 16:(q0 + qn) // 16].unsqueeze(2).broadcast_to(
            [128, qn // 16, 16])
        sa3 = sa.rearrange("p (b s) -> p b s", s=16)
        ot = out_ap[:, q0:q0 + qn]
        ot3 = ot.rearrange("p (b s) -> p b s", s=16)
        t = s.op("activation", sa, xs, AF.Abs, deps=rd(x_buf) + SA.wdeps())
        SA.did_write(t); x_buf.did_read(t)
        t = g.op("tensor_tensor", sa3, sa3, rdq_b, OP.mult,
                 deps=SA.wdeps() + rd(blkbuf))
        SA.did_write(t); blkbuf.did_read(t)
        t = v.op("tensor_scalar_min", sa, sa, 6.0, deps=SA.wdeps())
        SA.did_write(t)
        t = v.op("tensor_scalar", sb, sa, 2.0, 1.0, OP.is_ge, OP.add,
                 deps=rd(SA) + SB.wdeps())
        SB.did_write(t)
        t = v.op("tensor_scalar", ot, sa, 4.0, 1.0, OP.is_ge, OP.add,
                 deps=rd(SA) + out_buf.wdeps())
        out_buf.did_write(t)
        t = g.op("tensor_tensor", sb, sb, ot, OP.mult,
                 deps=SB.wdeps() + rd(out_buf))
        SB.did_write(t); out_buf.did_read(t)
        M32c, M64c = float(1.5 * 2.0**22), float(1.5 * 2.0**51)
        t = v.op("scalar_tensor_tensor", sc, sb, M32c, sa, OP.mult, OP.add,
                 deps=rd(SA, SB) + SC.wdeps())
        SC.did_write(t); SA.did_read(t); SB.did_read(t)
        t = v.op("scalar_tensor_tensor", sc, sb, -M32c, sc, OP.mult, OP.add,
                 deps=rd(SB) + SC.wdeps())
        SC.did_write(t); SB.did_read(t)
        t = v.op("tensor_scalar", sc, sc, pre.j.ap[:], None, OP.mult,
                 deps=SC.wdeps() + rd(pre.j))
        SC.did_write(t); pre.j.did_read(t)
        t = v.op("scalar_tensor_tensor", sa, sb, M64c, sa, OP.mult, OP.add,
                 deps=SA.wdeps() + rd(SB))
        SA.did_write(t); SB.did_read(t)
        t = v.op("scalar_tensor_tensor", sa, sb, -M64c, sa, OP.mult, OP.add,
                 deps=SA.wdeps() + rd(SB))
        SA.did_write(t); SB.did_read(t)
        t = g.op("tensor_tensor", sa, sa, sc, OP.add,
                 deps=SA.wdeps() + rd(SC))
        SA.did_write(t); SC.did_read(t)
        t = s.op("activation", sb, xs, AF.Sign, deps=rd(x_buf) + SB.wdeps())
        SB.did_write(t); x_buf.did_read(t)
        t = g.op("tensor_tensor", sa, sa, sb, OP.mult,
                 deps=SA.wdeps() + rd(SB))
        SA.did_write(t); SB.did_read(t)
        t = v.op("tensor_tensor", ot3, sa3, sc8_b, OP.mult,
                 deps=rd(SA, blkbuf) + out_buf.wdeps())
        out_buf.did_write(t); SA.did_read(t); blkbuf.did_read(t)
    return


def emit_global_absmax(k, B, w_dram, nch, ncol, dst_col):
    v, sy = k.vector, k.sync
    SMALL = B["SMALL"]
    SCR = B["SCRW"]
    SCR.wtoks = list(set(SCR.wtoks + B["G1"].wtoks))
    SCR.rtoks = list(set(SCR.rtoks + B["G1"].rtoks))
    t = v.op("memset", SMALL.ap[:, 30:31], 0.0, deps=SMALL.wdeps())
    SMALL.did_write(t)
    for c in range(nch):
        t = sy.dma(SCR.ap[:, :ncol], w_dram[c * 128:(c + 1) * 128, :],
                   deps=SCR.wdeps())
        SCR.did_write(t)
        t = v.op("reduce_max", SMALL.ap[:, 31:32], SCR.ap[:, :ncol], AX.X,
                 apply_absolute_value=True, deps=rd(SCR) + SMALL.wdeps())
        SMALL.did_write(t); SCR.did_read(t)
        t = v.op("tensor_tensor", SMALL.ap[:, 30:31], SMALL.ap[:, 30:31],
                 SMALL.ap[:, 31:32], OP.max, deps=SMALL.wdeps())
        SMALL.did_write(t)
    t = k.gpsimd.op("tensor_reduce", SMALL.ap[0:1, dst_col:dst_col + 1],
                    SMALL.ap[:, 30:31], AX.C, OP.max, deps=SMALL.wdeps())
    SMALL.did_write(t)
    B["G1"].wtoks = list(set(B["G1"].wtoks + SCR.wtoks))
    B["G1"].rtoks = list(set(B["G1"].rtoks + SCR.rtoks))
    return


def emit_bcast_scalar(k, B, src_ap, dst_ap, dst_buf, src_buf):
    v = k.vector
    ps = B["PS4"]
    tok = k.tensor.op("matmul", ps.ap[:, 0:1], B["ONESROW"].ap[0:1, :],
                      src_ap, start=True, stop=True,
                      deps=rd(B["ONESROW"], src_buf) + ps.wdeps())
    ps.did_write(tok); B["ONESROW"].did_read(tok); src_buf.did_read(tok)
    t = v.op("tensor_copy", dst_ap, ps.ap[:, 0:1], deps=rd(ps) + dst_buf.wdeps())
    dst_buf.did_write(t); ps.did_read(t)
    return t


def emit_ffn_nvfp4(k, B, pre, x_src_ap, x_src_buf, g_dram, g_buf, w,
                   hq_dram, hq_buf):
    v, s, sy, g = k.vector, k.scalar, k.sync, k.gpsimd
    W, G0, G1 = B["WRES"], B["G0"], B["G1"]
    SMALL, SM2, BLK, HQ = B["SMALL"], B["SMALL2"], B["BLK"], B["HQ16"]
    SCRW = B["SCRW"]
    B["SCRW"].wtoks = list(G1.wtoks)
    B["SCRW"].rtoks = list(G1.rtoks)

    tok = sy.dma(SMALL.ap[0:1, 20:21], w["s_in"][None, :], deps=SMALL.wdeps())
    SMALL.did_write(tok)
    tok = sy.dma(SMALL.ap[0:1, 21:22], w["s_in_down"][None, :],
                 deps=SMALL.wdeps())
    SMALL.did_write(tok)

    def scalar_gs_tiles(src_col, c6, cr):
        t = v.op("tensor_scalar_mul", SMALL.ap[0:1, 24:25],
                 SMALL.ap[0:1, src_col:src_col + 1], float(1.0 / 6.0),
                 deps=SMALL.wdeps())
        SMALL.did_write(t)
        emit_bcast_scalar(k, B, SMALL.ap[0:1, 24:25], SM2.ap[:, c6:c6 + 1],
                          SM2, SMALL)
        t = v.op("reciprocal", SMALL.ap[0:1, 24:25],
                 SMALL.ap[0:1, src_col:src_col + 1], deps=SMALL.wdeps())
        SMALL.did_write(t)
        emit_bcast_scalar(k, B, SMALL.ap[0:1, 24:25], SM2.ap[:, cr:cr + 1],
                          SM2, SMALL)

    scalar_gs_tiles(20, 40, 41)
    scalar_gs_tiles(21, 42, 43)

    def weight_gs(dram, nch, ncol, gs_col):
        emit_global_absmax(k, B, dram, nch, ncol, 25)
        t = v.op("reciprocal", SMALL.ap[0:1, gs_col:gs_col + 1],
                 SMALL.ap[0:1, 25:26], deps=SMALL.wdeps())
        SMALL.did_write(t)
        t = v.op("tensor_scalar_mul", SMALL.ap[0:1, gs_col:gs_col + 1],
                 SMALL.ap[0:1, gs_col:gs_col + 1], 2688.0,
                 deps=SMALL.wdeps())
        SMALL.did_write(t)

    def out_scale_tile(gsx_col, gsw_col, dst_col):
        t = v.op("tensor_tensor", SMALL.ap[0:1, 24:25],
                 SMALL.ap[0:1, gsx_col:gsx_col + 1],
                 SMALL.ap[0:1, gsw_col:gsw_col + 1], OP.mult,
                 deps=SMALL.wdeps())
        SMALL.did_write(t)
        t = v.op("reciprocal", SMALL.ap[0:1, 24:25], SMALL.ap[0:1, 24:25],
                 deps=SMALL.wdeps())
        SMALL.did_write(t)
        emit_bcast_scalar(k, B, SMALL.ap[0:1, 24:25], SM2.ap[:, dst_col:dst_col + 1],
                          SM2, SMALL)

    def quant_weight(dram, nch, ncol, gs_col, wq_off_bf):
        B["SCRW"].wtoks = list(G1.wtoks)
        B["SCRW"].rtoks = list(G1.rtoks)
        t = v.op("tensor_scalar_mul", SMALL.ap[0:1, 24:25],
                 SMALL.ap[0:1, gs_col:gs_col + 1], float(1.0 / 6.0),
                 deps=SMALL.wdeps())
        SMALL.did_write(t)
        emit_bcast_scalar(k, B, SMALL.ap[0:1, 24:25], SM2.ap[:, 44:45], SM2, SMALL)
        t = v.op("reciprocal", SMALL.ap[0:1, 24:25],
                 SMALL.ap[0:1, gs_col:gs_col + 1], deps=SMALL.wdeps())
        SMALL.did_write(t)
        emit_bcast_scalar(k, B, SMALL.ap[0:1, 24:25], SM2.ap[:, 45:46], SM2, SMALL)
        n_out = nch * 128
        for c in range(nch):
            tok = sy.dma(SCRW.ap[:, :ncol], dram[c * 128:(c + 1) * 128, :],
                         deps=SCRW.wdeps())
            SCRW.did_write(tok)
            emit_nvfp4_quant(k, B, pre, SCRW, SCRW.ap[:, :ncol], HQ,
                             HQ.ap[:, :ncol], ncol,
                             SM2.ap[:, 44:45], SM2.ap[:, 45:46], SM2, BLK, 0)
            for h0 in range(0, ncol // 128, 4):
                nh = min(4, ncol // 128 - h0)
                bank = B["PS2"] if (h0 // 4) % 2 == 0 else B["PS3"]
                bv = bank.ap.bitcast(BF16)
                wdeps2 = bank.wdeps() + rd(HQ, pre.ident_bf)
                tk = None
                for ii in range(nh):
                    hch = h0 + ii
                    tk = k.tensor.op(
                        "transpose", bv[:, ii * 128:(ii + 1) * 128],
                        HQ.ap[:, hch * 128:(hch + 1) * 128],
                        pre.ident_bf.ap[:],
                        deps=wdeps2 if ii == 0 else (), inc=(ii == nh - 1))
                bank.did_write(tk); HQ.did_read(tk); pre.ident_bf.did_read(tk)
                for ii in range(nh):
                    hch = h0 + ii
                    dst = W.ap[:, wq_off_bf + hch * n_out + c * 128:
                               wq_off_bf + hch * n_out + (c + 1) * 128]
                    t2 = s.op("activation", dst, bv[:, ii * 128:(ii + 1) * 128],
                              AF.Copy, deps=rd(bank) + W.wdeps())
                    W.did_write(t2); bank.did_read(t2)
        alias_sync(G1, SCRW)

    weight_gs(w["gate_w"], 32, 1024, 26)
    weight_gs(w["up_w"], 32, 1024, 27)
    out_scale_tile(20, 26, 46)
    out_scale_tile(20, 27, 47)
    quant_weight(w["gate_w"], 32, 1024, 26, 0)
    quant_weight(w["up_w"], 32, 1024, 27, 32768)
    barrier(k)

    def xq_ap(c, j):
        return B["H0"].ap[:, c * 512 + j * 128: c * 512 + (j + 1) * 128]

    def wq_ap(off):
        def f(c, o):
            return W.ap[:, off + c * 4096 + o * 512: off + c * 4096 + (o + 1) * 512]
        return f

    for t0 in range(0, T, 512):
        emit_ln(k, x_src_ap, x_src_buf, G0, B["STATS"], G1, t0, pre)
        for j in range(4):
            emit_nvfp4_quant(k, B, pre, G0, G0.ap[:, j * 1024:(j + 1) * 1024],
                             HQ, HQ.ap[:, j * 1024:(j + 1) * 1024], 1024,
                             SM2.ap[:, 40:41], SM2.ap[:, 41:42], SM2, BLK, 0)
        xq_fm = sub_buf(B["H0"], 0, 4096)
        emit_tm_to_fm(k, HQ, xq_fm, pre, B["PS2"], B["PS3"], src_bf16=True)
        merge_sub(B["H0"], xq_fm)
        for j in range(4):
            for o in range(8):
                ps = B["PS0"] if o % 2 == 0 else B["PS1"]
                emit_bitlin_mm(k, xq_ap, B["H0"], wq_ap(0), W, 8, ps, j, o, 512)
                t = s.op("activation", G1.ap[:, o * 512:(o + 1) * 512],
                         ps.ap[:, 0:512], AF.Silu, scale=SM2.ap[:, 46:47],
                         deps=rd(ps, SM2) + G1.wdeps())
                G1.did_write(t); ps.did_read(t); SM2.did_read(t)
            for o in range(8):
                ps = B["PS0"] if o % 2 == 0 else B["PS1"]
                emit_bitlin_mm(k, xq_ap, B["H0"], wq_ap(32768), W, 8, ps, j, o, 512)
                t = s.op("activation", G0.ap[:, o * 512:(o + 1) * 512],
                         ps.ap[:, 0:512], AF.Copy, scale=SM2.ap[:, 47:48],
                         deps=rd(ps, SM2) + G0.wdeps())
                G0.did_write(t); ps.did_read(t); SM2.did_read(t)
            if t0 == 512 and j == 1 and "dbg_gate" in B:
                tok = sy.dma(B["dbg_gate"].ap[0:128, :], G1.ap[:],
                             deps=rd(G1) + B["dbg_gate"].wdeps())
                B["dbg_gate"].did_write(tok); G1.did_read(tok)
                tok = sy.dma(B["dbg_up"].ap[0:128, :], G0.ap[:],
                             deps=rd(G0) + B["dbg_up"].wdeps())
                B["dbg_up"].did_write(tok); G0.did_read(tok)
            t = v.op("tensor_tensor", G1.ap[:], G1.ap[:], G0.ap[:], OP.mult,
                     deps=rd(G0) + G1.wdeps())
            G1.did_write(t); G0.did_read(t)
            emit_nvfp4_quant(k, B, pre, G1, G1.ap[:], HQ, HQ.ap[:, 0:4096],
                             4096, SM2.ap[:, 42:43], SM2.ap[:, 43:44], SM2,
                             BLK, 0)
            r0 = t0 + j * 128
            tok = sy.dma(hq_dram[r0:r0 + 128, :], HQ.ap[:, 0:4096],
                         deps=rd(HQ) + hq_buf.wdeps())
            hq_buf.did_write(tok); HQ.did_read(tok)

    barrier(k)
    weight_gs(w["down_w"], 8, 4096, 28)
    out_scale_tile(21, 28, 48)
    quant_weight(w["down_w"], 8, 4096, 28, 0)
    barrier(k)
    for t0 in range(0, T, 512):
        for j in range(4):
            r0 = t0 + j * 128
            tok = sy.dma(HQ.ap[:, 0:4096], hq_dram[r0:r0 + 128, :],
                         deps=HQ.wdeps() + rd(hq_buf))
            HQ.did_write(tok); hq_buf.did_read(tok)
            hq_fm = sub_buf(B["H0"], 4096, 4096)
            emit_tm_to_fm(k, HQ, hq_fm, pre, B["PS2"], B["PS3"], n_sub=1,
                          width=4096, src_bf16=True)
            merge_sub(B["H0"], hq_fm)
            if t0 == 0 and j == 0 and "dbg_hqfm" in B:
                t = v.op("tensor_copy", G0.ap[:], B["H0"].ap[:, 4096:8192],
                         deps=rd(B["H0"]) + G0.wdeps())
                G0.did_write(t); B["H0"].did_read(t)
                tok = sy.dma(B["dbg_hqfm"].ap[0:128, :], G0.ap[:],
                             deps=rd(G0) + B["dbg_hqfm"].wdeps())
                B["dbg_hqfm"].did_write(tok); G0.did_read(tok)
            for o2 in range(2):
                ps = B["PS0"] if o2 == 0 else B["PS1"]
                pairs = []
                for c in range(32):
                    pairs.append((B["H0"],
                                  B["H0"].ap[:, 4096 + c * 128: 4096 + (c + 1) * 128],
                                  W,
                                  W.ap[:, c * 1024 + o2 * 512:
                                       c * 1024 + (o2 + 1) * 512]))
                mm_group(k, ps, ps.ap[:, 0:512], pairs)
                t = s.op("activation", G1.ap[:, o2 * 512:(o2 + 1) * 512],
                         ps.ap[:, 0:512], AF.Silu, scale=SM2.ap[:, 48:49],
                         deps=rd(ps, SM2) + G1.wdeps())
                G1.did_write(t); ps.did_read(t); SM2.did_read(t)
            tok = sy.dma(g_dram[r0:r0 + 128, :], G1.ap[:, 0:1024],
                         deps=rd(G1) + g_buf.wdeps())
            g_buf.did_write(tok); G1.did_read(tok)



def emit_l0_attn(k, B, pre, x_in_ap, x_in_buf, x_out_ap, x_out_buf,
                 g_dram, g_buf, w):
    v, s, sy, gp = k.vector, k.scalar, k.sync, k.gpsimd
    W = Buf(B["WRES"].ap.bitcast(F32))
    G0, G1 = B["G0"], B["G1"]
    SMALL, SM2, DQH = B["SMALL"], B["SMALL2"], B["DQH"]
    QTM, KTM, VTM, RF = 0, 4096, 8192, 12288
    SOFF, ZER, TOP, MSK = 16384, 17408, 17536, 17664
    FF, LL, DD, ELN, E2 = 17792, 17920, 18048, 18176, 18304
    SFM, KS, QF, KF2, KHF, KHT, ASB = 18432, 18560, 18688, 18816, 18944, 19072, 19200
    OO = 19328
    QNW, KNW, GNW = 23424, 23552, 23680

    tok = sy.dma(W.ap[:, QNW:QNW + 128],
                 w["q_norm_w"][None, :].partition_broadcast(128), deps=W.wdeps())
    W.did_write(tok)
    tok = sy.dma(W.ap[:, KNW:KNW + 128],
                 w["k_norm_w"][None, :].partition_broadcast(128), deps=W.wdeps())
    W.did_write_acc(tok)
    tok = sy.dma(W.ap[:, GNW:GNW + 1024],
                 w["g_norm_w"][None, :].partition_broadcast(128), deps=W.wdeps())
    W.did_write_acc(tok)
    tok = sy.dma(SM2.ap[:, 48:56], w["A_log"][None, :].partition_broadcast(128),
                 deps=SM2.wdeps())
    SM2.did_write(tok)
    tok = sy.dma(SM2.ap[:, 56:64], w["dt_bias"][None, :].partition_broadcast(128),
                 deps=SM2.wdeps())
    SM2.did_write_acc(tok)
    t = s.op("activation", SM2.ap[:, 48:56], SM2.ap[:, 48:56], AF.Exp,
             deps=SM2.wdeps())
    SM2.did_write(t)
    t = v.op("memset", W.ap[:, SOFF:SOFF + 1024], 0.0, deps=W.wdeps())
    W.did_write_acc(t)
    t = v.op("memset", W.ap[:, ZER:ZER + 128], 0.0, deps=W.wdeps())
    W.did_write_acc(t)

    def stage_w(dram_ap, col0, ncol, dst=None):
        dstb = dst if dst is not None else G0
        wdeps = dstb.wdeps()
        first = True
        for c in range(8):
            t = sy.dma(dstb.ap[:, c * ncol:(c + 1) * ncol],
                       dram_ap[c * 128:(c + 1) * 128, col0:col0 + ncol],
                       deps=wdeps)
            if first:
                dstb.did_write(t); first = False
            else:
                dstb.did_write_acc(t)

    for ti in range(8):
        t0 = ti * 512
        emit_ln(k, x_in_ap, x_in_buf, G0, B["STATS"], G1, t0, pre)
        xn_fm = sub_buf(G1, 0, 4096)
        emit_tm_to_fm(k, G0, xn_fm, pre, B["PS2"], B["PS3"])
        alias_sync(G1, xn_fm)
        stg_i = 0
        for proj, dsto in (("Wq", QTM), ("Wk", KTM), ("Wv", VTM), ("Wr", RF)):
            fn = AF.Sigmoid if proj == "Wr" else AF.Silu
            for half in range(2):
                sgb = G0 if stg_i % 2 == 0 else B["XT"]
                stg_i += 1
                stage_w(w[proj], half * 512, 512, dst=sgb)
                for j in range(4):
                    ps = B["PS0"] if j % 2 == 0 else B["PS1"]
                    pairs = [(G1, G1.ap[:, c * 512 + j * 128:
                                  c * 512 + j * 128 + 128],
                              sgb, sgb.ap[:, c * 512:(c + 1) * 512])
                             for c in range(8)]
                    mm_group(k, ps, ps.ap[:, 0:512], pairs)
                    dst = W.ap[:, dsto + j * 1024 + half * 512:
                               dsto + j * 1024 + (half + 1) * 512]
                    t = s.op("activation", dst, ps.ap[:, 0:512], fn,
                             deps=rd(ps) + W.wdeps())
                    W.did_write(t); ps.did_read(t)
        stage_w(w["Wa"], 0, 8)
        for j in range(4):
            ps = B["PS4"]
            pairs = [(G1, G1.ap[:, c * 512 + j * 128: c * 512 + j * 128 + 128],
                      G0, G0.ap[:, c * 8:(c + 1) * 8]) for c in range(8)]
            mm_group(k, ps, ps.ap[:, 0:8], pairs)
            t = v.op("scalar_tensor_tensor", SMALL.ap[:, 40:48], ps.ap[:, 0:8],
                     1.0, SM2.ap[:, 56:64], OP.mult, OP.add,
                     deps=rd(ps, SM2) + SMALL.wdeps())
            SMALL.did_write(t); ps.did_read(t); SM2.did_read(t)
            t = s.op("activation", SMALL.ap[:, 40:48], SMALL.ap[:, 40:48],
                     AF.Exp, deps=SMALL.wdeps())
            SMALL.did_write(t)
            t = v.op("tensor_scalar_add", SMALL.ap[:, 40:48],
                     SMALL.ap[:, 40:48], 1.0, deps=SMALL.wdeps())
            SMALL.did_write(t)
            t = s.op("activation", SMALL.ap[:, 40:48], SMALL.ap[:, 40:48],
                     AF.Ln, deps=SMALL.wdeps())
            SMALL.did_write(t)
            t = v.op("scalar_tensor_tensor", DQH.ap[:, 8 * j:8 * j + 8],
                     SMALL.ap[:, 40:48], -1.0, SM2.ap[:, 48:56], OP.mult,
                     OP.mult, deps=rd(SMALL, SM2) + DQH.wdeps())
            DQH.did_write(t); SMALL.did_read(t); SM2.did_read(t)
        for dsto, nw in ((QTM, QNW), (KTM, KNW)):
            for j in range(4):
                sl = W.ap[:, dsto + j * 1024: dsto + (j + 1) * 1024]
                sl3 = sl.rearrange("p (h d) -> p h d", h=8)
                t = s.op("activation", G0.ap[:, 0:1024], sl, AF.Square,
                         deps=rd(W) + G0.wdeps())
                G0.did_write(t); W.did_read(t)
                t = v.op("reduce_sum", SMALL.ap[:, 0:8],
                         G0.ap[:, 0:1024].rearrange("p (h d) -> p h d", h=8),
                         AX.X, deps=rd(G0) + SMALL.wdeps())
                SMALL.did_write(t); G0.did_read(t)
                t = v.op("tensor_scalar", SMALL.ap[:, 0:8], SMALL.ap[:, 0:8],
                         float(1.0 / 128.0), float(EPS5), OP.mult, OP.add,
                         deps=SMALL.wdeps())
                SMALL.did_write(t)
                t = s.op("activation", SMALL.ap[:, 8:16], SMALL.ap[:, 0:8],
                         AF.Sqrt, deps=SMALL.wdeps())
                SMALL.did_write(t)
                t = v.op("reciprocal", SMALL.ap[:, 8:16], SMALL.ap[:, 8:16],
                         deps=SMALL.wdeps())
                SMALL.did_write(t)
                t = v.op("tensor_tensor", SMALL.ap[:, 16:24],
                         SMALL.ap[:, 8:16], SMALL.ap[:, 8:16], OP.mult,
                         deps=SMALL.wdeps())
                SMALL.did_write(t)
                t = v.op("tensor_tensor", SMALL.ap[:, 16:24],
                         SMALL.ap[:, 16:24], SMALL.ap[:, 0:8], OP.mult,
                         deps=SMALL.wdeps())
                SMALL.did_write(t)
                t = v.op("tensor_scalar", SMALL.ap[:, 16:24],
                         SMALL.ap[:, 16:24], -0.5, 1.5, OP.mult, OP.add,
                         deps=SMALL.wdeps())
                SMALL.did_write(t)
                t = v.op("tensor_tensor", SMALL.ap[:, 8:16],
                         SMALL.ap[:, 8:16], SMALL.ap[:, 16:24], OP.mult,
                         deps=SMALL.wdeps())
                SMALL.did_write(t)
                t = v.op("tensor_tensor", sl3, sl3,
                         SMALL.ap[:, 8:16].unsqueeze(2).broadcast_to([128, 8, 128]),
                         OP.mult, deps=rd(SMALL) + W.wdeps())
                W.did_write(t); SMALL.did_read(t)
                t = v.op("tensor_tensor", sl3, sl3,
                         W.ap[:, nw:nw + 128].unsqueeze(1).broadcast_to([128, 8, 128]),
                         OP.mult, deps=W.wdeps())
                W.did_write(t)
        for j in range(4):
            for h in range(8):
                sl = W.ap[:, RF + j * 1024 + h * 128: RF + j * 1024 + (h + 1) * 128]
                src = sl
                for r in range(4):
                    t = v.op("max", SMALL.ap[:, 8 * r:8 * r + 8], src,
                             deps=rd(W) + SMALL.wdeps())
                    SMALL.did_write(t); W.did_read(t)
                    t = v.op("match_replace", W.ap[:, TOP:TOP + 128],
                             SMALL.ap[:, 8 * r:8 * r + 8], src, -1e30,
                             deps=rd(W, SMALL) + W.wdeps())
                    W.did_write(t); SMALL.did_read(t)
                    src = W.ap[:, TOP:TOP + 128]
                t = v.op("reduce_sum", SMALL.ap[:, 32:33], SMALL.ap[:, 0:32],
                         AX.X, deps=SMALL.wdeps())
                SMALL.did_write(t)
                t = v.op("tensor_scalar_add", SMALL.ap[:, 32:33],
                         SMALL.ap[:, 32:33], 1e-9, deps=SMALL.wdeps())
                SMALL.did_write(t)
                t = v.op("reciprocal", SMALL.ap[:, 32:33], SMALL.ap[:, 32:33],
                         deps=SMALL.wdeps())
                SMALL.did_write(t)
                t = v.op("tensor_scalar", W.ap[:, MSK:MSK + 128], sl,
                         SMALL.ap[:, 31:32], None, OP.is_ge,
                         deps=rd(SMALL) + W.wdeps())
                W.did_write(t); SMALL.did_read(t)
                t = v.op("scalar_tensor_tensor", sl, sl, SMALL.ap[:, 32:33],
                         W.ap[:, MSK:MSK + 128], OP.mult, OP.mult,
                         deps=W.wdeps() + rd(SMALL))
                W.did_write(t); SMALL.did_read(t)
                t = v.op("tensor_scalar", sl, sl,
                         DQH.ap[:, 8 * j + h:8 * j + h + 1], None, OP.mult,
                         deps=W.wdeps() + rd(DQH))
                W.did_write(t); DQH.did_read(t)
        if ti == 0:
            slots = ["FF", "LL", "DD", "ELN", "E2", "SFM", "KS", "QF",
                     "KF2", "KHF", "KHT", "ASB"]
            sc2 = []
            for p in range(2):
                base = 17792 + p * 6912
                sc2.append({n: sub_buf(W, base + i * 128, 128)
                            for i, n in enumerate(slots)})
            sbh = [sub_buf(W, SOFF + h * 128, 128) for h in range(8)]
        oo = sub_buf(W, OO, 4096)
        for j in range(4):
            for h in range(8):
                p = (j * 8 + h) % 2
                sl = sc2[p]
                trbank = B["PS2"] if p == 0 else B["PS3"]
                psA = B["PS0"] if p == 0 else B["PS5"]
                psO = B["PS1"] if p == 0 else B["PS6"]
                psU = B["PS4"] if p == 0 else B["PS7"]

                def tr128(src_ap, src_b, dst_b):
                    tk = k.tensor.op("transpose", trbank.ap[:, 0:128],
                                     src_ap, pre.ident.ap[:],
                                     deps=trbank.wdeps() + rd(src_b, pre.ident))
                    trbank.did_write(tk); src_b.did_read(tk)
                    pre.ident.did_read(tk)
                    t2 = s.op("activation", dst_b.ap[:], trbank.ap[:, 0:128],
                              AF.Copy, deps=rd(trbank) + dst_b.wdeps())
                    dst_b.did_write(t2); trbank.did_read(t2)
                fsl = W.ap[:, RF + j * 1024 + h * 128:
                           RF + j * 1024 + (h + 1) * 128]
                tr128(fsl, W, sl["FF"])
                t = v.op("tensor_tensor_scan", sl["LL"].ap[:], sl["FF"].ap[:],
                         W.ap[:, ZER:ZER + 128], 0.0, OP.add, OP.add,
                         deps=rd(sl["FF"], W) + sl["LL"].wdeps())
                sl["LL"].did_write(t); sl["FF"].did_read(t); W.did_read(t)
                t = s.op("activation", sl["DD"].ap[:], sl["LL"].ap[:], AF.Exp,
                         deps=rd(sl["LL"]) + sl["DD"].wdeps())
                sl["DD"].did_write(t); sl["LL"].did_read(t)
                t = v.op("tensor_scalar_mul", sl["ELN"].ap[:], sl["LL"].ap[:],
                         -1.0, deps=rd(sl["LL"]) + sl["ELN"].wdeps())
                sl["ELN"].did_write(t); sl["LL"].did_read(t)
                t = s.op("activation", sl["ELN"].ap[:], sl["ELN"].ap[:],
                         AF.Exp, deps=sl["ELN"].wdeps())
                sl["ELN"].did_write(t)
                t = v.op("tensor_scalar", sl["E2"].ap[:], sl["LL"].ap[:],
                         -1.0, sl["LL"].ap[:, 127:128], OP.mult, OP.add,
                         deps=rd(sl["LL"]) + sl["E2"].wdeps())
                sl["E2"].did_write(t); sl["LL"].did_read(t)
                t = s.op("activation", sl["E2"].ap[:], sl["E2"].ap[:], AF.Exp,
                         deps=sl["E2"].wdeps())
                sl["E2"].did_write(t)
                t = s.op("activation", sl["SFM"].ap[:], sl["FF"].ap[:],
                         AF.Exp, deps=rd(sl["FF"]) + sl["SFM"].wdeps())
                sl["SFM"].did_write(t); sl["FF"].did_read(t)
                t = v.op("tensor_scalar", sl["SFM"].ap[:], sl["SFM"].ap[:],
                         -1.0, 1.0, OP.mult, OP.add, deps=sl["SFM"].wdeps())
                sl["SFM"].did_write(t)
                qsl = W.ap[:, QTM + j * 1024 + h * 128:
                           QTM + j * 1024 + (h + 1) * 128]
                tr128(qsl, W, sl["QF"])
                t = v.op("tensor_tensor", sl["QF"].ap[:], sl["QF"].ap[:],
                         sl["DD"].ap[:], OP.mult,
                         deps=rd(sl["DD"]) + sl["QF"].wdeps())
                sl["QF"].did_write(t); sl["DD"].did_read(t)
                ksl = W.ap[:, KTM + j * 1024 + h * 128:
                           KTM + j * 1024 + (h + 1) * 128]
                tr128(ksl, W, sl["KS"])
                t = v.op("tensor_tensor", sl["KS"].ap[:], sl["KS"].ap[:],
                         sl["SFM"].ap[:], OP.mult,
                         deps=rd(sl["SFM"]) + sl["KS"].wdeps())
                sl["KS"].did_write(t); sl["SFM"].did_read(t)
                t = v.op("tensor_tensor", sl["KF2"].ap[:], sl["KS"].ap[:],
                         sl["ELN"].ap[:], OP.mult,
                         deps=rd(sl["KS"], sl["ELN"]) + sl["KF2"].wdeps())
                sl["KF2"].did_write(t); sl["KS"].did_read(t)
                sl["ELN"].did_read(t)
                t = v.op("tensor_tensor", sl["KHF"].ap[:], sl["KS"].ap[:],
                         sl["E2"].ap[:], OP.mult,
                         deps=rd(sl["KS"], sl["E2"]) + sl["KHF"].wdeps())
                sl["KHF"].did_write(t); sl["KS"].did_read(t)
                sl["E2"].did_read(t)
                tr128(sl["KHF"].ap[:], sl["KHF"], sl["KHT"])
                tk = k.tensor.op("matmul", psA.ap[:, 0:128], sl["KF2"].ap[:],
                                 sl["QF"].ap[:], start=True, stop=True,
                                 deps=rd(sl["KF2"], sl["QF"]) + psA.wdeps())
                psA.did_write(tk); sl["KF2"].did_read(tk)
                sl["QF"].did_read(tk)
                t = v.op("tensor_tensor", sl["ASB"].ap[:], psA.ap[:, 0:128],
                         pre.triu.ap[:], OP.mult,
                         deps=rd(psA, pre.triu) + sl["ASB"].wdeps())
                sl["ASB"].did_write(t); psA.did_read(t); pre.triu.did_read(t)
                vsl = W.ap[:, VTM + j * 1024 + h * 128:
                           VTM + j * 1024 + (h + 1) * 128]
                tk = k.tensor.op("matmul", psO.ap[:, 0:128], sl["ASB"].ap[:],
                                 vsl, start=True, stop=False,
                                 deps=rd(sl["ASB"], W) + psO.wdeps(), inc=True)
                psO.did_write(tk); sl["ASB"].did_read(tk); W.did_read(tk)
                tk = k.tensor.op("matmul", psO.ap[:, 0:128], sl["QF"].ap[:],
                                 sbh[h].ap[:], start=False, stop=True,
                                 deps=rd(sl["QF"], sbh[h]), inc=True)
                psO.did_write(tk); sl["QF"].did_read(tk); sbh[h].did_read(tk)
                tk = k.tensor.op("matmul", psU.ap[:, 0:128], sl["KHT"].ap[:],
                                 vsl, start=True, stop=True,
                                 deps=rd(sl["KHT"], W) + psU.wdeps())
                psU.did_write(tk); sl["KHT"].did_read(tk); W.did_read(tk)
                t = v.op("scalar_tensor_tensor", sbh[h].ap[:], sbh[h].ap[:],
                         sl["DD"].ap[:, 127:128], psU.ap[:, 0:128],
                         OP.mult, OP.add,
                         deps=sbh[h].wdeps() + rd(sl["DD"], psU))
                sbh[h].did_write(t); sl["DD"].did_read(t); psU.did_read(t)
                t = s.op("activation",
                         oo.ap[:, j * 1024 + h * 128:j * 1024 + (h + 1) * 128],
                         psO.ap[:, 0:128], AF.Copy,
                         deps=rd(psO) + oo.wdeps())
                oo.did_write_acc(t); psO.did_read(t)
            osl = oo.ap[:, j * 1024:(j + 1) * 1024]
            t = s.op("activation", G0.ap[:, 0:1024], osl, AF.Square,
                     accum_out=SMALL.ap[:, 0:1],
                     deps=rd(oo) + G0.wdeps() + SMALL.wdeps())
            G0.did_write(t); oo.did_read(t); SMALL.did_write(t)
            t = v.op("tensor_scalar", SMALL.ap[:, 0:1], SMALL.ap[:, 0:1],
                     float(1.0 / 1024.0), float(EPS5), OP.mult, OP.add,
                     deps=SMALL.wdeps())
            SMALL.did_write(t)
            t = s.op("activation", SMALL.ap[:, 1:2], SMALL.ap[:, 0:1], AF.Sqrt,
                     deps=SMALL.wdeps())
            SMALL.did_write(t)
            t = v.op("reciprocal", SMALL.ap[:, 1:2], SMALL.ap[:, 1:2],
                     deps=SMALL.wdeps())
            SMALL.did_write(t)
            t = v.op("tensor_tensor", SMALL.ap[:, 2:3], SMALL.ap[:, 1:2],
                     SMALL.ap[:, 1:2], OP.mult, deps=SMALL.wdeps())
            SMALL.did_write(t)
            t = v.op("tensor_tensor", SMALL.ap[:, 2:3], SMALL.ap[:, 2:3],
                     SMALL.ap[:, 0:1], OP.mult, deps=SMALL.wdeps())
            SMALL.did_write(t)
            t = v.op("tensor_scalar", SMALL.ap[:, 2:3], SMALL.ap[:, 2:3],
                     -0.5, 1.5, OP.mult, OP.add, deps=SMALL.wdeps())
            SMALL.did_write(t)
            t = v.op("tensor_tensor", SMALL.ap[:, 1:2], SMALL.ap[:, 1:2],
                     SMALL.ap[:, 2:3], OP.mult, deps=SMALL.wdeps())
            SMALL.did_write(t)
            t = v.op("tensor_scalar", osl, osl, SMALL.ap[:, 1:2], None,
                     OP.mult, deps=oo.wdeps() + rd(SMALL))
            oo.did_write(t); SMALL.did_read(t)
            t = v.op("tensor_tensor", osl, osl, W.ap[:, GNW:GNW + 1024],
                     OP.mult, deps=oo.wdeps() + rd(W))
            oo.did_write(t); W.did_read(t)
        o_fm = sub_buf(G0, 0, 4096)
        emit_tm_to_fm(k, oo, o_fm, pre, B["PS2"], B["PS3"])
        merge_sub(W, oo)
        merge_sub(G0, o_fm)
        for half in range(2):
            stage_w(w["Wo"], half * 512, 512, dst=G1)
            for j in range(4):
                ps = B["PS5"] if j % 2 == 0 else B["PS6"]
                pairs = [(G0, G0.ap[:, c * 512 + j * 128:
                              c * 512 + j * 128 + 128],
                          G1, G1.ap[:, c * 512:(c + 1) * 512])
                         for c in range(8)]
                mm_group(k, ps, ps.ap[:, 0:512], pairs)
                db = B["SA"] if j % 2 == 0 else B["SB"]
                t = s.op("activation", db.ap[:, 0:512], ps.ap[:, 0:512],
                         AF.Copy, deps=rd(ps) + db.wdeps())
                db.did_write(t); ps.did_read(t)
                r0 = t0 + j * 128
                csl = slice(half * 512, (half + 1) * 512)
                tok = sy.dma(B["SC"].ap[:, 0:512], g_dram[r0:r0 + 128, csl],
                             deps=B["SC"].wdeps() + rd(g_buf))
                B["SC"].did_write(tok); g_buf.did_read(tok)
                tok = sy.dma(B["SC"].ap[:, 512:1024], x_in_ap[r0:r0 + 128, csl],
                             deps=B["SC"].wdeps() + rd(x_in_buf))
                B["SC"].did_write_acc(tok); x_in_buf.did_read(tok)
                t = v.op("tensor_tensor", db.ap[:, 0:512], db.ap[:, 0:512],
                         B["SC"].ap[:, 0:512], OP.mult,
                         deps=db.wdeps() + rd(B["SC"]))
                db.did_write(t); B["SC"].did_read(t)
                t = v.op("tensor_tensor", db.ap[:, 0:512], db.ap[:, 0:512],
                         B["SC"].ap[:, 512:1024], OP.add,
                         deps=db.wdeps() + rd(B["SC"]))
                db.did_write(t); B["SC"].did_read(t)
                tok = sy.dma(x_out_ap[r0:r0 + 128, csl], db.ap[:, 0:512],
                             deps=rd(db) + x_out_buf.wdeps())
                x_out_buf.did_write_acc(tok); db.did_read(tok)


def emit_l2_attn(k, B, pre, x_in_ap, x_in_buf, x_out_ap, x_out_buf,
                 g_dram, g_buf, w):
    v, s, sy, gp = k.vector, k.scalar, k.sync, k.gpsimd
    W = Buf(B["WRES"].ap.bitcast(F32))
    G0, G1 = B["G0"], B["G1"]
    SMALL = B["SMALL"]
    QKV, CTX, CTF = 0, 12288, 16384
    SCR0, SCR1, SCO = 20480, 21504, 22528
    BIA, BOU = 23552, 26624
    ONES = B["ONESROW"]

    tok = sy.dma(W.ap[0:1, BIA:BIA + 3072], w["b_qkv"][None, :],
                 deps=W.wdeps())
    W.did_write(tok)
    tok = sy.dma(W.ap[0:1, BOU:BOU + 1024], w["b_out"][None, :],
                 deps=W.wdeps())
    W.did_write_acc(tok)

    for ti in range(8):
        t0 = ti * 512
        emit_ln(k, x_in_ap, x_in_buf, G0, B["STATS"], G1, t0, pre)
        xn_fm = sub_buf(G1, 0, 4096)
        emit_tm_to_fm(k, G0, xn_fm, pre, B["PS2"], B["PS3"])
        alias_sync(G1, xn_fm)
        stg = [sub_buf(W, CTX, 4096), sub_buf(W, CTF, 4096)]
        qkvb = sub_buf(W, QKV, 12288)
        for p in range(6):
            sb = stg[p % 2]
            wdeps = sb.wdeps()
            first = True
            for c in range(8):
                t = sy.dma(sb.ap[:, c * 512:(c + 1) * 512],
                           w["W_qkv"][c * 128:(c + 1) * 128,
                                      p * 512:(p + 1) * 512], deps=wdeps)
                if first:
                    sb.did_write(t); first = False
                else:
                    sb.did_write_acc(t)
            for j in range(4):
                ps = B["PS0"] if j % 2 == 0 else B["PS1"]
                pairs = [(G1, G1.ap[:, c * 512 + j * 128:
                              c * 512 + j * 128 + 128],
                          sb, sb.ap[:, c * 512:(c + 1) * 512])
                         for c in range(8)]
                pairs.append((ONES, ONES.ap[0:1, 0:128],
                              W, W.ap[0:1, BIA + p * 512:BIA + (p + 1) * 512]))
                mm_group(k, ps, ps.ap[:, 0:512], pairs)
                t = s.op("activation",
                         qkvb.ap[:, j * 3072 + p * 512:j * 3072 + (p + 1) * 512],
                         ps.ap[:, 0:512], AF.Copy, deps=rd(ps) + qkvb.wdeps())
                qkvb.did_write_acc(t); ps.did_read(t)
        for b_ in stg:
            merge_sub(W, b_)
        merge_sub(W, qkvb)
        scrb = [sub_buf(W, SCR0, 1024), sub_buf(W, SCR1, 1024)]
        scob = sub_buf(W, SCO, 1024)
        ctxb = sub_buf(W, CTX, 4096)
        for j in range(4):
            qo = QKV + j * 3072
            q3 = W.ap[:, qo:qo + 3072].rearrange("p (h e) -> p h e", h=16)
            sco3 = scob.ap[:, j * 256:(j + 1) * 256].rearrange(
                "p (h g) -> p h g", h=16)
            for g in range(16):
                sc = scrb[g % 2]
                kg = q3[:, g:g + 1, 64:128].broadcast_to([128, 16, 64])
                eng = v if g < 6 else gp
                t = eng.op("tensor_tensor", sc.ap[:, 0:1024].rearrange(
                               "p (h e) -> p h e", h=16),
                           q3[:, :, 0:64], kg, OP.mult,
                           deps=rd(W) + sc.wdeps())
                sc.did_write(t); W.did_read(t)
                t = v.op("reduce_sum", sco3[:, :, g:g + 1],
                         sc.ap[:, 0:1024].rearrange("p (h e) -> p h e", h=16),
                         AX.X, deps=rd(sc) + scob.wdeps())
                scob.did_write_acc(t); sc.did_read(t)
            t = v.op("reduce_max", SMALL.ap[:, 0:16], sco3, AX.X,
                     deps=rd(scob) + SMALL.wdeps())
            SMALL.did_write(t); scob.did_read(t)
            t = v.op("tensor_scalar_mul", SMALL.ap[:, 0:16],
                     SMALL.ap[:, 0:16], 0.125, deps=SMALL.wdeps())
            SMALL.did_write(t)
            t = v.op("scalar_tensor_tensor", sco3, sco3, 0.125,
                     SMALL.ap[:, 0:16].unsqueeze(2).broadcast_to([128, 16, 16]),
                     OP.mult, OP.subtract,
                     deps=scob.wdeps() + rd(SMALL))
            scob.did_write(t); SMALL.did_read(t)
            t = s.op("activation", scob.ap[:, j * 256:(j + 1) * 256],
                     scob.ap[:, j * 256:(j + 1) * 256], AF.Exp,
                     deps=scob.wdeps())
            scob.did_write(t)
            t = v.op("reduce_sum", SMALL.ap[:, 16:32], sco3, AX.X,
                     deps=rd(scob) + SMALL.wdeps())
            SMALL.did_write(t); scob.did_read(t)
            t = v.op("reciprocal", SMALL.ap[:, 16:32], SMALL.ap[:, 16:32],
                     deps=SMALL.wdeps())
            SMALL.did_write(t)
            t = v.op("tensor_tensor", sco3, sco3,
                     SMALL.ap[:, 16:32].unsqueeze(2).broadcast_to([128, 16, 16]),
                     OP.mult, deps=scob.wdeps() + rd(SMALL))
            scob.did_write(t); SMALL.did_read(t)
            ctx3 = ctxb.ap[:, j * 1024:(j + 1) * 1024].rearrange(
                "p (h e) -> p h e", h=16)
            for g in range(16):
                sc = scrb[g % 2]
                vg = q3[:, g:g + 1, 128:192].broadcast_to([128, 16, 64])
                ag = sco3[:, :, g:g + 1].broadcast_to([128, 16, 64])
                eng = v if g < 6 else gp
                t = eng.op("tensor_tensor", sc.ap[:, 0:1024].rearrange(
                               "p (h e) -> p h e", h=16),
                           vg, ag, OP.mult,
                           deps=rd(W, scob) + sc.wdeps())
                sc.did_write(t); W.did_read(t); scob.did_read(t)
                if g == 0:
                    t = v.op("tensor_copy",
                             ctx3, sc.ap[:, 0:1024].rearrange(
                                 "p (h e) -> p h e", h=16),
                             deps=rd(sc) + ctxb.wdeps())
                else:
                    t = v.op("tensor_tensor", ctx3, ctx3,
                             sc.ap[:, 0:1024].rearrange("p (h e) -> p h e",
                                                        h=16),
                             OP.add, deps=rd(sc) + ctxb.wdeps())
                ctxb.did_write(t); sc.did_read(t)
        for b_ in scrb:
            merge_sub(W, b_)
        merge_sub(W, scob)
        merge_sub(W, ctxb)
        c_tm = sub_buf(W, CTX, 4096)
        c_fm = sub_buf(W, CTF, 4096)
        emit_tm_to_fm(k, c_tm, c_fm, pre, B["PS2"], B["PS3"])
        merge_sub(W, c_tm)
        merge_sub(W, c_fm)
        for half in range(2):
            wdeps = G1.wdeps()
            first = True
            for c in range(8):
                tt = sy.dma(G1.ap[:, c * 512:(c + 1) * 512],
                            w["W_out"][c * 128:(c + 1) * 128,
                                       half * 512:(half + 1) * 512],
                            deps=wdeps)
                if first:
                    G1.did_write(tt); first = False
                else:
                    G1.did_write_acc(tt)
            for j in range(4):
                ps = B["PS5"] if j % 2 == 0 else B["PS6"]
                pairs = [(W, W.ap[:, CTF + c * 512 + j * 128:
                              CTF + c * 512 + j * 128 + 128],
                          G1, G1.ap[:, c * 512:(c + 1) * 512])
                         for c in range(8)]
                pairs.append((ONES, ONES.ap[0:1, 0:128],
                              W, W.ap[0:1, BOU + half * 512:
                                      BOU + (half + 1) * 512]))
                mm_group(k, ps, ps.ap[:, 0:512], pairs)
                db = B["SA"] if j % 2 == 0 else B["SB"]
                t = s.op("activation", db.ap[:, 0:512], ps.ap[:, 0:512],
                         AF.Copy, deps=rd(ps) + db.wdeps())
                db.did_write(t); ps.did_read(t)
                r0 = t0 + j * 128
                csl = slice(half * 512, (half + 1) * 512)
                tok = sy.dma(B["SC"].ap[:, 0:512], g_dram[r0:r0 + 128, csl],
                             deps=B["SC"].wdeps() + rd(g_buf))
                B["SC"].did_write(tok); g_buf.did_read(tok)
                tok = sy.dma(B["SC"].ap[:, 512:1024], x_in_ap[r0:r0 + 128, csl],
                             deps=B["SC"].wdeps() + rd(x_in_buf))
                B["SC"].did_write_acc(tok); x_in_buf.did_read(tok)
                t = v.op("tensor_tensor", db.ap[:, 0:512], db.ap[:, 0:512],
                         B["SC"].ap[:, 0:512], OP.mult,
                         deps=db.wdeps() + rd(B["SC"]))
                db.did_write(t); B["SC"].did_read(t)
                t = v.op("tensor_tensor", db.ap[:, 0:512], db.ap[:, 0:512],
                         B["SC"].ap[:, 512:1024], OP.add,
                         deps=db.wdeps() + rd(B["SC"]))
                db.did_write(t); B["SC"].did_read(t)
                tok = sy.dma(x_out_ap[r0:r0 + 128, csl], db.ap[:, 0:512],
                             deps=rd(db) + x_out_buf.wdeps())
                x_out_buf.did_write_acc(tok); db.did_read(tok)


DEBUG_DUMPS = False


def build_program():
    nc = bass.Bass("TRN2", target_bir_lowering=False)

    def inp(name, shape):
        return nc.dram_tensor(name, shape, F32, kind="ExternalInput").ap()

    x = inp("x", [4096, 1024])
    w = {}
    for n in ("L0_attn_Wq", "L0_attn_Wk", "L0_attn_Wv", "L0_attn_Wr",
              "L0_attn_Wo", "L1_attn_W_out", "L2_attn_W_out",
              "L3_attn_W_proj"):
        w[n] = inp(n, [1024, 1024])
    w["L0_attn_Wa"] = inp("L0_attn_Wa", [1024, 8])
    for n in ("L0_attn_A_log", "L0_attn_dt_bias"):
        w[n] = inp(n, [8])
    for n in ("L0_attn_q_norm_w", "L0_attn_k_norm_w"):
        w[n] = inp(n, [128])
    w["L0_attn_g_norm_w"] = inp("L0_attn_g_norm_w", [1024])
    w["L0_ffn_w1T"] = inp("L0_ffn_w1T", [1024, 4096])
    w["L0_ffn_w2T"] = inp("L0_ffn_w2T", [4096, 1024])
    for n in ("L1_attn_W_qkv", "L2_attn_W_qkv", "L3_attn_W_attn"):
        w[n] = inp(n, [1024, 3072])
    for n in ("L1_attn_b_qkv", "L2_attn_b_qkv", "L3_attn_b_attn"):
        w[n] = inp(n, [3072])
    for n in ("L1_attn_b_out", "L2_attn_b_out", "L3_attn_b_proj"):
        w[n] = inp(n, [1024])
    w["L1_ffn_Ws1"] = inp("L1_ffn_Ws1", [1024, 512])
    w["L1_ffn_Ws2"] = inp("L1_ffn_Ws2", [512, 1024])
    w["L1_ffn_W1"] = inp("L1_ffn_W1", [8, 1024, 512])
    w["L1_ffn_W2"] = inp("L1_ffn_W2", [8, 512, 1024])
    w["L1_ffn_Wr"] = inp("L1_ffn_Wr", [1024, 8])
    w["L2_ffn_w_gateT"] = inp("L2_ffn_w_gateT", [1024, 4096])
    w["L2_ffn_w_upT"] = inp("L2_ffn_w_upT", [1024, 4096])
    w["L2_ffn_w_downT"] = inp("L2_ffn_w_downT", [4096, 1024])
    w["L3_ffn_gate_w"] = inp("L3_ffn_gate_w", [4096, 1024])
    w["L3_ffn_up_w"] = inp("L3_ffn_up_w", [4096, 1024])
    w["L3_ffn_down_w"] = inp("L3_ffn_down_w", [1024, 4096])
    w["L3_ffn_s_in"] = inp("L3_ffn_s_in", [1])
    w["L3_ffn_s_in_down"] = inp("L3_ffn_s_in_down", [1])

    xo = nc.dram_tensor("xo", [4096, 1024], F32, kind="ExternalOutput").ap()
    xA = nc.dram_tensor("xA", [4096, 1024], F32, kind="Internal").ap()
    xB = nc.dram_tensor("xB", [4096, 1024], F32, kind="Internal").ap()
    g = nc.dram_tensor("g_scr", [4096, 1024], F32, kind="Internal").ap()
    hq = nc.dram_tensor("hq_scr", [4096, 4096], BF16, kind="Internal").ap()
    dqh = nc.dram_tensor("dqh_scr", [4096], F32, kind="Internal").ap()
    l3q = nc.dram_tensor("l3q", [2, 8, 128, 4096], BF16, kind="Internal").ap()
    l3k = nc.dram_tensor("l3k", [2, 8, 128, 4096], BF16, kind="Internal").ap()
    l3v = nc.dram_tensor("l3v", [32, 128, 1040], F32, kind="Internal").ap()

    with ExitStack() as ctx:
        k = K(nc, ctx)
        pre = Prelude(k)
        B = make_buffers(k)
        pre.emit(k)
        emit_buffer_init(k, B)
        xib, xab, xbb, xob = Buf(x), Buf(xA), Buf(xB), Buf(xo)
        gb, hqb, dqhb = Buf(g), Buf(hq), Buf(dqh)
        scr = {"q": Buf(l3q), "kk": Buf(l3k), "v": Buf(l3v)}
        phases = [
            ("L0_ffn", lambda: emit_ffn_bitlin(
                k, B, pre, x, xib, g, gb,
                {"w1": w["L0_ffn_w1T"], "w2": w["L0_ffn_w2T"]},
                "gelu", hq, hqb, dqh, dqhb)),
            ("L0_attn", lambda: emit_l0_attn(
                k, B, pre, x, xib, xA, xab, g, gb,
                {n: w["L0_attn_" + n] for n in
                 ("Wq", "Wk", "Wv", "Wr", "Wa", "A_log", "dt_bias",
                  "q_norm_w", "k_norm_w", "g_norm_w", "Wo")})),
            ("L1_ffn", lambda: emit_moe(
                k, B, pre, xA, xab, g, gb,
                {n: w["L1_ffn_" + n] for n in
                 ("Ws1", "Ws2", "W1", "W2", "Wr")})),
            ("L1_attn", lambda: emit_l1_attn(
                k, B, pre, xA, xab, xB, xbb, g, gb,
                {n: w["L1_attn_" + n] for n in
                 ("W_qkv", "b_qkv", "W_out", "b_out")})),
            ("L2_ffn", lambda: emit_ffn_bitlin(
                k, B, pre, xB, xbb, g, gb,
                {"wg": w["L2_ffn_w_gateT"], "wu": w["L2_ffn_w_upT"],
                 "wd": w["L2_ffn_w_downT"]},
                "swiglu", hq, hqb, dqh, dqhb)),
            ("L2_attn", lambda: emit_l2_attn(
                k, B, pre, xB, xbb, xA, xab, g, gb,
                {n: w["L2_attn_" + n] for n in
                 ("W_qkv", "b_qkv", "W_out", "b_out")})),
            ("L3_ffn", lambda: emit_ffn_nvfp4(
                k, B, pre, xA, xab, g, gb,
                {"gate_w": w["L3_ffn_gate_w"], "up_w": w["L3_ffn_up_w"],
                 "down_w": w["L3_ffn_down_w"], "s_in": w["L3_ffn_s_in"],
                 "s_in_down": w["L3_ffn_s_in_down"]},
                hq, hqb)),
            ("L3_attn", lambda: emit_l3_attn(
                k, B, pre, xA, xab, xo, xob, g, gb,
                {"W_attn": w["L3_attn_W_attn"],
                 "b_attn": w["L3_attn_b_attn"],
                 "W_proj": w["L3_attn_W_proj"],
                 "b_proj": w["L3_attn_b_proj"]}, scr)),
        ]
        dumps = {}
        if DEBUG_DUMPS:
            for i in range(4):
                dumps[i] = nc.dram_tensor(f"dump{i}", [4096, 1024], F32,
                                          kind="ExternalOutput").ap()
        dump_after = {"L0_attn": (0, xA), "L1_attn": (1, xB),
                      "L2_attn": (2, xA), "L3_attn": (3, xo)}
        for name, fn in phases:
            with nc.named_scope(name):
                fn()
                barrier(k)
                if DEBUG_DUMPS and name in dump_after:
                    i, src = dump_after[name]
                    k.sync.dma(dumps[i], src[:, :])
                    barrier(k)
    return nc


def kernel(**inputs):
    nc = build_program()
    x = inputs["L0_attn_hidden_states"]
    base = {}
    for n, a in inputs.items():
        if n == "L0_attn_hidden_states":
            continue
        if n in ("L0_ffn_w1", "L0_ffn_w2", "L2_ffn_w_gate", "L2_ffn_w_up",
                 "L2_ffn_w_down"):
            nn = {"L0_ffn_w1": "L0_ffn_w1T", "L0_ffn_w2": "L0_ffn_w2T",
                  "L2_ffn_w_gate": "L2_ffn_w_gateT",
                  "L2_ffn_w_up": "L2_ffn_w_upT",
                  "L2_ffn_w_down": "L2_ffn_w_downT"}[n]
            base[nn] = np.ascontiguousarray(np.asarray(a).T)
        else:
            base[n] = np.asarray(a)
    inmaps = [dict(base, x=np.ascontiguousarray(x[b])) for b in range(8)]
    res = run_bass_kernel_spmd(nc, inmaps, core_ids=list(range(8)))
    out = np.stack([np.asarray(res.results[b]["xo"]) for b in range(8)])
    return out.astype(np.float32)


# version 11
# speedup vs baseline: 1.0210x; 1.0157x over previous
import math
from contextlib import ExitStack

import numpy as np

import concourse.bass as bass
import concourse.mybir as mybir
from concourse.bass_utils import run_bass_kernel_spmd

F32 = mybir.dt.float32
BF16 = mybir.dt.bfloat16
I32 = mybir.dt.int32
AX = mybir.AxisListType
OP = mybir.AluOpType
AF = mybir.ActivationFunctionType

N_CORES = 8
T = 4096
D = 1024
NT = T // 128



class Sem:
    __slots__ = ("h", "cnt", "name", "dma_sem_q")

    def __init__(self, h, name):
        self.h = h
        self.cnt = 0
        self.name = name
        self.dma_sem_q = None


class EngineQ:

    def __init__(self, k, eng, name, dma_queue=False):
        self.k = k
        self.e = eng
        self.name = name
        self.sem = k.new_sem("sq_" + name)
        self.waited = {}
        self.dma_queue = dma_queue
        if dma_queue:
            self.pool = [k.new_sem(f"dq_{name}{i}") for i in range(8)]
            self.rr = 0

    def _wait(self, deps):
        for t in deps:
            if t is None:
                continue
            sem, val = t
            if sem is self.sem:
                val = sem.cnt
            if self.waited.get(sem, 0) >= val:
                continue
            self.e.wait_ge(sem.h, val)
            self.waited[sem] = val

    def op(self, name, *args, deps=(), inc=True, **kw):
        self._wait(deps)
        ins = getattr(self.e, name)(*args, **kw)
        if inc:
            self.sem.cnt += 1
            ins.then_inc(self.sem.h, 1)
            return (self.sem, self.sem.cnt)
        return None

    def dma(self, out, in_, deps=(), **kw):
        s = self.pool[self.rr % len(self.pool)]
        self.rr += 1
        self._wait(list(deps) + [(s, s.cnt)])
        ins = self.e.dma_start(out=out, in_=in_, **kw)
        s.cnt += 16
        ins.then_inc(s.h, 16)
        return (s, s.cnt)

    def wait(self, deps):
        self._wait(deps)


class Buf:

    __slots__ = ("ap", "wtoks", "rtoks")

    def __init__(self, ap):
        self.ap = ap
        self.wtoks = []
        self.rtoks = []

    @property
    def wtok(self):
        return self.wtoks[-1] if self.wtoks else None

    def rdeps(self):
        return list(self.wtoks)

    def wdeps(self):
        return list(self.rtoks) + list(self.wtoks)

    def did_read(self, tok):
        if tok is not None:
            self.rtoks.append(tok)

    def did_write(self, tok):
        self.wtoks = [tok] if tok is not None else []
        self.rtoks = []

    def did_write_acc(self, tok):
        if tok is not None:
            self.wtoks.append(tok)

    def __getitem__(self, key):
        return self.ap[key]


def rd(*bufs):
    deps = []
    for b in bufs:
        deps += b.rdeps()
    return deps


class K:

    def __init__(self, nc, ctx):
        self.nc = nc
        self.ctx = ctx
        self.sems = []
        self.tensor = EngineQ(self, nc.tensor, "pe")
        self.vector = EngineQ(self, nc.vector, "dve")
        self.scalar = EngineQ(self, nc.scalar, "act")
        self.gpsimd = EngineQ(self, nc.gpsimd, "pool", dma_queue=True)
        self.sync = EngineQ(self, nc.sync, "sp", dma_queue=True)
        self._uid = 0

    def new_sem(self, name):
        h = self.ctx.enter_context(self.nc.semaphore(name))
        s = Sem(h, name)
        self.sems.append(s)
        return s

    def uid(self):
        self._uid += 1
        return self._uid

    def sbuf(self, name, shape, dt=F32):
        t = self.ctx.enter_context(self.nc.sbuf_tensor(name, shape, dt))
        return Buf(t.ap() if hasattr(t, "ap") else t)

    def psum(self, name, shape, dt=F32):
        t = self.ctx.enter_context(self.nc.psum_tensor(name, shape, dt))
        return Buf(t.ap() if hasattr(t, "ap") else t)

    def copy(self, eng, dst_buf, dst_ap, src_buf, src_ap, **kw):
        tok = eng.op("tensor_copy", dst_ap, src_ap,
                     deps=rd(src_buf) + dst_buf.wdeps(), **kw)
        dst_buf.did_write(tok)
        src_buf.did_read(tok)
        return tok



def mm_group(k, out_buf, out_ap, pairs, deps_extra=()):
    deps = list(deps_extra) + out_buf.wdeps()
    seen = set()
    for lb, _, rbuf, _ in pairs:
        for b in (lb, rbuf):
            if id(b) not in seen:
                seen.add(id(b))
                deps += b.rdeps()
    n = len(pairs)
    tok = None
    for i, (lb, la, rbuf, ra) in enumerate(pairs):
        tok = k.tensor.op(
            "matmul", out_ap, la, ra,
            deps=deps if i == 0 else (),
            inc=(i == n - 1),
            start=(i == 0), stop=(i == n - 1),
        )
    out_buf.did_write(tok)
    for lb, _, rbuf, _ in pairs:
        lb.did_read(tok)
        rbuf.did_read(tok)
    return tok



C1 = np.float32(1.5 * 2.0**23)
C2 = np.float32(1.5 * 2.0**52)
EPS5 = np.float32(1e-5)


class Prelude:

    def __init__(self, k):
        nc = k.nc
        ident_np = np.eye(128, dtype=np.float32)
        self.ident_dram = nc.inline_tensor(ident_np, name="c_ident").ap()
        self.ident = k.sbuf("sb_ident", [128, 128], F32)
        import ml_dtypes
        self.identbf_dram = nc.inline_tensor(
            ident_np.astype(ml_dtypes.bfloat16), name="c_identbf").ap()
        self.ident_bf = k.sbuf("sb_identbf", [128, 128], BF16)
        triu_np = np.triu(np.ones((128, 128), dtype=np.float32))
        self.triu_dram = nc.inline_tensor(triu_np, name="c_triu").ap()
        self.triu = k.sbuf("sb_triu", [128, 128], F32)
        self.j = k.sbuf("sb_j", [128, 1], F32)
        self.ones_col = k.sbuf("sb_ones", [128, 8], F32)
        self.consts = k.sbuf("sb_consts", [128, 8], F32)
        self.eps5 = self.consts.ap[:, 0:1]
        self.half = self.consts.ap[:, 1:2]
        self.zero = self.consts.ap[:, 2:3]
        self.one = self.consts.ap[:, 3:4]
        self.c_eighth = self.consts.ap[:, 4:5]
        self.c_nshift = self.consts.ap[:, 5:6]

    def emit(self, k):
        v = k.vector
        t1 = k.sync.dma(self.ident.ap[:], self.ident_dram[:])
        self.ident.did_write(t1)
        t2 = k.sync.dma(self.triu.ap[:], self.triu_dram[:])
        self.triu.did_write(t2)
        t3 = k.sync.dma(self.ident_bf.ap[:], self.identbf_dram[:])
        self.ident_bf.did_write(t3)
        tok = v.op("memset", self.ones_col.ap[:], 1.0, deps=self.ones_col.wdeps())
        self.ones_col.did_write(tok)
        for col, val in [(0, 1e-5), (1, 0.5), (2, 0.0), (3, 1.0),
                         (4, 0.125), (5, -8.0)]:
            tok = v.op("memset", self.consts.ap[:, col:col + 1], float(val),
                       deps=self.consts.wdeps())
            self.consts.did_write(tok)
        tok = v.op("tensor_scalar", self.j.ap[:], self.ones_col.ap[:, 0:1],
                   float(2.0**-30), 1.0, OP.add, OP.subtract,
                   deps=rd(self.ones_col) + self.j.wdeps())
        self.j.did_write(tok)
        self.ones_col.did_read(tok)
        tok = v.op("tensor_scalar", self.j.ap[:], self.j.ap[:],
                   float(-(2.0**30)), 1.0, OP.mult, OP.add,
                   deps=self.j.wdeps())
        self.j.did_write(tok)


def rne_round(k, out_buf, out_ap, x_buf, x_ap, scale_ap, pre, scratch_a, scratch_b,
              scale_buf=None):
    v = k.vector
    sa, sb = scratch_a, scratch_b
    a_ap = sa.ap[:, : x_ap.shape[-1]] if sa.ap.shape[-1] != x_ap.shape[-1] else sa.ap[:]
    b_ap = sb.ap[:, : x_ap.shape[-1]] if sb.ap.shape[-1] != x_ap.shape[-1] else sb.ap[:]
    sdeps = rd(scale_buf) if scale_buf is not None else []
    t = v.op("tensor_scalar", a_ap, x_ap, scale_ap, float(C1), OP.mult, OP.add,
             deps=rd(x_buf, pre.j) + sa.wdeps() + sdeps)
    sa.did_write(t); x_buf.did_read(t)
    if scale_buf is not None:
        scale_buf.did_read(t)
    t = v.op("tensor_scalar", a_ap, a_ap, float(C1), pre.j.ap[:], OP.subtract, OP.mult,
             deps=sa.wdeps())
    sa.did_write(t); pre.j.did_read(t)
    t = v.op("tensor_scalar", b_ap, x_ap, scale_ap, float(C2), OP.mult, OP.add,
             deps=rd(x_buf) + sb.wdeps() + sdeps)
    sb.did_write(t); x_buf.did_read(t)
    if scale_buf is not None:
        scale_buf.did_read(t)
    t = v.op("scalar_tensor_tensor", out_ap, b_ap, float(C2), a_ap,
             OP.subtract, OP.add,
             deps=rd(sa, sb) + out_buf.wdeps())
    out_buf.did_write(t); sa.did_read(t); sb.did_read(t)
    return t


def emit_ln(k, x_src_ap, x_src_buf, xn_tm, stats, scratch, t0, pre=None):
    v, s, sy = k.vector, k.scalar, k.sync
    tok = sy.dma(xn_tm.ap[:].rearrange("p (a d) -> p a d", a=4),
                 x_src_ap[t0:t0 + 512, :].rearrange("(a p) d -> p a d", p=128),
                 deps=xn_tm.wdeps() + rd(x_src_buf))
    xn_tm.did_write(tok)
    x_src_buf.did_read(tok)
    for j in range(4):
        xs = xn_tm.ap[:, j * 1024:(j + 1) * 1024]
        t = v.op("reduce_sum", stats.ap[:, j:j + 1], xs, AX.X,
                 deps=rd(xn_tm) + stats.wdeps())
        stats.did_write(t); xn_tm.did_read(t)
    t = v.op("tensor_scalar_mul", stats.ap[:, 0:4], stats.ap[:, 0:4], 1.0 / 1024.0,
             deps=stats.wdeps())
    stats.did_write(t)
    for j in range(4):
        xs = xn_tm.ap[:, j * 1024:(j + 1) * 1024]
        t = v.op("tensor_scalar", xs, xs, stats.ap[:, j:j + 1], None,
                 OP.subtract, deps=rd(stats) + xn_tm.wdeps())
        xn_tm.did_write(t); stats.did_read(t)
        t = s.op("activation", scratch.ap[:, j * 1024:(j + 1) * 1024], xs,
                 AF.Square, accum_out=stats.ap[:, 8 + j:9 + j],
                 deps=rd(xn_tm) + scratch.wdeps() + stats.wdeps())
        stats.did_write(t); scratch.did_write(t); xn_tm.did_read(t)
    t = v.op("tensor_scalar_mul", stats.ap[:, 8:12], stats.ap[:, 8:12],
             1.0 / 1024.0, deps=stats.wdeps())
    stats.did_write(t)
    t = v.op("tensor_scalar_add", stats.ap[:, 8:12], stats.ap[:, 8:12],
             float(EPS5), deps=stats.wdeps())
    stats.did_write(t)
    t = s.op("activation", stats.ap[:, 12:16], stats.ap[:, 8:12], AF.Sqrt,
             deps=stats.wdeps())
    stats.did_write(t)
    t = v.op("reciprocal", stats.ap[:, 12:16], stats.ap[:, 12:16], deps=stats.wdeps())
    stats.did_write(t)
    t = v.op("tensor_tensor", stats.ap[:, 4:8], stats.ap[:, 12:16],
             stats.ap[:, 12:16], OP.mult, deps=stats.wdeps())
    stats.did_write(t)
    t = v.op("tensor_tensor", stats.ap[:, 4:8], stats.ap[:, 4:8],
             stats.ap[:, 8:12], OP.mult, deps=stats.wdeps())
    stats.did_write(t)
    t = v.op("tensor_scalar", stats.ap[:, 4:8], stats.ap[:, 4:8], -0.5, 1.5,
             OP.mult, OP.add, deps=stats.wdeps())
    stats.did_write(t)
    t = v.op("tensor_tensor", stats.ap[:, 12:16], stats.ap[:, 12:16],
             stats.ap[:, 4:8], OP.mult, deps=stats.wdeps())
    stats.did_write(t)
    for j in range(4):
        xs = xn_tm.ap[:, j * 1024:(j + 1) * 1024]
        t = v.op("tensor_scalar", xs, xs, stats.ap[:, 12 + j:13 + j], None,
                 OP.mult, deps=rd(stats) + xn_tm.wdeps())
        xn_tm.did_write(t); stats.did_read(t)
    return xn_tm


def emit_tm_to_fm(k, src_tm, dst_fm, pre, psA, psB, n_sub=4, width=1024,
                  src_bf16=False):
    te, s = k.tensor, k.scalar
    nchunk = width // 128
    ntok = n_sub * 128
    ps = [psA, psB]
    grp = 0
    pairs = [(c, j) for c in range(nchunk) for j in range(n_sub)]
    for g0 in range(0, len(pairs), 4):
        bank = ps[grp % 2]
        grp += 1
        gtoks = []
        wd = bank.wdeps() + rd(src_tm, pre.ident)
        ident_ap = pre.ident_bf.ap[:] if src_bf16 else pre.ident.ap[:]
        bank_view = bank.ap.bitcast(BF16) if src_bf16 else bank.ap
        for i, (c, j) in enumerate(pairs[g0:g0 + 4]):
            src = src_tm.ap[:, j * width + c * 128: j * width + (c + 1) * 128]
            t = k.tensor.op("transpose", bank_view[:, i * 128:(i + 1) * 128], src,
                            ident_ap,
                            deps=wd if i == 0 else (), inc=(i == len(pairs[g0:g0 + 4]) - 1))
        bank.did_write(t)
        src_tm.did_read(t)
        (pre.ident_bf if src_bf16 else pre.ident).did_read(t)
        for i, (c, j) in enumerate(pairs[g0:g0 + 4]):
            dst = dst_fm.ap[:, c * ntok + j * 128: c * ntok + (j + 1) * 128]
            tt = s.op("activation", dst, bank_view[:, i * 128:(i + 1) * 128], AF.Copy,
                      deps=rd(bank) + dst_fm.wdeps())
            dst_fm.did_write(tt)
            bank.did_read(tt)
    return dst_fm



def emit_actquant(k, xn_tm, n_sub, width, small, small_col, xq_f32, pre,
                  scr_a, scr_b, out_scale_col, sw_b, clip1=False):
    v = k.vector
    sc0 = small_col
    dq0 = small_col + n_sub
    src3 = xn_tm.ap[:, :n_sub * width].rearrange("p (a w) -> p a w", a=n_sub)
    t = v.op("reduce_max", small.ap[:, sc0:sc0 + n_sub], src3, AX.X,
             apply_absolute_value=True, deps=rd(xn_tm) + small.wdeps())
    small.did_write(t); xn_tm.did_read(t)
    t = v.op("tensor_scalar_max", small.ap[:, sc0:sc0 + n_sub],
             small.ap[:, sc0:sc0 + n_sub], float(EPS5), deps=small.wdeps())
    small.did_write(t)
    if sw_b is not None:
        t = v.op("scalar_tensor_tensor", small.ap[:, dq0:dq0 + n_sub],
                 small.ap[:, sc0:sc0 + n_sub], float(1.0 / 127.0),
                 sw_b.ap[:, 0:1].broadcast_to([128, n_sub]),
                 OP.mult, OP.mult, deps=small.wdeps() + rd(sw_b))
        small.did_write(t); sw_b.did_read(t)
    else:
        t = v.op("tensor_scalar_mul", small.ap[:, dq0:dq0 + n_sub],
                 small.ap[:, sc0:sc0 + n_sub], float(1.0 / 127.0),
                 deps=small.wdeps())
        small.did_write(t)
    t = v.op("reciprocal", small.ap[:, sc0:sc0 + n_sub],
             small.ap[:, sc0:sc0 + n_sub], deps=small.wdeps())
    small.did_write(t)
    t = v.op("tensor_scalar_mul", small.ap[:, sc0:sc0 + n_sub],
             small.ap[:, sc0:sc0 + n_sub], 127.0, deps=small.wdeps())
    small.did_write(t)
    for j in range(n_sub):
        for q0 in range(0, width, 1024):
            qw = min(1024, width - q0)
            off = j * width + q0
            rne_round(k, xq_f32, xq_f32.ap[:, off:off + qw],
                      xn_tm, xn_tm.ap[:, off:off + qw],
                      small.ap[:, sc0 + j:sc0 + j + 1], pre, scr_a, scr_b,
                      scale_buf=small)
    return dq0


def emit_wquant_bitlin(k, w_dram, n_in, n_out, wq_dst, wq_off, pre,
                       scr_w, scr_a, scr_b, small, sw_b, ones_row, ps):
    v, s, sy = k.vector, k.scalar, k.sync
    nch = n_in // 128
    acc = small.ap[:, 14:15]
    tmp = small.ap[:, 13:14]
    t = v.op("memset", acc, 0.0, deps=small.wdeps())
    small.did_write(t)
    for c in range(nch):
        t = sy.dma(scr_w.ap[:, :n_out], w_dram[c * 128:(c + 1) * 128, :],
                   deps=scr_w.wdeps())
        scr_w.did_write(t)
        t = v.op("reduce_sum", tmp, scr_w.ap[:, :n_out], AX.X,
                 apply_absolute_value=True, deps=rd(scr_w) + small.wdeps())
        small.did_write(t); scr_w.did_read(t)
        t = v.op("tensor_tensor", acc, acc, tmp, OP.add, deps=small.wdeps())
        small.did_write(t)
    t = k.gpsimd.op("tensor_reduce", small.ap[0:1, 12:13], acc, AX.C, OP.add,
                    deps=small.wdeps())
    small.did_write(t)
    t = v.op("tensor_scalar_mul", small.ap[0:1, 12:13], small.ap[0:1, 12:13],
             float(1.0 / (n_in * n_out)), deps=small.wdeps())
    small.did_write(t)
    t = v.op("tensor_scalar_max", small.ap[0:1, 12:13], small.ap[0:1, 12:13],
             float(EPS5), deps=small.wdeps())
    small.did_write(t)
    tk = k.tensor.op("matmul", ps.ap[:, 0:1], ones_row.ap[0:1, :],
                     small.ap[0:1, 12:13], start=True, stop=True,
                     deps=rd(ones_row, small) + ps.wdeps())
    ps.did_write(tk); ones_row.did_read(tk); small.did_read(tk)
    t = v.op("tensor_copy", sw_b.ap[:, 0:1], ps.ap[:, 0:1],
             deps=rd(ps) + sw_b.wdeps())
    sw_b.did_write(t); ps.did_read(t)
    t = v.op("reciprocal", sw_b.ap[:, 1:2], sw_b.ap[:, 0:1], deps=sw_b.wdeps())
    sw_b.did_write(t)
    for c in range(nch):
        t = sy.dma(scr_w.ap[:, :n_out], w_dram[c * 128:(c + 1) * 128, :],
                   deps=scr_w.wdeps())
        scr_w.did_write(t)
        for j in range(0, n_out, 1024):
            w = min(1024, n_out - j)
            rne_round(k, scr_w, scr_w.ap[:, j:j + w], scr_w,
                      scr_w.ap[:, j:j + w], sw_b.ap[:, 1:2], pre, scr_a, scr_b,
                      scale_buf=sw_b)
        dst = wq_dst.ap.bitcast(BF16)[:, wq_off + c * n_out: wq_off + (c + 1) * n_out]
        t = v.op("tensor_scalar", dst, scr_w.ap[:, :n_out], -1.0, 1.0,
                 OP.max, OP.min, deps=rd(scr_w) + wq_dst.wdeps())
        wq_dst.did_write(t); scr_w.did_read(t)
    return



def emit_bitlin_mm(k, xq_fm_ap, xq_fm_buf, wq_ap_fn, wq_buf, n_k, psum, j,
                   o, owidth):
    pairs = []
    for c in range(n_k):
        pairs.append((xq_fm_buf, xq_fm_ap(c, j), wq_buf, wq_ap_fn(c, o)))
    return mm_group(k, psum, psum.ap[:, :owidth], pairs)


def emit_ffn_bitlin(k, B, pre, x_src_ap, x_src_buf, g_dram, g_buf,
                    w_drams, variant, hq_dram, hq_buf, dqh_dram, dqh_buf):
    v, s, sy = k.vector, k.scalar, k.sync
    WRES, G0, G1 = B["WRES"], B["G0"], B["G1"]
    B["SCRW"].wtoks = list(G1.wtoks)
    B["SCRW"].rtoks = list(G1.rtoks)
    H0, SMALL, SM2 = B["H0"], B["SMALL"], B["SMALL2"]
    SWA, SWB, SWC = B["SWA"], B["SWB"], B["SWC"]

    if variant == "gelu":
        wlist = [("w1", SWA, 0)]
    else:
        wlist = [("wg", SWA, 0), ("wu", SWB, 32768)]
    for nm, swb, off in wlist:
        emit_wquant_bitlin(k, w_drams[nm], 1024, 4096, WRES, off, pre,
                           B["SCRW"], B["SA"], B["SB"], SMALL, swb,
                           B["ONESROW"], B["PS4"])
    alias_sync(G1, B["SCRW"])

    def xq_ap(c, j):
        return H0.ap[:, c * 512 + j * 128: c * 512 + (j + 1) * 128]

    def wq_ap(off):
        def f(c, o):
            return WRES.ap[:, off + c * 4096 + o * 512: off + c * 4096 + (o + 1) * 512]
        return f

    for t0 in range(0, T, 512):
        emit_ln(k, x_src_ap, x_src_buf, G0, B["STATS"], G1, t0, pre)
        emit_actquant(k, G0, 4, 1024, SMALL, 0, G1, pre, B["SA"], B["SB"],
                      4, None)
        tok = v.op("scalar_tensor_tensor", SM2.ap[:, 0:4], SMALL.ap[:, 4:8],
                   1.0, SWA.ap[:, 0:1].broadcast_to([128, 4]), OP.mult,
                   OP.mult, deps=rd(SMALL, SWA) + SM2.wdeps())
        SM2.did_write(tok); SMALL.did_read(tok); SWA.did_read(tok)
        if variant == "swiglu":
            tok = v.op("scalar_tensor_tensor", SM2.ap[:, 4:8],
                       SMALL.ap[:, 4:8], 1.0,
                       SWB.ap[:, 0:1].broadcast_to([128, 4]), OP.mult,
                       OP.mult, deps=rd(SMALL, SWB) + SM2.wdeps())
            SM2.did_write(tok); SMALL.did_read(tok); SWB.did_read(tok)
        xq_fm = sub_buf(H0, 0, 4096)
        emit_tm_to_fm(k, G1, xq_fm, pre, B["PS2"], B["PS3"])
        merge_sub(H0, xq_fm)
        for j in range(4):
            for o in range(8):
                ps = B["PS0"] if o % 2 == 0 else B["PS1"]
                emit_bitlin_mm(k, xq_ap, H0, wq_ap(0), WRES, 8, ps, j, o, 512)
                fn = AF.Gelu if variant == "gelu" else AF.Silu
                tok = s.op("activation", G1.ap[:, o * 512:(o + 1) * 512],
                           ps.ap[:, 0:512], fn, scale=SM2.ap[:, j:j + 1],
                           deps=rd(ps, SM2) + G1.wdeps())
                G1.did_write(tok); ps.did_read(tok); SM2.did_read(tok)
            if variant == "swiglu":
                for o in range(8):
                    ps = B["PS0"] if o % 2 == 0 else B["PS1"]
                    emit_bitlin_mm(k, xq_ap, H0, wq_ap(32768), WRES, 8, ps,
                                   j, o, 512)
                    tok = s.op("activation", G0.ap[:, o * 512:(o + 1) * 512],
                               ps.ap[:, 0:512], AF.Copy,
                               scale=SM2.ap[:, 4 + j:5 + j],
                               deps=rd(ps, SM2) + G0.wdeps())
                    G0.did_write(tok); ps.did_read(tok); SM2.did_read(tok)
                tok = v.op("tensor_tensor", G1.ap[:], G1.ap[:], G0.ap[:],
                           OP.mult, deps=rd(G0) + G1.wdeps())
                G1.did_write(tok); G0.did_read(tok)
            emit_actquant(k, G1, 1, 4096, SMALL, 8, G0, pre, B["SA"],
                          B["SB"], 1, None)
            tok = v.op("tensor_copy", B["HQ16"].ap[:, 0:4096], G0.ap[:],
                       deps=rd(G0) + B["HQ16"].wdeps())
            B["HQ16"].did_write(tok); G0.did_read(tok)
            r0 = t0 + j * 128
            tok = sy.dma(hq_dram[r0:r0 + 128, :], B["HQ16"].ap[:, 0:4096],
                         deps=rd(B["HQ16"]) + hq_buf.wdeps())
            hq_buf.did_write(tok); B["HQ16"].did_read(tok)
            col = t0 // 128 + j
            tok = v.op("tensor_copy", B["DQH"].ap[:, col:col + 1],
                       SMALL.ap[:, 9:10], deps=rd(SMALL) + B["DQH"].wdeps())
            B["DQH"].did_write(tok); SMALL.did_read(tok)

    w2 = "w2" if variant == "gelu" else "wd"
    B["SCRW"].wtoks = list(G1.wtoks)
    B["SCRW"].rtoks = list(G1.rtoks)
    emit_wquant_bitlin(k, w_drams[w2], 4096, 1024, WRES, 0, pre,
                       B["SCRW"], B["SA"], B["SB"], SMALL, SWC,
                       B["ONESROW"], B["PS4"])
    alias_sync(G1, B["SCRW"])
    for t0 in range(0, T, 512):
        c0 = t0 // 128
        tok = v.op("scalar_tensor_tensor", SM2.ap[:, 8:12],
                   B["DQH"].ap[:, c0:c0 + 4], 1.0,
                   SWC.ap[:, 0:1].broadcast_to([128, 4]), OP.mult,
                   OP.mult, deps=rd(SWC, B["DQH"]) + SM2.wdeps())
        SM2.did_write(tok); SWC.did_read(tok); B["DQH"].did_read(tok)
        for j in range(4):
            r0 = t0 + j * 128
            tok = sy.dma(B["HQ16"].ap[:, 0:4096], hq_dram[r0:r0 + 128, :],
                         deps=B["HQ16"].wdeps() + rd(hq_buf))
            B["HQ16"].did_write(tok); hq_buf.did_read(tok)
            hq_fm = sub_buf(H0, 4096, 4096)
            emit_tm_to_fm(k, B["HQ16"], hq_fm, pre, B["PS2"], B["PS3"],
                          n_sub=1, width=4096, src_bf16=True)
            merge_sub(H0, hq_fm)
            for o2 in range(2):
                ps = B["PS0"] if o2 == 0 else B["PS1"]
                pairs = []
                for c in range(32):
                    pairs.append((H0,
                                  H0.ap[:, 4096 + c * 128: 4096 + (c + 1) * 128],
                                  WRES,
                                  WRES.ap[:, c * 1024 + o2 * 512:
                                          c * 1024 + (o2 + 1) * 512]))
                mm_group(k, ps, ps.ap[:, 0:512], pairs)
                tok = s.op("activation", G1.ap[:, o2 * 512:(o2 + 1) * 512],
                           ps.ap[:, 0:512], AF.Silu,
                           scale=SM2.ap[:, 8 + j:9 + j],
                           deps=rd(ps, SM2) + G1.wdeps())
                G1.did_write(tok); ps.did_read(tok); SM2.did_read(tok)
            tok = sy.dma(g_dram[r0:r0 + 128, :], G1.ap[:, 0:1024],
                         deps=rd(G1) + g_buf.wdeps())
            g_buf.did_write(tok); G1.did_read(tok)


def alias_sync(dst, src_alias):
    dst.wtoks = list(src_alias.wtoks)
    dst.rtoks = list(src_alias.rtoks)


def sub_buf(parent, off, n):
    b = Buf(parent.ap[:, off:off + n])
    b.wtoks = list(parent.wtoks)
    b.rtoks = list(parent.rtoks)
    return b


def merge_sub(parent, child):
    parent.wtoks = list(set(parent.wtoks + child.wtoks))
    parent.rtoks = list(set(parent.rtoks + child.rtoks))


def make_buffers(k):
    B = {}
    B["WRES"] = k.sbuf("WRES", [128, 65536], BF16)
    B["G0"] = k.sbuf("G0", [128, 4096], F32)
    B["G1"] = k.sbuf("G1", [128, 4096], F32)
    B["H0"] = k.sbuf("H0", [128, 8192], BF16)
    B["HQ16"] = k.sbuf("HQ16", [128, 4096], BF16)
    B["SCRW"] = Buf(B["G1"].ap[:])
    B["SA"] = k.sbuf("SA", [128, 1024], F32)
    B["SB"] = k.sbuf("SB", [128, 1024], F32)
    B["SC"] = k.sbuf("SC", [128, 1024], F32)
    B["BLK"] = k.sbuf("BLK", [128, 1024], F32)
    B["SMALL"] = k.sbuf("SMALL", [128, 64], F32)
    B["SMALL2"] = k.sbuf("SMALL2", [128, 64], F32)
    B["STATS"] = k.sbuf("STATS", [128, 16], F32)
    B["SWA"] = k.sbuf("SWA", [128, 2], F32)
    B["SWB"] = k.sbuf("SWB", [128, 2], F32)
    B["SWC"] = k.sbuf("SWC", [128, 2], F32)
    B["ONESROW"] = k.sbuf("ONESROW", [1, 128], F32)
    B["DQH"] = k.sbuf("DQH", [128, 32], F32)
    B["XT"] = Buf(B["H0"].ap.bitcast(F32))
    B["BIAS1"] = Buf(B["HQ16"].ap.bitcast(F32)[:, 0:1024])
    B["BIAS2"] = Buf(B["HQ16"].ap.bitcast(F32)[:, 1024:2048])
    for i in range(8):
        B[f"PS{i}"] = k.psum(f"PS{i}", [128, 512], F32)
    return B


def emit_buffer_init(k, B):
    tok = k.vector.op("memset", B["ONESROW"].ap[:], 1.0,
                      deps=B["ONESROW"].wdeps())
    B["ONESROW"].did_write(tok)



def emit_moe(k, B, pre, x_src_ap, x_src_buf, g_dram, g_buf, w):
    v, s, sy = k.vector, k.scalar, k.sync
    W = Buf(B["WRES"].ap.bitcast(F32))
    G0, G1 = B["G0"], B["G1"]
    SMALL, SM2 = B["SMALL"], B["SMALL2"]
    HFM = Buf(B["H0"].ap.bitcast(F32))

    def wload(dst_off, src_ap, nch, ncol):
        wdeps = W.wdeps()
        first = True
        for c in range(nch):
            t = sy.dma(W.ap[:, dst_off + c * ncol: dst_off + (c + 1) * ncol],
                       src_ap[c * 128:(c + 1) * 128, :], deps=wdeps)
            if first:
                W.did_write(t)
                first = False
            else:
                W.did_write_acc(t)
        return

    wload(0, w["Ws1"], 8, 512)
    wload(4096, w["Ws2"], 4, 1024)
    wload(8192, w["Wr"], 8, 8)
    EX0 = 8448

    def exp_off(e):
        return EX0 + (e % 2) * 8192

    wload(exp_off(0), w["W1"][0], 8, 512)
    wload(exp_off(0) + 4096, w["W2"][0], 4, 1024)

    for t0 in range(0, T, 512):
        emit_ln(k, x_src_ap, x_src_buf, G0, B["STATS"], G1, t0, pre)
        xn_fm = sub_buf(G1, 0, 4096)
        emit_tm_to_fm(k, G0, xn_fm, pre, B["PS2"], B["PS3"])
        alias_sync(G1, xn_fm)

        def xn_ap(c, j):
            return G1.ap[:, c * 512 + j * 128: c * 512 + (j + 1) * 128]

        def xn_mv(c):
            return G1.ap[:, c * 512:(c + 1) * 512]

        for j in range(4):
            ps = B["PS4"]
            pairs = [(G1, xn_ap(c, j), W, W.ap[:, 8192 + c * 8: 8192 + (c + 1) * 8])
                     for c in range(8)]
            mm_group(k, ps, ps.ap[:, 0:8], pairs)
            t = v.op("max", SMALL.ap[:, 0:8], ps.ap[:, 0:8],
                     deps=rd(ps) + SMALL.wdeps())
            SMALL.did_write(t); ps.did_read(t)
            t = v.op("tensor_tensor", SMALL.ap[:, 8:9], SMALL.ap[:, 1:2],
                     SMALL.ap[:, 0:1], OP.subtract, deps=SMALL.wdeps())
            SMALL.did_write(t)
            t = s.op("activation", SMALL.ap[:, 9:10], SMALL.ap[:, 8:9], AF.Exp,
                     deps=SMALL.wdeps())
            SMALL.did_write(t)
            t = v.op("tensor_scalar_add", SMALL.ap[:, 9:10], SMALL.ap[:, 9:10],
                     1.0, deps=SMALL.wdeps())
            SMALL.did_write(t)
            t = v.op("reciprocal", SMALL.ap[:, 9:10], SMALL.ap[:, 9:10],
                     deps=SMALL.wdeps())
            SMALL.did_write(t)
            t = v.op("tensor_single_scalar", SMALL.ap[:, 9:10],
                     SMALL.ap[:, 9:10], 1.0, OP.is_equal, deps=SMALL.wdeps())
            SMALL.did_write(t)
            t = v.op("tensor_scalar", SM2.ap[:, 8 * j:8 * j + 8], ps.ap[:, 0:8],
                     SMALL.ap[:, 0:1], SMALL.ap[:, 9:10], OP.is_equal, OP.mult,
                     deps=rd(ps, SMALL) + SM2.wdeps())
            SM2.did_write(t); ps.did_read(t); SMALL.did_read(t)

        for e in range(-1, 8):
            if 0 <= e < 7:
                wload(exp_off(e + 1), w["W1"][e + 1], 8, 512)
                wload(exp_off(e + 1) + 4096, w["W2"][e + 1], 4, 1024)
            w1off = 0 if e < 0 else exp_off(e)
            w2off = 4096 if e < 0 else exp_off(e) + 4096
            for f in range(4):
                ps = B["PS0"] if f % 2 == 0 else B["PS1"]
                pairs = [(W, W.ap[:, w1off + c * 512 + f * 128:
                               w1off + c * 512 + (f + 1) * 128],
                          G1, xn_mv(c)) for c in range(8)]
                mm_group(k, ps, ps.ap[:, 0:512], pairs)
                hslice = HFM.ap[:, f * 512:(f + 1) * 512]
                t = s.op("activation", hslice, ps.ap[:, 0:512], AF.Relu,
                         deps=rd(ps) + HFM.wdeps())
                HFM.did_write(t); ps.did_read(t)
                t = v.op("tensor_tensor", hslice, hslice, ps.ap[:, 0:512],
                         OP.mult, deps=rd(ps) + HFM.wdeps())
                HFM.did_write(t); ps.did_read(t)
            for j in range(4):
                for o2 in range(2):
                    ps = B["PS0"] if o2 == 0 else B["PS1"]
                    pairs = [(HFM, HFM.ap[:, f * 512 + j * 128:
                                  f * 512 + (j + 1) * 128],
                              W, W.ap[:, w2off + f * 1024 + o2 * 512:
                                      w2off + f * 1024 + (o2 + 1) * 512])
                             for f in range(4)]
                    mm_group(k, ps, ps.ap[:, 0:512], pairs)
                    dst = G0.ap[:, j * 1024 + o2 * 512: j * 1024 + (o2 + 1) * 512]
                    if e < 0:
                        t = s.op("activation", dst, ps.ap[:, 0:512], AF.Copy,
                                 deps=rd(ps) + G0.wdeps())
                        G0.did_write(t); ps.did_read(t)
                    else:
                        t = v.op("scalar_tensor_tensor", dst, ps.ap[:, 0:512],
                                 SM2.ap[:, 8 * j + e:8 * j + e + 1], dst,
                                 OP.mult, OP.add, deps=rd(ps, SM2) + G0.wdeps())
                        G0.did_write(t); ps.did_read(t); SM2.did_read(t)
        for j in range(4):
            t = s.op("activation", G0.ap[:, j * 1024:(j + 1) * 1024],
                     G0.ap[:, j * 1024:(j + 1) * 1024], AF.Silu,
                     deps=G0.wdeps())
            G0.did_write(t)
            r0 = t0 + j * 128
            t = sy.dma(g_dram[r0:r0 + 128, :],
                       G0.ap[:, j * 1024:(j + 1) * 1024],
                       deps=rd(G0) + g_buf.wdeps())
            g_buf.did_write(t); G0.did_read(t)



def emit_l1_attn(k, B, pre, x_in_ap, x_in_buf, x_out_ap, x_out_buf,
                 g_dram, g_buf, w):
    v, s, sy = k.vector, k.scalar, k.sync
    W = Buf(B["WRES"].ap.bitcast(F32))
    G0, G1 = B["G0"], B["G1"]
    QO, KO, VO, PO, OO = 0, 8192, 16384, 24704, 26752

    BQ = B["SMALL2"]
    tok = sy.dma(BQ.ap[:, 0:24], w["b_qkv"].rearrange("(c p) -> p c", p=128),
                 deps=BQ.wdeps(), allow_slow_non_contiguous=True)
    BQ.did_write(tok)
    tok = sy.dma(B["BIAS1"].ap[:, 0:1024],
                 w["b_out"][None, :].partition_broadcast(128),
                 deps=B["BIAS1"].wdeps())
    B["BIAS1"].did_write(tok)
    tok = sy.dma(B["BIAS2"].ap[:, 0:1024],
                 w["b_qkv"][None, 2048:3072].partition_broadcast(128),
                 deps=B["BIAS2"].wdeps())
    B["BIAS2"].did_write(tok)
    qb = [sub_buf(W, QO, 4096), sub_buf(W, QO + 4096, 4096)]
    kb = [sub_buf(W, KO, 4096), sub_buf(W, KO + 4096, 4096)]
    vb = [sub_buf(W, VO, 4160), sub_buf(W, VO + 4160, 4160)]
    pb = [sub_buf(W, PO + tj * 512, 512) for tj in range(4)]
    ob = sub_buf(W, OO, 4096)
    for sl in range(2):
        dst = vb[sl].ap[:].rearrange(
            "p (a h o) -> p a h o", a=4, h=16)[:, :, :, 64:65]
        tok = v.op("memset", dst, 1.0, deps=vb[sl].wdeps())
        vb[sl].did_write(tok)

    def stage_w(dram_ap, col0, ncol, dstb):
        wdeps = dstb.wdeps()
        first = True
        for c in range(8):
            t = sy.dma(dstb.ap[:, c * ncol:(c + 1) * ncol],
                       dram_ap[c * 128:(c + 1) * 128, col0:col0 + ncol],
                       deps=wdeps)
            if first:
                dstb.did_write(t)
                first = False
            else:
                dstb.did_write_acc(t)

    def compute_kv(i_blk, slot, with_q):
        t0 = i_blk * 512
        emit_ln(k, x_in_ap, x_in_buf, G0, B["STATS"], G1, t0, pre)
        xn_fm = sub_buf(G1, 0, 4096)
        emit_tm_to_fm(k, G0, xn_fm, pre, B["PS2"], B["PS3"])
        alias_sync(G1, xn_fm)
        proj_list = [("k", 1024, kb[slot])] + \
            ([("q", 0, qb[slot])] if with_q else [])
        sgi = [0]
        for nm, wcol0, dstb in proj_list:
            for half in range(2):
                sgb = G0 if sgi[0] % 2 == 0 else B["XT"]
                sgi[0] += 1
                stage_w(w["W_qkv"], wcol0 + half * 512, 512, sgb)
                for hp in range(4):
                    ps = B["PS0"] if hp % 2 == 0 else B["PS1"]
                    pairs = [(sgb, sgb.ap[:, c * 512 + hp * 128:
                                  c * 512 + hp * 128 + 128],
                              G1, G1.ap[:, c * 512:(c + 1) * 512])
                             for c in range(8)]
                    mm_group(k, ps, ps.ap[:, 0:512], pairs)
                    bias_col = (wcol0 + half * 512 + hp * 128) // 128
                    bstart = dstb.ap.offset * 2
                    WB16 = B["WRES"].ap
                    hi = WB16[:, bstart + (half * 4 + hp) * 512:
                              bstart + (half * 4 + hp + 1) * 512]
                    lo = WB16[:, bstart + 4096 + (half * 4 + hp) * 512:
                              bstart + 4096 + (half * 4 + hp + 1) * 512]
                    t = s.op("activation", hi, ps.ap[:, 0:512], AF.Identity,
                             bias=BQ.ap[:, bias_col:bias_col + 1],
                             deps=rd(ps, BQ) + dstb.wdeps())
                    if half == 0 and hp == 0:
                        dstb.did_write(t)
                    else:
                        dstb.did_write_acc(t)
                    ps.did_read(t); BQ.did_read(t)
                    t = v.op("scalar_tensor_tensor", lo, ps.ap[:, 0:512],
                             BQ.ap[:, bias_col:bias_col + 1], hi,
                             OP.add, OP.subtract,
                             deps=rd(ps, BQ) + dstb.wdeps())
                    dstb.did_write_acc(t); ps.did_read(t); BQ.did_read(t)
        for half in range(2):
            sgb = G0 if sgi[0] % 2 == 0 else B["XT"]
            sgi[0] += 1
            stage_w(w["W_qkv"], 2048 + half * 512, 512, sgb)
            for j in range(4):
                ps = B["PS0"] if j % 2 == 0 else B["PS1"]
                pairs = [(G1, G1.ap[:, c * 512 + j * 128:
                              c * 512 + j * 128 + 128],
                          sgb, sgb.ap[:, c * 512:(c + 1) * 512])
                         for c in range(8)]
                mm_group(k, ps, ps.ap[:, 0:512], pairs)
                base = j * 1040 + half * 520
                dst = vb[slot].ap[:, base:base + 520].rearrange(
                    "p (h o) -> p h o", h=8)[:, :, 0:64]
                t = v.op("scalar_tensor_tensor", dst, ps.ap[:, 0:512], 1.0,
                         B["BIAS2"].ap[:, half * 512:(half + 1) * 512]
                         .rearrange("p (h o) -> p h o", h=8),
                         OP.mult, OP.add, deps=rd(ps, B["BIAS2"]) + vb[slot].wdeps())
                vb[slot].did_write_acc(t)
                ps.did_read(t); B["BIAS2"].did_read(t)

    def attend(j):
        qslot = j % 2
        for h in range(16):
            hp, hh = h // 2, h % 2
            q_ap = qb[qslot].ap[64 * hh:64 * hh + 64,
                                hp * 512:(hp + 1) * 512]
            for tgt in range(2):
                kslot = (j + tgt) % 2
                psO = B["PS2"] if (2 * h + tgt) % 2 == 0 else B["PS3"]
                for tj in range(4):
                    ps_s = B["PS0"] if tj % 2 == 0 else B["PS1"]
                    WB16 = B["WRES"].ap
                    kbst = kb[kslot].ap.offset * 2
                    qbst = qb[qslot].ap.offset * 2
                    ksl = slice(kbst + hp * 512 + tj * 128,
                                kbst + hp * 512 + tj * 128 + 128)
                    qsl = slice(qbst + hp * 512, qbst + (hp + 1) * 512)
                    khi = WB16[64 * hh:64 * hh + 64, ksl]
                    klo = WB16[64 * hh:64 * hh + 64,
                               ksl.start + 4096:ksl.stop + 4096]
                    qhi = WB16[64 * hh:64 * hh + 64, qsl]
                    qlo = WB16[64 * hh:64 * hh + 64,
                               qsl.start + 4096:qsl.stop + 4096]
                    tok = k.tensor.op("matmul", ps_s.ap[:, 0:512], khi, qhi,
                                      start=True, stop=False,
                                      deps=rd(kb[kslot], qb[qslot]) + ps_s.wdeps())
                    tok = k.tensor.op("matmul", ps_s.ap[:, 0:512], khi, qlo,
                                      start=False, stop=False)
                    tok = k.tensor.op("matmul", ps_s.ap[:, 0:512], klo, qhi,
                                      start=False, stop=True)
                    ps_s.did_write(tok)
                    kb[kslot].did_read(tok); qb[qslot].did_read(tok)
                    t = s.op("activation", pb[tj].ap[:], ps_s.ap[:, 0:512],
                             AF.Exp, scale=pre.c_eighth, bias=pre.c_nshift,
                             deps=rd(ps_s, pre.consts) + pb[tj].wdeps())
                    pb[tj].did_write(t); ps_s.did_read(t)
                    pre.consts.did_read(t)
                    vap = vb[kslot].ap[:, tj * 1040 + h * 65:
                                       tj * 1040 + (h + 1) * 65]
                    tok = k.tensor.op("matmul", psO.ap[0:65, 0:512], vap,
                                      pb[tj].ap[:],
                                      start=(tj == 0), stop=(tj == 3),
                                      deps=rd(vb[kslot], pb[tj]) +
                                      (psO.wdeps() if tj == 0 else []),
                                      inc=True)
                    vb[kslot].did_read(tok); pb[tj].did_read(tok)
                    if tj == 3:
                        psO.did_write(tok)
                t = v.op("reciprocal", B["SA"].ap[0:1, 0:512],
                         psO.ap[64:65, 0:512], deps=rd(psO) + B["SA"].wdeps())
                B["SA"].did_write(t); psO.did_read(t)
                tok = k.tensor.op("matmul", B["PS4"].ap[0:64, 0:512],
                                  B["ONESROW"].ap[0:1, 0:64],
                                  B["SA"].ap[0:1, 0:512], start=True, stop=True,
                                  deps=rd(B["ONESROW"], B["SA"]) + B["PS4"].wdeps())
                B["PS4"].did_write(tok)
                B["ONESROW"].did_read(tok); B["SA"].did_read(tok)
                t = s.op("activation", B["SB"].ap[0:64, 0:512],
                         B["PS4"].ap[0:64, 0:512], AF.Copy,
                         deps=rd(B["PS4"]) + B["SB"].wdeps())
                B["SB"].did_write(t); B["PS4"].did_read(t)
                if tgt == 0:
                    t = v.op("tensor_tensor", B["SB"].ap[0:64, 512:1024],
                             psO.ap[0:64, 0:512], B["SB"].ap[0:64, 0:512],
                             OP.mult, deps=rd(psO) + B["SB"].wdeps())
                    B["SB"].did_write(t); psO.did_read(t)
                else:
                    t = v.op("tensor_tensor", B["SA"].ap[0:64, 0:512],
                             psO.ap[0:64, 0:512], B["SB"].ap[0:64, 0:512],
                             OP.mult, deps=rd(psO) + B["SA"].wdeps() + rd(B["SB"]))
                    B["SA"].did_write(t); psO.did_read(t); B["SB"].did_read(t)
                    t = v.op("tensor_tensor", B["SB"].ap[0:64, 512:1024],
                             B["SB"].ap[0:64, 512:1024],
                             B["SA"].ap[0:64, 0:512], OP.add,
                             deps=rd(B["SA"]) + B["SB"].wdeps())
                    B["SB"].did_write(t); B["SA"].did_read(t)
                    odst = ob.ap[64 * hh:64 * hh + 64,
                                 hp * 512:(hp + 1) * 512]
                    t = s.op("activation", odst, B["SB"].ap[0:64, 512:1024],
                             AF.Copy, deps=rd(B["SB"]) + ob.wdeps())
                    if h == 0:
                        ob.did_write(t)
                    else:
                        ob.did_write_acc(t)
                    B["SB"].did_read(t)
        for half in range(2):
            sgb = G0 if half == 0 else B["XT"]
            stage_w(w["W_out"], half * 512, 512, sgb)
            for sub in range(4):
                ps = B["PS5"] if sub % 2 == 0 else B["PS6"]
                pairs = [(ob, ob.ap[:, c * 512 + sub * 128:
                              c * 512 + sub * 128 + 128],
                          sgb, sgb.ap[:, c * 512:(c + 1) * 512])
                         for c in range(8)]
                mm_group(k, ps, ps.ap[:, 0:512], pairs)
                dst = G1.ap[:, sub * 1024 + half * 512:
                            sub * 1024 + (half + 1) * 512]
                t = v.op("scalar_tensor_tensor", dst, ps.ap[:, 0:512], 1.0,
                         B["BIAS1"].ap[:, half * 512:(half + 1) * 512],
                         OP.mult, OP.add, deps=rd(ps, B["BIAS1"]) + G1.wdeps())
                G1.did_write(t); ps.did_read(t); B["BIAS1"].did_read(t)
        t0 = j * 512
        tok = sy.dma(B["SC"].ap[:, 0:512].rearrange("p (a d) -> p a d", a=1),
                     x_in_ap[t0:t0 + 128, 0:512].rearrange("(a p) d -> p a d", p=128),
                     deps=B["SC"].wdeps() + rd(x_in_buf)) if False else None
        tok = sy.dma(B["XT"].ap[:].rearrange("p (a d) -> p a d", a=4),
                     x_in_ap[t0:t0 + 512, :].rearrange("(a p) d -> p a d", p=128),
                     deps=B["XT"].wdeps() + rd(x_in_buf))
        B["XT"].did_write(tok); x_in_buf.did_read(tok)
        tok = sy.dma(G0.ap[:].rearrange("p (a d) -> p a d", a=4),
                     g_dram[t0:t0 + 512, :].rearrange("(a p) d -> p a d", p=128),
                     deps=G0.wdeps() + rd(g_buf))
        G0.did_write(tok); g_buf.did_read(tok)
        t = v.op("tensor_tensor", G1.ap[:], G1.ap[:], G0.ap[:], OP.mult,
                 deps=rd(G0) + G1.wdeps())
        G1.did_write(t); G0.did_read(t)
        t = v.op("tensor_tensor", G1.ap[:], G1.ap[:], B["XT"].ap[:], OP.add,
                 deps=rd(B["XT"]) + G1.wdeps())
        G1.did_write(t); B["XT"].did_read(t)
        tok = sy.dma(x_out_ap[t0:t0 + 512, :].rearrange("(a p) d -> p a d", p=128),
                     G1.ap[:].rearrange("p (a d) -> p a d", a=4),
                     deps=rd(G1) + x_out_buf.wdeps())
        x_out_buf.did_write(tok); G1.did_read(tok)

    for i in range(9):
        blk = i if i < 8 else 0
        compute_kv(blk, i % 2, with_q=(i < 8))
        if i >= 1:
            attend(i - 1)


def barrier(k):
    engs = [k.tensor, k.vector, k.scalar, k.gpsimd, k.sync]
    toks = [(e.sem, e.sem.cnt) for e in engs]
    for e in engs:
        if getattr(e, "dma_queue", False):
            toks += [(s, s.cnt) for s in e.pool]
    for e in engs:
        e.wait(toks)



PI = float(np.pi)
TWO_PI_HI = float(np.float32(6.28125))
TWO_PI_LO = float(np.float32(2 * np.pi) - np.float32(6.28125))
INV_2PI = float(np.float32(1.0 / (2 * np.pi)))


def emit_rope_tables(k, B, pre, rope_buf):
    v, s, g = k.vector, k.scalar, k.gpsimd
    SA, SB, SM = B["SA"], B["SB"], B["SMALL"]

    t = g.op("iota", SA.ap[0:1, 0:32].bitcast(I32), pattern=[[1, 32]],
             channel_multiplier=0, deps=SA.wdeps())
    SA.did_write(t)
    t = v.op("tensor_copy", SA.ap[0:1, 32:64], SA.ap[0:1, 0:32].bitcast(I32),
             deps=SA.wdeps())
    SA.did_write(t)
    t = s.op("activation", SA.ap[0:1, 0:32], SA.ap[0:1, 32:64], AF.Exp,
             scale=float(-np.log(10000.0) / 32.0), deps=SA.wdeps())
    SA.did_write(t)
    ps = B["PS4"]
    tok = k.tensor.op("matmul", ps.ap[:, 0:32], B["ONESROW"].ap[0:1, :],
                      SA.ap[0:1, 0:32], start=True, stop=True,
                      deps=rd(B["ONESROW"], SA) + ps.wdeps())
    ps.did_write(tok); B["ONESROW"].did_read(tok); SA.did_read(tok)
    th = Buf(SM.ap[:, 32:64])
    t = v.op("tensor_copy", th.ap, ps.ap[:, 0:32], deps=rd(ps) + SM.wdeps())
    th.did_write(t); SM.did_write(t); ps.did_read(t)
    t = g.op("iota", SM.ap[:, 0:1].bitcast(I32), pattern=[[0, 1]],
             channel_multiplier=1, deps=SM.wdeps())
    SM.did_write(t)
    t = v.op("tensor_copy", SM.ap[:, 1:2], SM.ap[:, 0:1].bitcast(I32),
             deps=SM.wdeps())
    SM.did_write(t)

    def sincos(dst_ap, base_is_cos, sub):
        t = v.op("tensor_scalar_add", SM.ap[:, 2:3], SM.ap[:, 1:2],
                 float(128 * sub), deps=SM.wdeps())
        SM.did_write(t)
        t = v.op("tensor_scalar", SA.ap[:, 0:32], th.ap, SM.ap[:, 2:3],
                 (PI / 2.0) if base_is_cos else 0.0, OP.mult, OP.add,
                 deps=rd(th) + SA.wdeps() + SM.rdeps())
        SA.did_write(t); th.did_read(t)
        t = v.op("tensor_scalar", SB.ap[:, 32:64], SA.ap[:, 0:32],
                 float(INV_2PI), -0.5, OP.mult, OP.add, deps=rd(SA) + SB.wdeps())
        SB.did_write(t)
        t = v.op("tensor_scalar", SA.ap[:, 32:64], SB.ap[:, 32:64], float(C1),
                 float(C1), OP.add, OP.subtract, deps=rd(SB) + SA.wdeps())
        SA.did_write(t)
        t = v.op("tensor_scalar", SA.ap[:, 32:64], SA.ap[:, 32:64],
                 pre.j.ap[:], None, OP.mult, deps=SA.wdeps() + rd(pre.j))
        SA.did_write(t); pre.j.did_read(t)
        t = v.op("tensor_scalar", SB.ap[:, 0:32], SB.ap[:, 32:64], float(C2),
                 float(C2), OP.add, OP.subtract, deps=SB.wdeps())
        SB.did_write(t)
        t = v.op("tensor_tensor", SB.ap[:, 0:32], SB.ap[:, 0:32],
                 SA.ap[:, 32:64], OP.add, deps=SB.wdeps() + rd(SA))
        SB.did_write(t)
        t = v.op("scalar_tensor_tensor", SA.ap[:, 0:32], SB.ap[:, 0:32],
                 float(-TWO_PI_HI), SA.ap[:, 0:32], OP.mult, OP.add,
                 deps=rd(SB) + SA.wdeps())
        SA.did_write(t); SB.did_read(t)
        t = v.op("scalar_tensor_tensor", SA.ap[:, 0:32], SB.ap[:, 0:32],
                 float(-TWO_PI_LO), SA.ap[:, 0:32], OP.mult, OP.add,
                 deps=rd(SB) + SA.wdeps())
        SA.did_write(t); SB.did_read(t)
        t = v.op("tensor_scalar", SA.ap[:, 0:32], SA.ap[:, 0:32], float(-PI),
                 None, OP.add, deps=SA.wdeps())
        SA.did_write(t)
        t = v.op("tensor_scalar", SA.ap[:, 0:32], SA.ap[:, 0:32], float(-PI),
                 float(PI), OP.max, OP.min, deps=SA.wdeps())
        SA.did_write(t)
        t = s.op("activation", SA.ap[:, 32:64], SA.ap[:, 0:32], AF.Sin,
                 deps=SA.wdeps())
        SA.did_write(t)
        t = v.op("tensor_scalar_mul", dst_ap, SA.ap[:, 32:64], -1.0,
                 deps=rd(SA) + rope_buf.wdeps())
        rope_buf.did_write(t); SA.did_read(t)

    for sub in range(32):
        sincos(rope_buf.ap[:, sub * 64:sub * 64 + 32], True, sub)
        sincos(rope_buf.ap[:, sub * 64 + 32:sub * 64 + 64], False, sub)


def emit_rope_apply(k, B, pre, q_tm, rope_buf, sub, nheads=16):
    v = k.vector
    SA, SB = B["SA"], B["SB"]
    n = nheads * 32
    q3 = q_tm.ap[:, 0:nheads * 64].rearrange("p (h i two) -> p h i two", h=nheads, two=2)
    qe, qo = q3[:, :, :, 0], q3[:, :, :, 1]
    cosb = rope_buf.ap[:, sub * 64:sub * 64 + 32].unsqueeze(1).broadcast_to([128, nheads, 32])
    sinb = rope_buf.ap[:, sub * 64 + 32:sub * 64 + 64].unsqueeze(1).broadcast_to([128, nheads, 32])
    sa3 = SA.ap[:, 0:n].rearrange("p (h i) -> p h i", h=nheads)
    sa3b = SA.ap[:, n:2 * n].rearrange("p (h i) -> p h i", h=nheads)
    sb3 = SB.ap[:, 0:n].rearrange("p (h i) -> p h i", h=nheads)
    t = v.op("tensor_copy", sa3, qe, deps=rd(q_tm) + SA.wdeps())
    SA.did_write(t); q_tm.did_read(t)
    t = v.op("tensor_copy", sb3, qo, deps=rd(q_tm) + SB.wdeps())
    SB.did_write(t); q_tm.did_read(t)
    t = v.op("tensor_tensor", qe, sa3, cosb, OP.mult,
             deps=rd(SA, rope_buf) + q_tm.wdeps())
    q_tm.did_write(t); SA.did_read(t); rope_buf.did_read(t)
    t = v.op("tensor_tensor", sa3b, sb3, sinb, OP.mult,
             deps=rd(SB, rope_buf) + SA.wdeps())
    SA.did_write(t); SB.did_read(t); rope_buf.did_read(t)
    t = v.op("tensor_tensor", qe, qe, sa3b, OP.subtract,
             deps=rd(SA) + q_tm.wdeps())
    q_tm.did_write(t); SA.did_read(t)
    t = v.op("tensor_tensor", qo, sb3, cosb, OP.mult,
             deps=rd(SB, rope_buf) + q_tm.wdeps())
    q_tm.did_write(t); SB.did_read(t); rope_buf.did_read(t)
    t = v.op("tensor_tensor", sb3, sa3, sinb, OP.mult,
             deps=rd(SA, rope_buf) + SB.wdeps())
    SB.did_write(t); SA.did_read(t); rope_buf.did_read(t)
    t = v.op("tensor_tensor", qo, qo, sb3, OP.add,
             deps=rd(SB) + q_tm.wdeps())
    q_tm.did_write(t); SB.did_read(t)


def emit_rne_plain(k, out_buf, out_ap, x_buf, x_ap, pre, bias=0.0):
    v = k.vector
    sa, sb = k._rne_sa, k._rne_sb
    w = x_ap.shape[-1]
    t = v.op("tensor_scalar", sa.ap[:, :w], x_ap, float(bias) + float(C1), None,
             OP.add, deps=rd(x_buf) + sa.wdeps())
    sa.did_write(t); x_buf.did_read(t)
    t = v.op("tensor_scalar", sa.ap[:, :w], sa.ap[:, :w], float(C1),
             pre.j.ap[:], OP.subtract, OP.mult, deps=sa.wdeps() + rd(pre.j))
    sa.did_write(t); pre.j.did_read(t)
    t = v.op("tensor_scalar", sb.ap[:, :w], x_ap, float(bias) + float(C2), None,
             OP.add, deps=rd(x_buf) + sb.wdeps())
    sb.did_write(t); x_buf.did_read(t)
    t = v.op("scalar_tensor_tensor", out_ap, sb.ap[:, :w], float(C2),
             sa.ap[:, :w], OP.subtract, OP.add,
             deps=rd(sa, sb) + out_buf.wdeps())
    out_buf.did_write(t); sa.did_read(t); sb.did_read(t)
    return t


def emit_l3_attn(k, B, pre, x_in_ap, x_in_buf, x_out_ap, x_out_buf,
                 g_dram, g_buf, w, scr):
    v, s, sy = k.vector, k.scalar, k.sync
    W = Buf(B["WRES"].ap.bitcast(F32))
    G0, G1 = B["G0"], B["G1"]

    tok = sy.dma(B["BIAS1"].ap[:, 0:1024],
                 w["b_attn"][None, 0:1024].partition_broadcast(128),
                 deps=B["BIAS1"].wdeps())
    B["BIAS1"].did_write(tok)
    tok = sy.dma(B["BIAS2"].ap[:, 0:1024],
                 w["b_attn"][None, 1024:2048].partition_broadcast(128),
                 deps=B["BIAS2"].wdeps())
    B["BIAS2"].did_write(tok)
    VB_OFF = 19520
    tok = sy.dma(W.ap[:, VB_OFF:VB_OFF + 1024],
                 w["b_attn"][None, 2048:3072].partition_broadcast(128),
                 deps=W.wdeps())
    W.did_write(tok)

    def stage_w(dram_ap, col0, ncol):
        wdeps = W.wdeps()
        first = True
        for c in range(8):
            t = sy.dma(W.ap[:, c * ncol:(c + 1) * ncol],
                       dram_ap[c * 128:(c + 1) * 128, col0:col0 + ncol],
                       deps=wdeps)
            if first:
                W.did_write(t)
                first = False
            else:
                W.did_write_acc(t)

    QTM, KTM, VTM = 6144, 10240, 14336
    VB = 19520
    STG8 = 20544
    for i in range(8):
        t0 = i * 512
        emit_ln(k, x_in_ap, x_in_buf, G0, B["STATS"], G1, t0, pre)
        xn_fm = sub_buf(G1, 0, 4096)
        emit_tm_to_fm(k, G0, xn_fm, pre, B["PS2"], B["PS3"])
        alias_sync(G1, xn_fm)
        for proj, dsto in (("q", QTM), ("k", KTM), ("v", VTM)):
            wcol0 = {"q": 0, "k": 1024, "v": 2048}[proj]
            for half in range(2):
                stage_w(w["W_attn"], wcol0 + half * 512, 512)
                for j in range(4):
                    ps = B["PS0"] if j % 2 == 0 else B["PS1"]
                    pairs = [(G1, G1.ap[:, c * 512 + j * 128:
                                  c * 512 + j * 128 + 128],
                              W, W.ap[:, c * 512:(c + 1) * 512])
                             for c in range(8)]
                    mm_group(k, ps, ps.ap[:, 0:512], pairs)
                    if proj != "v":
                        bt = B["BIAS1"] if proj == "q" else B["BIAS2"]
                        dst = W.ap[:, dsto + j * 1024 + half * 512:
                                   dsto + j * 1024 + (half + 1) * 512]
                        t = v.op("scalar_tensor_tensor", dst, ps.ap[:, 0:512],
                                 1.0, bt.ap[:, half * 512:(half + 1) * 512],
                                 OP.mult, OP.add, deps=rd(ps, bt) + W.wdeps())
                        W.did_write(t); ps.did_read(t); bt.did_read(t)
                    else:
                        base = VTM + j * 1040 + half * 520
                        dst = W.ap[:, base:base + 520].rearrange(
                            "p (h o) -> p h o", h=8)[:, :, 0:64]
                        t = v.op("scalar_tensor_tensor", dst, ps.ap[:, 0:512],
                                 1.0,
                                 W.ap[:, VB + half * 512:VB + (half + 1) * 512]
                                 .rearrange("p (h o) -> p h o", h=8),
                                 OP.mult, OP.add, deps=rd(ps) + W.wdeps())
                        W.did_write(t); ps.did_read(t)
            for j in range(4):
                sub = 4 * i + j
                if proj == "v":
                    ones_dst = W.ap[:, VTM + j * 1040:VTM + (j + 1) * 1040].rearrange(
                        "p (h o) -> p h o", h=16)[:, :, 64:65]
                    t = v.op("memset", ones_dst, 1.0, deps=W.wdeps())
                    W.did_write(t)
                    tok = sy.dma(scr["v"].ap[sub],
                                 W.ap[:, VTM + j * 1040:VTM + (j + 1) * 1040],
                                 deps=rd(W) + scr["v"].wdeps())
                    scr["v"].did_write(tok); W.did_read(tok)
                else:
                    qk_tm = sub_buf(W, dsto + j * 1024, 1024)
                    stg = sub_buf(W, STG8, 1024)
                    emit_tm_to_fm(k, qk_tm, stg, pre, B["PS2"],
                                  B["PS3"], n_sub=1, width=1024)
                    merge_sub(W, stg)
                    WB = B["WRES"].ap
                    SPH, SPL = 43136, 44160
                    t = v.op("tensor_copy", WB[:, SPH:SPH + 1024],
                             W.ap[:, STG8:STG8 + 1024],
                             deps=rd(W) + W.wdeps())
                    W.did_write(t)
                    t = v.op("tensor_tensor", WB[:, SPL:SPL + 1024],
                             W.ap[:, STG8:STG8 + 1024], WB[:, SPH:SPH + 1024],
                             OP.subtract, deps=W.wdeps())
                    W.did_write(t)
                    dname = "q" if proj == "q" else "kk"
                    for comp, off in ((0, SPH), (1, SPL)):
                        tok = sy.dma(
                            scr[dname].ap[comp][:, :, sub * 128:(sub + 1) * 128]
                            .rearrange("c p t -> p c t"),
                            WB[:, off:off + 1024].rearrange(
                                "p (c t) -> p c t", c=8),
                            deps=rd(W) + scr[dname].wdeps())
                        scr[dname].did_write_acc(tok); W.did_read(tok)

    QF, KF, VT2, OACC = 4096, 8192, 9216, 10768
    hgroups = [list(range(0, 7)), list(range(7, 14)), [14, 15]]
    KFB, VTB, P2 = 14864, 15888, 16928
    qfb = sub_buf(W, QF, 4096)
    kfb = [sub_buf(W, KF, 1024), sub_buf(W, KFB, 1024)]
    vtb = [sub_buf(W, VT2, 1040), sub_buf(W, VTB, 1040)]
    pb = [sub_buf(W, 10256, 512), sub_buf(W, P2, 512)]

    WB = B["WRES"].ap
    KB_BF = [16384, 29728]
    QF_BF = 8192

    def k_dma(tj):
        b = kfb[tj % 2]
        base = KB_BF[tj % 2]
        wdeps = b.wdeps()
        for comp in range(2):
            tok = sy.dma(WB[:, base + comp * 1024:base + (comp + 1) * 1024]
                         .rearrange("p (c t) -> p c t", c=8),
                         scr["kk"].ap[comp][:, :, tj * 128:(tj + 1) * 128]
                         .rearrange("c p t -> p c t"),
                         deps=wdeps + rd(scr["kk"]))
            if comp == 0:
                b.did_write(tok)
            else:
                b.did_write_acc(tok)
            scr["kk"].did_read(tok)
        b2 = vtb[tj % 2]
        tok = sy.dma(b2.ap[:], scr["v"].ap[tj], deps=b2.wdeps() + rd(scr["v"]))
        b2.did_write(tok); scr["v"].did_read(tok)

    for qt in range(8):
        t0 = qt * 512
        qdeps = qfb.wdeps()
        for comp in range(2):
            tok = sy.dma(WB[:, QF_BF + comp * 4096:QF_BF + (comp + 1) * 4096]
                         .rearrange("p (c t) -> p c t", c=8),
                         scr["q"].ap[comp][:, :, t0:t0 + 512]
                         .rearrange("c p t -> p c t"),
                         deps=qdeps + rd(scr["q"]))
            if comp == 0:
                qfb.did_write(tok)
            else:
                qfb.did_write_acc(tok)
            scr["q"].did_read(tok)
        ntj = 4 * qt + 4
        hgroups = [[0, 1, 2, 3, 4, 5], [6, 7, 8, 9, 10, 11], [12, 13, 14, 15]]
        for hg in hgroups:
            obanks = {h: B[f"PS{2 + idx}"] for idx, h in enumerate(hg)}
            k_dma(0)
            for tj in range(ntj):
                if tj + 1 < ntj:
                    k_dma(tj + 1)
                kb, vb = kfb[tj % 2], vtb[tj % 2]
                r = tj - 4 * qt
                for hi, h in enumerate(hg):
                    hp, hh = h // 2, h % 2
                    psS = B["PS0"] if hi % 2 == 0 else B["PS1"]
                    pp = pb[hi % 2]
                    kbase = KB_BF[tj % 2]
                    khi = WB[64 * hh:64 * hh + 64,
                             kbase + hp * 128:kbase + hp * 128 + 128]
                    klo = WB[64 * hh:64 * hh + 64,
                             kbase + 1024 + hp * 128:kbase + 1024 + hp * 128 + 128]
                    qhi = WB[64 * hh:64 * hh + 64,
                             QF_BF + hp * 512:QF_BF + (hp + 1) * 512]
                    qlo = WB[64 * hh:64 * hh + 64,
                             QF_BF + 4096 + hp * 512:QF_BF + 4096 + (hp + 1) * 512]
                    tok = k.tensor.op("matmul", psS.ap[:, 0:512], khi, qhi,
                                      start=True, stop=False,
                                      deps=rd(kb, qfb) + psS.wdeps())
                    tok = k.tensor.op("matmul", psS.ap[:, 0:512], khi, qlo,
                                      start=False, stop=False)
                    tok = k.tensor.op("matmul", psS.ap[:, 0:512], klo, qhi,
                                      start=False, stop=True)
                    psS.did_write(tok); kb.did_read(tok); qfb.did_read(tok)
                    if r < 0:
                        t = s.op("activation", pp.ap[:], psS.ap[:, 0:512],
                                 AF.Exp, scale=pre.c_eighth, bias=pre.c_nshift,
                                 deps=rd(psS, pre.consts) + pp.wdeps())
                        pp.did_write(t); psS.did_read(t)
                        pre.consts.did_read(t)
                    else:
                        for sb_i in range(4):
                            pdst = pp.ap[:, sb_i * 128:(sb_i + 1) * 128]
                            pssl = psS.ap[:, sb_i * 128:(sb_i + 1) * 128]
                            if sb_i > r:
                                t = s.op("activation", pdst, pssl, AF.Exp,
                                         scale=pre.c_eighth, bias=pre.c_nshift,
                                         deps=rd(psS, pre.consts) + pp.wdeps())
                                pp.did_write_acc(t); psS.did_read(t)
                                pre.consts.did_read(t)
                            elif sb_i == r:
                                t = s.op("activation", pdst, pssl, AF.Exp,
                                         scale=pre.c_eighth, bias=pre.c_nshift,
                                         deps=rd(psS, pre.consts) + pp.wdeps())
                                pp.did_write_acc(t); psS.did_read(t)
                                pre.consts.did_read(t)
                                t = v.op("tensor_tensor", pdst, pdst,
                                         pre.triu.ap[:], OP.mult,
                                         deps=rd(pre.triu) + pp.wdeps())
                                pp.did_write_acc(t); pre.triu.did_read(t)
                            else:
                                t = v.op("memset", pdst, 0.0, deps=pp.wdeps())
                                pp.did_write_acc(t)
                    vap = vb.ap[:, h * 65:(h + 1) * 65]
                    tok = k.tensor.op("matmul", obanks[h].ap[0:65, 0:512], vap,
                                      pp.ap[:],
                                      start=(tj == 0), stop=(tj == ntj - 1),
                                      deps=rd(vb, pp) + (obanks[h].wdeps() if tj == 0 else []),
                                      inc=True)
                    vb.did_read(tok); pp.did_read(tok)
                    if tj == ntj - 1:
                        obanks[h].did_write(tok)
            for h in hg:
                hp, hh = h // 2, h % 2
                psO = obanks[h]
                t = v.op("reciprocal", B["SA"].ap[0:1, 0:512],
                         psO.ap[64:65, 0:512], deps=rd(psO) + B["SA"].wdeps())
                B["SA"].did_write(t); psO.did_read(t)
                tok = k.tensor.op("matmul", B["PS0"].ap[0:64, 0:512],
                                  B["ONESROW"].ap[0:1, 0:64],
                                  B["SA"].ap[0:1, 0:512], start=True, stop=True,
                                  deps=rd(B["ONESROW"], B["SA"]) + B["PS0"].wdeps())
                B["PS0"].did_write(tok)
                B["ONESROW"].did_read(tok); B["SA"].did_read(tok)
                t = s.op("activation", B["SB"].ap[0:64, 0:512],
                         B["PS0"].ap[0:64, 0:512], AF.Copy,
                         deps=rd(B["PS0"]) + B["SB"].wdeps())
                B["SB"].did_write(t); B["PS0"].did_read(t)
                t = v.op("tensor_tensor", B["SB"].ap[0:64, 512:1024],
                         psO.ap[0:64, 0:512], B["SB"].ap[0:64, 0:512],
                         OP.mult, deps=rd(psO) + B["SB"].wdeps())
                B["SB"].did_write(t); psO.did_read(t)
                odst = W.ap[64 * hh:64 * hh + 64,
                            OACC + hp * 512:OACC + (hp + 1) * 512]
                t = s.op("activation", odst, B["SB"].ap[0:64, 512:1024],
                         AF.Copy, deps=rd(B["SB"]) + W.wdeps())
                W.did_write(t); B["SB"].did_read(t)
        tok = sy.dma(B["BIAS1"].ap[:, 0:1024],
                     w["b_proj"][None, :].partition_broadcast(128),
                     deps=B["BIAS1"].wdeps())
        B["BIAS1"].did_write(tok)
        for half in range(2):
            stage_w(w["W_proj"], half * 512, 512)
            for sub in range(4):
                ps = B["PS0"]
                pairs = [(W, W.ap[:, OACC + c * 512 + sub * 128:
                              OACC + c * 512 + sub * 128 + 128],
                          W, W.ap[:, c * 512:(c + 1) * 512])
                         for c in range(8)]
                mm_group(k, ps, ps.ap[:, 0:512], pairs)
                dst = G1.ap[:, sub * 1024 + half * 512:
                            sub * 1024 + (half + 1) * 512]
                t = v.op("scalar_tensor_tensor", dst, ps.ap[:, 0:512], 1.0,
                         B["BIAS1"].ap[:, half * 512:(half + 1) * 512],
                         OP.mult, OP.add, deps=rd(ps, B["BIAS1"]) + G1.wdeps())
                G1.did_write(t); ps.did_read(t); B["BIAS1"].did_read(t)
        tok = sy.dma(G0.ap[:].rearrange("p (a d) -> p a d", a=4),
                     x_in_ap[t0:t0 + 512, :].rearrange("(a p) d -> p a d", p=128),
                     deps=G0.wdeps() + rd(x_in_buf))
        G0.did_write(tok); x_in_buf.did_read(tok)
        tok = sy.dma(B["XT"].ap[:].rearrange("p (a d) -> p a d", a=4),
                     g_dram[t0:t0 + 512, :].rearrange("(a p) d -> p a d", p=128),
                     deps=B["XT"].wdeps() + rd(g_buf))
        B["XT"].did_write(tok); g_buf.did_read(tok)
        t = v.op("tensor_tensor", G1.ap[:], G1.ap[:], B["XT"].ap[:], OP.mult,
                 deps=rd(B["XT"]) + G1.wdeps())
        G1.did_write(t); B["XT"].did_read(t)
        t = v.op("tensor_tensor", G1.ap[:], G1.ap[:], G0.ap[:], OP.add,
                 deps=rd(G0) + G1.wdeps())
        G1.did_write(t); G0.did_read(t)
        tok = sy.dma(x_out_ap[t0:t0 + 512, :].rearrange("(a p) d -> p a d", p=128),
                     G1.ap[:].rearrange("p (a d) -> p a d", a=4),
                     deps=rd(G1) + x_out_buf.wdeps())
        x_out_buf.did_write(tok); G1.did_read(tok)



def emit_nvfp4_quant(k, B, pre, x_buf, x_ap, out_buf, out_ap, width,
                     gs6_pp, rgs_pp, gs_buf, blkbuf, blk_off):
    v, s, g = k.vector, k.scalar, k.gpsimd
    nb = width // 16
    am = blkbuf.ap[:, blk_off:blk_off + nb]
    sc8 = blkbuf.ap[:, blk_off + nb:blk_off + 2 * nb]
    rdq = blkbuf.ap[:, blk_off + 2 * nb:blk_off + 3 * nb]
    tmp = blkbuf.ap[:, blk_off + 3 * nb:blk_off + 4 * nb]
    t = v.op("reduce_max", am, x_ap.rearrange("p (b s) -> p b s", s=16), AX.X,
             apply_absolute_value=True, deps=rd(x_buf) + blkbuf.wdeps())
    blkbuf.did_write(t); x_buf.did_read(t)
    t = v.op("tensor_scalar", sc8, am, gs6_pp, None, OP.mult,
             deps=blkbuf.wdeps() + rd(gs_buf))
    blkbuf.did_write(t); gs_buf.did_read(t)
    t = v.op("tensor_scalar", sc8, sc8, 0.0, 448.0, OP.max, OP.min,
             deps=blkbuf.wdeps())
    blkbuf.did_write(t)
    t = v.op("memset", tmp, 0.015625, deps=blkbuf.wdeps())
    blkbuf.did_write(t)
    for thr in [0.03125, 0.0625, 0.125, 0.25, 0.5, 1.0, 2.0, 4.0, 8.0, 16.0,
                32.0, 64.0, 128.0, 256.0]:
        t = v.op("tensor_scalar", am, sc8, float(thr), 1.0, OP.is_ge, OP.add,
                 deps=blkbuf.wdeps())
        blkbuf.did_write(t)
        t = v.op("tensor_tensor", tmp, tmp, am, OP.mult, deps=blkbuf.wdeps())
        blkbuf.did_write(t)
    t = v.op("tensor_scalar_mul", am, tmp, float(1.5 * 2.0**20),
             deps=blkbuf.wdeps())
    blkbuf.did_write(t)
    t = v.op("tensor_tensor", rdq, sc8, am, OP.add, deps=blkbuf.wdeps())
    blkbuf.did_write(t)
    t = v.op("tensor_tensor", rdq, rdq, am, OP.subtract, deps=blkbuf.wdeps())
    blkbuf.did_write(t)
    t = v.op("tensor_scalar", rdq, rdq, pre.j.ap[:], None, OP.mult,
             deps=blkbuf.wdeps() + rd(pre.j))
    blkbuf.did_write(t); pre.j.did_read(t)
    t = v.op("tensor_scalar_mul", am, tmp, float(1.5 * 2.0**49),
             deps=blkbuf.wdeps())
    blkbuf.did_write(t)
    t = v.op("tensor_tensor", tmp, sc8, am, OP.add, deps=blkbuf.wdeps())
    blkbuf.did_write(t)
    t = v.op("tensor_tensor", tmp, tmp, am, OP.subtract, deps=blkbuf.wdeps())
    blkbuf.did_write(t)
    t = v.op("tensor_tensor", sc8, tmp, rdq, OP.add, deps=blkbuf.wdeps())
    blkbuf.did_write(t)
    t = v.op("tensor_scalar", tmp, sc8, rgs_pp, None, OP.mult,
             deps=blkbuf.wdeps() + rd(gs_buf))
    blkbuf.did_write(t); gs_buf.did_read(t)
    t = v.op("tensor_single_scalar", rdq, tmp, 0.0, OP.is_equal,
             deps=blkbuf.wdeps())
    blkbuf.did_write(t)
    t = v.op("tensor_tensor", tmp, tmp, rdq, OP.add, deps=blkbuf.wdeps())
    blkbuf.did_write(t)
    t = v.op("reciprocal", rdq, tmp, deps=blkbuf.wdeps())
    blkbuf.did_write(t)
    SA, SB, SC = B["SA"], B["SB"], B["SC"]
    for q0 in range(0, width, 1024):
        qn = min(1024, width - q0)
        xs = x_ap[:, q0:q0 + qn]
        sa, sb, sc = SA.ap[:, 0:qn], SB.ap[:, 0:qn], SC.ap[:, 0:qn]
        rdq_b = rdq[:, q0 // 16:(q0 + qn) // 16].unsqueeze(2).broadcast_to(
            [128, qn // 16, 16])
        sc8_b = sc8[:, q0 // 16:(q0 + qn) // 16].unsqueeze(2).broadcast_to(
            [128, qn // 16, 16])
        sa3 = sa.rearrange("p (b s) -> p b s", s=16)
        ot = out_ap[:, q0:q0 + qn]
        ot3 = ot.rearrange("p (b s) -> p b s", s=16)
        t = s.op("activation", sa, xs, AF.Abs, deps=rd(x_buf) + SA.wdeps())
        SA.did_write(t); x_buf.did_read(t)
        t = g.op("tensor_tensor", sa3, sa3, rdq_b, OP.mult,
                 deps=SA.wdeps() + rd(blkbuf))
        SA.did_write(t); blkbuf.did_read(t)
        t = v.op("tensor_scalar_min", sa, sa, 6.0, deps=SA.wdeps())
        SA.did_write(t)
        t = v.op("tensor_scalar", sb, sa, 2.0, 1.0, OP.is_ge, OP.add,
                 deps=rd(SA) + SB.wdeps())
        SB.did_write(t)
        t = v.op("tensor_scalar", ot, sa, 4.0, 1.0, OP.is_ge, OP.add,
                 deps=rd(SA) + out_buf.wdeps())
        out_buf.did_write(t)
        t = g.op("tensor_tensor", sb, sb, ot, OP.mult,
                 deps=SB.wdeps() + rd(out_buf))
        SB.did_write(t); out_buf.did_read(t)
        M32c, M64c = float(1.5 * 2.0**22), float(1.5 * 2.0**51)
        t = v.op("scalar_tensor_tensor", sc, sb, M32c, sa, OP.mult, OP.add,
                 deps=rd(SA, SB) + SC.wdeps())
        SC.did_write(t); SA.did_read(t); SB.did_read(t)
        t = v.op("scalar_tensor_tensor", sc, sb, -M32c, sc, OP.mult, OP.add,
                 deps=rd(SB) + SC.wdeps())
        SC.did_write(t); SB.did_read(t)
        t = v.op("tensor_scalar", sc, sc, pre.j.ap[:], None, OP.mult,
                 deps=SC.wdeps() + rd(pre.j))
        SC.did_write(t); pre.j.did_read(t)
        t = v.op("scalar_tensor_tensor", sa, sb, M64c, sa, OP.mult, OP.add,
                 deps=SA.wdeps() + rd(SB))
        SA.did_write(t); SB.did_read(t)
        t = v.op("scalar_tensor_tensor", sa, sb, -M64c, sa, OP.mult, OP.add,
                 deps=SA.wdeps() + rd(SB))
        SA.did_write(t); SB.did_read(t)
        t = g.op("tensor_tensor", sa, sa, sc, OP.add,
                 deps=SA.wdeps() + rd(SC))
        SA.did_write(t); SC.did_read(t)
        t = s.op("activation", sb, xs, AF.Sign, deps=rd(x_buf) + SB.wdeps())
        SB.did_write(t); x_buf.did_read(t)
        t = g.op("tensor_tensor", sa, sa, sb, OP.mult,
                 deps=SA.wdeps() + rd(SB))
        SA.did_write(t); SB.did_read(t)
        t = v.op("tensor_tensor", ot3, sa3, sc8_b, OP.mult,
                 deps=rd(SA, blkbuf) + out_buf.wdeps())
        out_buf.did_write(t); SA.did_read(t); blkbuf.did_read(t)
    return


def emit_global_absmax(k, B, w_dram, nch, ncol, dst_col):
    v, sy = k.vector, k.sync
    SMALL = B["SMALL"]
    SCR = B["SCRW"]
    SCR.wtoks = list(set(SCR.wtoks + B["G1"].wtoks))
    SCR.rtoks = list(set(SCR.rtoks + B["G1"].rtoks))
    t = v.op("memset", SMALL.ap[:, 30:31], 0.0, deps=SMALL.wdeps())
    SMALL.did_write(t)
    for c in range(nch):
        t = sy.dma(SCR.ap[:, :ncol], w_dram[c * 128:(c + 1) * 128, :],
                   deps=SCR.wdeps())
        SCR.did_write(t)
        t = v.op("reduce_max", SMALL.ap[:, 31:32], SCR.ap[:, :ncol], AX.X,
                 apply_absolute_value=True, deps=rd(SCR) + SMALL.wdeps())
        SMALL.did_write(t); SCR.did_read(t)
        t = v.op("tensor_tensor", SMALL.ap[:, 30:31], SMALL.ap[:, 30:31],
                 SMALL.ap[:, 31:32], OP.max, deps=SMALL.wdeps())
        SMALL.did_write(t)
    t = k.gpsimd.op("tensor_reduce", SMALL.ap[0:1, dst_col:dst_col + 1],
                    SMALL.ap[:, 30:31], AX.C, OP.max, deps=SMALL.wdeps())
    SMALL.did_write(t)
    B["G1"].wtoks = list(set(B["G1"].wtoks + SCR.wtoks))
    B["G1"].rtoks = list(set(B["G1"].rtoks + SCR.rtoks))
    return


def emit_bcast_scalar(k, B, src_ap, dst_ap, dst_buf, src_buf):
    v = k.vector
    ps = B["PS4"]
    tok = k.tensor.op("matmul", ps.ap[:, 0:1], B["ONESROW"].ap[0:1, :],
                      src_ap, start=True, stop=True,
                      deps=rd(B["ONESROW"], src_buf) + ps.wdeps())
    ps.did_write(tok); B["ONESROW"].did_read(tok); src_buf.did_read(tok)
    t = v.op("tensor_copy", dst_ap, ps.ap[:, 0:1], deps=rd(ps) + dst_buf.wdeps())
    dst_buf.did_write(t); ps.did_read(t)
    return t


def emit_ffn_nvfp4(k, B, pre, x_src_ap, x_src_buf, g_dram, g_buf, w,
                   hq_dram, hq_buf):
    v, s, sy, g = k.vector, k.scalar, k.sync, k.gpsimd
    W, G0, G1 = B["WRES"], B["G0"], B["G1"]
    SMALL, SM2, BLK, HQ = B["SMALL"], B["SMALL2"], B["BLK"], B["HQ16"]
    SCRW = B["SCRW"]
    B["SCRW"].wtoks = list(G1.wtoks)
    B["SCRW"].rtoks = list(G1.rtoks)

    tok = sy.dma(SMALL.ap[0:1, 20:21], w["s_in"][None, :], deps=SMALL.wdeps())
    SMALL.did_write(tok)
    tok = sy.dma(SMALL.ap[0:1, 21:22], w["s_in_down"][None, :],
                 deps=SMALL.wdeps())
    SMALL.did_write(tok)

    def scalar_gs_tiles(src_col, c6, cr):
        t = v.op("tensor_scalar_mul", SMALL.ap[0:1, 24:25],
                 SMALL.ap[0:1, src_col:src_col + 1], float(1.0 / 6.0),
                 deps=SMALL.wdeps())
        SMALL.did_write(t)
        emit_bcast_scalar(k, B, SMALL.ap[0:1, 24:25], SM2.ap[:, c6:c6 + 1],
                          SM2, SMALL)
        t = v.op("reciprocal", SMALL.ap[0:1, 24:25],
                 SMALL.ap[0:1, src_col:src_col + 1], deps=SMALL.wdeps())
        SMALL.did_write(t)
        emit_bcast_scalar(k, B, SMALL.ap[0:1, 24:25], SM2.ap[:, cr:cr + 1],
                          SM2, SMALL)

    scalar_gs_tiles(20, 40, 41)
    scalar_gs_tiles(21, 42, 43)

    def weight_gs(dram, nch, ncol, gs_col):
        emit_global_absmax(k, B, dram, nch, ncol, 25)
        t = v.op("reciprocal", SMALL.ap[0:1, gs_col:gs_col + 1],
                 SMALL.ap[0:1, 25:26], deps=SMALL.wdeps())
        SMALL.did_write(t)
        t = v.op("tensor_scalar_mul", SMALL.ap[0:1, gs_col:gs_col + 1],
                 SMALL.ap[0:1, gs_col:gs_col + 1], 2688.0,
                 deps=SMALL.wdeps())
        SMALL.did_write(t)

    def out_scale_tile(gsx_col, gsw_col, dst_col):
        t = v.op("tensor_tensor", SMALL.ap[0:1, 24:25],
                 SMALL.ap[0:1, gsx_col:gsx_col + 1],
                 SMALL.ap[0:1, gsw_col:gsw_col + 1], OP.mult,
                 deps=SMALL.wdeps())
        SMALL.did_write(t)
        t = v.op("reciprocal", SMALL.ap[0:1, 24:25], SMALL.ap[0:1, 24:25],
                 deps=SMALL.wdeps())
        SMALL.did_write(t)
        emit_bcast_scalar(k, B, SMALL.ap[0:1, 24:25], SM2.ap[:, dst_col:dst_col + 1],
                          SM2, SMALL)

    def quant_weight(dram, nch, ncol, gs_col, wq_off_bf):
        B["SCRW"].wtoks = list(G1.wtoks)
        B["SCRW"].rtoks = list(G1.rtoks)
        t = v.op("tensor_scalar_mul", SMALL.ap[0:1, 24:25],
                 SMALL.ap[0:1, gs_col:gs_col + 1], float(1.0 / 6.0),
                 deps=SMALL.wdeps())
        SMALL.did_write(t)
        emit_bcast_scalar(k, B, SMALL.ap[0:1, 24:25], SM2.ap[:, 44:45], SM2, SMALL)
        t = v.op("reciprocal", SMALL.ap[0:1, 24:25],
                 SMALL.ap[0:1, gs_col:gs_col + 1], deps=SMALL.wdeps())
        SMALL.did_write(t)
        emit_bcast_scalar(k, B, SMALL.ap[0:1, 24:25], SM2.ap[:, 45:46], SM2, SMALL)
        n_out = nch * 128
        for c in range(nch):
            tok = sy.dma(SCRW.ap[:, :ncol], dram[c * 128:(c + 1) * 128, :],
                         deps=SCRW.wdeps())
            SCRW.did_write(tok)
            emit_nvfp4_quant(k, B, pre, SCRW, SCRW.ap[:, :ncol], HQ,
                             HQ.ap[:, :ncol], ncol,
                             SM2.ap[:, 44:45], SM2.ap[:, 45:46], SM2, BLK, 0)
            for h0 in range(0, ncol // 128, 4):
                nh = min(4, ncol // 128 - h0)
                bank = B["PS2"] if (h0 // 4) % 2 == 0 else B["PS3"]
                bv = bank.ap.bitcast(BF16)
                wdeps2 = bank.wdeps() + rd(HQ, pre.ident_bf)
                tk = None
                for ii in range(nh):
                    hch = h0 + ii
                    tk = k.tensor.op(
                        "transpose", bv[:, ii * 128:(ii + 1) * 128],
                        HQ.ap[:, hch * 128:(hch + 1) * 128],
                        pre.ident_bf.ap[:],
                        deps=wdeps2 if ii == 0 else (), inc=(ii == nh - 1))
                bank.did_write(tk); HQ.did_read(tk); pre.ident_bf.did_read(tk)
                for ii in range(nh):
                    hch = h0 + ii
                    dst = W.ap[:, wq_off_bf + hch * n_out + c * 128:
                               wq_off_bf + hch * n_out + (c + 1) * 128]
                    t2 = s.op("activation", dst, bv[:, ii * 128:(ii + 1) * 128],
                              AF.Copy, deps=rd(bank) + W.wdeps())
                    W.did_write(t2); bank.did_read(t2)
        alias_sync(G1, SCRW)

    weight_gs(w["gate_w"], 32, 1024, 26)
    weight_gs(w["up_w"], 32, 1024, 27)
    out_scale_tile(20, 26, 46)
    out_scale_tile(20, 27, 47)
    quant_weight(w["gate_w"], 32, 1024, 26, 0)
    quant_weight(w["up_w"], 32, 1024, 27, 32768)
    barrier(k)

    def xq_ap(c, j):
        return B["H0"].ap[:, c * 512 + j * 128: c * 512 + (j + 1) * 128]

    def wq_ap(off):
        def f(c, o):
            return W.ap[:, off + c * 4096 + o * 512: off + c * 4096 + (o + 1) * 512]
        return f

    for t0 in range(0, T, 512):
        emit_ln(k, x_src_ap, x_src_buf, G0, B["STATS"], G1, t0, pre)
        for j in range(4):
            emit_nvfp4_quant(k, B, pre, G0, G0.ap[:, j * 1024:(j + 1) * 1024],
                             HQ, HQ.ap[:, j * 1024:(j + 1) * 1024], 1024,
                             SM2.ap[:, 40:41], SM2.ap[:, 41:42], SM2, BLK, 0)
        xq_fm = sub_buf(B["H0"], 0, 4096)
        emit_tm_to_fm(k, HQ, xq_fm, pre, B["PS2"], B["PS3"], src_bf16=True)
        merge_sub(B["H0"], xq_fm)
        for j in range(4):
            for o in range(8):
                ps = B["PS0"] if o % 2 == 0 else B["PS1"]
                emit_bitlin_mm(k, xq_ap, B["H0"], wq_ap(0), W, 8, ps, j, o, 512)
                t = s.op("activation", G1.ap[:, o * 512:(o + 1) * 512],
                         ps.ap[:, 0:512], AF.Silu, scale=SM2.ap[:, 46:47],
                         deps=rd(ps, SM2) + G1.wdeps())
                G1.did_write(t); ps.did_read(t); SM2.did_read(t)
            for o in range(8):
                ps = B["PS0"] if o % 2 == 0 else B["PS1"]
                emit_bitlin_mm(k, xq_ap, B["H0"], wq_ap(32768), W, 8, ps, j, o, 512)
                t = s.op("activation", G0.ap[:, o * 512:(o + 1) * 512],
                         ps.ap[:, 0:512], AF.Copy, scale=SM2.ap[:, 47:48],
                         deps=rd(ps, SM2) + G0.wdeps())
                G0.did_write(t); ps.did_read(t); SM2.did_read(t)
            if t0 == 512 and j == 1 and "dbg_gate" in B:
                tok = sy.dma(B["dbg_gate"].ap[0:128, :], G1.ap[:],
                             deps=rd(G1) + B["dbg_gate"].wdeps())
                B["dbg_gate"].did_write(tok); G1.did_read(tok)
                tok = sy.dma(B["dbg_up"].ap[0:128, :], G0.ap[:],
                             deps=rd(G0) + B["dbg_up"].wdeps())
                B["dbg_up"].did_write(tok); G0.did_read(tok)
            t = v.op("tensor_tensor", G1.ap[:], G1.ap[:], G0.ap[:], OP.mult,
                     deps=rd(G0) + G1.wdeps())
            G1.did_write(t); G0.did_read(t)
            emit_nvfp4_quant(k, B, pre, G1, G1.ap[:], HQ, HQ.ap[:, 0:4096],
                             4096, SM2.ap[:, 42:43], SM2.ap[:, 43:44], SM2,
                             BLK, 0)
            r0 = t0 + j * 128
            tok = sy.dma(hq_dram[r0:r0 + 128, :], HQ.ap[:, 0:4096],
                         deps=rd(HQ) + hq_buf.wdeps())
            hq_buf.did_write(tok); HQ.did_read(tok)

    barrier(k)
    weight_gs(w["down_w"], 8, 4096, 28)
    out_scale_tile(21, 28, 48)
    quant_weight(w["down_w"], 8, 4096, 28, 0)
    barrier(k)
    for t0 in range(0, T, 512):
        for j in range(4):
            r0 = t0 + j * 128
            tok = sy.dma(HQ.ap[:, 0:4096], hq_dram[r0:r0 + 128, :],
                         deps=HQ.wdeps() + rd(hq_buf))
            HQ.did_write(tok); hq_buf.did_read(tok)
            hq_fm = sub_buf(B["H0"], 4096, 4096)
            emit_tm_to_fm(k, HQ, hq_fm, pre, B["PS2"], B["PS3"], n_sub=1,
                          width=4096, src_bf16=True)
            merge_sub(B["H0"], hq_fm)
            if t0 == 0 and j == 0 and "dbg_hqfm" in B:
                t = v.op("tensor_copy", G0.ap[:], B["H0"].ap[:, 4096:8192],
                         deps=rd(B["H0"]) + G0.wdeps())
                G0.did_write(t); B["H0"].did_read(t)
                tok = sy.dma(B["dbg_hqfm"].ap[0:128, :], G0.ap[:],
                             deps=rd(G0) + B["dbg_hqfm"].wdeps())
                B["dbg_hqfm"].did_write(tok); G0.did_read(tok)
            for o2 in range(2):
                ps = B["PS0"] if o2 == 0 else B["PS1"]
                pairs = []
                for c in range(32):
                    pairs.append((B["H0"],
                                  B["H0"].ap[:, 4096 + c * 128: 4096 + (c + 1) * 128],
                                  W,
                                  W.ap[:, c * 1024 + o2 * 512:
                                       c * 1024 + (o2 + 1) * 512]))
                mm_group(k, ps, ps.ap[:, 0:512], pairs)
                t = s.op("activation", G1.ap[:, o2 * 512:(o2 + 1) * 512],
                         ps.ap[:, 0:512], AF.Silu, scale=SM2.ap[:, 48:49],
                         deps=rd(ps, SM2) + G1.wdeps())
                G1.did_write(t); ps.did_read(t); SM2.did_read(t)
            tok = sy.dma(g_dram[r0:r0 + 128, :], G1.ap[:, 0:1024],
                         deps=rd(G1) + g_buf.wdeps())
            g_buf.did_write(tok); G1.did_read(tok)



def emit_l0_attn(k, B, pre, x_in_ap, x_in_buf, x_out_ap, x_out_buf,
                 g_dram, g_buf, w):
    v, s, sy, gp = k.vector, k.scalar, k.sync, k.gpsimd
    W = Buf(B["WRES"].ap.bitcast(F32))
    G0, G1 = B["G0"], B["G1"]
    SMALL, SM2, DQH = B["SMALL"], B["SMALL2"], B["DQH"]
    QTM, KTM, VTM, RF = 0, 4096, 8192, 12288
    SOFF, ZER, TOP, MSK = 16384, 17408, 17536, 17664
    FF, LL, DD, ELN, E2 = 17792, 17920, 18048, 18176, 18304
    SFM, KS, QF, KF2, KHF, KHT, ASB = 18432, 18560, 18688, 18816, 18944, 19072, 19200
    OO = 19328
    QNW, KNW, GNW = 23424, 23552, 23680

    tok = sy.dma(W.ap[:, QNW:QNW + 128],
                 w["q_norm_w"][None, :].partition_broadcast(128), deps=W.wdeps())
    W.did_write(tok)
    tok = sy.dma(W.ap[:, KNW:KNW + 128],
                 w["k_norm_w"][None, :].partition_broadcast(128), deps=W.wdeps())
    W.did_write_acc(tok)
    tok = sy.dma(W.ap[:, GNW:GNW + 1024],
                 w["g_norm_w"][None, :].partition_broadcast(128), deps=W.wdeps())
    W.did_write_acc(tok)
    tok = sy.dma(SM2.ap[:, 48:56], w["A_log"][None, :].partition_broadcast(128),
                 deps=SM2.wdeps())
    SM2.did_write(tok)
    tok = sy.dma(SM2.ap[:, 56:64], w["dt_bias"][None, :].partition_broadcast(128),
                 deps=SM2.wdeps())
    SM2.did_write_acc(tok)
    t = s.op("activation", SM2.ap[:, 48:56], SM2.ap[:, 48:56], AF.Exp,
             deps=SM2.wdeps())
    SM2.did_write(t)
    t = v.op("memset", W.ap[:, SOFF:SOFF + 1024], 0.0, deps=W.wdeps())
    W.did_write_acc(t)
    t = v.op("memset", W.ap[:, ZER:ZER + 128], 0.0, deps=W.wdeps())
    W.did_write_acc(t)

    def stage_w(dram_ap, col0, ncol, dst=None):
        dstb = dst if dst is not None else G0
        wdeps = dstb.wdeps()
        first = True
        for c in range(8):
            t = sy.dma(dstb.ap[:, c * ncol:(c + 1) * ncol],
                       dram_ap[c * 128:(c + 1) * 128, col0:col0 + ncol],
                       deps=wdeps)
            if first:
                dstb.did_write(t); first = False
            else:
                dstb.did_write_acc(t)

    for ti in range(8):
        t0 = ti * 512
        emit_ln(k, x_in_ap, x_in_buf, G0, B["STATS"], G1, t0, pre)
        xn_fm = sub_buf(G1, 0, 4096)
        emit_tm_to_fm(k, G0, xn_fm, pre, B["PS2"], B["PS3"])
        alias_sync(G1, xn_fm)
        HQ = B["HQ16"]
        SAB = [B["SA"], B["SB"]]
        sabf = [b.ap.bitcast(BF16) for b in SAB]
        t = v.op("tensor_copy", HQ.ap[:], G1.ap[:, 0:4096],
                 deps=rd(G1) + HQ.wdeps())
        HQ.did_write(t); G1.did_read(t)
        for hhalf in range(2):
            t = v.op("tensor_tensor", sabf[hhalf][:, 0:2048],
                     G1.ap[:, hhalf * 2048:(hhalf + 1) * 2048],
                     HQ.ap[:, hhalf * 2048:(hhalf + 1) * 2048], OP.subtract,
                     deps=rd(G1, HQ) + SAB[hhalf].wdeps())
            SAB[hhalf].did_write(t); G1.did_read(t); HQ.did_read(t)

        def xhi(c, j):
            return HQ.ap[:, c * 512 + j * 128: c * 512 + j * 128 + 128]

        def xlo(c, j):
            return sabf[c // 4][:, (c % 4) * 512 + j * 128:
                                (c % 4) * 512 + j * 128 + 128]

        g0bf = G0.ap.bitcast(BF16)
        for proj, dsto in (("Wq", QTM), ("Wk", KTM), ("Wv", VTM), ("Wr", RF)):
            fn = AF.Sigmoid if proj == "Wr" else AF.Silu
            fp32_path = proj == "Wr"
            for half in range(2):
                stage_w(w[proj], half * 512, 512, dst=B["XT"])
                if not fp32_path:
                    t = v.op("tensor_copy", g0bf[:, 0:4096], B["XT"].ap[:],
                             deps=rd(B["XT"]) + G0.wdeps())
                    G0.did_write(t); B["XT"].did_read(t)
                    t = v.op("tensor_tensor", g0bf[:, 4096:8192], B["XT"].ap[:],
                             g0bf[:, 0:4096], OP.subtract,
                             deps=rd(B["XT"]) + G0.wdeps())
                    G0.did_write_acc(t); B["XT"].did_read(t)
                for j in range(4):
                    ps = B["PS0"] if j % 2 == 0 else B["PS1"]
                    if fp32_path:
                        pairs = [(G1, G1.ap[:, c * 512 + j * 128:
                                      c * 512 + j * 128 + 128],
                                  B["XT"], B["XT"].ap[:, c * 512:(c + 1) * 512])
                                 for c in range(8)]
                    else:
                        pairs = (
                            [(HQ, xhi(c, j),
                              G0, g0bf[:, c * 512:(c + 1) * 512])
                             for c in range(8)] +
                            [(HQ, xhi(c, j),
                              G0, g0bf[:, 4096 + c * 512:4096 + (c + 1) * 512])
                             for c in range(8)] +
                            [(SAB[c // 4], xlo(c, j),
                              G0, g0bf[:, c * 512:(c + 1) * 512])
                             for c in range(8)])
                    mm_group(k, ps, ps.ap[:, 0:512], pairs)
                    dst = W.ap[:, dsto + j * 1024 + half * 512:
                               dsto + j * 1024 + (half + 1) * 512]
                    t = s.op("activation", dst, ps.ap[:, 0:512], fn,
                             deps=rd(ps) + W.wdeps())
                    W.did_write(t); ps.did_read(t)
        stage_w(w["Wa"], 0, 8, dst=B["XT"])
        for j in range(4):
            ps = B["PS4"]
            pairs = [(G1, G1.ap[:, c * 512 + j * 128: c * 512 + j * 128 + 128],
                      B["XT"], B["XT"].ap[:, c * 8:(c + 1) * 8])
                     for c in range(8)]
            mm_group(k, ps, ps.ap[:, 0:8], pairs)
            t = v.op("scalar_tensor_tensor", SMALL.ap[:, 40:48], ps.ap[:, 0:8],
                     1.0, SM2.ap[:, 56:64], OP.mult, OP.add,
                     deps=rd(ps, SM2) + SMALL.wdeps())
            SMALL.did_write(t); ps.did_read(t); SM2.did_read(t)
            t = s.op("activation", SMALL.ap[:, 40:48], SMALL.ap[:, 40:48],
                     AF.Exp, deps=SMALL.wdeps())
            SMALL.did_write(t)
            t = v.op("tensor_scalar_add", SMALL.ap[:, 40:48],
                     SMALL.ap[:, 40:48], 1.0, deps=SMALL.wdeps())
            SMALL.did_write(t)
            t = s.op("activation", SMALL.ap[:, 40:48], SMALL.ap[:, 40:48],
                     AF.Ln, deps=SMALL.wdeps())
            SMALL.did_write(t)
            t = v.op("scalar_tensor_tensor", DQH.ap[:, 8 * j:8 * j + 8],
                     SMALL.ap[:, 40:48], -1.0, SM2.ap[:, 48:56], OP.mult,
                     OP.mult, deps=rd(SMALL, SM2) + DQH.wdeps())
            DQH.did_write(t); SMALL.did_read(t); SM2.did_read(t)
        for dsto, nw in ((QTM, QNW), (KTM, KNW)):
            for j in range(4):
                sl = W.ap[:, dsto + j * 1024: dsto + (j + 1) * 1024]
                sl3 = sl.rearrange("p (h d) -> p h d", h=8)
                t = s.op("activation", G0.ap[:, 0:1024], sl, AF.Square,
                         deps=rd(W) + G0.wdeps())
                G0.did_write(t); W.did_read(t)
                t = v.op("reduce_sum", SMALL.ap[:, 0:8],
                         G0.ap[:, 0:1024].rearrange("p (h d) -> p h d", h=8),
                         AX.X, deps=rd(G0) + SMALL.wdeps())
                SMALL.did_write(t); G0.did_read(t)
                t = v.op("tensor_scalar", SMALL.ap[:, 0:8], SMALL.ap[:, 0:8],
                         float(1.0 / 128.0), float(EPS5), OP.mult, OP.add,
                         deps=SMALL.wdeps())
                SMALL.did_write(t)
                t = s.op("activation", SMALL.ap[:, 8:16], SMALL.ap[:, 0:8],
                         AF.Sqrt, deps=SMALL.wdeps())
                SMALL.did_write(t)
                t = v.op("reciprocal", SMALL.ap[:, 8:16], SMALL.ap[:, 8:16],
                         deps=SMALL.wdeps())
                SMALL.did_write(t)
                t = v.op("tensor_tensor", SMALL.ap[:, 16:24],
                         SMALL.ap[:, 8:16], SMALL.ap[:, 8:16], OP.mult,
                         deps=SMALL.wdeps())
                SMALL.did_write(t)
                t = v.op("tensor_tensor", SMALL.ap[:, 16:24],
                         SMALL.ap[:, 16:24], SMALL.ap[:, 0:8], OP.mult,
                         deps=SMALL.wdeps())
                SMALL.did_write(t)
                t = v.op("tensor_scalar", SMALL.ap[:, 16:24],
                         SMALL.ap[:, 16:24], -0.5, 1.5, OP.mult, OP.add,
                         deps=SMALL.wdeps())
                SMALL.did_write(t)
                t = v.op("tensor_tensor", SMALL.ap[:, 8:16],
                         SMALL.ap[:, 8:16], SMALL.ap[:, 16:24], OP.mult,
                         deps=SMALL.wdeps())
                SMALL.did_write(t)
                t = v.op("tensor_tensor", sl3, sl3,
                         SMALL.ap[:, 8:16].unsqueeze(2).broadcast_to([128, 8, 128]),
                         OP.mult, deps=rd(SMALL) + W.wdeps())
                W.did_write(t); SMALL.did_read(t)
                t = v.op("tensor_tensor", sl3, sl3,
                         W.ap[:, nw:nw + 128].unsqueeze(1).broadcast_to([128, 8, 128]),
                         OP.mult, deps=W.wdeps())
                W.did_write(t)
        for j in range(4):
            for h in range(8):
                sl = W.ap[:, RF + j * 1024 + h * 128: RF + j * 1024 + (h + 1) * 128]
                src = sl
                for r in range(4):
                    t = v.op("max", SMALL.ap[:, 8 * r:8 * r + 8], src,
                             deps=rd(W) + SMALL.wdeps())
                    SMALL.did_write(t); W.did_read(t)
                    t = v.op("match_replace", W.ap[:, TOP:TOP + 128],
                             SMALL.ap[:, 8 * r:8 * r + 8], src, -1e30,
                             deps=rd(W, SMALL) + W.wdeps())
                    W.did_write(t); SMALL.did_read(t)
                    src = W.ap[:, TOP:TOP + 128]
                t = v.op("reduce_sum", SMALL.ap[:, 32:33], SMALL.ap[:, 0:32],
                         AX.X, deps=SMALL.wdeps())
                SMALL.did_write(t)
                t = v.op("tensor_scalar_add", SMALL.ap[:, 32:33],
                         SMALL.ap[:, 32:33], 1e-9, deps=SMALL.wdeps())
                SMALL.did_write(t)
                t = v.op("reciprocal", SMALL.ap[:, 32:33], SMALL.ap[:, 32:33],
                         deps=SMALL.wdeps())
                SMALL.did_write(t)
                t = v.op("tensor_scalar", W.ap[:, MSK:MSK + 128], sl,
                         SMALL.ap[:, 31:32], None, OP.is_ge,
                         deps=rd(SMALL) + W.wdeps())
                W.did_write(t); SMALL.did_read(t)
                t = v.op("scalar_tensor_tensor", sl, sl, SMALL.ap[:, 32:33],
                         W.ap[:, MSK:MSK + 128], OP.mult, OP.mult,
                         deps=W.wdeps() + rd(SMALL))
                W.did_write(t); SMALL.did_read(t)
                t = v.op("tensor_scalar", sl, sl,
                         DQH.ap[:, 8 * j + h:8 * j + h + 1], None, OP.mult,
                         deps=W.wdeps() + rd(DQH))
                W.did_write(t); DQH.did_read(t)
        if ti == 0:
            slots = ["FF", "LL", "DD", "ELN", "E2", "SFM", "KS", "QF",
                     "KF2", "KHF", "KHT", "ASB"]
            sc2 = []
            for p in range(2):
                base = 17792 + p * 6912
                sc2.append({n: sub_buf(W, base + i * 128, 128)
                            for i, n in enumerate(slots)})
            sbh = [sub_buf(W, SOFF + h * 128, 128) for h in range(8)]
        oo = sub_buf(W, OO, 4096)
        for j in range(4):
            for h in range(8):
                p = (j * 8 + h) % 2
                sl = sc2[p]
                trbank = B["PS2"] if p == 0 else B["PS3"]
                psA = B["PS0"] if p == 0 else B["PS5"]
                psO = B["PS1"] if p == 0 else B["PS6"]
                psU = B["PS4"] if p == 0 else B["PS7"]

                def tr128(src_ap, src_b, dst_b):
                    tk = k.tensor.op("transpose", trbank.ap[:, 0:128],
                                     src_ap, pre.ident.ap[:],
                                     deps=trbank.wdeps() + rd(src_b, pre.ident))
                    trbank.did_write(tk); src_b.did_read(tk)
                    pre.ident.did_read(tk)
                    t2 = s.op("activation", dst_b.ap[:], trbank.ap[:, 0:128],
                              AF.Copy, deps=rd(trbank) + dst_b.wdeps())
                    dst_b.did_write(t2); trbank.did_read(t2)
                fsl = W.ap[:, RF + j * 1024 + h * 128:
                           RF + j * 1024 + (h + 1) * 128]
                tr128(fsl, W, sl["FF"])
                t = v.op("tensor_tensor_scan", sl["LL"].ap[:], sl["FF"].ap[:],
                         W.ap[:, ZER:ZER + 128], 0.0, OP.add, OP.add,
                         deps=rd(sl["FF"], W) + sl["LL"].wdeps())
                sl["LL"].did_write(t); sl["FF"].did_read(t); W.did_read(t)
                t = s.op("activation", sl["DD"].ap[:], sl["LL"].ap[:], AF.Exp,
                         deps=rd(sl["LL"]) + sl["DD"].wdeps())
                sl["DD"].did_write(t); sl["LL"].did_read(t)
                t = v.op("tensor_scalar_mul", sl["ELN"].ap[:], sl["LL"].ap[:],
                         -1.0, deps=rd(sl["LL"]) + sl["ELN"].wdeps())
                sl["ELN"].did_write(t); sl["LL"].did_read(t)
                t = s.op("activation", sl["ELN"].ap[:], sl["ELN"].ap[:],
                         AF.Exp, deps=sl["ELN"].wdeps())
                sl["ELN"].did_write(t)
                t = v.op("tensor_scalar", sl["E2"].ap[:], sl["LL"].ap[:],
                         -1.0, sl["LL"].ap[:, 127:128], OP.mult, OP.add,
                         deps=rd(sl["LL"]) + sl["E2"].wdeps())
                sl["E2"].did_write(t); sl["LL"].did_read(t)
                t = s.op("activation", sl["E2"].ap[:], sl["E2"].ap[:], AF.Exp,
                         deps=sl["E2"].wdeps())
                sl["E2"].did_write(t)
                t = s.op("activation", sl["SFM"].ap[:], sl["FF"].ap[:],
                         AF.Exp, deps=rd(sl["FF"]) + sl["SFM"].wdeps())
                sl["SFM"].did_write(t); sl["FF"].did_read(t)
                t = v.op("tensor_scalar", sl["SFM"].ap[:], sl["SFM"].ap[:],
                         -1.0, 1.0, OP.mult, OP.add, deps=sl["SFM"].wdeps())
                sl["SFM"].did_write(t)
                qsl = W.ap[:, QTM + j * 1024 + h * 128:
                           QTM + j * 1024 + (h + 1) * 128]
                tr128(qsl, W, sl["QF"])
                t = v.op("tensor_tensor", sl["QF"].ap[:], sl["QF"].ap[:],
                         sl["DD"].ap[:], OP.mult,
                         deps=rd(sl["DD"]) + sl["QF"].wdeps())
                sl["QF"].did_write(t); sl["DD"].did_read(t)
                ksl = W.ap[:, KTM + j * 1024 + h * 128:
                           KTM + j * 1024 + (h + 1) * 128]
                tr128(ksl, W, sl["KS"])
                t = v.op("tensor_tensor", sl["KS"].ap[:], sl["KS"].ap[:],
                         sl["SFM"].ap[:], OP.mult,
                         deps=rd(sl["SFM"]) + sl["KS"].wdeps())
                sl["KS"].did_write(t); sl["SFM"].did_read(t)
                t = v.op("tensor_tensor", sl["KF2"].ap[:], sl["KS"].ap[:],
                         sl["ELN"].ap[:], OP.mult,
                         deps=rd(sl["KS"], sl["ELN"]) + sl["KF2"].wdeps())
                sl["KF2"].did_write(t); sl["KS"].did_read(t)
                sl["ELN"].did_read(t)
                t = v.op("tensor_tensor", sl["KHF"].ap[:], sl["KS"].ap[:],
                         sl["E2"].ap[:], OP.mult,
                         deps=rd(sl["KS"], sl["E2"]) + sl["KHF"].wdeps())
                sl["KHF"].did_write(t); sl["KS"].did_read(t)
                sl["E2"].did_read(t)
                tr128(sl["KHF"].ap[:], sl["KHF"], sl["KHT"])
                tk = k.tensor.op("matmul", psA.ap[:, 0:128], sl["KF2"].ap[:],
                                 sl["QF"].ap[:], start=True, stop=True,
                                 deps=rd(sl["KF2"], sl["QF"]) + psA.wdeps())
                psA.did_write(tk); sl["KF2"].did_read(tk)
                sl["QF"].did_read(tk)
                t = v.op("tensor_tensor", sl["ASB"].ap[:], psA.ap[:, 0:128],
                         pre.triu.ap[:], OP.mult,
                         deps=rd(psA, pre.triu) + sl["ASB"].wdeps())
                sl["ASB"].did_write(t); psA.did_read(t); pre.triu.did_read(t)
                vsl = W.ap[:, VTM + j * 1024 + h * 128:
                           VTM + j * 1024 + (h + 1) * 128]
                tk = k.tensor.op("matmul", psO.ap[:, 0:128], sl["ASB"].ap[:],
                                 vsl, start=True, stop=False,
                                 deps=rd(sl["ASB"], W) + psO.wdeps(), inc=True)
                psO.did_write(tk); sl["ASB"].did_read(tk); W.did_read(tk)
                tk = k.tensor.op("matmul", psO.ap[:, 0:128], sl["QF"].ap[:],
                                 sbh[h].ap[:], start=False, stop=True,
                                 deps=rd(sl["QF"], sbh[h]), inc=True)
                psO.did_write(tk); sl["QF"].did_read(tk); sbh[h].did_read(tk)
                tk = k.tensor.op("matmul", psU.ap[:, 0:128], sl["KHT"].ap[:],
                                 vsl, start=True, stop=True,
                                 deps=rd(sl["KHT"], W) + psU.wdeps())
                psU.did_write(tk); sl["KHT"].did_read(tk); W.did_read(tk)
                t = v.op("scalar_tensor_tensor", sbh[h].ap[:], sbh[h].ap[:],
                         sl["DD"].ap[:, 127:128], psU.ap[:, 0:128],
                         OP.mult, OP.add,
                         deps=sbh[h].wdeps() + rd(sl["DD"], psU))
                sbh[h].did_write(t); sl["DD"].did_read(t); psU.did_read(t)
                t = s.op("activation",
                         oo.ap[:, j * 1024 + h * 128:j * 1024 + (h + 1) * 128],
                         psO.ap[:, 0:128], AF.Copy,
                         deps=rd(psO) + oo.wdeps())
                oo.did_write_acc(t); psO.did_read(t)
            osl = oo.ap[:, j * 1024:(j + 1) * 1024]
            t = s.op("activation", G0.ap[:, 0:1024], osl, AF.Square,
                     accum_out=SMALL.ap[:, 0:1],
                     deps=rd(oo) + G0.wdeps() + SMALL.wdeps())
            G0.did_write(t); oo.did_read(t); SMALL.did_write(t)
            t = v.op("tensor_scalar", SMALL.ap[:, 0:1], SMALL.ap[:, 0:1],
                     float(1.0 / 1024.0), float(EPS5), OP.mult, OP.add,
                     deps=SMALL.wdeps())
            SMALL.did_write(t)
            t = s.op("activation", SMALL.ap[:, 1:2], SMALL.ap[:, 0:1], AF.Sqrt,
                     deps=SMALL.wdeps())
            SMALL.did_write(t)
            t = v.op("reciprocal", SMALL.ap[:, 1:2], SMALL.ap[:, 1:2],
                     deps=SMALL.wdeps())
            SMALL.did_write(t)
            t = v.op("tensor_tensor", SMALL.ap[:, 2:3], SMALL.ap[:, 1:2],
                     SMALL.ap[:, 1:2], OP.mult, deps=SMALL.wdeps())
            SMALL.did_write(t)
            t = v.op("tensor_tensor", SMALL.ap[:, 2:3], SMALL.ap[:, 2:3],
                     SMALL.ap[:, 0:1], OP.mult, deps=SMALL.wdeps())
            SMALL.did_write(t)
            t = v.op("tensor_scalar", SMALL.ap[:, 2:3], SMALL.ap[:, 2:3],
                     -0.5, 1.5, OP.mult, OP.add, deps=SMALL.wdeps())
            SMALL.did_write(t)
            t = v.op("tensor_tensor", SMALL.ap[:, 1:2], SMALL.ap[:, 1:2],
                     SMALL.ap[:, 2:3], OP.mult, deps=SMALL.wdeps())
            SMALL.did_write(t)
            t = v.op("tensor_scalar", osl, osl, SMALL.ap[:, 1:2], None,
                     OP.mult, deps=oo.wdeps() + rd(SMALL))
            oo.did_write(t); SMALL.did_read(t)
            t = v.op("tensor_tensor", osl, osl, W.ap[:, GNW:GNW + 1024],
                     OP.mult, deps=oo.wdeps() + rd(W))
            oo.did_write(t); W.did_read(t)
        o_fm = sub_buf(G0, 0, 4096)
        emit_tm_to_fm(k, oo, o_fm, pre, B["PS2"], B["PS3"])
        merge_sub(W, oo)
        merge_sub(G0, o_fm)
        for half in range(2):
            stage_w(w["Wo"], half * 512, 512, dst=G1)
            for j in range(4):
                ps = B["PS5"] if j % 2 == 0 else B["PS6"]
                pairs = [(G0, G0.ap[:, c * 512 + j * 128:
                              c * 512 + j * 128 + 128],
                          G1, G1.ap[:, c * 512:(c + 1) * 512])
                         for c in range(8)]
                mm_group(k, ps, ps.ap[:, 0:512], pairs)
                db = B["SA"] if j % 2 == 0 else B["SB"]
                t = s.op("activation", db.ap[:, 0:512], ps.ap[:, 0:512],
                         AF.Copy, deps=rd(ps) + db.wdeps())
                db.did_write(t); ps.did_read(t)
                r0 = t0 + j * 128
                csl = slice(half * 512, (half + 1) * 512)
                tok = sy.dma(B["SC"].ap[:, 0:512], g_dram[r0:r0 + 128, csl],
                             deps=B["SC"].wdeps() + rd(g_buf))
                B["SC"].did_write(tok); g_buf.did_read(tok)
                tok = sy.dma(B["SC"].ap[:, 512:1024], x_in_ap[r0:r0 + 128, csl],
                             deps=B["SC"].wdeps() + rd(x_in_buf))
                B["SC"].did_write_acc(tok); x_in_buf.did_read(tok)
                t = v.op("tensor_tensor", db.ap[:, 0:512], db.ap[:, 0:512],
                         B["SC"].ap[:, 0:512], OP.mult,
                         deps=db.wdeps() + rd(B["SC"]))
                db.did_write(t); B["SC"].did_read(t)
                t = v.op("tensor_tensor", db.ap[:, 0:512], db.ap[:, 0:512],
                         B["SC"].ap[:, 512:1024], OP.add,
                         deps=db.wdeps() + rd(B["SC"]))
                db.did_write(t); B["SC"].did_read(t)
                tok = sy.dma(x_out_ap[r0:r0 + 128, csl], db.ap[:, 0:512],
                             deps=rd(db) + x_out_buf.wdeps())
                x_out_buf.did_write_acc(tok); db.did_read(tok)


def emit_l2_attn(k, B, pre, x_in_ap, x_in_buf, x_out_ap, x_out_buf,
                 g_dram, g_buf, w):
    v, s, sy, gp = k.vector, k.scalar, k.sync, k.gpsimd
    W = Buf(B["WRES"].ap.bitcast(F32))
    G0, G1 = B["G0"], B["G1"]
    SMALL = B["SMALL"]
    QKV, CTX, CTF = 0, 12288, 16384
    SCR0, SCR1, SCO = 20480, 21504, 22528
    BIA, BOU = 23552, 26624
    ONES = B["ONESROW"]

    tok = sy.dma(W.ap[0:1, BIA:BIA + 3072], w["b_qkv"][None, :],
                 deps=W.wdeps())
    W.did_write(tok)
    tok = sy.dma(W.ap[0:1, BOU:BOU + 1024], w["b_out"][None, :],
                 deps=W.wdeps())
    W.did_write_acc(tok)

    for ti in range(8):
        t0 = ti * 512
        emit_ln(k, x_in_ap, x_in_buf, G0, B["STATS"], G1, t0, pre)
        xn_fm = sub_buf(G1, 0, 4096)
        emit_tm_to_fm(k, G0, xn_fm, pre, B["PS2"], B["PS3"])
        alias_sync(G1, xn_fm)
        stg = [sub_buf(W, CTX, 4096), sub_buf(W, CTF, 4096)]
        qkvb = sub_buf(W, QKV, 12288)
        for p in range(6):
            sb = stg[p % 2]
            wdeps = sb.wdeps()
            first = True
            for c in range(8):
                t = sy.dma(sb.ap[:, c * 512:(c + 1) * 512],
                           w["W_qkv"][c * 128:(c + 1) * 128,
                                      p * 512:(p + 1) * 512], deps=wdeps)
                if first:
                    sb.did_write(t); first = False
                else:
                    sb.did_write_acc(t)
            for j in range(4):
                ps = B["PS0"] if j % 2 == 0 else B["PS1"]
                pairs = [(G1, G1.ap[:, c * 512 + j * 128:
                              c * 512 + j * 128 + 128],
                          sb, sb.ap[:, c * 512:(c + 1) * 512])
                         for c in range(8)]
                pairs.append((ONES, ONES.ap[0:1, 0:128],
                              W, W.ap[0:1, BIA + p * 512:BIA + (p + 1) * 512]))
                mm_group(k, ps, ps.ap[:, 0:512], pairs)
                t = s.op("activation",
                         qkvb.ap[:, j * 3072 + p * 512:j * 3072 + (p + 1) * 512],
                         ps.ap[:, 0:512], AF.Copy, deps=rd(ps) + qkvb.wdeps())
                qkvb.did_write_acc(t); ps.did_read(t)
        for b_ in stg:
            merge_sub(W, b_)
        merge_sub(W, qkvb)
        scrb = [sub_buf(W, SCR0, 1024), sub_buf(W, SCR1, 1024)]
        scob = sub_buf(W, SCO, 1024)
        ctxb = sub_buf(W, CTX, 4096)
        for j in range(4):
            qo = QKV + j * 3072
            q3 = W.ap[:, qo:qo + 3072].rearrange("p (h e) -> p h e", h=16)
            sco3 = scob.ap[:, j * 256:(j + 1) * 256].rearrange(
                "p (h g) -> p h g", h=16)
            for g in range(16):
                sc = scrb[g % 2]
                kg = q3[:, g:g + 1, 64:128].broadcast_to([128, 16, 64])
                eng = v if g < 6 else gp
                t = eng.op("tensor_tensor", sc.ap[:, 0:1024].rearrange(
                               "p (h e) -> p h e", h=16),
                           q3[:, :, 0:64], kg, OP.mult,
                           deps=rd(W) + sc.wdeps())
                sc.did_write(t); W.did_read(t)
                t = v.op("reduce_sum", sco3[:, :, g:g + 1],
                         sc.ap[:, 0:1024].rearrange("p (h e) -> p h e", h=16),
                         AX.X, deps=rd(sc) + scob.wdeps())
                scob.did_write_acc(t); sc.did_read(t)
            t = v.op("reduce_max", SMALL.ap[:, 0:16], sco3, AX.X,
                     deps=rd(scob) + SMALL.wdeps())
            SMALL.did_write(t); scob.did_read(t)
            t = v.op("tensor_scalar_mul", SMALL.ap[:, 0:16],
                     SMALL.ap[:, 0:16], 0.125, deps=SMALL.wdeps())
            SMALL.did_write(t)
            t = v.op("scalar_tensor_tensor", sco3, sco3, 0.125,
                     SMALL.ap[:, 0:16].unsqueeze(2).broadcast_to([128, 16, 16]),
                     OP.mult, OP.subtract,
                     deps=scob.wdeps() + rd(SMALL))
            scob.did_write(t); SMALL.did_read(t)
            t = s.op("activation", scob.ap[:, j * 256:(j + 1) * 256],
                     scob.ap[:, j * 256:(j + 1) * 256], AF.Exp,
                     deps=scob.wdeps())
            scob.did_write(t)
            t = v.op("reduce_sum", SMALL.ap[:, 16:32], sco3, AX.X,
                     deps=rd(scob) + SMALL.wdeps())
            SMALL.did_write(t); scob.did_read(t)
            t = v.op("reciprocal", SMALL.ap[:, 16:32], SMALL.ap[:, 16:32],
                     deps=SMALL.wdeps())
            SMALL.did_write(t)
            t = v.op("tensor_tensor", sco3, sco3,
                     SMALL.ap[:, 16:32].unsqueeze(2).broadcast_to([128, 16, 16]),
                     OP.mult, deps=scob.wdeps() + rd(SMALL))
            scob.did_write(t); SMALL.did_read(t)
            ctx3 = ctxb.ap[:, j * 1024:(j + 1) * 1024].rearrange(
                "p (h e) -> p h e", h=16)
            for g in range(16):
                sc = scrb[g % 2]
                vg = q3[:, g:g + 1, 128:192].broadcast_to([128, 16, 64])
                ag = sco3[:, :, g:g + 1].broadcast_to([128, 16, 64])
                eng = v if g < 6 else gp
                t = eng.op("tensor_tensor", sc.ap[:, 0:1024].rearrange(
                               "p (h e) -> p h e", h=16),
                           vg, ag, OP.mult,
                           deps=rd(W, scob) + sc.wdeps())
                sc.did_write(t); W.did_read(t); scob.did_read(t)
                if g == 0:
                    t = v.op("tensor_copy",
                             ctx3, sc.ap[:, 0:1024].rearrange(
                                 "p (h e) -> p h e", h=16),
                             deps=rd(sc) + ctxb.wdeps())
                else:
                    t = v.op("tensor_tensor", ctx3, ctx3,
                             sc.ap[:, 0:1024].rearrange("p (h e) -> p h e",
                                                        h=16),
                             OP.add, deps=rd(sc) + ctxb.wdeps())
                ctxb.did_write(t); sc.did_read(t)
        for b_ in scrb:
            merge_sub(W, b_)
        merge_sub(W, scob)
        merge_sub(W, ctxb)
        c_tm = sub_buf(W, CTX, 4096)
        c_fm = sub_buf(W, CTF, 4096)
        emit_tm_to_fm(k, c_tm, c_fm, pre, B["PS2"], B["PS3"])
        merge_sub(W, c_tm)
        merge_sub(W, c_fm)
        for half in range(2):
            wdeps = G1.wdeps()
            first = True
            for c in range(8):
                tt = sy.dma(G1.ap[:, c * 512:(c + 1) * 512],
                            w["W_out"][c * 128:(c + 1) * 128,
                                       half * 512:(half + 1) * 512],
                            deps=wdeps)
                if first:
                    G1.did_write(tt); first = False
                else:
                    G1.did_write_acc(tt)
            for j in range(4):
                ps = B["PS5"] if j % 2 == 0 else B["PS6"]
                pairs = [(W, W.ap[:, CTF + c * 512 + j * 128:
                              CTF + c * 512 + j * 128 + 128],
                          G1, G1.ap[:, c * 512:(c + 1) * 512])
                         for c in range(8)]
                pairs.append((ONES, ONES.ap[0:1, 0:128],
                              W, W.ap[0:1, BOU + half * 512:
                                      BOU + (half + 1) * 512]))
                mm_group(k, ps, ps.ap[:, 0:512], pairs)
                db = B["SA"] if j % 2 == 0 else B["SB"]
                t = s.op("activation", db.ap[:, 0:512], ps.ap[:, 0:512],
                         AF.Copy, deps=rd(ps) + db.wdeps())
                db.did_write(t); ps.did_read(t)
                r0 = t0 + j * 128
                csl = slice(half * 512, (half + 1) * 512)
                tok = sy.dma(B["SC"].ap[:, 0:512], g_dram[r0:r0 + 128, csl],
                             deps=B["SC"].wdeps() + rd(g_buf))
                B["SC"].did_write(tok); g_buf.did_read(tok)
                tok = sy.dma(B["SC"].ap[:, 512:1024], x_in_ap[r0:r0 + 128, csl],
                             deps=B["SC"].wdeps() + rd(x_in_buf))
                B["SC"].did_write_acc(tok); x_in_buf.did_read(tok)
                t = v.op("tensor_tensor", db.ap[:, 0:512], db.ap[:, 0:512],
                         B["SC"].ap[:, 0:512], OP.mult,
                         deps=db.wdeps() + rd(B["SC"]))
                db.did_write(t); B["SC"].did_read(t)
                t = v.op("tensor_tensor", db.ap[:, 0:512], db.ap[:, 0:512],
                         B["SC"].ap[:, 512:1024], OP.add,
                         deps=db.wdeps() + rd(B["SC"]))
                db.did_write(t); B["SC"].did_read(t)
                tok = sy.dma(x_out_ap[r0:r0 + 128, csl], db.ap[:, 0:512],
                             deps=rd(db) + x_out_buf.wdeps())
                x_out_buf.did_write_acc(tok); db.did_read(tok)


DEBUG_DUMPS = False


def build_program():
    nc = bass.Bass("TRN2", target_bir_lowering=False)

    def inp(name, shape):
        return nc.dram_tensor(name, shape, F32, kind="ExternalInput").ap()

    x = inp("x", [4096, 1024])
    w = {}
    for n in ("L0_attn_Wq", "L0_attn_Wk", "L0_attn_Wv", "L0_attn_Wr",
              "L0_attn_Wo", "L1_attn_W_out", "L2_attn_W_out",
              "L3_attn_W_proj"):
        w[n] = inp(n, [1024, 1024])
    w["L0_attn_Wa"] = inp("L0_attn_Wa", [1024, 8])
    for n in ("L0_attn_A_log", "L0_attn_dt_bias"):
        w[n] = inp(n, [8])
    for n in ("L0_attn_q_norm_w", "L0_attn_k_norm_w"):
        w[n] = inp(n, [128])
    w["L0_attn_g_norm_w"] = inp("L0_attn_g_norm_w", [1024])
    w["L0_ffn_w1T"] = inp("L0_ffn_w1T", [1024, 4096])
    w["L0_ffn_w2T"] = inp("L0_ffn_w2T", [4096, 1024])
    for n in ("L1_attn_W_qkv", "L2_attn_W_qkv", "L3_attn_W_attn"):
        w[n] = inp(n, [1024, 3072])
    for n in ("L1_attn_b_qkv", "L2_attn_b_qkv", "L3_attn_b_attn"):
        w[n] = inp(n, [3072])
    for n in ("L1_attn_b_out", "L2_attn_b_out", "L3_attn_b_proj"):
        w[n] = inp(n, [1024])
    w["L1_ffn_Ws1"] = inp("L1_ffn_Ws1", [1024, 512])
    w["L1_ffn_Ws2"] = inp("L1_ffn_Ws2", [512, 1024])
    w["L1_ffn_W1"] = inp("L1_ffn_W1", [8, 1024, 512])
    w["L1_ffn_W2"] = inp("L1_ffn_W2", [8, 512, 1024])
    w["L1_ffn_Wr"] = inp("L1_ffn_Wr", [1024, 8])
    w["L2_ffn_w_gateT"] = inp("L2_ffn_w_gateT", [1024, 4096])
    w["L2_ffn_w_upT"] = inp("L2_ffn_w_upT", [1024, 4096])
    w["L2_ffn_w_downT"] = inp("L2_ffn_w_downT", [4096, 1024])
    w["L3_ffn_gate_w"] = inp("L3_ffn_gate_w", [4096, 1024])
    w["L3_ffn_up_w"] = inp("L3_ffn_up_w", [4096, 1024])
    w["L3_ffn_down_w"] = inp("L3_ffn_down_w", [1024, 4096])
    w["L3_ffn_s_in"] = inp("L3_ffn_s_in", [1])
    w["L3_ffn_s_in_down"] = inp("L3_ffn_s_in_down", [1])

    xo = nc.dram_tensor("xo", [4096, 1024], F32, kind="ExternalOutput").ap()
    xA = nc.dram_tensor("xA", [4096, 1024], F32, kind="Internal").ap()
    xB = nc.dram_tensor("xB", [4096, 1024], F32, kind="Internal").ap()
    g = nc.dram_tensor("g_scr", [4096, 1024], F32, kind="Internal").ap()
    hq = nc.dram_tensor("hq_scr", [4096, 4096], BF16, kind="Internal").ap()
    dqh = nc.dram_tensor("dqh_scr", [4096], F32, kind="Internal").ap()
    l3q = nc.dram_tensor("l3q", [2, 8, 128, 4096], BF16, kind="Internal").ap()
    l3k = nc.dram_tensor("l3k", [2, 8, 128, 4096], BF16, kind="Internal").ap()
    l3v = nc.dram_tensor("l3v", [32, 128, 1040], F32, kind="Internal").ap()

    with ExitStack() as ctx:
        k = K(nc, ctx)
        pre = Prelude(k)
        B = make_buffers(k)
        pre.emit(k)
        emit_buffer_init(k, B)
        xib, xab, xbb, xob = Buf(x), Buf(xA), Buf(xB), Buf(xo)
        gb, hqb, dqhb = Buf(g), Buf(hq), Buf(dqh)
        scr = {"q": Buf(l3q), "kk": Buf(l3k), "v": Buf(l3v)}
        phases = [
            ("L0_ffn", lambda: emit_ffn_bitlin(
                k, B, pre, x, xib, g, gb,
                {"w1": w["L0_ffn_w1T"], "w2": w["L0_ffn_w2T"]},
                "gelu", hq, hqb, dqh, dqhb)),
            ("L0_attn", lambda: emit_l0_attn(
                k, B, pre, x, xib, xA, xab, g, gb,
                {n: w["L0_attn_" + n] for n in
                 ("Wq", "Wk", "Wv", "Wr", "Wa", "A_log", "dt_bias",
                  "q_norm_w", "k_norm_w", "g_norm_w", "Wo")})),
            ("L1_ffn", lambda: emit_moe(
                k, B, pre, xA, xab, g, gb,
                {n: w["L1_ffn_" + n] for n in
                 ("Ws1", "Ws2", "W1", "W2", "Wr")})),
            ("L1_attn", lambda: emit_l1_attn(
                k, B, pre, xA, xab, xB, xbb, g, gb,
                {n: w["L1_attn_" + n] for n in
                 ("W_qkv", "b_qkv", "W_out", "b_out")})),
            ("L2_ffn", lambda: emit_ffn_bitlin(
                k, B, pre, xB, xbb, g, gb,
                {"wg": w["L2_ffn_w_gateT"], "wu": w["L2_ffn_w_upT"],
                 "wd": w["L2_ffn_w_downT"]},
                "swiglu", hq, hqb, dqh, dqhb)),
            ("L2_attn", lambda: emit_l2_attn(
                k, B, pre, xB, xbb, xA, xab, g, gb,
                {n: w["L2_attn_" + n] for n in
                 ("W_qkv", "b_qkv", "W_out", "b_out")})),
            ("L3_ffn", lambda: emit_ffn_nvfp4(
                k, B, pre, xA, xab, g, gb,
                {"gate_w": w["L3_ffn_gate_w"], "up_w": w["L3_ffn_up_w"],
                 "down_w": w["L3_ffn_down_w"], "s_in": w["L3_ffn_s_in"],
                 "s_in_down": w["L3_ffn_s_in_down"]},
                hq, hqb)),
            ("L3_attn", lambda: emit_l3_attn(
                k, B, pre, xA, xab, xo, xob, g, gb,
                {"W_attn": w["L3_attn_W_attn"],
                 "b_attn": w["L3_attn_b_attn"],
                 "W_proj": w["L3_attn_W_proj"],
                 "b_proj": w["L3_attn_b_proj"]}, scr)),
        ]
        dumps = {}
        if DEBUG_DUMPS:
            for i in range(4):
                dumps[i] = nc.dram_tensor(f"dump{i}", [4096, 1024], F32,
                                          kind="ExternalOutput").ap()
        dump_after = {"L0_attn": (0, xA), "L1_attn": (1, xB),
                      "L2_attn": (2, xA), "L3_attn": (3, xo)}
        for name, fn in phases:
            with nc.named_scope(name):
                fn()
                barrier(k)
                if DEBUG_DUMPS and name in dump_after:
                    i, src = dump_after[name]
                    k.sync.dma(dumps[i], src[:, :])
                    barrier(k)
    return nc


def kernel(**inputs):
    nc = build_program()
    x = inputs["L0_attn_hidden_states"]
    base = {}
    for n, a in inputs.items():
        if n == "L0_attn_hidden_states":
            continue
        if n in ("L0_ffn_w1", "L0_ffn_w2", "L2_ffn_w_gate", "L2_ffn_w_up",
                 "L2_ffn_w_down"):
            nn = {"L0_ffn_w1": "L0_ffn_w1T", "L0_ffn_w2": "L0_ffn_w2T",
                  "L2_ffn_w_gate": "L2_ffn_w_gateT",
                  "L2_ffn_w_up": "L2_ffn_w_upT",
                  "L2_ffn_w_down": "L2_ffn_w_downT"}[n]
            base[nn] = np.ascontiguousarray(np.asarray(a).T)
        else:
            base[n] = np.asarray(a)
    inmaps = [dict(base, x=np.ascontiguousarray(x[b])) for b in range(8)]
    res = run_bass_kernel_spmd(nc, inmaps, core_ids=list(range(8)))
    out = np.stack([np.asarray(res.results[b]["xo"]) for b in range(8)])
    return out.astype(np.float32)
